# version 05
# baseline (speedup 1.0000x reference)
import numpy as np
import ml_dtypes
import concourse.bass as bass
import concourse.mybir as mybir
from concourse.bass_utils import run_bass_kernel_spmd

F32 = mybir.dt.float32
BF16 = mybir.dt.bfloat16
AF = mybir.ActivationFunctionType
ALU = mybir.AluOpType
AX = mybir.AxisListType

ENGINES = ("sync", "tensor", "vector", "scalar", "gpsimd")
T = 2048


class Sched:

    CHAIN = ("vector", "scalar", "gpsimd")

    def __init__(self, nc):
        self.nc = nc
        self.q = {e: [] for e in ENGINES}
        self.counts = {}
        self.hmap = {}
        self.hbase = {}
        self.htotal = {}
        self.ns = 0
        self.ns_names = {0: []}
        self.free_handles = []
        self._ctxs = []
        self.marker = {e: None for e in ENGINES}
        self.chgen = 0
        self.semeng = {}
        self.dmasems = set()
        self.barn = 0
        self.sem("barA")
        self.sem("barB")

    def _pname(self, name):
        if name in ("barA", "barB"):
            return name
        return f"p{self.ns}_{name}"

    def sem(self, name):
        pname = self._pname(name)
        if pname not in self.hmap:
            if self.free_handles and pname not in ("barA", "barB"):
                h = self.free_handles.pop()
            else:
                cm = self.nc.semaphore(pname)
                h = cm.__enter__()
                self._ctxs.append(cm)
                self.htotal[h] = 0
            self.hmap[pname] = h
            self.hbase[pname] = self.htotal[h]
            self.counts[pname] = 0
            self.ns_names.setdefault(self.ns, []).append(pname)
        return pname

    def val(self, s):
        return self.counts[self._pn2(s)]

    def _pn2(self, s):
        return s if s in self.counts else self._pname(s)

    def after(self, *sems):
        return [(self._pn2(s), self.counts[self._pn2(s)]) for s in sems]

    def mark(self, eng):
        m = self.marker[eng]
        return [m] if m is not None else []

    def rotate_chain(self):
        self.chgen += 1
        for e in self.CHAIN:
            self.marker[e] = None

    def nsnew(self):
        self.ns += 1
        self.ns_names.setdefault(self.ns, [])
        for k in list(self.ns_names):
            if k <= self.ns - 2:
                for pname in self.ns_names.pop(k):
                    self.free_handles.append(self.hmap[pname])

    def barrier(self):
        self.barn += 1
        self.sem("barA")
        self.op("tensor", lambda e: e.nop(), inc=[("barA", 1)])
        self.op("sync", lambda e: e.nop(), inc=[("barA", 1)])
        w = (self.mark("vector") + self.mark("scalar") + self.mark("gpsimd")
             + [("barA", self.counts["barA"])])
        self.op("gpsimd", lambda e: e.nop(), wait=w, inc=[("barB", 1)], chain=False)
        tgt = [("barB", self.counts["barB"])]
        for e in ENGINES:
            self.op(e, lambda eng: None, wait=tgt, chain=False)
        for e in self.CHAIN:
            self.marker[e] = None
        self.nsnew()

    def op(self, eng, emit, wait=(), inc=(), chain=True, is_dma=False):
        wait = [(self._pn2(s), v) for s, v in wait]
        if eng in self.CHAIN and chain:
            assert not inc, f"user incs not allowed on chained {eng} op"
            if self.marker[eng] is not None:
                wait.append(self.marker[eng])
            cs = self.sem(f"ch{self.chgen}_{eng}")
            assert self.semeng.setdefault(cs, eng) == eng
            self.counts[cs] += 1
            self.htotal[self.hmap[cs]] += 1
            inc2 = [(cs, 1)]
            self.marker[eng] = (cs, self.counts[cs])
            self.q[eng].append((wait, emit, inc2))
            return self.marker[eng]
        inc2 = []
        for s, n in inc:
            s = self._pn2(s)
            if s not in ("barA", "barB"):
                if is_dma:
                    self.dmasems.add(s)
                else:
                    assert s not in self.dmasems, f"sem {s} used as dma+compute"
                assert self.semeng.setdefault(s, eng) == eng, \
                    f"sem {s} inc'd by {self.semeng[s]} and {eng}"
            self.counts[s] += n
            self.htotal[self.hmap[s]] += n
            inc2.append((s, n))
        self.q[eng].append((wait, emit, inc2))
        return inc2[-1] if inc2 else None

    def dma(self, eng, out, in_, sem, wait=(), **kw):
        self.op(eng, lambda e, out=out, in_=in_, kw=kw: e.dma_start(out=out, in_=in_, **kw),
                wait=wait, inc=[(sem, 16)], chain=False, is_dma=True)
        return (self._pn2(sem), self.counts[self._pn2(sem)])

    def emit_all(self, block):
        for ename in ENGINES:
            ops = self.q[ename]
            if not ops:
                continue

            def mk(ops, sched=self):
                def body(eng):
                    lastw = {}
                    for wait, emit, inc in ops:
                        for s, v in wait:
                            hv = sched.hbase[s] + v
                            h = sched.hmap[s]
                            if v > 0 and lastw.get(h, -1) < hv:
                                eng.wait_ge(h, hv)
                                lastw[h] = hv
                        instr = emit(eng)
                        for s, n in inc:
                            if instr is None:
                                instr = eng.nop()
                            instr.then_inc(sched.hmap[s], n)
                return body

            getattr(block, ename)(mk(list(ops)))

    def close(self):
        for cm in reversed(self._ctxs):
            cm.__exit__(None, None, None)


class NS:
    def __init__(self, **kw):
        self.__dict__.update(kw)


class Mem:

    def __init__(self, nc):
        self.nc = nc
        self.base = (nc.sbuf_base + 31) // 32 * 32
        self.top = nc.sbuf_top
        self.off = self.base

    def alloc(self, name, shape, dtype, at=None):
        per_part = int(np.prod(shape[1:])) * mybir.dt.size(dtype)
        off = self.off if at is None else at
        off = (off + 31) // 32 * 32
        assert off + per_part <= self.top, (
            f"SBUF overflow: {name} at {off} + {per_part} > {self.top}")
        h = self.nc.alloc_sbuf_tensor_at(name, shape, dtype, offset=off)
        if at is None:
            self.off = off + per_part
        return h

    def mark(self):
        return self.off

    def reset(self, mark):
        self.off = mark



def host_fm(x):
    Tn, Dn = x.shape
    return np.ascontiguousarray(x.T.reshape(Dn // 128, 128, Tn).transpose(1, 0, 2))


def host_fm_inv(y):
    p, nch, Tn = y.shape
    return np.ascontiguousarray(y.transpose(1, 0, 2).reshape(nch * 128, Tn).T)


def host_w(w):
    Din, Dout = w.shape
    return np.ascontiguousarray(w.reshape(Din // 128, 128, Dout).transpose(1, 0, 2))


def host_col(v):
    Dn = v.shape[0]
    return np.ascontiguousarray(v.reshape(Dn // 128, 128).T)



def emit_ln_sbuf(C, xs, lnb, name, nchunk=8, Tn=T, ready=(), affine=None):
    s = C.sched
    nts = Tn // 512
    Dn = nchunk * 128
    mmsum = s.sem(name + "_mmsum")
    mmsq = s.sem(name + "_mmsq")
    ps_sum, ps_sq, ps_b0, ps_b1 = C.pb[0], C.pb[1], C.pb[2], C.pb[3]
    sq8, bc0, bc1 = C.sq8, C.bc0, C.bc1
    r_mu, r_e2, r_tm = C.r_mu, C.r_e2, C.r_tm
    sq_marks = {}
    last_apply = None
    prev_bc_copies = None
    prev_row_reads = None
    for ts in range(nts):
        tsl = slice(ts * 512, ts * 512 + 512)
        for c in range(nchunk):
            w = list(ready)
            if ts > 0:
                w += [(mmsq, ts)]
                w += [last_apply]
            sq_marks[c] = s.op("scalar", lambda e, c=c, tsl=tsl: e.activation(
                sq8[:, c, :], xs[:, c, tsl], AF.Square), wait=w)
        for c in range(nchunk):
            w = list(ready) if c == 0 else []
            if c == 0 and prev_row_reads:
                w += prev_row_reads
            s.op("tensor", lambda e, c=c, tsl=tsl: e.matmul(
                ps_sum[0:1, 0:512], C.ones_col[:, 0:1], xs[:, c, tsl],
                start=(c == 0), stop=(c == nchunk - 1)),
                 wait=w, inc=([(mmsum, 1)] if c == nchunk - 1 else []))
        for c in range(nchunk):
            s.op("tensor", lambda e, c=c: e.matmul(
                ps_sq[0:1, 0:512], C.ones_col[:, 0:1], sq8[:, c, :],
                start=(c == 0), stop=(c == nchunk - 1)),
                 wait=[sq_marks[c]],
                 inc=([(mmsq, 1)] if c == nchunk - 1 else []))
        m_mu = s.op("vector", lambda e: e.tensor_scalar(
            r_mu[:], ps_sum[0:1, 0:512], 1.0 / Dn, None, ALU.mult),
             wait=[(mmsum, ts + 1)] + (prev_bc_copies or []))
        m_e2 = s.op("vector", lambda e: e.tensor_scalar(
            r_e2[:], ps_sq[0:1, 0:512], 1.0 / Dn, None, ALU.mult),
             wait=[(mmsq, ts + 1)])
        prev_row_reads = [m_e2]
        s.op("vector", lambda e: e.tensor_tensor(
            r_tm[:], r_mu[:], r_mu[:], ALU.mult))
        m_var = s.op("vector", lambda e: e.tensor_tensor(
            r_e2[:], r_e2[:], r_tm[:], ALU.subtract))
        s.op("scalar", lambda e: e.activation(
            r_tm[:], r_e2[:], AF.Ln, bias=C.cpool[0:1, 0:1]),
             wait=[m_var] + C.cready)
        m_rstd = s.op("scalar", lambda e: e.activation(
            r_e2[:], r_tm[:], AF.Exp, scale=-0.5))
        bcm = s.sem(name + "_bcm")
        s.op("tensor", lambda e: e.matmul(
            ps_b0[:, 0:512], C.ones_row[0:1, 0:128], r_e2[:],
            start=True, stop=True),
             wait=[m_rstd] + (prev_bc_copies or []), inc=[(bcm, 1)])
        s.op("tensor", lambda e: e.matmul(
            ps_b1[:, 0:512], C.ones_row[0:1, 0:128], r_mu[:],
            start=True, stop=True),
             wait=[m_mu] + (prev_bc_copies or []), inc=[(bcm, 1)])
        c0 = s.op("scalar", lambda e: e.copy(bc0[:], ps_b0[:, 0:512]),
                  wait=[(bcm, 2 * (ts + 1))] + ([last_apply] if last_apply else []))
        c1 = s.op("scalar", lambda e: e.copy(bc1[:], ps_b1[:, 0:512]))
        prev_bc_copies = [c0, c1]
        for c in range(nchunk):
            s.op("vector", lambda e, c=c, tsl=tsl: e.tensor_tensor(
                sq8[:, c, :], xs[:, c, tsl], bc1[:], ALU.subtract),
                 wait=[c1, (mmsq, ts + 1)])
            if affine is None:
                last_apply = s.op("vector", lambda e, c=c, tsl=tsl: e.tensor_tensor(
                    lnb[:, c, tsl], sq8[:, c, :], bc0[:], ALU.mult))
            else:
                gfn, bfn = affine
                s.op("vector", lambda e, c=c: e.tensor_tensor(
                    sq8[:, c, :], sq8[:, c, :], bc0[:], ALU.mult))
                last_apply = s.op("vector", lambda e, c=c, tsl=tsl: e.tensor_scalar(
                    lnb[:, c, tsl], sq8[:, c, :], gfn(c), bfn(c),
                    ALU.mult, ALU.add))
    return last_apply


def setup_prog(nc):
    mem = Mem(nc)
    C = NS(sched=Sched(nc), nc=nc, mem=mem)
    C._psum_ctxs = []

    def ps(name, shape, dt):
        cm = nc.psum_tensor(name, shape, dt)
        h = cm.__enter__()
        C._psum_ctxs.append(cm)
        return h

    C.pbS = ps("pbS", [128, 1536], F32)
    C.pb4 = ps("pb4", [128, 512], F32)
    C.pb5 = ps("pb5", [128, 512], F32)
    C.pb6 = ps("pb6", [128, 512], F32)
    C.pbTa = ps("pbTa", [128, 1024], BF16)
    C.pbTb = ps("pbTb", [128, 1024], BF16)
    C.pb = [C.pbS[:, 0:512], C.pbS[:, 512:1024], C.pbS[:, 1024:1536], C.pb4[:]]
    alloc_common(mem, C)
    C.common_end = mem.mark()
    return C, mem


def finish_prog(C):
    with C.nc.Block() as block:
        C.sched.emit_all(block)
    C.sched.close()
    for cm in reversed(C._psum_ctxs):
        cm.__exit__(None, None, None)


def alloc_common(mem, C, Tn=T):
    C.ones_col = mem.alloc("ones_col", [128, 8], F32)
    C.ones_row = mem.alloc("ones_row", [128, 128], F32)
    C.ones_row_bf = mem.alloc("ones_row_bf", [128, 512], BF16)
    C.eye = mem.alloc("eye", [128, 128], F32)
    C.eyeb = mem.alloc("eyeb", [128, 128], BF16)
    C.cpool = mem.alloc("cpool", [128, 8], F32)
    C.cvals = [1e-5, 1.0, 1e-6, 1e-9, 0.0, 0.5, 2.0, -1.0]
    C.EPS5 = C.cpool[:, 0:1]
    C.EPS6 = C.cpool[:, 2:3]
    C.lnscr_off = (mem.mark() + 31) // 32 * 32
    C.sq8 = mem.alloc("ln_sq8", [128, 8, 512], F32)
    C.r_mu = mem.alloc("ln_rmu", [1, 512], F32)
    C.r_e2 = mem.alloc("ln_re2", [1, 512], F32)
    C.r_tm = mem.alloc("ln_rtm", [1, 512], F32)
    C.bc0 = mem.alloc("ln_bc0", [128, 512], F32)
    C.bc1 = mem.alloc("ln_bc1", [128, 512], F32)
    mem.reset(max(mem.mark(), C.lnscr_off + 32768))


def emit_init_consts(C):
    s = C.sched
    s.op("vector", lambda e: e.memset(C.ones_col[:], 1.0))
    for i, v in enumerate(C.cvals):
        s.op("vector", lambda e, i=i, v=v: e.memset(C.cpool[:, i:i + 1], v))
    s.op("vector", lambda e: e.memset(C.ones_row[:], 1.0))
    m = s.op("vector", lambda e: e.memset(C.ones_row_bf[:], 1.0))
    C.cready = [m]
    return m



def build_l0aa(nc, C, mem, x_dram, y_dram, W, tag="l0aa"):
    s = C.sched
    GN_N = float(T * 64)

    C.hmask = mem.alloc(tag + "_hmask", [128, 2], F32)
    C.hmaskT = mem.alloc(tag + "_hmaskT", [2, 128], F32)
    LNB = mem.alloc(tag + "_lnb", [128, 8, T], BF16)
    COLS = mem.alloc(tag + "_cols", [128, 56], F32)
    DCOLS = mem.alloc(tag + "_dcols", [128, 40], F32)
    SCOL = mem.alloc(tag + "_scol", [128, 2], F32)
    STAT2 = mem.alloc(tag + "_stat2", [2, 8], F32)
    CBUF = mem.alloc(tag + "_cbuf", [128, 4], F32)
    mA = mem.mark()
    XS = mem.alloc(tag + "_xs", [128, 8, T], F32)
    endA = mem.mark()
    XR = mem.alloc(tag + "_xr", [128, 8, T], BF16, at=mA)
    XK = mem.alloc(tag + "_xk", [128, 8, T], BF16, at=mA + 32768)
    mem.reset(endA)
    XV = mem.alloc(tag + "_xv", [128, 8, T], BF16)
    WK = mem.alloc(tag + "_wk", [128, 8, 1024], BF16, at=C.lnscr_off)
    WV = mem.alloc(tag + "_wv", [128, 8, 1024], BF16, at=C.lnscr_off + 16384)
    mB = mem.mark()
    WR = mem.alloc(tag + "_wr", [128, 8, 1024], BF16, at=mB)
    EK = mem.alloc(tag + "_ek", [128, T], F32)
    VB = mem.alloc(tag + "_vb", [128, T], F32)
    PB = mem.alloc(tag + "_pb", [128, T], F32)
    NUM = mem.alloc(tag + "_num", [128, T], F32)
    DEN = mem.alloc(tag + "_den", [128, T], F32)
    WO = mem.alloc(tag + "_wo", [128, 8, 1024], BF16, at=mB)
    XST = [mem.alloc(tag + f"_xst{i}", [128, 512], F32, at=mB + 16384 + 2048 * i)
           for i in range(2)]
    YST = [mem.alloc(tag + f"_yst{i}", [128, 512], F32, at=mB + 16384 + 4096 + 2048 * i)
           for i in range(2)]
    NORMED = XR
    RSIG = LNB

    xld = s.sem("g"  + "_xld")
    cld = s.sem("g"  + "_cld")
    wld1 = s.sem("g"  + "_wld1")
    wld2 = s.sem("g"  + "_wld2")
    wld3 = s.sem("g"  + "_wld3")
    rmm = s.sem("g"  + "_rmm")
    kvmm = s.sem("g"  + "_kvmm")
    auxmm = s.sem("g"  + "_auxmm")
    bcmm2 = s.sem("g"  + "_bcmm2")
    womm = s.sem("g"  + "_womm")
    xin2 = [s.sem("g"  + "_xin2a"), s.sem("g"  + "_xin2b")]
    yout = s.sem("g"  + "_yout")

    for i in range(4):
        s.dma("sync", XS[:, 2 * i:2 * i + 2, :], x_dram[:, 2 * i:2 * i + 2, :], xld)
    s.dma("sync", COLS[:], W["cols"], cld)
    s.dma("sync", C.hmask[:], W["hmask"], cld)
    s.dma("sync", C.hmaskT[:], W["hmaskT"], cld)
    s.dma("gpsimd", WR[:], W["wr"], wld1)
    hm_ready = s.after(cld)

    lam_m = None
    for j, so in enumerate((0, 8, 16)):
        lam_m = s.op("vector", lambda e, j=j, so=so: e.tensor_scalar(
            DCOLS[:, 8 * j:8 * j + 8], COLS[:, so:so + 8], -1.0, 1.0,
            ALU.mult, ALU.add), wait=s.after(cld))
    s.op("scalar", lambda e: e.activation(DCOLS[:, 24:32], COLS[:, 24:32], AF.Exp),
         wait=s.after(cld))
    s.op("scalar", lambda e: e.activation(DCOLS[:, 24:32], DCOLS[:, 24:32],
                                          AF.Exp, scale=-1.0))
    eu_m = s.op("scalar", lambda e: e.activation(DCOLS[:, 32:40], COLS[:, 32:40], AF.Exp))

    emit_init_consts(C)
    LNM = emit_ln_sbuf(C, XS, LNB, "g_ln", ready=s.after(xld) + C.cready)

    s.dma("gpsimd", WK[:], W["wk"], wld2, wait=[LNM])
    s.dma("gpsimd", WV[:], W["wv"], wld2, wait=[LNM])
    W2R = s.after(wld2)

    mix_last = {}
    for (XM, co, do, nm) in ((XR, 0, 0, "r"), (XK, 8, 8, "k"), (XV, 16, 16, "v")):
        for c in range(8):
            s.op("vector", lambda e, XM=XM, c=c, co=co: e.tensor_scalar(
                XM[:, c, :], LNB[:, c, :], COLS[:, co + c:co + c + 1], None, ALU.mult),
                 wait=[LNM, lam_m])
            mix_last[nm] = s.op("vector", lambda e, XM=XM, c=c, do=do: e.scalar_tensor_tensor(
                XM[:, c, 1:T], LNB[:, c, 0:T - 1], DCOLS[:, do + c:do + c + 1],
                XM[:, c, 1:T], ALU.mult, ALU.add))
    MIXED = mix_last["v"]

    rbanks = [C.pb5[:], C.pb6[:], C.pb[0], C.pb[1]]
    revm = {}
    ri = 0
    for ec in range(8):
        esl = slice(ec * 128, ec * 128 + 128)
        for ts in range(4):
            tsl = slice(ts * 512, ts * 512 + 512)
            ps = rbanks[ri % 4]
            waits = [MIXED] + s.after(wld1)
            if ri >= 4:
                waits.append(revm[ri - 4])

            def mms(e, esl=esl, tsl=tsl, ps=ps):
                last = None
                for dc in range(8):
                    last = e.matmul(ps[:, 0:512], WR[:, dc, esl], XR[:, dc, tsl],
                                    start=(dc == 0), stop=(dc == 7))
                return last

            s.op("tensor", mms, wait=waits, inc=[(rmm, 1)])
            revm[ri] = s.op("scalar", lambda e, ps=ps, tsl=tsl, ec=ec:
                            e.activation(RSIG[:, ec, tsl], ps[:, 0:512], AF.Sigmoid),
                            wait=[(rmm, ri + 1), MIXED])
            ri += 1
    RDONE = revm[31]

    kvi = 0
    kvbanks = [C.pb5[:], C.pb6[:]]
    kvevm = {}
    ckdm = {}
    prev_stat_rd = None
    prev_cbuf_rd = None
    for ec in range(8):
        esl = slice(ec * 128, ec * 128 + 128)
        for which, (WX, XMIX, dst) in enumerate(((WK, XK, EK), (WV, XV, VB))):
            for ts in range(4):
                tsl = slice(ts * 512, ts * 512 + 512)
                ps = kvbanks[kvi % 2]
                waits = [MIXED] + W2R
                if kvi < 2:
                    waits.append(RDONE)
                else:
                    waits.append(kvevm[kvi - 2])

                def mms(e, WX=WX, XMIX=XMIX, esl=esl, tsl=tsl, ps=ps):
                    last = None
                    for dc in range(8):
                        last = e.matmul(ps[:, 0:512], WX[:, dc, esl], XMIX[:, dc, tsl],
                                        start=(dc == 0), stop=(dc == 7))
                    return last

                s.op("tensor", mms, wait=waits, inc=[(kvmm, 1)])
                ewaits = [(kvmm, kvi + 1)]
                if ec > 0:
                    ewaits.append(ckdm[ec - 1])
                if which == 0:
                    kvevm[kvi] = s.op("scalar", lambda e, ps=ps, tsl=tsl: e.activation(
                        EK[:, tsl], ps[:, 0:512], AF.Exp), wait=ewaits)
                else:
                    kvevm[kvi] = s.op("scalar", lambda e, ps=ps, tsl=tsl: e.copy(
                        VB[:, tsl], ps[:, 0:512]), wait=ewaits)
                kvi += 1
        wbc = bass.AP(DCOLS, 24 + ec, [[40, 128], [0, T]])
        euc = DCOLS[:, 32 + ec:32 + ec + 1]
        s.op("vector", lambda e: e.tensor_tensor(PB[:], EK[:], VB[:], ALU.mult),
             wait=[kvevm[kvi - 1], eu_m])
        s.op("vector", lambda e, wbc=wbc: e.tensor_tensor_scan(
            NUM[:], wbc, PB[:], 0.0, ALU.mult, ALU.add))
        s.op("vector", lambda e, wbc=wbc: e.tensor_tensor_scan(
            DEN[:], wbc, EK[:], 0.0, ALU.mult, ALU.add))
        s.op("vector", lambda e, euc=euc: e.scalar_tensor_tensor(
            VB[:, 1:T], PB[:, 1:T], euc, NUM[:, 0:T - 1], ALU.mult, ALU.add))
        s.op("vector", lambda e, euc=euc: e.tensor_scalar(
            VB[:, 0:1], PB[:, 0:1], euc, None, ALU.mult))
        s.op("vector", lambda e, euc=euc: e.tensor_scalar(
            EK[:], EK[:], euc, 1e-9, ALU.mult, ALU.add))
        s.op("vector", lambda e: e.tensor_tensor(
            PB[:, 1:T], EK[:, 1:T], DEN[:, 0:T - 1], ALU.add))
        mBv = s.op("vector", lambda e: e.tensor_copy(PB[:, 0:1], EK[:, 0:1]))
        s.op("scalar", lambda e: e.activation(NUM[:], PB[:], AF.Ln), wait=[mBv])
        mrb = s.op("scalar", lambda e: e.activation(EK[:], NUM[:], AF.Exp, scale=-1.0))
        s.op("vector", lambda e: e.tensor_tensor(DEN[:], VB[:], EK[:], ALU.mult),
             wait=[mrb, RDONE])
        s.op("vector", lambda e, ec=ec: e.scalar_tensor_tensor(
            VB[:], DEN[:], 1.0, RSIG[:, ec, :], ALU.bypass, ALU.mult,
            accum_out=SCOL[:, 0:1]))
        mscr = s.op("vector", lambda e: e.scalar_tensor_tensor(
            PB[:], VB[:], 1.0, VB[:], ALU.bypass, ALU.mult,
            accum_out=SCOL[:, 1:2]))
        s.op("tensor", lambda e: e.matmul(C.pb[2][0:2, 0:2], C.hmask[:, 0:2],
                                          SCOL[:, 0:2], start=True, stop=True),
             wait=[mscr] + hm_ready + ([prev_stat_rd] if prev_stat_rd else []),
             inc=[(auxmm, 1)])
        prev_stat_rd = s.op("vector", lambda e: e.tensor_scalar(
            STAT2[:, 0:2], C.pb[2][0:2, 0:2], 1.0 / GN_N, None, ALU.mult),
             wait=[(auxmm, ec + 1)])
        s.op("vector", lambda e: e.tensor_tensor(
            STAT2[:, 2:3], STAT2[:, 0:1], STAT2[:, 0:1], ALU.mult))
        mvar = s.op("vector", lambda e: e.tensor_tensor(
            STAT2[:, 3:4], STAT2[:, 1:2], STAT2[:, 2:3], ALU.subtract))
        s.op("scalar", lambda e: e.activation(
            STAT2[:, 4:5], STAT2[:, 3:4], AF.Ln, bias=C.cpool[0:2, 0:1]),
             wait=[mvar] + C.cready)
        mrstd = s.op("scalar", lambda e: e.activation(
            STAT2[:, 1:2], STAT2[:, 4:5], AF.Exp, scale=-0.5))
        s.op("tensor", lambda e: e.matmul(C.pb[3][:, 0:2], C.hmaskT[0:2, :],
                                          STAT2[:, 0:2], start=True, stop=True),
             wait=[mrstd] + ([prev_cbuf_rd] if prev_cbuf_rd else []),
             inc=[(bcmm2, 1)])
        s.op("vector", lambda e, ec=ec: e.tensor_tensor(
            CBUF[:, 0:1], C.pb[3][:, 1:2], COLS[:, 40 + ec:40 + ec + 1], ALU.mult),
             wait=[(bcmm2, ec + 1)])
        prev_cbuf_rd = s.op("vector", lambda e: e.tensor_tensor(
            CBUF[:, 1:2], C.pb[3][:, 0:1], CBUF[:, 0:1], ALU.mult))
        s.op("vector", lambda e, ec=ec: e.tensor_tensor(
            CBUF[:, 2:3], COLS[:, 48 + ec:48 + ec + 1], CBUF[:, 1:2], ALU.subtract))
        ckdm[ec] = s.op("vector", lambda e, ec=ec: e.tensor_scalar(
            NORMED[:, ec, :], VB[:], CBUF[:, 0:1], CBUF[:, 2:3], ALU.mult, ALU.add))

    CKD = ckdm[7]
    s.dma("gpsimd", WO[:], W["wo"], wld3, wait=[CKD])

    wi = 0
    wbanks = [C.pb5[:], C.pb6[:], C.pb[0], C.pb[1]]
    wodvem = {}
    for ec in range(8):
        esl = slice(ec * 128, ec * 128 + 128)
        for ts in range(4):
            tsl = slice(ts * 512, ts * 512 + 512)
            ps = wbanks[wi % 4]
            waits = [CKD] + s.after(wld3)
            if wi >= 4:
                waits.append(wodvem[wi - 4])

            def mms(e, esl=esl, tsl=tsl, ps=ps):
                last = None
                for dc in range(8):
                    last = e.matmul(ps[:, 0:512], WO[:, dc, esl], NORMED[:, dc, tsl],
                                    start=(dc == 0), stop=(dc == 7))
                return last

            s.op("tensor", mms, wait=waits, inc=[(womm, 1)])
            b = wi % 2
            xw = [CKD] if wi < 2 else [wodvem[wi - 2]]
            s.dma("sync", XST[b][:], x_dram[:, ec, tsl], xin2[b], wait=xw)
            wodvem[wi] = s.op("vector", lambda e, ps=ps, b=b: e.scalar_tensor_tensor(
                YST[b][:], ps[:, 0:512], 1.0, XST[b][:], ALU.bypass, ALU.add),
                 wait=[(womm, wi + 1), (xin2[b], s.val(xin2[b]))] +
                      ([(yout, 16 * wi)] if wi >= 2 else []))
            s.dma("sync", y_dram[:, ec, tsl], YST[b][:], yout, wait=[wodvem[wi]])
            wi += 1
    s.op("sync", lambda e: None, wait=[(yout, 16 * 32)])



def build_l0ffn(nc, C, mem, x_dram, y_dram, W, tag="l0f"):
    s = C.sched
    TB = 1024
    NTB = T // TB
    NE = 9

    LNB = mem.alloc(tag + "_lnb", [128, 8, T], BF16)
    WEX = mem.alloc(tag + "_wex", [128, 16, 9], F32)
    RT = mem.alloc(tag + "_rt", [128, 8], F32)
    P8 = mem.alloc(tag + "_p8", [128, 8], F32)
    M8 = mem.alloc(tag + "_m8", [128, 8], F32)
    PM8 = mem.alloc(tag + "_pm8", [128, 8], F32)
    GW = mem.alloc(tag + "_gw", [128, 8, 8], BF16)
    GB = mem.alloc(tag + "_gb", [1, 8], BF16)
    EB1C = mem.alloc(tag + "_eb1c", [128, 9, 32], F32)
    _eb2r = mem.alloc(tag + "_eb2r", [1, 1024], BF16)
    EB2R = [_eb2r, _eb2r]
    EW1S = [mem.alloc(tag + f"_ew1s{i}", [128, 8, 128], BF16) for i in range(2)]
    AST = [mem.alloc(tag + f"_ast{i}", [128, 512], F32) for i in range(2)]
    mA = mem.mark()
    XS = mem.alloc(tag + "_xs", [128, 8, T], F32)
    endA = mem.mark()
    H = mem.alloc(tag + "_h", [128, 32, TB], BF16, at=mA)
    XT = [mem.alloc(tag + f"_xt{i}", [128, 512], F32, at=mA + 4096 * i) for i in range(2)]
    YT = [mem.alloc(tag + f"_yt{i}", [128, 512], F32, at=mA + 8192 + 4096 * i) for i in range(2)]
    AT = [mem.alloc(tag + f"_at{i}", [128, 128], F32, at=mA + 16384 + 1024 * i) for i in range(4)]
    mem.reset(endA)
    EW2 = mem.alloc(tag + "_ew2", [128, 32, 1024], BF16)

    xld = s.sem("g"  + "_xld")
    cld = s.sem("g"  + "_cld")
    cldg = s.sem("g"  + "_cldg")
    w1ld = s.sem("g"  + "_w1ld")
    w2ld = s.sem("g"  + "_w2ld")
    gmm = s.sem("g"  + "_gmm")
    g1mm = s.sem("g"  + "_g1mm")
    g2mm = s.sem("g"  + "_g2mm")
    acd = [s.sem("g"  + "_acd0"), s.sem("g"  + "_acd1")]
    w1l2 = [s.sem("g"  + "_w1l0"), s.sem("g"  + "_w1l1")]
    fat_s = [s.sem(tag + f"_fat{j}") for j in range(4)]
    fxt_s = [s.sem("g"  + "_fxa"), s.sem("g"  + "_fxb")]
    trmm = s.sem("g"  + "_trmm")
    yout = s.sem("g"  + "_yout")

    for i in range(4):
        s.dma("sync", XS[:, 2 * i:2 * i + 2, :], x_dram[:, 2 * i:2 * i + 2, :], xld)
    s.dma("gpsimd", GW[:], W["gw"], cldg)
    s.dma("gpsimd", GB[:], W["gb"], cldg)
    s.dma("sync", EB1C[:], W["eb1c"], cld)
    s.dma("sync", C.eye[:], W["eye"], cld)
    s.dma("gpsimd", C.eyeb[:], W["eye"], cldg)

    emit_init_consts(C)
    LNM = emit_ln_sbuf(C, XS, LNB, "g_ln", ready=s.after(xld) + C.cready)

    mwex = s.op("vector", lambda e: e.memset(WEX[:, :, 8], 1.0))

    grd_prev = None
    gr_last = None
    for tc in range(16):
        tsl = slice(tc * 128, tc * 128 + 128)
        psg = C.pb[2]

        def gmms(e, tsl=tsl):
            last = None
            for dc in range(8):
                last = e.matmul(psg[0:128, 0:8], LNB[:, dc, tsl], GW[:, dc, 0:8],
                                start=(dc == 0), stop=False)
            return e.matmul(psg[0:128, 0:8], C.ones_row_bf[0:1, 0:128], GB[0:1, 0:8],
                            start=False, stop=True)

        w = [LNM] + s.after(cld, cldg) + C.cready
        if grd_prev:
            w += [grd_prev, gr_last]
        s.op("tensor", gmms, wait=w, inc=[(gmm, 1)])
        s.op("vector", lambda e: e.tensor_reduce(RT[:, 0:1], psg[0:128, 0:8], AX.X, ALU.max),
             wait=[(gmm, tc + 1)])
        mnm1 = s.op("vector", lambda e: e.tensor_scalar(RT[:, 1:2], RT[:, 0:1], -1.0,
                                                        None, ALU.mult))
        grd_prev = s.op("scalar", lambda e: e.activation(
            P8[:], psg[0:128, 0:8], AF.Exp, bias=RT[:, 1:2], accum_out=RT[:, 2:3]),
             wait=[(gmm, tc + 1), mnm1] + ([gr_last] if gr_last else []))
        s.op("scalar", lambda e: e.activation(RT[:, 3:4], RT[:, 2:3], AF.Ln))
        mrS = s.op("scalar", lambda e: e.activation(RT[:, 4:5], RT[:, 3:4], AF.Exp,
                                                    scale=-1.0))
        s.op("vector", lambda e: e.tensor_scalar(P8[:], P8[:], RT[:, 4:5], None, ALU.mult),
             wait=[mrS])
        s.op("vector", lambda e: e.tensor_reduce(RT[:, 5:6], P8[:], AX.X, ALU.max))
        s.op("vector", lambda e: e.tensor_scalar(M8[:], P8[:], RT[:, 5:6], None, ALU.is_ge))
        s.op("vector", lambda e: e.scalar_tensor_tensor(PM8[:], P8[:], 1.0, M8[:],
                                                        ALU.bypass, ALU.mult))
        s.op("vector", lambda e: e.tensor_tensor(PM8[:], P8[:], PM8[:], ALU.subtract))
        s.op("vector", lambda e: e.tensor_reduce(RT[:, 6:7], PM8[:], AX.X, ALU.max))
        s.op("vector", lambda e: e.tensor_tensor(RT[:, 7:8], RT[:, 5:6], RT[:, 6:7], ALU.add))
        mws = s.op("vector", lambda e: e.tensor_scalar(RT[:, 7:8], RT[:, 7:8], 1.0, 1e-9,
                                                       ALU.mult, ALU.add))
        s.op("scalar", lambda e: e.activation(RT[:, 3:4], RT[:, 7:8], AF.Ln), wait=[mws])
        mrw = s.op("scalar", lambda e: e.activation(RT[:, 4:5], RT[:, 3:4], AF.Exp,
                                                    scale=-1.0))
        s.op("vector", lambda e: e.tensor_scalar(M8[:], P8[:], RT[:, 6:7], None, ALU.is_ge),
             wait=[mrw])
        s.op("vector", lambda e: e.tensor_tensor(M8[:], P8[:], M8[:], ALU.mult))
        gr_last = s.op("vector", lambda e, tc=tc: e.tensor_scalar(
            WEX[:, tc, 0:8], M8[:], RT[:, 4:5], None, ALU.mult))
    GATED = [gr_last, mwex]

    g1t = 0
    g2t = 0
    nacc = 0
    w1n = 0
    g1evm = {}
    g2evm = {}
    ew1_hist = {}
    for e in range(NE):
        ew2wait = [(g2mm, g2t)] if e > 0 else []
        s.dma("gpsimd", EW2[:], W["ew2"][e], w2ld, wait=ew2wait)
        s.dma("gpsimd", EB2R[e % 2][:], W["eb2r"][e:e + 1, :], w2ld)
        w2tgt = s.val(w2ld)
        for tb in range(NTB):
            for ic in range(32):
                b = w1n % 2
                dmawait = [(g1mm, ew1_hist[w1n - 2])] if w1n >= 2 else []
                s.dma("gpsimd", EW1S[b][:], W["ew1"][e, ic], w1l2[b], wait=dmawait)
                w1tgt = s.val(w1l2[b])
                for tsp in range(TB // 512):
                    ps = (C.pb5 if g1t % 2 == 0 else C.pb6)[:]

                    def mms(eng, b=b, tb=tb, tsp=tsp, ps=ps):
                        last = None
                        rs = slice(tb * TB + tsp * 512, tb * TB + tsp * 512 + 512)
                        for dc in range(8):
                            last = eng.matmul(ps[:, 0:512], EW1S[b][:, dc, :],
                                              LNB[:, dc, rs],
                                              start=(dc == 0), stop=(dc == 7))
                        return last

                    waits = [(w1l2[b], w1tgt), LNM]
                    if g1t >= 2:
                        waits.append(g1evm[g1t - 2])
                    s.op("tensor", mms, wait=waits, inc=[(g1mm, 1)])
                    psl = slice(tsp * 512, tsp * 512 + 512)
                    g1evm[g1t] = s.op("scalar", lambda eng, ps=ps, ic=ic, psl=psl, e=e:
                                      eng.activation(H[:, ic, psl], ps[:, 0:512],
                                                     AF.Gelu_apprx_tanh,
                                                     bias=EB1C[:, e, ic:ic + 1]),
                                      wait=[(g1mm, g1t + 1), (g2mm, g2t)] + s.after(cld))
                    g1t += 1
                ew1_hist[w1n] = g1t
                w1n += 1
            HREADY = g1evm[g1t - 1]
            for tsub in range(TB // 128):
                tc = tb * (TB // 128) + tsub
                hsl = slice(tsub * 128, tsub * 128 + 128)
                for dsp in range(2):
                    dsl = slice(dsp * 512, dsp * 512 + 512)
                    ps = C.pb[0 + (g2t % 2)]

                    def mms2(eng, hsl=hsl, dsl=dsl, ps=ps, e=e):
                        last = None
                        for ic in range(32):
                            last = eng.matmul(ps[:, 0:512], H[:, ic, hsl],
                                              EW2[:, ic, dsl],
                                              start=(ic == 0), stop=False)
                        return eng.matmul(ps[:, 0:512], C.ones_row_bf[0:1, 0:128],
                                          EB2R[e % 2][0:1, dsl], start=False, stop=True)

                    waits = [HREADY, (w2ld, w2tgt)]
                    if g2t >= 2:
                        waits.append(g2evm[g2t - 2])
                    s.op("tensor", mms2, wait=waits, inc=[(g2mm, 1)])
                    b2 = nacc % 2
                    evw = [(g2mm, g2t + 1)] + GATED
                    if nacc >= 2:
                        evw.append((acd[b2], s.val(acd[b2])))
                    g2evm[g2t] = s.op("scalar", lambda eng, ps=ps, b2=b2, tc=tc, e=e:
                                      eng.activation(AST[b2][:], ps[:, 0:512], AF.Copy,
                                                     scale=WEX[:, tc, e:e + 1]),
                                      wait=evw)
                    accw = [g2evm[g2t]]
                    if e > 0 and tb == 0 and tsub == 0 and dsp == 0:
                        accw += s.after(acd[0], acd[1])
                    kw = {} if e == 0 else {"accum_op": ALU.add}
                    s.dma("gpsimd", W["acc"][:, tc, dsl], AST[b2][:], acd[b2],
                          wait=accw, **kw)
                    nacc += 1
                    g2t += 1

    EXPDONE = s.after(acd[0], acd[1])

    fi = 0
    fdvem = {}
    for ec in range(8):
        for ts in range(4):
            tsl = slice(ts * 512, ts * 512 + 512)
            ps = C.pb[0 + (fi % 2)]
            for j in range(4):
                tc = ts * 4 + j
                aw = list(EXPDONE)
                if fi >= 1:
                    aw.append((trmm, fi))
                s.dma("sync", AT[j][:], W["acc"][:, tc, ec * 128:ec * 128 + 128],
                      fat_s[j], wait=aw if j == 0 else [])
            trw = [(fs, s.val(fs)) for fs in fat_s] + s.after(cld)
            if fi >= 2:
                trw.append(fdvem[fi - 2])

            def trans(eng, ps=ps):
                last = None
                for j in range(4):
                    last = eng.transpose(ps[:, j * 128:j * 128 + 128], AT[j][:],
                                         C.eye[:, :])
                return last

            s.op("tensor", trans, wait=trw, inc=[(trmm, 1)])
            b = fi % 2
            xw = list(EXPDONE) if fi < 2 else [fdvem[fi - 2]]
            s.dma("sync", XT[b][:], x_dram[:, ec, tsl], fxt_s[b], wait=xw)
            s.op("vector", lambda e, b=b, ec=ec, tsl=tsl: e.tensor_tensor(
                YT[b][:], XT[b][:], LNB[:, ec, tsl], ALU.add),
                 wait=[(fxt_s[b], s.val(fxt_s[b])), (trmm, fi + 1)] +
                      ([(yout, 16 * fi)] if fi >= 2 else []))
            fdvem[fi] = s.op("vector", lambda e, b=b, ps=ps: e.scalar_tensor_tensor(
                YT[b][:], ps[:, 0:512], 1.0, YT[b][:], ALU.bypass, ALU.add))
            s.dma("sync", y_dram[:, ec, tsl], YT[b][:], yout, wait=[fdvem[fi]])
            fi += 1
    s.op("sync", lambda e: None, wait=[(yout, 16 * 32)])



def build_l0ab(nc, C, mem, x_dram, y_dram, W, tag="l0b"):
    s = C.sched
    LNB = mem.alloc(tag + "_lnb", [128, 8, T], BF16)
    P = mem.alloc(tag + "_p", [128, T], BF16)
    PTS = mem.alloc(tag + "_pts", [128, 16, 128], BF16)
    WQKV = mem.alloc(tag + "_wqkv", [128, 8, 384], BF16)
    WPROJ = mem.alloc(tag + "_wproj", [128, 8, 128], BF16)
    BP = mem.alloc(tag + "_bp", [128, 8], F32)
    TRILM = mem.alloc(tag + "_trilm", [128, 128], F32)
    MX = mem.alloc(tag + "_mx", [128, 6], F32)
    XST = [mem.alloc(tag + f"_xst{i}", [128, 512], F32) for i in range(2)]
    YST = [mem.alloc(tag + f"_yst{i}", [128, 512], F32) for i in range(2)]
    mA = mem.mark()
    XS = mem.alloc(tag + "_xs", [128, 8, T], F32)
    endA = mem.mark()
    Q = mem.alloc(tag + "_q", [128, 8, T], BF16, at=mA)
    K_ = mem.alloc(tag + "_k", [128, 8, T], BF16, at=mA + 32768)
    mem.reset(endA)
    VTM = mem.alloc(tag + "_vtm", [128, 16, 1024], BF16, at=C.lnscr_off)
    AOUT = LNB

    xld = s.sem("g"  + "_xld")
    cldh = s.sem("g"  + "_cldh")
    cldg = s.sem("g"  + "_cldg")
    qkmm = s.sem("g"  + "_qkmm")
    smm = s.sem("g"  + "_smm")
    trm = s.sem("g"  + "_trm")
    avm = s.sem("g"  + "_avm")
    pjmm = s.sem("g"  + "_pjmm")
    xin2 = [s.sem("g"  + "_xin2a"), s.sem("g"  + "_xin2b")]
    yout = s.sem("g"  + "_yout")

    for i in range(4):
        s.dma("sync", XS[:, 2 * i:2 * i + 2, :], x_dram[:, 2 * i:2 * i + 2, :], xld)
    s.dma("sync", TRILM[:], W["trilm"], cldh)
    s.dma("sync", BP[:], W["bp"], cldh)
    s.dma("sync", C.eye[:], W["eye"], cldh)
    s.dma("gpsimd", WQKV[:], W["wqkv"], cldg)
    s.dma("gpsimd", WPROJ[:], W["wproj"], cldg)
    s.dma("gpsimd", C.eyeb[:], W["eye"], cldg)

    emit_init_consts(C)
    LNM = emit_ln_sbuf(C, XS, LNB, "g_ln", ready=s.after(xld) + C.cready)
    WRDY = s.after(cldg)

    qk_ev = {}
    nmm = 0
    banks = [C.pb5, C.pb6]
    for oc in range(16):
        g, lo = oc // 3 if oc < 8 else (oc - 8) // 3 + 0, 0
        flat = oc * 128 if oc < 8 else 1024 + (oc - 8) * 128
        g = flat // 384
        lo = flat % 384
        dst = Q if oc < 8 else K_
        ch = oc % 8
        for ts in range(4):
            tsl = slice(ts * 512, ts * 512 + 512)
            ps = banks[nmm % 2][:]
            w = [LNM] + WRDY
            if nmm >= 2:
                w.append(qk_ev[nmm - 2])
            s.op("tensor", lambda e, g=g, lo=lo, tsl=tsl, ps=ps: e.matmul(
                ps[:, 0:512], WQKV[:, g, lo:lo + 128], LNB[:, g, tsl],
                start=True, stop=True), wait=w, inc=[(qkmm, 1)])
            qk_ev[nmm] = s.op("scalar", lambda e, ps=ps, dst=dst, ch=ch, tsl=tsl:
                              e.copy(dst[:, ch, tsl], ps[:, 0:512]),
                              wait=[(qkmm, nmm + 1)])
            nmm += 1
    for tc in range(16):
        for (g, lv0, lv1, v0) in ((5, 128, 384, 0), (6, 0, 384, 256), (7, 0, 384, 640)):
            w_ = lv1 - lv0
            ps = banks[nmm % 2][:]
            wts = [LNM] + WRDY
            if nmm >= 2:
                wts.append(qk_ev[nmm - 2])
            s.op("tensor", lambda e, g=g, tc=tc, lv0=lv0, w_=w_, ps=ps: e.matmul(
                ps[:, 0:w_], LNB[:, g, tc * 128:tc * 128 + 128],
                WQKV[:, g, lv0:lv0 + w_], start=True, stop=True),
                 wait=wts, inc=[(qkmm, 1)])
            qk_ev[nmm] = s.op("scalar", lambda e, ps=ps, tc=tc, v0=v0, w_=w_:
                              e.copy(VTM[:, tc, v0:v0 + w_], ps[:, 0:w_]),
                              wait=[(qkmm, nmm + 1), LNM])
            nmm += 1
    QKVMM = s.after(qkmm)
    QKVM = qk_ev[nmm - 1]

    aexp_prev = None
    pts_prev = None
    aev_prev = None
    trn = 0
    avn = 0
    for h in range(16):
        qch, qb = h // 2, (h % 2) * 64
        for qi in range(16):
            kw_ = T
            kw1 = 1536
            ntail = kw_ - kw1
            nb = 0
            last_s = None
            for kb in range(0, kw_, 512):
                w_ = min(512, kw_ - kb)
                ps = C.pbS[:, kb:kb + w_] if kb < 1536 else C.pb4[:, 0:w_]
                wts = QKVMM + [QKVM]
                if aexp_prev is not None and nb == 0:
                    wts.append(aexp_prev)
                last_s = s.op("tensor", lambda e, qb=qb, qch=qch, qi=qi, kb=kb, w_=w_, ps=ps:
                              e.matmul(ps, Q[qb:qb + 64, qch, qi * 128:qi * 128 + 128],
                                       K_[qb:qb + 64, qch, kb:kb + w_],
                                       start=True, stop=True),
                              wait=wts, inc=[(smm, 1)])
                nb += 1
            smm_w = [(smm, s.val(smm))]
            s.op("vector", lambda e: e.tensor_reduce(
                MX[:, 0:1], C.pbS[:, 0:1536], AX.X, ALU.max), wait=smm_w)
            s.op("vector", lambda e, ntail=ntail: e.tensor_reduce(
                MX[:, 4:5], C.pb4[:, 0:ntail], AX.X, ALU.max))
            s.op("vector", lambda e: e.tensor_tensor(
                MX[:, 0:1], MX[:, 0:1], MX[:, 4:5], ALU.max))
            mneg = s.op("vector", lambda e: e.tensor_scalar(
                MX[:, 1:2], MX[:, 0:1], -0.125, None, ALU.mult))
            exw = [mneg]
            if trn > 0:
                exw.append((trm, trn))
            mex = s.op("scalar", lambda e, kw1=kw1: e.activation(
                P[:, 0:kw1], C.pbS[:, 0:kw1], AF.Exp, bias=MX[:, 1:2],
                scale=0.125, accum_out=MX[:, 2:3]), wait=exw)
            mex = s.op("scalar", lambda e, kw1=kw1, ntail=ntail: e.activation(
                P[:, kw1:kw1 + ntail], C.pb4[:, 0:ntail], AF.Exp,
                bias=MX[:, 1:2], scale=0.125, accum_out=MX[:, 5:6]))
            s.op("vector", lambda e: e.tensor_tensor(
                MX[:, 2:3], MX[:, 2:3], MX[:, 5:6], ALU.add), wait=[mex])
            aexp_prev = mex
            s.op("scalar", lambda e: e.activation(MX[:, 3:4], MX[:, 2:3], AF.Ln),
                 wait=s.mark("vector"))
            mri = s.op("scalar", lambda e: e.activation(
                MX[:, 3:4], MX[:, 3:4], AF.Exp, scale=-1.0))
            msc = s.op("vector", lambda e, kw_=kw_: e.tensor_scalar(
                P[:, 0:kw_], P[:, 0:kw_], MX[:, 3:4], None, ALU.mult), wait=[mri])
            nt = 16
            pts_marks = {}
            for b0 in range(0, nt, 8):
                bank = C.pbTa if (b0 // 8) % 2 == 0 else C.pbTb
                cnt = min(8, nt - b0)
                trw = [msc] + s.after(cldg)
                if pts_prev is not None:
                    trw.append(pts_prev)

                def trans(e, b0=b0, cnt=cnt, bank=bank):
                    last = None
                    for j in range(cnt):
                        last = e.transpose(bank[:, j * 128:j * 128 + 128],
                                           P[:, (b0 + j) * 128:(b0 + j) * 128 + 128],
                                           C.eyeb[:, :])
                    return last

                s.op("tensor", trans, wait=trw, inc=[(trm, 1)])
                trn += 1
                for j in range(cnt):
                    cpw = [(trm, trn)]
                    if aev_prev is not None:
                        cpw.append(aev_prev)
                    pts_marks[b0 + j] = s.op(
                        "vector", lambda e, j=j, b0=b0, bank=bank: e.tensor_copy(
                            PTS[:, b0 + j, :], bank[:, j * 128:j * 128 + 128]),
                        wait=cpw)
                pts_prev = pts_marks[b0 + cnt - 1]
            avw0 = [aev_prev] if aev_prev is not None else []

            def avmm(e, h=h, qi=qi, qb=qb):
                last = None
                for j in range(16):
                    last = e.matmul(C.pb6[qb:qb + 64, 0:128],
                                    VTM[:, j, h * 64:h * 64 + 64],
                                    PTS[:, j, :], start=(j == 0), stop=(j == 15))
                return last

            s.op("tensor", avmm, wait=avw0 + [pts_prev], inc=[(avm, 1)])
            avn += 1
            aev_prev = s.op("scalar", lambda e, qb=qb, qch=qch, qi=qi: e.copy(
                AOUT[qb:qb + 64, qch, qi * 128:qi * 128 + 128],
                C.pb6[qb:qb + 64, 0:128]),
                 wait=[(avm, avn)] + QKVMM)
    ATTND = aev_prev

    pi = 0
    pdvem = {}
    for oc in range(8):
        for ts in range(4):
            tsl = slice(ts * 512, ts * 512 + 512)
            ps = (C.pb5 if pi % 2 == 0 else C.pbS[:, 0:512])
            w = [ATTND] + WRDY
            if pi >= 2:
                w.append(pdvem[pi - 2])
            s.op("tensor", lambda e, oc=oc, tsl=tsl, ps=ps: e.matmul(
                ps[:, 0:512] if ps is C.pb5 else ps, WPROJ[:, oc, :],
                AOUT[:, oc, tsl], start=True, stop=True),
                 wait=w, inc=[(pjmm, 1)])
            b = pi % 2
            xw = [ATTND] if pi < 2 else [pdvem[pi - 2]]
            s.dma("sync", XST[b][:], x_dram[:, oc, tsl], xin2[b], wait=xw)
            s.op("vector", lambda e, b=b, oc=oc: e.tensor_scalar(
                XST[b][:], XST[b][:], 1.0, BP[:, oc:oc + 1], ALU.mult, ALU.add),
                 wait=[(xin2[b], s.val(xin2[b]))] + s.after(cldh))
            pdvem[pi] = s.op("vector", lambda e, b=b, ps=ps: e.scalar_tensor_tensor(
                YST[b][:], ps[:, 0:512] if ps is C.pb5 else ps, 1.0, XST[b][:],
                ALU.bypass, ALU.add),
                 wait=[(pjmm, pi + 1)] + ([(yout, 16 * pi)] if pi >= 2 else []))
            s.dma("sync", y_dram[:, oc, tsl], YST[b][:], yout, wait=[pdvem[pi]])
            pi += 1
    s.op("sync", lambda e: None, wait=[(yout, 16 * 32)])
    return NS(Q=Q, K=K_, VTM=VTM, AOUT=AOUT, LNB=LNB, P=P, yout=yout)



def build_l1aa(nc, C, mem, x_dram, y_dram, W, tag="l1a"):
    s = C.sched
    LNB = mem.alloc(tag + "_lnb", [128, 8, T], BF16)
    PS1 = mem.alloc(tag + "_ps1", [128, T], F32)
    P = mem.alloc(tag + "_p", [128, T], BF16)
    PTS = mem.alloc(tag + "_pts", [128, 16, 128], BF16)
    DSCR = mem.alloc(tag + "_dscr", [128, 128], F32)
    WQh = mem.alloc(tag + "_wq", [128, 8, 256], BF16)
    WKh = mem.alloc(tag + "_wk", [128, 8, 256], BF16)
    TRIL = mem.alloc(tag + "_tril", [128, 128], F32)
    TRIU = mem.alloc(tag + "_triu", [128, 128], F32)
    TRILB = mem.alloc(tag + "_trilb", [128, 128], BF16)
    IOTA = mem.alloc(tag + "_iota", [128, 1], F32)
    MX = mem.alloc(tag + "_mx", [128, 8], F32)
    XST = [mem.alloc(tag + f"_xst{i}", [128, 512], F32) for i in range(2)]
    YST = [mem.alloc(tag + f"_yst{i}", [128, 512], F32) for i in range(2)]
    mA = mem.mark()
    XS = mem.alloc(tag + "_xs", [128, 8, T], F32)
    endA = mem.mark()
    QA = mem.alloc(tag + "_qa", [128, 8, T], BF16, at=mA)
    KA = mem.alloc(tag + "_ka", [128, 8, T], BF16, at=mA + 32768)
    mem.reset(endA)
    VA = mem.alloc(tag + "_va", [128, 16, 1024], BF16, at=C.lnscr_off)
    WVh = mem.alloc(tag + "_wv", [128, 8, 1024], BF16)
    WOh = mem.alloc(tag + "_wo", [128, 8, 1024], BF16, at=mem.mark())
    AOUT = LNB

    xld = s.sem("g"  + "_xld")
    cldh = s.sem("g"  + "_cldh")
    cldg = s.sem("g"  + "_cldg")
    qkmm = s.sem("g"  + "_qkmm")
    smm = s.sem("g"  + "_smm")
    trm = s.sem("g"  + "_trm")
    avm = s.sem("g"  + "_avm")
    pjmm = s.sem("g"  + "_pjmm")
    xin2 = [s.sem("g"  + "_xin2a"), s.sem("g"  + "_xin2b")]
    yout = s.sem("g"  + "_yout")

    for i in range(4):
        s.dma("sync", XS[:, 2 * i:2 * i + 2, :], x_dram[:, 2 * i:2 * i + 2, :], xld)
    s.dma("sync", TRIL[:], W["tril01"], cldh)
    s.dma("sync", TRIU[:], W["triu01"], cldh)
    s.dma("sync", IOTA[:], W["iota1"], cldh)
    s.dma("sync", C.eye[:], W["eye"], cldh)
    s.dma("gpsimd", TRILB[:], W["tril01"], cldg)
    s.dma("gpsimd", WQh[:], W["wq"], cldg)
    s.dma("gpsimd", WKh[:], W["wk"], cldg)
    s.dma("gpsimd", C.eyeb[:], W["eye"], cldg)
    s.dma("gpsimd", WVh[:], W["wv"], cldg)

    emit_init_consts(C)
    LNM = emit_ln_sbuf(C, XS, LNB, "g_ln", ready=s.after(xld) + C.cready)
    WRDY = s.after(cldg)

    qk_ev = {}
    nmm = 0
    banks = [C.pb5, C.pb6]
    for which, (WX, DST) in enumerate(((WQh, QA), (WKh, KA))):
        for h in range(16):
            base = (h % 2) * 64
            for ts in range(4):
                tsl = slice(ts * 512, ts * 512 + 512)
                ps = banks[nmm % 2]
                w = [LNM] + WRDY
                if nmm >= 2:
                    w.append(qk_ev[nmm - 2])

                def mms(e, WX=WX, h=h, tsl=tsl, ps=ps, base=base):
                    last = None
                    for dc in range(8):
                        last = e.matmul(ps[base:base + 16, 0:512],
                                        WX[:, dc, h * 16:h * 16 + 16],
                                        LNB[:, dc, tsl],
                                        start=(dc == 0), stop=(dc == 7))
                    return last

                s.op("tensor", mms, wait=w, inc=[(qkmm, 1)])
                qk_ev[nmm] = s.op("scalar", lambda e, ps=ps, DST=DST, h=h, tsl=tsl,
                                  base=base: e.copy(
                                      DST[base:base + 16, h // 2, tsl],
                                      ps[base:base + 16, 0:512]),
                                  wait=[(qkmm, nmm + 1)])
                nmm += 1
    for tc in range(16):
        for es in range(2):
            esl = slice(es * 512, es * 512 + 512)
            ps = banks[nmm % 2]
            w = [LNM] + WRDY
            if nmm >= 2:
                w.append(qk_ev[nmm - 2])

            def mms(e, tc=tc, esl=esl, ps=ps):
                last = None
                for dc in range(8):
                    last = e.matmul(ps[:, 0:512], LNB[:, dc, tc * 128:tc * 128 + 128],
                                    WVh[:, dc, esl], start=(dc == 0), stop=(dc == 7))
                return last

            s.op("tensor", mms, wait=w, inc=[(qkmm, 1)])
            qk_ev[nmm] = s.op("scalar", lambda e, ps=ps, tc=tc, esl=esl: e.copy(
                VA[:, tc, esl], ps[:, 0:512]), wait=[(qkmm, nmm + 1), LNM])
            nmm += 1
    QKVMM = s.after(qkmm)
    QKVM = qk_ev[nmm - 1]
    s.dma("gpsimd", WOh[:], W["wo"], cldg)
    WOR = s.after(cldg)

    aexp_prev = None
    pts_prev = None
    aev_prev = None
    trn = 0
    avn = 0
    for h in range(16):
        qch, qb = h // 2, (h % 2) * 64
        for qi in range(16):
            kw_ = (qi + 1) * 128
            kw1 = min(kw_, 1536)
            ntail = kw_ - kw1
            nb = 0
            for kb in range(0, kw_, 512):
                w_ = min(512, kw_ - kb)
                ps = C.pbS[:, kb:kb + w_] if kb < 1536 else C.pb4[:, 0:w_]
                wts = QKVMM + [QKVM]
                if aexp_prev is not None and nb == 0:
                    wts.append(aexp_prev)
                s.op("tensor", lambda e, qb=qb, qch=qch, qi=qi, kb=kb, w_=w_, ps=ps:
                     e.matmul(ps, QA[qb:qb + 16, qch, qi * 128:qi * 128 + 128],
                              KA[qb:qb + 16, qch, kb:kb + w_],
                              start=True, stop=True),
                     wait=wts, inc=[(smm, 1)])
                nb += 1
            smm_w = [(smm, s.val(smm))]
            mex = s.op("scalar", lambda e, kw1=kw1: e.activation(
                PS1[:, 0:kw1], C.pbS[:, 0:kw1], AF.Square, bias=1.0,
                scale=0.25, accum_out=MX[:, 0:1]),
                 wait=smm_w + ([aexp_prev] if aexp_prev else []) + C.cready)
            if ntail:
                mex = s.op("scalar", lambda e, kw1=kw1, ntail=ntail: e.activation(
                    PS1[:, kw1:kw1 + ntail], C.pb4[:, 0:ntail], AF.Square,
                    bias=1.0, scale=0.25, accum_out=MX[:, 1:2]))
            else:
                mex = s.op("scalar", lambda e: e.activation(
                    MX[:, 1:2], C.cpool[:, 4:5], AF.Copy))
            aexp_prev = mex
            dstart = qi * 128
            mcorr = s.op("vector", lambda e, dstart=dstart: e.scalar_tensor_tensor(
                DSCR[:], PS1[:, dstart:dstart + 128], 1.0, TRIU[:],
                ALU.bypass, ALU.mult, accum_out=MX[:, 2:3]),
                 wait=[mex] + s.after(cldh))
            s.op("vector", lambda e: e.tensor_tensor(
                MX[:, 3:4], MX[:, 0:1], MX[:, 1:2], ALU.add))
            s.op("vector", lambda e: e.tensor_tensor(
                MX[:, 3:4], MX[:, 3:4], MX[:, 2:3], ALU.subtract))
            s.op("vector", lambda e, qi=qi: e.tensor_scalar(
                MX[:, 4:5], IOTA[:], 1.0, float(qi * 128), ALU.mult, ALU.add))
            s.op("vector", lambda e: e.tensor_tensor(
                MX[:, 3:4], MX[:, 3:4], MX[:, 4:5], ALU.add))
            mden = s.op("vector", lambda e: e.tensor_scalar(
                MX[:, 3:4], MX[:, 3:4], 0.5, 1e-12, ALU.mult, ALU.add))
            s.op("scalar", lambda e: e.activation(MX[:, 5:6], MX[:, 3:4], AF.Ln),
                 wait=[mden])
            s.op("scalar", lambda e: e.activation(MX[:, 5:6], MX[:, 5:6], AF.Exp,
                                                  scale=-1.0))
            mz2 = s.op("scalar", lambda e: e.activation(MX[:, 6:7], MX[:, 5:6],
                                                        AF.Copy, scale=0.5))
            trn_w = [(trm, trn)] if trn else []
            msc = s.op("vector", lambda e, kw_=kw_: e.tensor_scalar(
                P[:, 0:kw_], PS1[:, 0:kw_], MX[:, 6:7], MX[:, 6:7],
                ALU.mult, ALU.add), wait=[mz2] + trn_w)
            msc = s.op("vector", lambda e, dstart=dstart: e.tensor_tensor(
                P[:, dstart:dstart + 128], P[:, dstart:dstart + 128], TRILB[:],
                ALU.mult))
            nt = qi + 1
            pts_marks = {}
            for b0 in range(0, nt, 8):
                bank = C.pbTa if (b0 // 8) % 2 == 0 else C.pbTb
                cnt = min(8, nt - b0)
                trw = [msc] + WRDY
                if pts_prev is not None:
                    trw.append(pts_prev)

                def trans(e, b0=b0, cnt=cnt, bank=bank):
                    last = None
                    for j in range(cnt):
                        last = e.transpose(bank[:, j * 128:j * 128 + 128],
                                           P[:, (b0 + j) * 128:(b0 + j) * 128 + 128],
                                           C.eyeb[:, :])
                    return last

                s.op("tensor", trans, wait=trw, inc=[(trm, 1)])
                trn += 1
                for j in range(cnt):
                    cpw = [(trm, trn)]
                    if aev_prev is not None:
                        cpw.append(aev_prev)
                    pts_marks[b0 + j] = s.op(
                        "vector", lambda e, j=j, b0=b0, bank=bank: e.tensor_copy(
                            PTS[:, b0 + j, :], bank[:, j * 128:j * 128 + 128]),
                        wait=cpw)
                pts_prev = pts_marks[b0 + cnt - 1]
            avw0 = [aev_prev] if aev_prev is not None else []

            def avmm(e, h=h, qi=qi, qb=qb):
                last = None
                for j in range(qi + 1):
                    last = e.matmul(C.pb6[qb:qb + 64, 0:128],
                                    VA[:, j, h * 64:h * 64 + 64],
                                    PTS[:, j, :], start=(j == 0), stop=(j == qi))
                return last

            s.op("tensor", avmm, wait=avw0 + [pts_prev], inc=[(avm, 1)])
            avn += 1
            aev_prev = s.op("scalar", lambda e, qb=qb, h=h, qi=qi: e.copy(
                AOUT[qb:qb + 64, h // 2, qi * 128:qi * 128 + 128],
                C.pb6[qb:qb + 64, 0:128]),
                 wait=[(avm, avn)] + QKVMM)
    ATTND = aev_prev

    pi = 0
    pdvem = {}
    for ec in range(8):
        esl = slice(ec * 128, ec * 128 + 128)
        for ts in range(4):
            tsl = slice(ts * 512, ts * 512 + 512)
            ps = C.pb5 if pi % 2 == 0 else C.pbS[:, 0:512]
            psap = ps[:, 0:512] if pi % 2 == 0 else ps
            w = [ATTND] + WOR
            if pi >= 2:
                w.append(pdvem[pi - 2])

            def mms(e, esl=esl, tsl=tsl, psap=psap):
                last = None
                for dc in range(8):
                    last = e.matmul(psap, WOh[:, dc, esl], AOUT[:, dc, tsl],
                                    start=(dc == 0), stop=(dc == 7))
                return last

            s.op("tensor", mms, wait=w, inc=[(pjmm, 1)])
            b = pi % 2
            xw = [ATTND] if pi < 2 else [pdvem[pi - 2]]
            s.dma("sync", XST[b][:], x_dram[:, ec, tsl], xin2[b], wait=xw)
            pdvem[pi] = s.op("vector", lambda e, b=b, psap=psap: e.scalar_tensor_tensor(
                YST[b][:], psap, 1.0, XST[b][:], ALU.bypass, ALU.add),
                 wait=[(pjmm, pi + 1), (xin2[b], s.val(xin2[b]))] +
                      ([(yout, 16 * pi)] if pi >= 2 else []))
            s.dma("sync", y_dram[:, ec, tsl], YST[b][:], yout, wait=[pdvem[pi]])
            pi += 1
    s.op("sync", lambda e: None, wait=[(yout, 16 * 32)])
    return NS(QA=QA, KA=KA, VA=VA, AOUT=AOUT, P=P, PS1=PS1, yout=yout)



def build_l1ffn(nc, C, mem, x_dram, y_dram, W, tag="l1f"):
    s = C.sched
    TB = 1024
    NE = 4

    RMS1B = mem.alloc(tag + "_rms1", [128, 8, T], BF16)
    WEX4 = mem.alloc(tag + "_wex4", [128, 16, 4], F32)
    GW4 = mem.alloc(tag + "_gw4", [128, 8, 4], BF16)
    GB4 = mem.alloc(tag + "_gb4", [1, 4], BF16)
    RT4 = mem.alloc(tag + "_rt4", [128, 4], F32)
    EB1C = mem.alloc(tag + "_eb1c", [128, 4, 32], F32)
    EB2R = [mem.alloc(tag + f"_eb2r{i}", [1, 1024], BF16) for i in range(2)]
    EW1S = [mem.alloc(tag + f"_ew1s{i}", [128, 8, 128], BF16) for i in range(2)]
    AST = [mem.alloc(tag + f"_ast{i}", [128, 512], F32) for i in range(2)]
    NCOL = mem.alloc(tag + "_ncol", [128, 24], F32)
    m_latb = (mem.mark() + 31) // 32 * 32
    LATB = mem.alloc(tag + "_latb", [128, 4, T], BF16)
    WQT = mem.alloc(tag + "_wqt", [128, 8, 1024], BF16)
    arena0 = WQT.manual_sbuf_range[0]
    WDKV = mem.alloc(tag + "_wdkv", [128, 8, 512], BF16)
    COSB = mem.alloc(tag + "_cosb", [128, T], BF16)
    SINB = mem.alloc(tag + "_sinb", [128, T], BF16)
    P = mem.alloc(tag + "_p", [128, T], BF16)
    PTS = mem.alloc(tag + "_pts", [128, 16, 128], BF16)
    p_off = P.manual_sbuf_range[0]
    TRILM = mem.alloc(tag + "_trilm", [128, 128], F32)
    SWAPM = mem.alloc(tag + "_swapm", [128, 128], BF16)
    MX = mem.alloc(tag + "_mx", [128, 8], F32)
    TMPR = mem.alloc(tag + "_tmpr", [128, 512], F32)
    mem.reset(max(mem.mark(), m_latb + 65536))
    mA = mem.mark()
    XS = mem.alloc(tag + "_xs", [128, 8, T], F32)
    mem.reset(mA)
    QF = mem.alloc(tag + "_qf", [128, 8, T], BF16, at=mA)
    LAT = mem.alloc(tag + "_lat", [128, 4, T], F32, at=mA + 32768)
    KF = mem.alloc(tag + "_kf", [128, 8, T], BF16, at=mA + 32768)
    H = mem.alloc(tag + "_h", [128, 32, TB], BF16, at=mA)
    mem.reset(mA + 65536)

    AOUT = RMS1B
    RMS2B = RMS1B
    VTM = mem.alloc(tag + "_vtm", [128, 16, 1024], BF16, at=C.lnscr_off)
    EW2 = mem.alloc(tag + "_ew2", [128, 32, 1024], BF16, at=m_latb)
    XPS = [mem.alloc(tag + f"_xp{i}", [128, 512], F32, at=p_off + 2048 * i)
           for i in range(2)]
    YPS = [mem.alloc(tag + f"_yp{i}", [128, 512], F32, at=p_off + 4096 + 2048 * i)
           for i in range(2)]
    FXT = [mem.alloc(tag + f"_fxt{i}", [128, 512], F32, at=C.lnscr_off + 2048 * i)
           for i in range(2)]
    FUT = [mem.alloc(tag + f"_fut{i}", [128, 512], F32, at=C.lnscr_off + 4096 + 2048 * i)
           for i in range(2)]
    FYT = [mem.alloc(tag + f"_fyt{i}", [128, 512], F32, at=C.lnscr_off + 8192 + 2048 * i)
           for i in range(2)]
    FAT = [mem.alloc(tag + f"_fat{i}", [128, 128], F32, at=C.lnscr_off + 12288 + 512 * i)
           for i in range(4)]

    xld = s.sem("g"  + "_xld")
    cldh = s.sem("g"  + "_cldh")
    cldg = s.sem("g"  + "_cldg")
    wld = s.sem("g"  + "_wld")
    xinio = s.sem("g"  + "_xinio")
    xps_s = [s.sem("g"  + "_xpsa"), s.sem("g"  + "_xpsb")]
    yps_s = [s.sem("g"  + "_ypsa"), s.sem("g"  + "_ypsb")]
    w1l2 = [s.sem("g"  + "_w1l0"), s.sem("g"  + "_w1l1")]
    fat_s = [s.sem(tag + f"_fat{j}") for j in range(4)]
    fxt_s = [s.sem("g"  + "_fxa"), s.sem("g"  + "_fxb")]
    fut_s = [s.sem("g"  + "_fua"), s.sem("g"  + "_fub")]
    qkmm = s.sem("g"  + "_qkmm")
    rpmm = s.sem("g"  + "_rpmm")
    smm = s.sem("g"  + "_smm")
    trm = s.sem("g"  + "_trm")
    avm = s.sem("g"  + "_avm")
    pjmm = s.sem("g"  + "_pjmm")
    gmm4 = s.sem("g"  + "_gmm4")
    g1mm = s.sem("g"  + "_g1mm")
    g2mm = s.sem("g"  + "_g2mm")
    w1ld = s.sem("g"  + "_w1ld")
    w2ld = s.sem("g"  + "_w2ld")
    acd = [s.sem("g"  + "_acd0"), s.sem("g"  + "_acd1")]
    acrd = s.sem("g"  + "_acrd")
    trf = s.sem("g"  + "_trf")
    yout = s.sem("g"  + "_yout")
    rsq = s.sem("g"  + "_rsq")
    rbc = s.sem("g"  + "_rbc")

    for i in range(4):
        s.dma("sync", XS[:, 2 * i:2 * i + 2, :], x_dram[:, 2 * i:2 * i + 2, :], xld)
    s.dma("sync", TRILM[:], W["trilm"], cldh)
    s.dma("sync", C.eye[:], W["eye"], cldh)
    s.dma("sync", NCOL[:, 0:8], W["n1c"], cldh)
    s.dma("sync", NCOL[:, 8:16], W["n2c"], cldh)
    s.dma("sync", NCOL[:, 16:20], W["lngc"], cldh)
    s.dma("sync", NCOL[:, 20:24], W["lnbc"], cldh)
    s.dma("sync", EB1C[:], W["eb1c"], cldh)
    s.dma("gpsimd", C.eyeb[:], W["eye"], cldg)
    s.dma("gpsimd", COSB[:], W["cosb"], cldg)
    s.dma("gpsimd", SINB[:], W["sinb"], cldg)
    s.dma("gpsimd", SWAPM[:], W["swapm"], cldg)
    s.dma("gpsimd", GW4[:], W["gw4"], cldg)
    s.dma("gpsimd", GB4[:], W["gb4"], cldg)
    s.dma("gpsimd", WQT[:], W["wqt"], wld)
    s.dma("gpsimd", WDKV[:], W["wdkvt"], wld)
    ARW1 = s.after(wld)

    emit_init_consts(C)
    LNM = emit_ln_sbuf(C, XS, XS, "g_ln", ready=s.after(xld) + C.cready)
    for i in range(4):
        s.dma("sync", W["xin"][:, 2 * i:2 * i + 2, :], XS[:, 2 * i:2 * i + 2, :],
              xinio, wait=[LNM])
    XINW = s.after(xinio)

    def emit_rms(xs_sb, dst, gcol, name, prev_lnscr_user):
        sq_m = {}
        lastap = None
        prev_bc = None
        prev_rd = None
        for ts in range(4):
            tsl = slice(ts * 512, ts * 512 + 512)
            for c in range(8):
                w = [prev_lnscr_user] if ts == 0 and c == 0 else []
                if ts > 0:
                    w += [(rsq, s.val(rsq))]
                    w += [lastap] if lastap else []
                sq_m[c] = s.op("scalar", lambda e, c=c, tsl=tsl, xs_sb=xs_sb:
                               e.activation(C.sq8[:, c, :], xs_sb[:, c, tsl],
                                            AF.Square), wait=w)
            for c in range(8):
                s.op("tensor", lambda e, c=c: e.matmul(
                    C.pb[1][0:1, 0:512], C.ones_col[:, 0:1], C.sq8[:, c, :],
                    start=(c == 0), stop=(c == 7)),
                     wait=[sq_m[c]] + ([prev_rd] if c == 0 and prev_rd else []),
                     inc=([(rsq, 1)] if c == 7 else []))
            m_e2 = s.op("vector", lambda e: e.tensor_scalar(
                C.r_e2[:], C.pb[1][0:1, 0:512], 1.0 / 1024.0, None, ALU.mult),
                 wait=[(rsq, s.val(rsq))])
            prev_rd = m_e2
            s.op("scalar", lambda e: e.activation(
                C.r_tm[:], C.r_e2[:], AF.Ln, bias=C.cpool[0:1, 2:3]),
                 wait=[m_e2] + C.cready)
            m_rstd = s.op("scalar", lambda e: e.activation(
                C.r_e2[:], C.r_tm[:], AF.Exp, scale=-0.5))
            s.op("tensor", lambda e: e.matmul(
                C.pb[2][:, 0:512], C.ones_row[0:1, 0:128], C.r_e2[:],
                start=True, stop=True),
                 wait=[m_rstd] + ([prev_bc] if prev_bc else []), inc=[(rbc, 1)])
            cb = s.op("scalar", lambda e: e.copy(C.bc0[:], C.pb[2][:, 0:512]),
                      wait=[(rbc, s.val(rbc))] + ([lastap] if lastap else []))
            prev_bc = cb
            for c in range(8):
                lastap = s.op("vector", lambda e, c=c, tsl=tsl, xs_sb=xs_sb,
                              dst=dst, gcol=gcol: e.scalar_tensor_tensor(
                                  dst[:, c, tsl], xs_sb[:, c, tsl], gcol(c),
                                  C.bc0[:], ALU.mult, ALU.mult),
                              wait=[cb])
        return lastap

    RMS1M = emit_rms(XS, RMS1B, lambda c: NCOL[:, c:c + 1], "g_r1", LNM)

    qk_ev = {}
    nmm = 0
    banks = [C.pb5, C.pb6]

    def fm_gemm(DSTBUF, WT, nk, ec, tsl, extra_wait, rhsbuf, evac=None):
        nonlocal nmm
        ps = banks[nmm % 2]
        w = list(extra_wait)
        if nmm >= 2:
            w.append(qk_ev[nmm - 2])

        def mms(e, WT=WT, ec=ec, tsl=tsl, ps=ps, nk=nk, rhsbuf=rhsbuf):
            last = None
            for dc in range(nk):
                last = e.matmul(ps[:, 0:512], WT[:, dc, ec * 128:ec * 128 + 128],
                                rhsbuf[:, dc, tsl], start=(dc == 0), stop=(dc == nk - 1))
            return last

        s.op("tensor", mms, wait=w, inc=[(qkmm, 1)])
        if evac is None:
            qk_ev[nmm] = s.op("scalar", lambda e, ps=ps, DSTBUF=DSTBUF, ec=ec,
                              tsl=tsl: e.copy(DSTBUF[:, ec, tsl], ps[:, 0:512]),
                              wait=[(qkmm, nmm + 1)])
        else:
            qk_ev[nmm] = evac(ps, nmm)
        nmm += 1
        return qk_ev[nmm - 1]

    QRD = [RMS1M] + XINW + ARW1
    for ec in range(8):
        for ts in range(4):
            fm_gemm(QF, WQT, 8, ec, slice(ts * 512, ts * 512 + 512), QRD, RMS1B)
    for ec in range(4):
        for ts in range(4):
            tsl = slice(ts * 512, ts * 512 + 512)
            fm_gemm(LAT, WDKV, 8, ec, tsl, QRD, RMS1B)
    QDKV = qk_ev[nmm - 1]
    LATM = emit_ln_sbuf(C, LAT, LATB, "g_lln", nchunk=4,
                        ready=[QDKV] + C.cready + s.after(cldh),
                        affine=(lambda c: NCOL[:, 16 + c:17 + c],
                                lambda c: NCOL[:, 20 + c:21 + c]))
    WUKT = mem.alloc(tag + "_wukt", [128, 4, 1024], BF16, at=arena0)
    WUVT = mem.alloc(tag + "_wuvt", [128, 4, 1024], BF16, at=arena0 + 8192)
    s.dma("gpsimd", WUKT[:], W["wukt"], wld, wait=[(qkmm, s.val(qkmm))])
    s.dma("gpsimd", WUVT[:], W["wuvt"], wld)
    ARW2 = s.after(wld)
    for ec in range(8):
        for ts in range(4):
            fm_gemm(KF, WUKT, 4, ec, slice(ts * 512, ts * 512 + 512),
                    [LATM] + ARW2, LATB)
    for tc in range(16):
        for es in range(2):
            esl = slice(es * 512, es * 512 + 512)
            ps = banks[nmm % 2]
            w = [LATM] + ARW2
            if nmm >= 2:
                w.append(qk_ev[nmm - 2])

            def mmsv(e, tc=tc, esl=esl, ps=ps):
                last = None
                for lc in range(4):
                    last = e.matmul(ps[:, 0:512], LATB[:, lc, tc * 128:tc * 128 + 128],
                                    WUVT[:, lc, esl], start=(lc == 0), stop=(lc == 3))
                return last

            s.op("tensor", mmsv, wait=w, inc=[(qkmm, 1)])
            qk_ev[nmm] = s.op("scalar", lambda e, ps=ps, tc=tc, esl=esl: e.copy(
                VTM[:, tc, esl], ps[:, 0:512]), wait=[(qkmm, nmm + 1), LATM])
            nmm += 1
    QKVMM = s.after(qkmm)
    QKVM = qk_ev[nmm - 1]
    WOT = mem.alloc(tag + "_wot", [128, 8, 1024], BF16, at=arena0)
    s.dma("gpsimd", WOT[:], W["wot"], wld, wait=[(qkmm, s.val(qkmm))])
    WOR = s.after(wld)

    rp_prev = {}
    nrp = 0
    for BUF in (QF, KF):
        for c in range(8):
            for ts in range(4):
                tsl = slice(ts * 512, ts * 512 + 512)
                ps = banks[nrp % 2]
                w = [QKVM] + s.after(cldg)
                if nrp >= 2:
                    w.append(rp_prev[nrp - 2])
                s.op("tensor", lambda e, BUF=BUF, c=c, tsl=tsl, ps=ps: e.matmul(
                    ps[:, 0:512], SWAPM[:], BUF[:, c, tsl], start=True, stop=True),
                     wait=w, inc=[(rpmm, 1)])
                s.op("vector", lambda e, ps=ps, tsl=tsl: e.tensor_tensor(
                    TMPR[:], ps[:, 0:512], SINB[:, tsl], ALU.mult),
                     wait=[(rpmm, nrp + 1)])
                s.op("vector", lambda e, BUF=BUF, c=c, tsl=tsl: e.tensor_tensor(
                    BUF[:, c, tsl], BUF[:, c, tsl], COSB[:, tsl], ALU.mult))
                rp_prev[nrp] = s.op("vector", lambda e, BUF=BUF, c=c, tsl=tsl:
                                    e.tensor_tensor(BUF[:, c, tsl], BUF[:, c, tsl],
                                                    TMPR[:], ALU.add))
                nrp += 1
    ROPED = rp_prev[nrp - 1]

    aexp_prev = None
    pts_prev = None
    aev_prev = None
    trn = 0
    avn = 0
    for h in range(16):
        qch, qb = h // 2, (h % 2) * 64
        for qi in range(16):
            kw_ = (qi + 1) * 128
            kw1 = min(kw_, 1536)
            ntail = kw_ - kw1
            nb = 0
            for kb in range(0, kw_, 512):
                w_ = min(512, kw_ - kb)
                ps = C.pbS[:, kb:kb + w_] if kb < 1536 else C.pb4[:, 0:w_]
                wts = [ROPED]
                if aexp_prev is not None and nb == 0:
                    wts.append(aexp_prev)
                s.op("tensor", lambda e, qb=qb, qch=qch, qi=qi, kb=kb, w_=w_, ps=ps:
                     e.matmul(ps, QF[qb:qb + 64, qch, qi * 128:qi * 128 + 128],
                              KF[qb:qb + 64, qch, kb:kb + w_],
                              start=True, stop=True),
                     wait=wts, inc=[(smm, 1)])
                nb += 1
            dstart = qi * 128
            dps = C.pbS[:, dstart:dstart + 128] if dstart < 1536 else \
                C.pb4[:, dstart - 1536:dstart - 1536 + 128]
            mmask = s.op("vector", lambda e, dps=dps: e.tensor_tensor(
                dps, dps, TRILM[:], ALU.add),
                 wait=[(smm, s.val(smm))] + s.after(cldh))
            if ntail == 0:
                s.op("vector", lambda e, kw1=kw1: e.tensor_reduce(
                    MX[:, 0:1], C.pbS[:, 0:kw1], AX.X, ALU.max))
            else:
                s.op("vector", lambda e: e.tensor_reduce(
                    MX[:, 0:1], C.pbS[:, 0:1536], AX.X, ALU.max))
                s.op("vector", lambda e, ntail=ntail: e.tensor_reduce(
                    MX[:, 4:5], C.pb4[:, 0:ntail], AX.X, ALU.max))
                s.op("vector", lambda e: e.tensor_tensor(
                    MX[:, 0:1], MX[:, 0:1], MX[:, 4:5], ALU.max))
            mneg = s.op("vector", lambda e: e.tensor_scalar(
                MX[:, 1:2], MX[:, 0:1], -0.125, None, ALU.mult))
            exw = [mneg]
            if trn > 0:
                exw.append((trm, trn))
            mex = s.op("scalar", lambda e, kw1=kw1: e.activation(
                P[:, 0:kw1], C.pbS[:, 0:kw1], AF.Exp, bias=MX[:, 1:2],
                scale=0.125, accum_out=MX[:, 2:3]), wait=exw)
            if ntail:
                mex = s.op("scalar", lambda e, kw1=kw1, ntail=ntail: e.activation(
                    P[:, kw1:kw1 + ntail], C.pb4[:, 0:ntail], AF.Exp,
                    bias=MX[:, 1:2], scale=0.125, accum_out=MX[:, 5:6]))
                s.op("vector", lambda e: e.tensor_tensor(
                    MX[:, 2:3], MX[:, 2:3], MX[:, 5:6], ALU.add), wait=[mex])
            aexp_prev = mex
            s.op("scalar", lambda e: e.activation(MX[:, 3:4], MX[:, 2:3], AF.Ln),
                 wait=s.mark("vector"))
            mri = s.op("scalar", lambda e: e.activation(
                MX[:, 3:4], MX[:, 3:4], AF.Exp, scale=-1.0))
            msc = s.op("vector", lambda e, kw_=kw_: e.tensor_scalar(
                P[:, 0:kw_], P[:, 0:kw_], MX[:, 3:4], None, ALU.mult), wait=[mri])
            nt = qi + 1
            pts_marks = {}
            for b0 in range(0, nt, 8):
                bank = C.pbTa if (b0 // 8) % 2 == 0 else C.pbTb
                cnt = min(8, nt - b0)
                trw = [msc] + s.after(cldg)
                if pts_prev is not None:
                    trw.append(pts_prev)

                def trans(e, b0=b0, cnt=cnt, bank=bank):
                    last = None
                    for j in range(cnt):
                        last = e.transpose(bank[:, j * 128:j * 128 + 128],
                                           P[:, (b0 + j) * 128:(b0 + j) * 128 + 128],
                                           C.eyeb[:, :])
                    return last

                s.op("tensor", trans, wait=trw, inc=[(trm, 1)])
                trn += 1
                for j in range(cnt):
                    cpw = [(trm, trn)]
                    if aev_prev is not None:
                        cpw.append(aev_prev)
                    pts_marks[b0 + j] = s.op(
                        "vector", lambda e, j=j, b0=b0, bank=bank: e.tensor_copy(
                            PTS[:, b0 + j, :], bank[:, j * 128:j * 128 + 128]),
                        wait=cpw)
                pts_prev = pts_marks[b0 + cnt - 1]
            avw0 = [aev_prev] if aev_prev is not None else []

            def avmm(e, h=h, qi=qi, qb=qb):
                last = None
                for j in range(qi + 1):
                    last = e.matmul(C.pb6[qb:qb + 64, 0:128],
                                    VTM[:, j, h * 64:h * 64 + 64],
                                    PTS[:, j, :], start=(j == 0), stop=(j == qi))
                return last

            s.op("tensor", avmm, wait=avw0 + [pts_prev] + QKVMM, inc=[(avm, 1)])
            avn += 1
            aev_prev = s.op("scalar", lambda e, qb=qb, h=h, qi=qi: e.copy(
                AOUT[qb:qb + 64, h // 2, qi * 128:qi * 128 + 128],
                C.pb6[qb:qb + 64, 0:128]),
                 wait=[(avm, avn)] + QKVMM)
    ATTND = aev_prev

    pi = 0
    pdvem = {}
    for ec in range(8):
        for ts in range(4):
            tsl = slice(ts * 512, ts * 512 + 512)
            ps = C.pb5 if pi % 2 == 0 else C.pbS[:, 0:512]
            psap = ps[:, 0:512] if pi % 2 == 0 else ps
            w = [ATTND] + WOR
            if pi >= 2:
                w.append(pdvem[pi - 2])

            def mmsp(e, ec=ec, tsl=tsl, psap=psap):
                last = None
                for dc in range(8):
                    last = e.matmul(psap, WOT[:, dc, ec * 128:ec * 128 + 128],
                                    AOUT[:, dc, tsl], start=(dc == 0), stop=(dc == 7))
                return last

            s.op("tensor", mmsp, wait=w, inc=[(pjmm, 1)])
            b = pi % 2
            xw = [ATTND] if pi < 2 else [pdvem[pi - 2]]
            s.dma("sync", XPS[b][:], W["xin"][:, ec, tsl], xps_s[b], wait=xw)
            pdvem[pi] = s.op("vector", lambda e, b=b, psap=psap: e.scalar_tensor_tensor(
                YPS[b][:], psap, 1.0, XPS[b][:], ALU.bypass, ALU.add),
                 wait=[(pjmm, pi + 1), (xps_s[b], s.val(xps_s[b]))])
            s.dma("sync", W["xin"][:, ec, tsl], YPS[b][:], xinio, wait=[pdvem[pi]])
            pi += 1
    X1W = s.after(xinio)
    X1M = pdvem[31]

    rs_m = {}
    lastap2 = None
    prev_bc2 = None
    prev_rd2 = None
    for ts in range(4):
        tsl = slice(ts * 512, ts * 512 + 512)
        for c in range(8):
            i = ts * 8 + c
            b = i % 2
            w = list(X1W) + [X1M] + ([ATTND] if i == 0 else [])
            if i >= 2:
                w.append(rs_m[i - 2])
            s.dma("sync", XPS[b][:], W["xin"][:, c, tsl], xps_s[b], wait=w)
            xt = s.val(xps_s[b])
            rs_m[i] = s.op("scalar", lambda e, b=b, c=c: e.activation(
                C.sq8[:, c, :], XPS[b][:], AF.Square),
                 wait=[(xps_s[b], xt), (pjmm, 32)] +
                      ([lastap2] if ts > 0 and c == 0 and lastap2 else []) +
                      ([(rsq, s.val(rsq))] if ts > 0 and c == 0 else []))
        for c in range(8):
            s.op("tensor", lambda e, c=c: e.matmul(
                C.pb[1][0:1, 0:512], C.ones_col[:, 0:1], C.sq8[:, c, :],
                start=(c == 0), stop=(c == 7)),
                 wait=[rs_m[ts * 8 + c]] + ([prev_rd2] if c == 0 and prev_rd2 else []),
                 inc=([(rsq, 1)] if c == 7 else []))
        m_e2 = s.op("vector", lambda e: e.tensor_scalar(
            C.r_e2[:], C.pb[1][0:1, 0:512], 1.0 / 1024.0, None, ALU.mult),
             wait=[(rsq, s.val(rsq))])
        prev_rd2 = m_e2
        s.op("scalar", lambda e: e.activation(
            C.r_tm[:], C.r_e2[:], AF.Ln, bias=C.cpool[0:1, 2:3]), wait=[m_e2])
        m_rstd = s.op("scalar", lambda e: e.activation(
            C.r_e2[:], C.r_tm[:], AF.Exp, scale=-0.5))
        s.op("tensor", lambda e: e.matmul(
            C.pb[2][:, 0:512], C.ones_row[0:1, 0:128], C.r_e2[:],
            start=True, stop=True),
             wait=[m_rstd] + ([prev_bc2] if prev_bc2 else []), inc=[(rbc, 1)])
        cb = s.op("scalar", lambda e: e.copy(C.bc0[:], C.pb[2][:, 0:512]),
                  wait=[(rbc, s.val(rbc))] + ([lastap2] if lastap2 else []))
        prev_bc2 = cb
        for c in range(8):
            i = ts * 8 + c
            b = i % 2
            s.dma("sync", YPS[b][:], W["xin"][:, c, tsl], yps_s[b],
                  wait=[lastap2] if lastap2 and i >= 2 else ([X1M] if i == 0 else []))
            xt = s.val(yps_s[b])
            lastap2 = s.op("vector", lambda e, b=b, c=c, tsl=tsl:
                           e.scalar_tensor_tensor(
                               RMS2B[:, c, tsl], YPS[b][:], NCOL[:, 8 + c:9 + c],
                               C.bc0[:], ALU.mult, ALU.mult),
                           wait=[(yps_s[b], xt), cb])
    RMS2M = lastap2

    g_prev = None
    for tc in range(16):
        tsl = slice(tc * 128, tc * 128 + 128)
        psg = C.pb[2]

        def gmms(e, tsl=tsl):
            last = None
            for dc in range(8):
                last = e.matmul(psg[0:128, 0:4], RMS2B[:, dc, tsl], GW4[:, dc, 0:4],
                                start=(dc == 0), stop=False)
            return e.matmul(psg[0:128, 0:4], C.ones_row_bf[0:1, 0:128], GB4[0:1, 0:4],
                            start=False, stop=True)

        w = [RMS2M] + s.after(cldg) + (s.mark("vector") if tc else []) + \
            ([(rbc, s.val(rbc))] if tc == 0 else [])
        s.op("tensor", gmms, wait=w, inc=[(gmm4, 1)])
        s.op("vector", lambda e: e.tensor_reduce(RT4[:, 0:1], psg[0:128, 0:4],
                                                 AX.X, ALU.max),
             wait=[(gmm4, tc + 1)])
        g_prev = s.op("vector", lambda e, tc=tc: e.tensor_scalar(
            WEX4[:, tc, 0:4], psg[0:128, 0:4], RT4[:, 0:1], None, ALU.is_ge))
    GATED4 = [g_prev]

    g1t = 0
    g2t = 0
    nacc = 0
    w1n = 0
    g1evm = {}
    g2evm = {}
    ew1_hist = {}
    for e in range(NE):
        ew2wait = [(g2mm, g2t)] if e > 0 else [X1M] + X1W + [RMS2M]
        s.dma("gpsimd", EW2[:], W["ew2"][e], w2ld, wait=ew2wait)
        s.dma("gpsimd", EB2R[e % 2][:], W["eb2r"][e:e + 1, :], w2ld)
        w2tgt = s.val(w2ld)
        for tb in range(2):
            for ic in range(32):
                b = w1n % 2
                dmawait = [(g1mm, ew1_hist[w1n - 2])] if w1n >= 2 else []
                s.dma("gpsimd", EW1S[b][:], W["ew1"][e, ic], w1l2[b], wait=dmawait)
                w1tgt = s.val(w1l2[b])
                for tsp in range(2):
                    ps = (C.pb5 if g1t % 2 == 0 else C.pb6)[:]

                    def mms(eng, b=b, tb=tb, tsp=tsp, ps=ps):
                        last = None
                        rs = slice(tb * TB + tsp * 512, tb * TB + tsp * 512 + 512)
                        for dc in range(8):
                            last = eng.matmul(ps[:, 0:512], EW1S[b][:, dc, :],
                                              RMS2B[:, dc, rs],
                                              start=(dc == 0), stop=(dc == 7))
                        return last

                    waits = [(w1l2[b], w1tgt), RMS2M]
                    if g1t >= 2:
                        waits.append(g1evm[g1t - 2])
                    s.op("tensor", mms, wait=waits, inc=[(g1mm, 1)])
                    psl = slice(tsp * 512, tsp * 512 + 512)
                    g1evm[g1t] = s.op("scalar", lambda eng, ps=ps, ic=ic, psl=psl,
                                      e=e: eng.activation(
                                          H[:, ic, psl], ps[:, 0:512], AF.Relu,
                                          bias=EB1C[:, e, ic:ic + 1]),
                                      wait=[(g1mm, g1t + 1), (g2mm, g2t)] +
                                           s.after(cldh))
                    g1t += 1
                ew1_hist[w1n] = g1t
                w1n += 1
            HREADY = g1evm[g1t - 1]
            for tsub in range(8):
                tc = tb * 8 + tsub
                hsl = slice(tsub * 128, tsub * 128 + 128)
                for dsp in range(2):
                    dsl = slice(dsp * 512, dsp * 512 + 512)
                    ps = C.pb[0 + (g2t % 2)]

                    def mms2(eng, hsl=hsl, dsl=dsl, ps=ps, e=e):
                        last = None
                        for ic in range(32):
                            last = eng.matmul(ps, H[:, ic, hsl], EW2[:, ic, dsl],
                                              start=(ic == 0), stop=False)
                        return eng.matmul(ps, C.ones_row_bf[0:1, 0:128],
                                          EB2R[e % 2][0:1, dsl], start=False, stop=True)

                    waits = [HREADY, (w2ld, w2tgt)]
                    if g2t >= 2:
                        waits.append(g2evm[g2t - 2])
                    s.op("tensor", mms2, wait=waits, inc=[(g2mm, 1)])
                    b2 = nacc % 2
                    evw = [(g2mm, g2t + 1)] + GATED4
                    if nacc >= 2:
                        evw.append((acd[b2], s.val(acd[b2])))
                    g2evm[g2t] = s.op("scalar", lambda eng, ps=ps, b2=b2, tc=tc,
                                      e=e: eng.activation(
                                          AST[b2][:], ps, AF.Copy,
                                          scale=WEX4[:, tc, e:e + 1]),
                                      wait=evw)
                    accw = [g2evm[g2t]]
                    if e > 0 and tb == 0 and tsub == 0 and dsp == 0:
                        accw += s.after(acd[0], acd[1])
                    kw = {} if e == 0 else {"accum_op": ALU.add}
                    s.dma("gpsimd", W["acc"][:, tc, dsl], AST[b2][:], acd[b2],
                          wait=accw, **kw)
                    nacc += 1
                    g2t += 1
    EXPDONE = s.after(acd[0], acd[1])

    fi = 0
    fdvem = {}
    for ec in range(8):
        for ts in range(4):
            tsl = slice(ts * 512, ts * 512 + 512)
            ps = C.pb[2] if fi % 2 == 0 else C.pb[3]
            for j in range(4):
                tc = ts * 4 + j
                aw = list(EXPDONE) + [RMS2M]
                if fi >= 1:
                    aw.append((trf, fi))
                s.dma("sync", FAT[j][:], W["acc"][:, tc, ec * 128:ec * 128 + 128],
                      fat_s[j], wait=aw if j == 0 else [])
            trw = [(fs, s.val(fs)) for fs in fat_s] + s.after(cldh)
            if fi >= 2:
                trw.append(fdvem[fi - 2])

            def trans(eng, ps=ps):
                last = None
                for j in range(4):
                    last = eng.transpose(ps[:, j * 128:j * 128 + 128], FAT[j][:],
                                         C.eye[:, :])
                return last

            s.op("tensor", trans, wait=trw, inc=[(trf, 1)])
            b = fi % 2
            xw = list(EXPDONE) if fi < 2 else [fdvem[fi - 2]]
            s.dma("sync", FXT[b][:], x_dram[:, ec, tsl], fxt_s[b], wait=xw)
            s.dma("sync", FUT[b][:], W["xin"][:, ec, tsl], fut_s[b])
            s.op("vector", lambda e, b=b: e.tensor_tensor(
                FYT[b][:], FXT[b][:], FUT[b][:], ALU.add),
                 wait=[(fxt_s[b], s.val(fxt_s[b])), (fut_s[b], s.val(fut_s[b])),
                       (trf, fi + 1)] +
                      ([(yout, 16 * fi)] if fi >= 2 else []))
            fdvem[fi] = s.op("vector", lambda e, b=b, ps=ps: e.scalar_tensor_tensor(
                FYT[b][:], ps[:, 0:512], 1.0, FYT[b][:], ALU.bypass, ALU.add))
            s.dma("sync", y_dram[:, ec, tsl], FYT[b][:], yout, wait=[fdvem[fi]])
            fi += 1
    s.op("sync", lambda e: None, wait=[(yout, 16 * 32)])
    return NS(QF=QF, KF=KF, VTM=VTM, AOUT=AOUT, LATB=LATB, RMS1B=RMS1B, yout=yout)






def build_l1ab(nc, C, mem, x_dram, y_dram, W, tag="l1b"):
    s = C.sched
    SEGS = ((512, 1), (1024, 2), (2048, 4))
    LNB = mem.alloc(tag + "_lnb", [128, 8, T], BF16)
    CBC = mem.alloc(tag + "_cbc", [128, 24], F32)
    TRILM = mem.alloc(tag + "_trilm", [128, 128], F32)
    BR = mem.alloc(tag + "_br", [1, 4096], BF16)
    MX = mem.alloc(tag + "_mx", [128, 8], F32)
    P = mem.alloc(tag + "_pp", [128, 512], BF16)
    QG = mem.alloc(tag + "_qg", [128, 4, 64], BF16)
    KG = mem.alloc(tag + "_kg", [128, 4, 64], BF16)
    VG = mem.alloc(tag + "_vg", [128, 4, 64], BF16)
    QGF = mem.alloc(tag + "_qgf", [64, 4, 128], BF16)
    KGF = mem.alloc(tag + "_kgf", [64, 4, 128], BF16)
    PTS = mem.alloc(tag + "_pts", [128, 4, 128], BF16)
    OBT = mem.alloc(tag + "_obt", [128, 64], BF16)
    MST = mem.alloc(tag + "_mst", [128, 2], F32)
    M3T = mem.alloc(tag + "_m3t", [128, 16, 3], F32)
    MHT = mem.alloc(tag + "_mht", [128, 16], F32)
    C3T = mem.alloc(tag + "_c3t", [128, 16, 3], F32)
    S3T = mem.alloc(tag + "_s3t", [128, 16, 3], F32)
    DEN = mem.alloc(tag + "_den", [128, 16], F32)
    NT = [mem.alloc(tag + f"_nt{i}", [128, 64], BF16) for i in range(3)]
    YH = mem.alloc(tag + "_yh", [128, 64], F32)
    ZT = mem.alloc(tag + "_zt", [128, 512], F32)
    TMS = mem.alloc(tag + "_tms", [128, 512], BF16)
    XST = [mem.alloc(tag + f"_xst{i}", [128, 512], F32) for i in range(2)]
    YST = [mem.alloc(tag + f"_yst{i}", [128, 512], F32) for i in range(2)]
    QXT = mem.alloc(tag + "_qxt", [128, 8, 1024], BF16)
    CW = mem.alloc(tag + "_cw", [128, 8, 1024], BF16)
    QW2 = mem.alloc(tag + "_qw2", [128, 8, 1024], BF16)
    XS = mem.alloc(tag + "_xs", [128, 8, T], F32)
    AOUT = LNB

    xld = s.sem("g"  + "_xld")
    cldh = s.sem("g"  + "_cldh")
    cldg = s.sem("g"  + "_cldg")
    wstr = s.sem("g"  + "_wstr")
    pmm = s.sem("g"  + "_pmm")
    qkvo = s.sem("g"  + "_qkvo")
    zinit = s.sem("g"  + "_zinit")
    gld = s.sem("g"  + "_gld")
    tgmm = s.sem("g"  + "_tgmm")
    smm = s.sem("g"  + "_smm")
    trm = s.sem("g"  + "_trm")
    avm = s.sem("g"  + "_avm")
    mso = s.sem("g"  + "_mso")
    mrd = s.sem("g"  + "_mrd")
    nbo = s.sem("g"  + "_nbo")
    aomm = s.sem("g"  + "_aomm")
    pjmm = s.sem("g"  + "_pjmm")
    xin2 = [s.sem("g"  + "_xin2a"), s.sem("g"  + "_xin2b")]
    yout = s.sem("g"  + "_yout")

    for i in range(4):
        s.dma("sync", XS[:, 2 * i:2 * i + 2, :], x_dram[:, 2 * i:2 * i + 2, :], xld)
    s.dma("sync", CBC[:], W["cbc"], cldh)
    s.dma("sync", TRILM[:], W["trilm"], cldh)
    s.dma("sync", C.eye[:], W["eye"], cldh)
    s.dma("gpsimd", BR[0:1, 0:1024], W["qbr"], cldg)
    s.dma("gpsimd", BR[0:1, 1024:2048], W["kbr"], cldg)
    s.dma("gpsimd", BR[0:1, 2048:3072], W["vbr"], cldg)
    s.dma("gpsimd", BR[0:1, 3072:4096], W["obr"], cldg)
    s.dma("gpsimd", C.eyeb[:], W["eye"], cldg)

    emit_init_consts(C)
    LNM = emit_ln_sbuf(C, XS, LNB, "g_ln", ready=s.after(xld) + C.cready)

    mz0 = s.op("vector", lambda e: e.memset(ZT[:], -1e30))
    s.dma("sync", W["m3"].rearrange("(a p) c -> p a c", p=128),
          bass.AP(ZT, 0, [[512, 128], [0, 16], [1, 48]]), zinit, wait=[mz0])
    mz1 = s.op("vector", lambda e: e.memset(ZT[:], 0.0), wait=[(zinit, s.val(zinit))])
    s.dma("sync", W["s3"].rearrange("(a p) c -> p a c", p=128),
          bass.AP(ZT, 0, [[512, 128], [0, 16], [1, 48]]), zinit, wait=[mz1])
    mzb = s.op("vector", lambda e: e.memset(TMS[:], 0.0))
    for nb in (W["nb2"], W["nb3"]):
        for half in range(2):
            s.dma("sync", nb.rearrange("(a p) c -> p a c", p=128)[:, :, 512 * half:512 * half + 512],
                  bass.AP(TMS, 0, [[512, 128], [0, 16], [1, 512]]), zinit, wait=[mzb])
    ZINIT = s.after(zinit)

    ev_hist = {}
    npm = 0
    banks = [C.pb5, C.pb6]
    for part, dst in enumerate((W["qd"], W["kd"], W["vd"])):
        s.dma("gpsimd", CW[:], W["caw"][:, :, 1024 * part:1024 * (part + 1)], wstr,
              wait=[(pmm, s.val(pmm))])
        s.dma("gpsimd", QW2[:], W[("qw", "kw", "vw")[part]], wstr)
        wtgt = s.val(wstr)
        brow = BR[0:1, 1024 * part:1024 * (part + 1)]
        for tb in range(2):
            for ec in range(8):
                for ts2 in range(2):
                    tsl = slice(tb * 1024 + ts2 * 512, tb * 1024 + ts2 * 512 + 512)
                    psl = slice(ts2 * 512, ts2 * 512 + 512)
                    ps = banks[npm % 2]
                    w = [LNM, (wstr, wtgt)] + s.after(cldh)
                    if npm >= 2:
                        w.append(ev_hist[npm - 2])

                    def mm1(e, ec=ec, tsl=tsl, ps=ps):
                        last = None
                        for dc in range(8):
                            last = e.matmul(ps[:, 0:512], CW[:, dc, ec * 128:ec * 128 + 128],
                                            LNB[:, dc, tsl], start=(dc == 0), stop=(dc == 7))
                        return last

                    s.op("tensor", mm1, wait=w, inc=[(pmm, 1)])
                    ev_hist[npm] = s.op("vector", lambda e, ps=ps, ec=ec, psl=psl,
                                        part=part: e.tensor_scalar(
                                            QXT[:, ec, psl], ps[:, 0:512],
                                            CBC[:, 8 * part + ec:8 * part + ec + 1],
                                            None, ALU.add),
                                        wait=[(pmm, npm + 1)])
                    npm += 1
            QXM = ev_hist[npm - 1]
            for tc8 in range(8):
                tcl = slice(tc8 * 128, tc8 * 128 + 128)
                for es in range(2):
                    esl = slice(es * 512, es * 512 + 512)
                    ps = banks[npm % 2]
                    w = [QXM] + s.after(cldg)
                    if npm >= 2:
                        w.append(ev_hist[npm - 2])

                    def mm2(e, tcl=tcl, esl=esl, ps=ps, brow=brow):
                        last = None
                        for dc in range(8):
                            last = e.matmul(ps[:, 0:512], QXT[:, dc, tcl],
                                            QW2[:, dc, esl], start=(dc == 0), stop=False)
                        return e.matmul(ps[:, 0:512], C.ones_row_bf[0:1, 0:128],
                                        brow[:, esl], start=False, stop=True)

                    s.op("tensor", mm2, wait=w, inc=[(pmm, 1)])
                    sc = 0.125 if part == 0 else 1.0
                    mev = s.op("scalar", lambda e, ps=ps, sc=sc: e.activation(
                        TMS[:], ps[:, 0:512], AF.Copy, scale=sc),
                        wait=[(pmm, npm + 1), (qkvo, s.val(qkvo)), (zinit, s.val(zinit))])
                    ev_hist[npm] = mev
                    tglob = tb * 1024 + tc8 * 128
                    s.dma("sync", dst.rearrange("(a p) c -> p a c", p=128)[:, tglob // 128, esl],
                          TMS[:], qkvo, wait=[mev])
                    npm += 1
    QKVW = s.after(qkvo)

    aev_prev = None
    exp_prev = None
    pts_prev = None
    obe_prev = None
    prev_merge = None
    for h in range(16):
        for b, (w_, r) in enumerate(SEGS):
            nseg = T // w_
            off = h % r
            for seg in range(nseg):
                base = seg * w_ + off
                gw = QKVW + ([exp_prev] if exp_prev else []) + \
                    ([obe_prev] if obe_prev else [])
                for j in range(4):
                    rs = base + (j * 128) * r
                    s.dma("sync", QG[:, j, :],
                          bass.AP(W["qd"].tensor, rs * 1024 + h * 64,
                                  [[r * 1024, 128], [1, 64]]), gld,
                          wait=gw if j == 0 else [])
                    s.dma("sync", KG[:, j, :],
                          bass.AP(W["kd"].tensor, rs * 1024 + h * 64,
                                  [[r * 1024, 128], [1, 64]]), gld)
                    s.dma("sync", VG[:, j, :],
                          bass.AP(W["vd"].tensor, rs * 1024 + h * 64,
                                  [[r * 1024, 128], [1, 64]]), gld)
                gtgt = s.val(gld)
                trw = [(gld, gtgt)] + s.after(cldg)
                if exp_prev is not None:
                    trw.append(exp_prev)
                if pts_prev is not None:
                    trw.append(pts_prev)

                def tg(e):
                    last = None
                    for j in range(4):
                        last = e.transpose(C.pbTa[0:64, j * 128:j * 128 + 128],
                                           QG[:, j, :], C.eyeb[:, :])
                    for j in range(4):
                        last = e.transpose(C.pbTb[0:64, j * 128:j * 128 + 128],
                                           KG[:, j, :], C.eyeb[:, :])
                    return last

                s.op("tensor", tg, wait=trw, inc=[(tgmm, 1)])
                tgv = s.val(tgmm)
                mq = s.op("vector", lambda e: e.tensor_copy(
                    QGF[0:64, :, :],
                    C.pbTa[0:64, 0:512].rearrange("p (a b) -> p a b", a=4)),
                     wait=[(tgmm, tgv)] + ([exp_prev] if exp_prev else []))
                mk = s.op("vector", lambda e: e.tensor_copy(
                    KGF[0:64, :, :],
                    C.pbTb[0:64, 0:512].rearrange("p (a b) -> p a b", a=4)))
                for qt in range(4):
                    kw_ = (qt + 1) * 128
                    wts = [mk]
                    if exp_prev is not None:
                        wts.append(exp_prev)
                    s.op("tensor", lambda e, qt=qt, kw_=kw_: e.matmul(
                        C.pbS[:, 0:kw_], QGF[0:64, qt, :],
                        KGF[0:64, :, :].rearrange("p a b -> p (a b)")[:, 0:kw_],
                        start=True, stop=True),
                         wait=wts, inc=[(smm, 1)])
                    dstart = qt * 128
                    s.op("vector", lambda e, dstart=dstart: e.tensor_tensor(
                        C.pbS[:, dstart:dstart + 128], C.pbS[:, dstart:dstart + 128],
                        TRILM[:], ALU.add),
                         wait=[(smm, s.val(smm))] + s.after(cldh))
                    s.op("vector", lambda e, kw_=kw_: e.tensor_reduce(
                        MX[:, 0:1], C.pbS[:, 0:kw_], AX.X, ALU.max))
                    mneg = s.op("vector", lambda e: e.tensor_scalar(
                        MX[:, 1:2], MX[:, 0:1], -1.0, None, ALU.mult))
                    exw = [mneg]
                    if pts_prev is not None:
                        exw.append((trm, s.val(trm)))
                    exp_prev = s.op("scalar", lambda e, kw_=kw_: e.activation(
                        P[:, 0:kw_], C.pbS[:, 0:kw_], AF.Exp, bias=MX[:, 1:2],
                        accum_out=MX[:, 2:3]), wait=exw)
                    mcol = s.op("vector", lambda e: e.tensor_copy(MST[:, 0:1], MX[:, 0:1]),
                                wait=[exp_prev, (mso, s.val(mso))])
                    scol = s.op("vector", lambda e: e.tensor_copy(MST[:, 1:2], MX[:, 2:3]))
                    rs0 = base + qt * 128 * r
                    s.dma("sync", bass.AP(W["m3"].tensor, rs0 * 48 + h * 3 + b,
                                          [[48 * r, 128], [1, 1]]), MST[:, 0:1],
                          mso, wait=[scol, (zinit, s.val(zinit))],
                          allow_slow_non_contiguous=True)
                    s.dma("sync", bass.AP(W["s3"].tensor, rs0 * 48 + h * 3 + b,
                                          [[48 * r, 128], [1, 1]]), MST[:, 1:2], mso,
                          allow_slow_non_contiguous=True)
                    trw2 = [exp_prev]
                    if pts_prev is not None:
                        trw2.append(pts_prev)

                    def trp(e, qt=qt):
                        last = None
                        for j in range(qt + 1):
                            last = e.transpose(
                                C.pbTa[:, 512 + j * 128:512 + j * 128 + 128],
                                P[:, j * 128:j * 128 + 128], C.eyeb[:, :])
                        return last

                    s.op("tensor", trp, wait=trw2, inc=[(trm, 1)])
                    ptm = None
                    for j in range(qt + 1):
                        cw_ = [(trm, s.val(trm))]
                        if obe_prev is not None:
                            cw_.append(obe_prev)
                        ptm = s.op("vector", lambda e, j=j: e.tensor_copy(
                            PTS[:, j, :], C.pbTa[:, 512 + j * 128:512 + j * 128 + 128]),
                            wait=cw_)
                    pts_prev = ptm

                    def avb(e, qt=qt):
                        last = None
                        for j in range(qt + 1):
                            last = e.matmul(C.pb6[:, 0:64], PTS[:, j, :],
                                            VG[:, j, :], start=(j == 0), stop=(j == qt))
                        return last

                    s.op("tensor", avb, wait=[pts_prev] +
                         ([obe_prev] if obe_prev else []), inc=[(avm, 1)])
                    obe_prev = s.op("scalar", lambda e: e.copy(OBT[:], C.pb6[:, 0:64]),
                                    wait=[(avm, s.val(avm)), (nbo, s.val(nbo))])
                    s.dma("sync", bass.AP(W[("nb1", "nb2", "nb3")[b]].tensor,
                                          rs0 * 1024 + h * 64,
                                          [[r * 1024, 128], [1, 64]]), OBT[:],
                          nbo, wait=[obe_prev, (zinit, s.val(zinit))])
        NBW = s.after(nbo, mso)
        mrw = list(NBW) + ([prev_merge] if prev_merge else [])
        s.dma("sync", M3T[:], bass.AP(W["m3"].tensor, h * 3,
                                      [[48, 128], [48 * 128, 16], [1, 3]]),
              mrd, wait=mrw)
        s.dma("sync", S3T[:], bass.AP(W["s3"].tensor, h * 3,
                                      [[48, 128], [48 * 128, 16], [1, 3]]), mrd)
        m3v = s.val(mrd)
        mh = s.op("vector", lambda e: e.tensor_reduce(MHT[:], M3T[:], AX.X, ALU.max),
                  wait=[(mrd, m3v)])
        mhb = bass.AP(MHT, 0, [[16, 128], [1, 16], [0, 3]])
        mdf = s.op("vector", lambda e: e.tensor_tensor(C3T[:], M3T[:], mhb, ALU.subtract))
        mc3 = s.op("scalar", lambda e: e.activation(C3T[:], C3T[:], AF.Exp), wait=[mdf])
        s.op("vector", lambda e: e.tensor_tensor(S3T[:], C3T[:], S3T[:], ALU.mult),
             wait=[mc3])
        mden = s.op("vector", lambda e: e.tensor_reduce(DEN[:], S3T[:], AX.X, ALU.add))
        s.op("scalar", lambda e: e.activation(DEN[:], DEN[:], AF.Ln), wait=[mden])
        mrden = s.op("scalar", lambda e: e.activation(DEN[:], DEN[:], AF.Exp, scale=-1.0))
        prev_tc_read = None
        for tc in range(16):
            for b in range(3):
                s.dma("sync", NT[b][:],
                      bass.AP(W[("nb1", "nb2", "nb3")[b]].tensor,
                              tc * 128 * 1024 + h * 64, [[1024, 128], [1, 64]]),
                      mrd,
                      wait=(list(NBW) + [mc3] if tc == 0 and b == 0 else [])
                      + ([prev_tc_read] if prev_tc_read and b == 0 else []))
            ntv = s.val(mrd)
            s.op("vector", lambda e, tc=tc: e.tensor_scalar(
                YH[:], NT[0][:], C3T[:, tc, 0:1], None, ALU.mult),
                 wait=[(mrd, ntv), mrden])
            for b in (1, 2):
                s.op("vector", lambda e, b=b, tc=tc: e.tensor_scalar(
                    NT[b][:], NT[b][:], C3T[:, tc, b:b + 1], None, ALU.mult))
                s.op("vector", lambda e, b=b: e.tensor_tensor(
                    YH[:], YH[:], NT[b][:], ALU.add))
            myh = s.op("vector", lambda e, tc=tc: e.tensor_scalar(
                YH[:], YH[:], DEN[:, tc:tc + 1], None, ALU.mult))
            prev_tc_read = myh
            prev_merge = myh
            s.op("tensor", lambda e: e.transpose(C.pb4[0:64, 0:128], YH[:], C.eye[:, :]),
                 wait=[myh] + s.after(cldh) + ([aev_prev] if aev_prev else []),
                 inc=[(aomm, 1)])
            aev_prev = s.op("scalar", lambda e, h=h, tc=tc: e.copy(
                AOUT[(h % 2) * 64:(h % 2) * 64 + 64, h // 2, tc * 128:tc * 128 + 128],
                C.pb4[0:64, 0:128]), wait=[(aomm, s.val(aomm))])
    ATTND = aev_prev

    s.dma("gpsimd", CW[:], W["ow"], wstr, wait=[(pmm, s.val(pmm))])
    OWR = s.after(wstr)
    pi = 0
    pdvem = {}
    for ec in range(8):
        esl = slice(ec * 128, ec * 128 + 128)
        for ts in range(4):
            tsl = slice(ts * 512, ts * 512 + 512)
            psobj = C.pb5 if pi % 2 == 0 else None
            psap = C.pb5[:, 0:512] if pi % 2 == 0 else C.pbS[:, 0:512]
            w = [ATTND] + OWR + s.after(cldg)
            if pi >= 2:
                w.append(pdvem[pi - 2])

            def mmsp(e, esl=esl, tsl=tsl, psap=psap):
                last = None
                for dc in range(8):
                    last = e.matmul(psap, CW[:, dc, esl], AOUT[:, dc, tsl],
                                    start=(dc == 0), stop=False)
                return e.matmul(psap, BR[0:1, 3072 + esl.start:3072 + esl.stop],
                                C.ones_row_bf[0:1, 0:512],
                                start=False, stop=True)

            s.op("tensor", mmsp, wait=w, inc=[(pjmm, 1)])
            b = pi % 2
            xw = [ATTND] if pi < 2 else [pdvem[pi - 2]]
            s.dma("sync", XST[b][:], x_dram[:, ec, tsl], xin2[b], wait=xw)
            pdvem[pi] = s.op("vector", lambda e, b=b, psap=psap: e.scalar_tensor_tensor(
                YST[b][:], psap, 1.0, XST[b][:], ALU.bypass, ALU.add),
                 wait=[(pjmm, pi + 1), (xin2[b], s.val(xin2[b]))] +
                      ([(yout, 16 * pi)] if pi >= 2 else []))
            s.dma("sync", y_dram[:, ec, tsl], YST[b][:], yout, wait=[pdvem[pi]])
            pi += 1
    s.op("sync", lambda e: None, wait=[(yout, 16 * 32)])
    return NS(AOUT=AOUT, yout=yout)



_PROGS = {}


def _dram_ins(nc, specs):
    return {k: nc.dram_tensor(k, shp, F32, kind="ExternalInput").ap()
            for k, shp in specs}


def _mk_prog(name):
    if name in _PROGS:
        return _PROGS[name]
    nc = bass.Bass("TRN2", target_bir_lowering=False, detect_race_conditions=False)
    x = nc.dram_tensor("x", [128, 8, T], F32, kind="ExternalInput").ap()
    y = nc.dram_tensor("y", [128, 8, T], F32, kind="ExternalOutput").ap()
    if name == "l0aa":
        W = _dram_ins(nc, [("wr", [128, 8, 1024]), ("wk", [128, 8, 1024]),
                           ("wv", [128, 8, 1024]), ("wo", [128, 8, 1024]),
                           ("cols", [128, 56]), ("hmask", [128, 2]),
                           ("hmaskT", [2, 128])])
    elif name == "l0ffn":
        W = _dram_ins(nc, [("ew1", [9, 32, 128, 8, 128]), ("ew2", [9, 128, 32, 1024]),
                           ("eb1c", [128, 9, 32]), ("eb2r", [9, 1024]),
                           ("gw", [128, 8, 8]), ("gb", [1, 8]), ("eye", [128, 128])])
        W["acc"] = nc.dram_tensor("acc", [128, 16, 1024], F32).ap()
    elif name == "l0ab":
        W = _dram_ins(nc, [("wqkv", [128, 8, 384]), ("wproj", [128, 8, 128]),
                           ("bp", [128, 8]), ("trilm", [128, 128]),
                           ("eye", [128, 128])])
    elif name == "l1aa":
        W = _dram_ins(nc, [("wq", [128, 8, 256]), ("wk", [128, 8, 256]),
                           ("wv", [128, 8, 1024]), ("wo", [128, 8, 1024]),
                           ("tril01", [128, 128]), ("triu01", [128, 128]),
                           ("iota1", [128, 1]), ("eye", [128, 128])])
    elif name == "l1ffn":
        W = _dram_ins(nc, [("wqt", [128, 8, 1024]), ("wdkvt", [128, 8, 512]),
                           ("wukt", [128, 4, 1024]), ("wuvt", [128, 4, 1024]),
                           ("wot", [128, 8, 1024]),
                           ("n1c", [128, 8]), ("n2c", [128, 8]),
                           ("lngc", [128, 4]), ("lnbc", [128, 4]),
                           ("cosb", [128, T]), ("sinb", [128, T]),
                           ("swapm", [128, 128]), ("trilm", [128, 128]),
                           ("eye", [128, 128]), ("gw4", [128, 8, 4]), ("gb4", [1, 4]),
                           ("ew1", [4, 32, 128, 8, 128]), ("ew2", [4, 128, 32, 1024]),
                           ("eb1c", [128, 4, 32]), ("eb2r", [4, 1024])])
        W["xin"] = nc.dram_tensor("xin", [128, 8, T], F32).ap()
        W["acc"] = nc.dram_tensor("acc", [128, 16, 1024], F32).ap()
    elif name == "l1ab":
        W = _dram_ins(nc, [("caw", [128, 8, 3072]), ("cbc", [128, 24]),
                           ("qw", [128, 8, 1024]), ("kw", [128, 8, 1024]),
                           ("vw", [128, 8, 1024]), ("ow", [128, 8, 1024]),
                           ("qbr", [1, 1024]), ("kbr", [1, 1024]),
                           ("vbr", [1, 1024]), ("obr", [1, 1024]),
                           ("trilm", [128, 128]), ("eye", [128, 128])])
        for k in ("qd", "kd", "vd", "nb1", "nb2", "nb3"):
            W[k] = nc.dram_tensor(k, [T, 1024], BF16).ap()
        for k in ("m3", "s3"):
            W[k] = nc.dram_tensor(k, [T, 48], F32).ap()
    C, mem = setup_prog(nc)
    {"l0aa": build_l0aa, "l0ffn": build_l0ffn, "l0ab": build_l0ab,
     "l1aa": build_l1aa, "l1ffn": build_l1ffn, "l1ab": build_l1ab}[name](
        nc, C, mem, x, y, W)
    finish_prog(C)
    _PROGS[name] = nc
    return nc


def _prep_weights(inp):
    f32 = np.float32
    q = np.arange(128)[:, None]
    j = np.arange(128)[None, :]
    eye = np.eye(128, dtype=f32)
    trilm = np.where(j <= q, 0.0, -1e30).astype(f32)
    WS = {}
    cols = np.concatenate([
        host_col(inp["L0_attn_a_time_mix_r"]), host_col(inp["L0_attn_a_time_mix_k"]),
        host_col(inp["L0_attn_a_time_mix_v"]),
        host_col(inp["L0_attn_a_time_decay"].reshape(-1)),
        host_col(inp["L0_attn_a_time_first"].reshape(-1)),
        host_col(inp["L0_attn_a_gn_weight"]), host_col(inp["L0_attn_a_gn_bias"]),
    ], axis=1).astype(f32)
    hmask = np.zeros((128, 2), f32)
    hmask[:64, 0] = 1
    hmask[64:, 1] = 1
    WS["l0aa"] = {"wr": host_w(inp["L0_attn_a_W_r"].T), "wk": host_w(inp["L0_attn_a_W_k"].T),
                  "wv": host_w(inp["L0_attn_a_W_v"].T), "wo": host_w(inp["L0_attn_a_W_o"].T),
                  "cols": cols, "hmask": hmask,
                  "hmaskT": np.ascontiguousarray(hmask.T)}
    ew1_l, ew2_l, eb1_l, eb2_l = [], [], [], []
    for e in range(9):
        if e < 8:
            w1, b1 = inp["L0_ffn_ew1"][e], inp["L0_ffn_eb1"][e]
            w2, b2 = inp["L0_ffn_ew2"][e], inp["L0_ffn_eb2"][e]
        else:
            w1, b1 = inp["L0_ffn_sh_w1"], inp["L0_ffn_sh_b1"]
            w2, b2 = inp["L0_ffn_sh_w2"], inp["L0_ffn_sh_b2"]
        hw = host_w(w1).reshape(128, 8, 32, 128).transpose(2, 0, 1, 3)
        ew1_l.append(np.ascontiguousarray(hw))
        ew2_l.append(host_w(w2))
        eb1_l.append(host_col(b1))
        eb2_l.append(b2)
    WS["l0ffn"] = {"ew1": np.stack(ew1_l).astype(f32), "ew2": np.stack(ew2_l).astype(f32),
                   "eb1c": np.stack(eb1_l, axis=1).astype(f32),
                   "eb2r": np.stack(eb2_l).astype(f32),
                   "gw": host_w(inp["L0_ffn_gate_w"]),
                   "gb": inp["L0_ffn_gate_b"][None, :].astype(f32), "eye": eye}
    WS["l0ab"] = {"wqkv": np.ascontiguousarray(inp["L0_attn_b_w_qkv"].transpose(1, 0, 2)).astype(f32),
                  "wproj": np.ascontiguousarray(inp["L0_attn_b_w_proj"].transpose(1, 0, 2)).astype(f32),
                  "bp": host_col(inp["L0_attn_b_b_proj"]), "trilm": trilm, "eye": eye}
    WS["l1aa"] = {"wq": host_w(inp["L1_attn_a_Wq"]), "wk": host_w(inp["L1_attn_a_Wk"]),
                  "wv": host_w(inp["L1_attn_a_Wv"]), "wo": host_w(inp["L1_attn_a_Wo"]),
                  "tril01": (j <= q).astype(f32), "triu01": (j > q).astype(f32),
                  "iota1": (np.arange(128, dtype=f32) + 1.0)[:, None], "eye": eye}
    p = np.arange(128)
    j64 = p % 64
    jj = np.where(j64 < 32, j64, j64 - 32)
    inv = 1.0 / (10000.0 ** (jj / 32.0))
    tt = np.arange(T)
    ang = tt[None, :] * inv[:, None]
    cosb = np.cos(ang).astype(f32)
    sinb = (np.where(j64 < 32, -1.0, 1.0)[:, None] * np.sin(ang)).astype(f32)
    swapm = np.zeros((128, 128), f32)
    for m in range(128):
        sw = m + 32 if (m % 64) < 32 else m - 32
        swapm[sw, m] = 1.0
    ew1_l, ew2_l, eb1_l = [], [], []
    for e in range(4):
        hw = host_w(inp["L1_ffn_W1"][e].T).reshape(128, 8, 32, 128).transpose(2, 0, 1, 3)
        ew1_l.append(np.ascontiguousarray(hw))
        ew2_l.append(host_w(inp["L1_ffn_W2"][e].T))
        eb1_l.append(host_col(inp["L1_ffn_b1"][e]))
    WS["l1ffn"] = {"wqt": host_w(inp["L1_ffn_Wq"].T), "wdkvt": host_w(inp["L1_ffn_Wdkv"].T),
                   "wukt": host_w(inp["L1_ffn_Wuk"].T), "wuvt": host_w(inp["L1_ffn_Wuv"].T),
                   "wot": host_w(inp["L1_ffn_Wo"].T),
                   "n1c": host_col(inp["L1_ffn_norm1_w"]), "n2c": host_col(inp["L1_ffn_norm2_w"]),
                   "lngc": host_col(inp["L1_ffn_ln_g"]), "lnbc": host_col(inp["L1_ffn_ln_b"]),
                   "cosb": cosb, "sinb": sinb, "swapm": swapm, "trilm": trilm, "eye": eye,
                   "gw4": host_w(inp["L1_ffn_gate_w"].T),
                   "gb4": inp["L1_ffn_gate_b"][None, :].astype(f32),
                   "ew1": np.stack(ew1_l).astype(f32), "ew2": np.stack(ew2_l).astype(f32),
                   "eb1c": np.stack(eb1_l, axis=1).astype(f32),
                   "eb2r": inp["L1_ffn_b2"].astype(f32)}
    WS["l1ab"] = {"caw": host_w(inp["L1_attn_b_c_attn_w"]),
                  "cbc": host_col(inp["L1_attn_b_c_attn_b"]),
                  "qw": host_w(inp["L1_attn_b_q_w"]), "kw": host_w(inp["L1_attn_b_k_w"]),
                  "vw": host_w(inp["L1_attn_b_v_w"]), "ow": host_w(inp["L1_attn_b_o_w"]),
                  "qbr": inp["L1_attn_b_q_b"][None, :].astype(f32),
                  "kbr": inp["L1_attn_b_k_b"][None, :].astype(f32),
                  "vbr": inp["L1_attn_b_v_b"][None, :].astype(f32),
                  "obr": inp["L1_attn_b_o_b"][None, :].astype(f32),
                  "trilm": trilm, "eye": eye}
    return WS


def kernel(**inputs):
    inp = {k: np.asarray(v, np.float32) for k, v in inputs.items()}
    WS = _prep_weights(inp)
    x = inp["L0_attn_a_x"]
    cores = [host_fm(np.ascontiguousarray(x[i])) for i in range(8)]
    for name in ("l0aa", "l0ffn", "l0ab", "l0ffn", "l1aa", "l1ffn", "l1ab", "l1ffn"):
        nc = _mk_prog(name)
        in_maps = [dict(WS[name], x=cores[i]) for i in range(8)]
        res = run_bass_kernel_spmd(nc, in_maps, core_ids=list(range(8)))
        cores = [np.asarray(r["y"]) for r in res.results]
    return np.stack([host_fm_inv(c) for c in cores]).astype(np.float32)



_GROUP_SPECS = {
    "l0aa": [("wr", [128, 8, 1024]), ("wk", [128, 8, 1024]), ("wv", [128, 8, 1024]),
             ("wo", [128, 8, 1024]), ("cols", [128, 56]), ("hmask", [128, 2]),
             ("hmaskT", [2, 128])],
    "l0ffn": [("ew1", [9, 32, 128, 8, 128]), ("ew2", [9, 128, 32, 1024]),
              ("eb1c", [128, 9, 32]), ("eb2r", [9, 1024]), ("gw", [128, 8, 8]),
              ("gb", [1, 8])],
    "l0ab": [("wqkv", [128, 8, 384]), ("wproj", [128, 8, 128]), ("bp", [128, 8])],
    "l1aa": [("wq", [128, 8, 256]), ("wk", [128, 8, 256]), ("wv", [128, 8, 1024]),
             ("wo", [128, 8, 1024]), ("tril01", [128, 128]), ("triu01", [128, 128]),
             ("iota1", [128, 1])],
    "l1ffn": [("wqt", [128, 8, 1024]), ("wdkvt", [128, 8, 512]),
              ("wukt", [128, 4, 1024]), ("wuvt", [128, 4, 1024]),
              ("wot", [128, 8, 1024]), ("n1c", [128, 8]), ("n2c", [128, 8]),
              ("lngc", [128, 4]), ("lnbc", [128, 4]), ("cosb", [128, T]),
              ("sinb", [128, T]), ("swapm", [128, 128]), ("gw4", [128, 8, 4]),
              ("gb4", [1, 4]), ("ew1", [4, 32, 128, 8, 128]),
              ("ew2", [4, 128, 32, 1024]), ("eb1c", [128, 4, 32]),
              ("eb2r", [4, 1024])],
    "l1ab": [("caw", [128, 8, 3072]), ("cbc", [128, 24]), ("qw", [128, 8, 1024]),
             ("kw", [128, 8, 1024]), ("vw", [128, 8, 1024]), ("ow", [128, 8, 1024]),
             ("qbr", [1, 1024]), ("kbr", [1, 1024]), ("vbr", [1, 1024]),
             ("obr", [1, 1024])],
}
_NEED_EYE = {"l0ffn", "l0ab", "l1aa", "l1ffn", "l1ab"}
_NEED_TRILM = {"l0ab", "l1ffn", "l1ab"}
_SEQ = [("l0aa", None), ("l0ffn", None), ("l0ab", None), ("l0ffn", None),
        ("l1aa", None), ("l1ffn", None), ("l1ab", None), ("l1ffn", None)]


def _mk_fused():
    if "fused" in _PROGS:
        return _PROGS["fused"]
    nc = bass.Bass("TRN2", target_bir_lowering=False, detect_race_conditions=False)
    x0 = nc.dram_tensor("x", [128, 8, T], F32, kind="ExternalInput").ap()
    y = nc.dram_tensor("y", [128, 8, T], F32, kind="ExternalOutput").ap()
    xb = [nc.dram_tensor(f"xb{i}", [128, 8, T], F32).ap() for i in range(7)]
    bufs = [x0] + xb + [y]
    EYE = nc.dram_tensor("eye", [128, 128], F32, kind="ExternalInput").ap()
    TRI = nc.dram_tensor("trilm", [128, 128], F32, kind="ExternalInput").ap()
    Wg = {}
    for g, specs in _GROUP_SPECS.items():
        Wg[g] = {k: nc.dram_tensor(f"{g}_{k}", shp, F32, kind="ExternalInput").ap()
                 for k, shp in specs}
        if g in _NEED_EYE:
            Wg[g]["eye"] = EYE
        if g in _NEED_TRILM:
            Wg[g]["trilm"] = TRI
    Wg["l0ffn"]["acc"] = nc.dram_tensor("l0f_acc", [128, 16, 1024], F32).ap()
    Wg["l1ffn"]["acc"] = nc.dram_tensor("l1f_acc", [128, 16, 1024], F32).ap()
    Wg["l1ffn"]["xin"] = nc.dram_tensor("l1f_xin", [128, 8, T], F32).ap()
    for k in ("qd", "kd", "vd", "nb1", "nb2", "nb3"):
        Wg["l1ab"][k] = nc.dram_tensor("l1b_" + k, [T, 1024], BF16).ap()
    for k in ("m3", "s3"):
        Wg["l1ab"][k] = nc.dram_tensor("l1b_" + k, [T, 48], F32).ap()

    builders = {"l0aa": build_l0aa, "l0ffn": build_l0ffn, "l0ab": build_l0ab,
                "l1aa": build_l1aa, "l1ffn": build_l1ffn, "l1ab": build_l1ab}
    C, mem0 = setup_prog(nc)
    for k, (g, _) in enumerate(_SEQ):
        m = Mem(nc)
        m.off = C.common_end
        builders[g](nc, C, m, bufs[k], bufs[k + 1], Wg[g], tag=f"s{k}")
        if k < 7:
            C.sched.barrier()
    finish_prog(C)
    _PROGS["fused"] = nc
    return nc


def kernel(**inputs):
    inp = {k: np.asarray(v, np.float32) for k, v in inputs.items()}
    WS = _prep_weights(inp)
    x = inp["L0_attn_a_x"]
    nc = _mk_fused()
    im = {"eye": WS["l0ab"]["eye"], "trilm": WS["l0ab"]["trilm"]}
    for g in _GROUP_SPECS:
        for k, v in WS[g].items():
            if k in ("eye", "trilm"):
                continue
            im[f"{g}_{k}"] = v
    in_maps = [dict(im, x=host_fm(np.ascontiguousarray(x[i]))) for i in range(8)]
    res = run_bass_kernel_spmd(nc, in_maps, core_ids=list(range(8)))
    return np.stack([host_fm_inv(np.asarray(r["y"])) for r in res.results]).astype(np.float32)


# version 07
# speedup vs baseline: 1.0564x; 1.0564x over previous
import numpy as np
import ml_dtypes
import concourse.bass as bass
import concourse.mybir as mybir
from concourse.bass_utils import run_bass_kernel_spmd

F32 = mybir.dt.float32
BF16 = mybir.dt.bfloat16
AF = mybir.ActivationFunctionType
ALU = mybir.AluOpType
AX = mybir.AxisListType

ENGINES = ("sync", "tensor", "vector", "scalar", "gpsimd")
T = 2048


class Sched:

    CHAIN = ("vector", "scalar", "gpsimd")

    def __init__(self, nc):
        self.nc = nc
        self.q = {e: [] for e in ENGINES}
        self.counts = {}
        self.hmap = {}
        self.hbase = {}
        self.htotal = {}
        self.ns = 0
        self.ns_names = {0: []}
        self.free_handles = []
        self._ctxs = []
        self.marker = {e: None for e in ENGINES}
        self.chgen = 0
        self.semeng = {}
        self.dmasems = set()
        self.barn = 0
        self.sem("barA")
        self.sem("barB")

    def _pname(self, name):
        if name in ("barA", "barB"):
            return name
        return f"p{self.ns}_{name}"

    def sem(self, name):
        pname = self._pname(name)
        if pname not in self.hmap:
            if self.free_handles and pname not in ("barA", "barB"):
                h = self.free_handles.pop()
            else:
                cm = self.nc.semaphore(pname)
                h = cm.__enter__()
                self._ctxs.append(cm)
                self.htotal[h] = 0
            self.hmap[pname] = h
            self.hbase[pname] = self.htotal[h]
            self.counts[pname] = 0
            self.ns_names.setdefault(self.ns, []).append(pname)
        return pname

    def val(self, s):
        return self.counts[self._pn2(s)]

    def _pn2(self, s):
        return s if s in self.counts else self._pname(s)

    def after(self, *sems):
        return [(self._pn2(s), self.counts[self._pn2(s)]) for s in sems]

    def mark(self, eng):
        m = self.marker[eng]
        return [m] if m is not None else []

    def rotate_chain(self):
        self.chgen += 1
        for e in self.CHAIN:
            self.marker[e] = None

    def nsnew(self):
        self.ns += 1
        self.ns_names.setdefault(self.ns, [])
        for k in list(self.ns_names):
            if k <= self.ns - 2:
                for pname in self.ns_names.pop(k):
                    self.free_handles.append(self.hmap[pname])

    def barrier(self):
        self.barn += 1
        self.sem("barA")
        self.op("tensor", lambda e: e.nop(), inc=[("barA", 1)])
        self.op("sync", lambda e: e.nop(), inc=[("barA", 1)])
        w = (self.mark("vector") + self.mark("scalar") + self.mark("gpsimd")
             + [("barA", self.counts["barA"])])
        self.op("gpsimd", lambda e: e.nop(), wait=w, inc=[("barB", 1)], chain=False)
        tgt = [("barB", self.counts["barB"])]
        for e in ENGINES:
            self.op(e, lambda eng: None, wait=tgt, chain=False)
        for e in self.CHAIN:
            self.marker[e] = None
        self.nsnew()

    def op(self, eng, emit, wait=(), inc=(), chain=True, is_dma=False):
        wait = [(self._pn2(s), v) for s, v in wait]
        if eng in self.CHAIN and chain:
            assert not inc, f"user incs not allowed on chained {eng} op"
            if self.marker[eng] is not None:
                wait.append(self.marker[eng])
            cs = self.sem(f"ch{self.chgen}_{eng}")
            assert self.semeng.setdefault(cs, eng) == eng
            self.counts[cs] += 1
            self.htotal[self.hmap[cs]] += 1
            inc2 = [(cs, 1)]
            self.marker[eng] = (cs, self.counts[cs])
            self.q[eng].append((wait, emit, inc2))
            return self.marker[eng]
        inc2 = []
        for s, n in inc:
            s = self._pn2(s)
            if s not in ("barA", "barB"):
                if is_dma:
                    self.dmasems.add(s)
                else:
                    assert s not in self.dmasems, f"sem {s} used as dma+compute"
                assert self.semeng.setdefault(s, eng) == eng, \
                    f"sem {s} inc'd by {self.semeng[s]} and {eng}"
            self.counts[s] += n
            self.htotal[self.hmap[s]] += n
            inc2.append((s, n))
        self.q[eng].append((wait, emit, inc2))
        return inc2[-1] if inc2 else None

    def dma(self, eng, out, in_, sem, wait=(), **kw):
        self.op(eng, lambda e, out=out, in_=in_, kw=kw: e.dma_start(out=out, in_=in_, **kw),
                wait=wait, inc=[(sem, 16)], chain=False, is_dma=True)
        return (self._pn2(sem), self.counts[self._pn2(sem)])

    def emit_all(self, block):
        for ename in ENGINES:
            ops = self.q[ename]
            if not ops:
                continue

            def mk(ops, sched=self):
                def body(eng):
                    lastw = {}
                    for wait, emit, inc in ops:
                        for s, v in wait:
                            hv = sched.hbase[s] + v
                            h = sched.hmap[s]
                            if v > 0 and lastw.get(h, -1) < hv:
                                eng.wait_ge(h, hv)
                                lastw[h] = hv
                        instr = emit(eng)
                        for s, n in inc:
                            if instr is None:
                                instr = eng.nop()
                            instr.then_inc(sched.hmap[s], n)
                return body

            getattr(block, ename)(mk(list(ops)))

    def close(self):
        for cm in reversed(self._ctxs):
            cm.__exit__(None, None, None)


class NS:
    def __init__(self, **kw):
        self.__dict__.update(kw)


class Mem:

    def __init__(self, nc):
        self.nc = nc
        self.base = (nc.sbuf_base + 31) // 32 * 32
        self.top = nc.sbuf_top
        self.off = self.base

    def alloc(self, name, shape, dtype, at=None):
        per_part = int(np.prod(shape[1:])) * mybir.dt.size(dtype)
        off = self.off if at is None else at
        off = (off + 31) // 32 * 32
        assert off + per_part <= self.top, (
            f"SBUF overflow: {name} at {off} + {per_part} > {self.top}")
        h = self.nc.alloc_sbuf_tensor_at(name, shape, dtype, offset=off)
        if at is None:
            self.off = off + per_part
        return h

    def mark(self):
        return self.off

    def reset(self, mark):
        self.off = mark



def host_fm(x):
    Tn, Dn = x.shape
    return np.ascontiguousarray(x.T.reshape(Dn // 128, 128, Tn).transpose(1, 0, 2))


def host_fm_inv(y):
    p, nch, Tn = y.shape
    return np.ascontiguousarray(y.transpose(1, 0, 2).reshape(nch * 128, Tn).T)


def host_w(w):
    Din, Dout = w.shape
    return np.ascontiguousarray(w.reshape(Din // 128, 128, Dout).transpose(1, 0, 2))


def host_col(v):
    Dn = v.shape[0]
    return np.ascontiguousarray(v.reshape(Dn // 128, 128).T)



def emit_ln_sbuf(C, xs, lnb, name, nchunk=8, Tn=T, ready=(), affine=None):
    s = C.sched
    nts = Tn // 512
    Dn = nchunk * 128
    mmsum = s.sem(name + "_mmsum")
    mmsq = s.sem(name + "_mmsq")
    ps_sum, ps_sq, ps_b0, ps_b1 = C.pb[0], C.pb[1], C.pb[2], C.pb[3]
    sq8, bc0, bc1 = C.sq8, C.bc0, C.bc1
    r_mu, r_e2, r_tm = C.r_mu, C.r_e2, C.r_tm
    sq_marks = {}
    last_apply = None
    prev_bc_copies = None
    prev_row_reads = None
    for ts in range(nts):
        tsl = slice(ts * 512, ts * 512 + 512)
        for c in range(nchunk):
            w = list(ready)
            if ts > 0:
                w += [(mmsq, ts)]
                w += [last_apply]
            sq_marks[c] = s.op("scalar", lambda e, c=c, tsl=tsl: e.activation(
                sq8[:, c, :], xs[:, c, tsl], AF.Square), wait=w)
        for c in range(nchunk):
            w = list(ready) if c == 0 else []
            if c == 0 and prev_row_reads:
                w += prev_row_reads
            s.op("tensor", lambda e, c=c, tsl=tsl: e.matmul(
                ps_sum[0:1, 0:512], C.ones_col[:, 0:1], xs[:, c, tsl],
                start=(c == 0), stop=(c == nchunk - 1)),
                 wait=w, inc=([(mmsum, 1)] if c == nchunk - 1 else []))
        for c in range(nchunk):
            s.op("tensor", lambda e, c=c: e.matmul(
                ps_sq[0:1, 0:512], C.ones_col[:, 0:1], sq8[:, c, :],
                start=(c == 0), stop=(c == nchunk - 1)),
                 wait=[sq_marks[c]],
                 inc=([(mmsq, 1)] if c == nchunk - 1 else []))
        m_mu = s.op("vector", lambda e: e.tensor_scalar(
            r_mu[:], ps_sum[0:1, 0:512], 1.0 / Dn, None, ALU.mult),
             wait=[(mmsum, ts + 1)] + (prev_bc_copies or []))
        m_e2 = s.op("vector", lambda e: e.tensor_scalar(
            r_e2[:], ps_sq[0:1, 0:512], 1.0 / Dn, None, ALU.mult),
             wait=[(mmsq, ts + 1)])
        prev_row_reads = [m_e2]
        s.op("vector", lambda e: e.tensor_tensor(
            r_tm[:], r_mu[:], r_mu[:], ALU.mult))
        m_var = s.op("vector", lambda e: e.tensor_tensor(
            r_e2[:], r_e2[:], r_tm[:], ALU.subtract))
        s.op("scalar", lambda e: e.activation(
            r_tm[:], r_e2[:], AF.Ln, bias=C.cpool[0:1, 0:1]),
             wait=[m_var] + C.cready)
        m_rstd = s.op("scalar", lambda e: e.activation(
            r_e2[:], r_tm[:], AF.Exp, scale=-0.5))
        bcm = s.sem(name + "_bcm")
        s.op("tensor", lambda e: e.matmul(
            ps_b0[:, 0:512], C.ones_row[0:1, 0:128], r_e2[:],
            start=True, stop=True),
             wait=[m_rstd] + (prev_bc_copies or []), inc=[(bcm, 1)])
        s.op("tensor", lambda e: e.matmul(
            ps_b1[:, 0:512], C.ones_row[0:1, 0:128], r_mu[:],
            start=True, stop=True),
             wait=[m_mu] + (prev_bc_copies or []), inc=[(bcm, 1)])
        c0 = s.op("scalar", lambda e: e.copy(bc0[:], ps_b0[:, 0:512]),
                  wait=[(bcm, 2 * (ts + 1))] + ([last_apply] if last_apply else []))
        c1 = s.op("scalar", lambda e: e.copy(bc1[:], ps_b1[:, 0:512]))
        prev_bc_copies = [c0, c1]
        for c in range(nchunk):
            s.op("vector", lambda e, c=c, tsl=tsl: e.tensor_tensor(
                sq8[:, c, :], xs[:, c, tsl], bc1[:], ALU.subtract),
                 wait=[c1, (mmsq, ts + 1)])
            if affine is None:
                last_apply = s.op("vector", lambda e, c=c, tsl=tsl: e.tensor_tensor(
                    lnb[:, c, tsl], sq8[:, c, :], bc0[:], ALU.mult))
            else:
                gfn, bfn = affine
                s.op("vector", lambda e, c=c: e.tensor_tensor(
                    sq8[:, c, :], sq8[:, c, :], bc0[:], ALU.mult))
                last_apply = s.op("vector", lambda e, c=c, tsl=tsl: e.tensor_scalar(
                    lnb[:, c, tsl], sq8[:, c, :], gfn(c), bfn(c),
                    ALU.mult, ALU.add))
    return last_apply


def setup_prog(nc):
    mem = Mem(nc)
    C = NS(sched=Sched(nc), nc=nc, mem=mem)
    C._psum_ctxs = []

    def ps(name, shape, dt):
        cm = nc.psum_tensor(name, shape, dt)
        h = cm.__enter__()
        C._psum_ctxs.append(cm)
        return h

    C.pbS = ps("pbS", [128, 2048], F32)
    C.pb4 = C.pbS[:, 1536:2048]
    C.pb5 = ps("pb5", [128, 512], F32)
    C.pb6 = ps("pb6", [128, 512], F32)
    C.pbTa = ps("pbTa", [128, 1024], BF16)
    C.pbTb = ps("pbTb", [128, 1024], BF16)
    C.pb = [C.pbS[:, 0:512], C.pbS[:, 512:1024], C.pbS[:, 1024:1536], C.pb4]
    alloc_common(mem, C)
    C.common_end = mem.mark()
    return C, mem


def finish_prog(C):
    with C.nc.Block() as block:
        C.sched.emit_all(block)
    C.sched.close()
    for cm in reversed(C._psum_ctxs):
        cm.__exit__(None, None, None)


def alloc_common(mem, C, Tn=T):
    C.ones_col = mem.alloc("ones_col", [128, 8], F32)
    C.ones_row = mem.alloc("ones_row", [128, 128], F32)
    C.ones_row_bf = mem.alloc("ones_row_bf", [128, 512], BF16)
    C.eye = mem.alloc("eye", [128, 128], F32)
    C.eyeb = mem.alloc("eyeb", [128, 128], BF16)
    C.cpool = mem.alloc("cpool", [128, 8], F32)
    C.cvals = [1e-5, 1.0, 1e-6, 1e-9, 0.0, 0.5, 2.0, -1.0]
    C.EPS5 = C.cpool[:, 0:1]
    C.EPS6 = C.cpool[:, 2:3]
    C.lnscr_off = (mem.mark() + 31) // 32 * 32
    C.sq8 = mem.alloc("ln_sq8", [128, 8, 512], F32)
    C.r_mu = mem.alloc("ln_rmu", [1, 512], F32)
    C.r_e2 = mem.alloc("ln_re2", [1, 512], F32)
    C.r_tm = mem.alloc("ln_rtm", [1, 512], F32)
    C.bc0 = mem.alloc("ln_bc0", [128, 512], F32)
    C.bc1 = mem.alloc("ln_bc1", [128, 512], F32)
    mem.reset(max(mem.mark(), C.lnscr_off + 32768))


def emit_init_consts(C):
    s = C.sched
    s.op("vector", lambda e: e.memset(C.ones_col[:], 1.0))
    for i, v in enumerate(C.cvals):
        s.op("vector", lambda e, i=i, v=v: e.memset(C.cpool[:, i:i + 1], v))
    s.op("vector", lambda e: e.memset(C.ones_row[:], 1.0))
    m = s.op("vector", lambda e: e.memset(C.ones_row_bf[:], 1.0))
    C.cready = [m]
    return m



def build_l0aa(nc, C, mem, x_dram, y_dram, W, tag="l0aa"):
    s = C.sched
    GN_N = float(T * 64)

    C.hmask = mem.alloc(tag + "_hmask", [128, 2], F32)
    C.hmaskT = mem.alloc(tag + "_hmaskT", [2, 128], F32)
    LNB = mem.alloc(tag + "_lnb", [128, 8, T], BF16)
    COLS = mem.alloc(tag + "_cols", [128, 56], F32)
    DCOLS = mem.alloc(tag + "_dcols", [128, 40], F32)
    SCOL = mem.alloc(tag + "_scol", [128, 2], F32)
    STAT2 = mem.alloc(tag + "_stat2", [2, 8], F32)
    CBUF = mem.alloc(tag + "_cbuf", [128, 4], F32)
    mA = mem.mark()
    XS = mem.alloc(tag + "_xs", [128, 8, T], F32)
    endA = mem.mark()
    XR = mem.alloc(tag + "_xr", [128, 8, T], BF16, at=mA)
    XK = mem.alloc(tag + "_xk", [128, 8, T], BF16, at=mA + 32768)
    mem.reset(endA)
    XV = mem.alloc(tag + "_xv", [128, 8, T], BF16)
    WK = mem.alloc(tag + "_wk", [128, 8, 1024], BF16, at=C.lnscr_off)
    WV = mem.alloc(tag + "_wv", [128, 8, 1024], BF16, at=C.lnscr_off + 16384)
    mB = mem.mark()
    WR = mem.alloc(tag + "_wr", [128, 8, 1024], BF16, at=mB)
    EK = mem.alloc(tag + "_ek", [128, T], F32)
    VB = mem.alloc(tag + "_vb", [128, T], F32)
    PB = mem.alloc(tag + "_pb", [128, T], F32)
    NUM = mem.alloc(tag + "_num", [128, T], F32)
    DEN = mem.alloc(tag + "_den", [128, T], F32)
    WO = mem.alloc(tag + "_wo", [128, 8, 1024], BF16, at=mB)
    XST = [mem.alloc(tag + f"_xst{i}", [128, 512], F32, at=mB + 16384 + 2048 * i)
           for i in range(2)]
    YST = [mem.alloc(tag + f"_yst{i}", [128, 512], F32, at=mB + 16384 + 4096 + 2048 * i)
           for i in range(2)]
    NORMED = XR
    RSIG = LNB

    xld = s.sem("g"  + "_xld")
    cld = s.sem("g"  + "_cld")
    wld1 = s.sem("g"  + "_wld1")
    wld2 = s.sem("g"  + "_wld2")
    wld3 = s.sem("g"  + "_wld3")
    rmm = s.sem("g"  + "_rmm")
    kvmm = s.sem("g"  + "_kvmm")
    auxmm = s.sem("g"  + "_auxmm")
    bcmm2 = s.sem("g"  + "_bcmm2")
    womm = s.sem("g"  + "_womm")
    xin2 = [s.sem("g"  + "_xin2a"), s.sem("g"  + "_xin2b")]
    yout = s.sem("g"  + "_yout")

    for i in range(4):
        s.dma("sync", XS[:, 2 * i:2 * i + 2, :], x_dram[:, 2 * i:2 * i + 2, :], xld)
    s.dma("sync", COLS[:], W["cols"], cld)
    s.dma("sync", C.hmask[:], W["hmask"], cld)
    s.dma("sync", C.hmaskT[:], W["hmaskT"], cld)
    s.dma("gpsimd", WR[:], W["wr"], wld1)
    hm_ready = s.after(cld)

    lam_m = None
    for j, so in enumerate((0, 8, 16)):
        lam_m = s.op("vector", lambda e, j=j, so=so: e.tensor_scalar(
            DCOLS[:, 8 * j:8 * j + 8], COLS[:, so:so + 8], -1.0, 1.0,
            ALU.mult, ALU.add), wait=s.after(cld))
    s.op("scalar", lambda e: e.activation(DCOLS[:, 24:32], COLS[:, 24:32], AF.Exp),
         wait=s.after(cld))
    s.op("scalar", lambda e: e.activation(DCOLS[:, 24:32], DCOLS[:, 24:32],
                                          AF.Exp, scale=-1.0))
    eu_m = s.op("scalar", lambda e: e.activation(DCOLS[:, 32:40], COLS[:, 32:40], AF.Exp))

    emit_init_consts(C)
    LNM = emit_ln_sbuf(C, XS, LNB, "g_ln", ready=s.after(xld) + C.cready)

    s.dma("gpsimd", WK[:], W["wk"], wld2, wait=[LNM])
    s.dma("gpsimd", WV[:], W["wv"], wld2, wait=[LNM])
    W2R = s.after(wld2)

    mix_last = {}
    for (XM, co, do, nm) in ((XR, 0, 0, "r"), (XK, 8, 8, "k"), (XV, 16, 16, "v")):
        for c in range(8):
            s.op("vector", lambda e, XM=XM, c=c, co=co: e.tensor_scalar(
                XM[:, c, :], LNB[:, c, :], COLS[:, co + c:co + c + 1], None, ALU.mult),
                 wait=[LNM, lam_m])
            mix_last[nm] = s.op("vector", lambda e, XM=XM, c=c, do=do: e.scalar_tensor_tensor(
                XM[:, c, 1:T], LNB[:, c, 0:T - 1], DCOLS[:, do + c:do + c + 1],
                XM[:, c, 1:T], ALU.mult, ALU.add))
    MIXED = mix_last["v"]

    rbanks = [C.pb5[:], C.pb6[:], C.pb[0], C.pb[1]]
    revm = {}
    ri = 0
    for ec in range(8):
        esl = slice(ec * 128, ec * 128 + 128)
        for ts in range(4):
            tsl = slice(ts * 512, ts * 512 + 512)
            ps = rbanks[ri % 4]
            waits = [MIXED] + s.after(wld1)
            if ri >= 4:
                waits.append(revm[ri - 4])

            def mms(e, esl=esl, tsl=tsl, ps=ps):
                last = None
                for dc in range(8):
                    last = e.matmul(ps[:, 0:512], WR[:, dc, esl], XR[:, dc, tsl],
                                    start=(dc == 0), stop=(dc == 7))
                return last

            s.op("tensor", mms, wait=waits, inc=[(rmm, 1)])
            revm[ri] = s.op("scalar", lambda e, ps=ps, tsl=tsl, ec=ec:
                            e.activation(RSIG[:, ec, tsl], ps[:, 0:512], AF.Sigmoid),
                            wait=[(rmm, ri + 1), MIXED])
            ri += 1
    RDONE = revm[31]

    kvi = 0
    kvbanks = [C.pb5[:], C.pb6[:]]
    kvevm = {}
    ckdm = {}
    prev_stat_rd = None
    prev_cbuf_rd = None
    for ec in range(8):
        esl = slice(ec * 128, ec * 128 + 128)
        for which, (WX, XMIX, dst) in enumerate(((WK, XK, EK), (WV, XV, VB))):
            for ts in range(4):
                tsl = slice(ts * 512, ts * 512 + 512)
                ps = kvbanks[kvi % 2]
                waits = [MIXED] + W2R
                if kvi < 2:
                    waits.append(RDONE)
                else:
                    waits.append(kvevm[kvi - 2])

                def mms(e, WX=WX, XMIX=XMIX, esl=esl, tsl=tsl, ps=ps):
                    last = None
                    for dc in range(8):
                        last = e.matmul(ps[:, 0:512], WX[:, dc, esl], XMIX[:, dc, tsl],
                                        start=(dc == 0), stop=(dc == 7))
                    return last

                s.op("tensor", mms, wait=waits, inc=[(kvmm, 1)])
                ewaits = [(kvmm, kvi + 1)]
                if ec > 0:
                    ewaits.append(ckdm[ec - 1])
                if which == 0:
                    kvevm[kvi] = s.op("scalar", lambda e, ps=ps, tsl=tsl: e.activation(
                        EK[:, tsl], ps[:, 0:512], AF.Exp), wait=ewaits)
                else:
                    kvevm[kvi] = s.op("scalar", lambda e, ps=ps, tsl=tsl: e.copy(
                        VB[:, tsl], ps[:, 0:512]), wait=ewaits)
                kvi += 1
        wbc = bass.AP(DCOLS, 24 + ec, [[40, 128], [0, T]])
        euc = DCOLS[:, 32 + ec:32 + ec + 1]
        s.op("vector", lambda e: e.tensor_tensor(PB[:], EK[:], VB[:], ALU.mult),
             wait=[kvevm[kvi - 1], eu_m])
        s.op("vector", lambda e, wbc=wbc: e.tensor_tensor_scan(
            NUM[:], wbc, PB[:], 0.0, ALU.mult, ALU.add))
        s.op("vector", lambda e, wbc=wbc: e.tensor_tensor_scan(
            DEN[:], wbc, EK[:], 0.0, ALU.mult, ALU.add))
        s.op("vector", lambda e, euc=euc: e.scalar_tensor_tensor(
            VB[:, 1:T], PB[:, 1:T], euc, NUM[:, 0:T - 1], ALU.mult, ALU.add))
        s.op("vector", lambda e, euc=euc: e.tensor_scalar(
            VB[:, 0:1], PB[:, 0:1], euc, None, ALU.mult))
        s.op("vector", lambda e, euc=euc: e.tensor_scalar(
            EK[:], EK[:], euc, 1e-9, ALU.mult, ALU.add))
        s.op("vector", lambda e: e.tensor_tensor(
            PB[:, 1:T], EK[:, 1:T], DEN[:, 0:T - 1], ALU.add))
        mBv = s.op("vector", lambda e: e.tensor_copy(PB[:, 0:1], EK[:, 0:1]))
        s.op("scalar", lambda e: e.activation(NUM[:], PB[:], AF.Ln), wait=[mBv])
        mrb = s.op("scalar", lambda e: e.activation(EK[:], NUM[:], AF.Exp, scale=-1.0))
        s.op("vector", lambda e: e.tensor_tensor(DEN[:], VB[:], EK[:], ALU.mult),
             wait=[mrb, RDONE])
        s.op("vector", lambda e, ec=ec: e.scalar_tensor_tensor(
            VB[:], DEN[:], 1.0, RSIG[:, ec, :], ALU.bypass, ALU.mult,
            accum_out=SCOL[:, 0:1]))
        mscr = s.op("vector", lambda e: e.scalar_tensor_tensor(
            PB[:], VB[:], 1.0, VB[:], ALU.bypass, ALU.mult,
            accum_out=SCOL[:, 1:2]))
        s.op("tensor", lambda e: e.matmul(C.pb[2][0:2, 0:2], C.hmask[:, 0:2],
                                          SCOL[:, 0:2], start=True, stop=True),
             wait=[mscr] + hm_ready + ([prev_stat_rd] if prev_stat_rd else []),
             inc=[(auxmm, 1)])
        prev_stat_rd = s.op("vector", lambda e: e.tensor_scalar(
            STAT2[:, 0:2], C.pb[2][0:2, 0:2], 1.0 / GN_N, None, ALU.mult),
             wait=[(auxmm, ec + 1)])
        s.op("vector", lambda e: e.tensor_tensor(
            STAT2[:, 2:3], STAT2[:, 0:1], STAT2[:, 0:1], ALU.mult))
        mvar = s.op("vector", lambda e: e.tensor_tensor(
            STAT2[:, 3:4], STAT2[:, 1:2], STAT2[:, 2:3], ALU.subtract))
        s.op("scalar", lambda e: e.activation(
            STAT2[:, 4:5], STAT2[:, 3:4], AF.Ln, bias=C.cpool[0:2, 0:1]),
             wait=[mvar] + C.cready)
        mrstd = s.op("scalar", lambda e: e.activation(
            STAT2[:, 1:2], STAT2[:, 4:5], AF.Exp, scale=-0.5))
        s.op("tensor", lambda e: e.matmul(C.pb[3][:, 0:2], C.hmaskT[0:2, :],
                                          STAT2[:, 0:2], start=True, stop=True),
             wait=[mrstd] + ([prev_cbuf_rd] if prev_cbuf_rd else []),
             inc=[(bcmm2, 1)])
        s.op("vector", lambda e, ec=ec: e.tensor_tensor(
            CBUF[:, 0:1], C.pb[3][:, 1:2], COLS[:, 40 + ec:40 + ec + 1], ALU.mult),
             wait=[(bcmm2, ec + 1)])
        prev_cbuf_rd = s.op("vector", lambda e: e.tensor_tensor(
            CBUF[:, 1:2], C.pb[3][:, 0:1], CBUF[:, 0:1], ALU.mult))
        s.op("vector", lambda e, ec=ec: e.tensor_tensor(
            CBUF[:, 2:3], COLS[:, 48 + ec:48 + ec + 1], CBUF[:, 1:2], ALU.subtract))
        ckdm[ec] = s.op("vector", lambda e, ec=ec: e.tensor_scalar(
            NORMED[:, ec, :], VB[:], CBUF[:, 0:1], CBUF[:, 2:3], ALU.mult, ALU.add))

    CKD = ckdm[7]
    s.dma("gpsimd", WO[:], W["wo"], wld3, wait=[CKD])

    wi = 0
    wbanks = [C.pb5[:], C.pb6[:], C.pb[0], C.pb[1]]
    wodvem = {}
    for ec in range(8):
        esl = slice(ec * 128, ec * 128 + 128)
        for ts in range(4):
            tsl = slice(ts * 512, ts * 512 + 512)
            ps = wbanks[wi % 4]
            waits = [CKD] + s.after(wld3)
            if wi >= 4:
                waits.append(wodvem[wi - 4])

            def mms(e, esl=esl, tsl=tsl, ps=ps):
                last = None
                for dc in range(8):
                    last = e.matmul(ps[:, 0:512], WO[:, dc, esl], NORMED[:, dc, tsl],
                                    start=(dc == 0), stop=(dc == 7))
                return last

            s.op("tensor", mms, wait=waits, inc=[(womm, 1)])
            b = wi % 2
            xw = [CKD] if wi < 2 else [wodvem[wi - 2]]
            s.dma("sync", XST[b][:], x_dram[:, ec, tsl], xin2[b], wait=xw)
            wodvem[wi] = s.op("vector", lambda e, ps=ps, b=b: e.scalar_tensor_tensor(
                YST[b][:], ps[:, 0:512], 1.0, XST[b][:], ALU.bypass, ALU.add),
                 wait=[(womm, wi + 1), (xin2[b], s.val(xin2[b]))] +
                      ([(yout, 16 * wi)] if wi >= 2 else []))
            s.dma("sync", y_dram[:, ec, tsl], YST[b][:], yout, wait=[wodvem[wi]])
            wi += 1
    s.op("sync", lambda e: None, wait=[(yout, 16 * 32)])



def build_l0ffn(nc, C, mem, x_dram, y_dram, W, tag="l0f"):
    s = C.sched
    TB = 1024
    NTB = T // TB
    NE = 9

    LNB = mem.alloc(tag + "_lnb", [128, 8, T], BF16)
    WEX = mem.alloc(tag + "_wex", [128, 16, 9], F32)
    RT = mem.alloc(tag + "_rt", [128, 8], F32)
    P8 = mem.alloc(tag + "_p8", [128, 8], F32)
    M8 = mem.alloc(tag + "_m8", [128, 8], F32)
    PM8 = mem.alloc(tag + "_pm8", [128, 8], F32)
    GW = mem.alloc(tag + "_gw", [128, 8, 8], BF16)
    GB = mem.alloc(tag + "_gb", [1, 8], BF16)
    EB1C = mem.alloc(tag + "_eb1c", [128, 9, 32], F32)
    _eb2r = mem.alloc(tag + "_eb2r", [1, 1024], BF16)
    EB2R = [_eb2r, _eb2r]
    EW1S = [mem.alloc(tag + f"_ew1s{i}", [128, 8, 128], BF16) for i in range(2)]
    AST = [mem.alloc(tag + f"_ast{i}", [128, 512], F32) for i in range(2)]
    mA = mem.mark()
    XS = mem.alloc(tag + "_xs", [128, 8, T], F32)
    endA = mem.mark()
    H = mem.alloc(tag + "_h", [128, 32, TB], BF16, at=mA)
    XT = [mem.alloc(tag + f"_xt{i}", [128, 512], F32, at=mA + 4096 * i) for i in range(2)]
    YT = [mem.alloc(tag + f"_yt{i}", [128, 512], F32, at=mA + 8192 + 4096 * i) for i in range(2)]
    AT = [mem.alloc(tag + f"_at{i}", [128, 128], F32, at=mA + 16384 + 1024 * i) for i in range(4)]
    mem.reset(endA)
    EW2 = mem.alloc(tag + "_ew2", [128, 32, 1024], BF16)

    xld = s.sem("g"  + "_xld")
    cld = s.sem("g"  + "_cld")
    cldg = s.sem("g"  + "_cldg")
    w1ld = s.sem("g"  + "_w1ld")
    w2ld = s.sem("g"  + "_w2ld")
    gmm = s.sem("g"  + "_gmm")
    g1mm = s.sem("g"  + "_g1mm")
    g2mm = s.sem("g"  + "_g2mm")
    acd = [s.sem("g"  + "_acd0"), s.sem("g"  + "_acd1")]
    w1l2 = [s.sem("g"  + "_w1l0"), s.sem("g"  + "_w1l1")]
    fat_s = [s.sem(tag + f"_fat{j}") for j in range(4)]
    fxt_s = [s.sem("g"  + "_fxa"), s.sem("g"  + "_fxb")]
    trmm = s.sem("g"  + "_trmm")
    yout = s.sem("g"  + "_yout")

    for i in range(4):
        s.dma("sync", XS[:, 2 * i:2 * i + 2, :], x_dram[:, 2 * i:2 * i + 2, :], xld)
    s.dma("gpsimd", GW[:], W["gw"], cldg)
    s.dma("gpsimd", GB[:], W["gb"], cldg)
    s.dma("sync", EB1C[:], W["eb1c"], cld)
    s.dma("sync", C.eye[:], W["eye"], cld)
    s.dma("gpsimd", C.eyeb[:], W["eye"], cldg)

    emit_init_consts(C)
    LNM = emit_ln_sbuf(C, XS, LNB, "g_ln", ready=s.after(xld) + C.cready)

    mwex = s.op("vector", lambda e: e.memset(WEX[:, :, 8], 1.0))

    grd_prev = None
    gr_last = None
    for tc in range(16):
        tsl = slice(tc * 128, tc * 128 + 128)
        psg = C.pb[2]

        def gmms(e, tsl=tsl):
            last = None
            for dc in range(8):
                last = e.matmul(psg[0:128, 0:8], LNB[:, dc, tsl], GW[:, dc, 0:8],
                                start=(dc == 0), stop=False)
            return e.matmul(psg[0:128, 0:8], C.ones_row_bf[0:1, 0:128], GB[0:1, 0:8],
                            start=False, stop=True)

        w = [LNM] + s.after(cld, cldg) + C.cready
        if grd_prev:
            w += [grd_prev, gr_last]
        s.op("tensor", gmms, wait=w, inc=[(gmm, 1)])
        s.op("vector", lambda e: e.tensor_reduce(RT[:, 0:1], psg[0:128, 0:8], AX.X, ALU.max),
             wait=[(gmm, tc + 1)])
        mnm1 = s.op("vector", lambda e: e.tensor_scalar(RT[:, 1:2], RT[:, 0:1], -1.0,
                                                        None, ALU.mult))
        grd_prev = s.op("scalar", lambda e: e.activation(
            P8[:], psg[0:128, 0:8], AF.Exp, bias=RT[:, 1:2], accum_out=RT[:, 2:3]),
             wait=[(gmm, tc + 1), mnm1] + ([gr_last] if gr_last else []))
        s.op("scalar", lambda e: e.activation(RT[:, 3:4], RT[:, 2:3], AF.Ln))
        mrS = s.op("scalar", lambda e: e.activation(RT[:, 4:5], RT[:, 3:4], AF.Exp,
                                                    scale=-1.0))
        s.op("vector", lambda e: e.tensor_scalar(P8[:], P8[:], RT[:, 4:5], None, ALU.mult),
             wait=[mrS])
        s.op("vector", lambda e: e.tensor_reduce(RT[:, 5:6], P8[:], AX.X, ALU.max))
        s.op("vector", lambda e: e.tensor_scalar(M8[:], P8[:], RT[:, 5:6], None, ALU.is_ge))
        s.op("vector", lambda e: e.scalar_tensor_tensor(PM8[:], P8[:], 1.0, M8[:],
                                                        ALU.bypass, ALU.mult))
        s.op("vector", lambda e: e.tensor_tensor(PM8[:], P8[:], PM8[:], ALU.subtract))
        s.op("vector", lambda e: e.tensor_reduce(RT[:, 6:7], PM8[:], AX.X, ALU.max))
        s.op("vector", lambda e: e.tensor_tensor(RT[:, 7:8], RT[:, 5:6], RT[:, 6:7], ALU.add))
        mws = s.op("vector", lambda e: e.tensor_scalar(RT[:, 7:8], RT[:, 7:8], 1.0, 1e-9,
                                                       ALU.mult, ALU.add))
        s.op("scalar", lambda e: e.activation(RT[:, 3:4], RT[:, 7:8], AF.Ln), wait=[mws])
        mrw = s.op("scalar", lambda e: e.activation(RT[:, 4:5], RT[:, 3:4], AF.Exp,
                                                    scale=-1.0))
        s.op("vector", lambda e: e.tensor_scalar(M8[:], P8[:], RT[:, 6:7], None, ALU.is_ge),
             wait=[mrw])
        s.op("vector", lambda e: e.tensor_tensor(M8[:], P8[:], M8[:], ALU.mult))
        gr_last = s.op("vector", lambda e, tc=tc: e.tensor_scalar(
            WEX[:, tc, 0:8], M8[:], RT[:, 4:5], None, ALU.mult))
    GATED = [gr_last, mwex]

    g1t = 0
    g2t = 0
    nacc = 0
    w1n = 0
    g1evm = {}
    g2evm = {}
    ew1_hist = {}
    for e in range(NE):
        ew2wait = [(g2mm, g2t)] if e > 0 else []
        s.dma("gpsimd", EW2[:], W["ew2"][e], w2ld, wait=ew2wait)
        s.dma("gpsimd", EB2R[e % 2][:], W["eb2r"][e:e + 1, :], w2ld)
        w2tgt = s.val(w2ld)
        for tb in range(NTB):
            for ic in range(32):
                b = w1n % 2
                dmawait = [(g1mm, ew1_hist[w1n - 2])] if w1n >= 2 else []
                s.dma("gpsimd", EW1S[b][:], W["ew1"][e, ic], w1l2[b], wait=dmawait)
                w1tgt = s.val(w1l2[b])
                for tsp in range(TB // 512):
                    ps = (C.pb5 if g1t % 2 == 0 else C.pb6)[:]

                    def mms(eng, b=b, tb=tb, tsp=tsp, ps=ps):
                        last = None
                        rs = slice(tb * TB + tsp * 512, tb * TB + tsp * 512 + 512)
                        for dc in range(8):
                            last = eng.matmul(ps[:, 0:512], EW1S[b][:, dc, :],
                                              LNB[:, dc, rs],
                                              start=(dc == 0), stop=(dc == 7))
                        return last

                    waits = [(w1l2[b], w1tgt), LNM]
                    if g1t >= 2:
                        waits.append(g1evm[g1t - 2])
                    s.op("tensor", mms, wait=waits, inc=[(g1mm, 1)])
                    psl = slice(tsp * 512, tsp * 512 + 512)
                    g1evm[g1t] = s.op("scalar", lambda eng, ps=ps, ic=ic, psl=psl, e=e:
                                      eng.activation(H[:, ic, psl], ps[:, 0:512],
                                                     AF.Gelu_apprx_tanh,
                                                     bias=EB1C[:, e, ic:ic + 1]),
                                      wait=[(g1mm, g1t + 1), (g2mm, g2t)] + s.after(cld))
                    g1t += 1
                ew1_hist[w1n] = g1t
                w1n += 1
            HREADY = g1evm[g1t - 1]
            for tsub in range(TB // 128):
                tc = tb * (TB // 128) + tsub
                hsl = slice(tsub * 128, tsub * 128 + 128)
                for dsp in range(2):
                    dsl = slice(dsp * 512, dsp * 512 + 512)
                    ps = C.pb[0 + (g2t % 2)]

                    def mms2(eng, hsl=hsl, dsl=dsl, ps=ps, e=e):
                        last = None
                        for ic in range(32):
                            last = eng.matmul(ps[:, 0:512], H[:, ic, hsl],
                                              EW2[:, ic, dsl],
                                              start=(ic == 0), stop=False)
                        return eng.matmul(ps[:, 0:512], C.ones_row_bf[0:1, 0:128],
                                          EB2R[e % 2][0:1, dsl], start=False, stop=True)

                    waits = [HREADY, (w2ld, w2tgt)]
                    if g2t >= 2:
                        waits.append(g2evm[g2t - 2])
                    s.op("tensor", mms2, wait=waits, inc=[(g2mm, 1)])
                    b2 = nacc % 2
                    evw = [(g2mm, g2t + 1)] + GATED
                    if nacc >= 2:
                        evw.append((acd[b2], s.val(acd[b2])))
                    g2evm[g2t] = s.op("scalar", lambda eng, ps=ps, b2=b2, tc=tc, e=e:
                                      eng.activation(AST[b2][:], ps[:, 0:512], AF.Copy,
                                                     scale=WEX[:, tc, e:e + 1]),
                                      wait=evw)
                    accw = [g2evm[g2t]]
                    if e > 0 and tb == 0 and tsub == 0 and dsp == 0:
                        accw += s.after(acd[0], acd[1])
                    kw = {} if e == 0 else {"accum_op": ALU.add}
                    s.dma("gpsimd", W["acc"][:, tc, dsl], AST[b2][:], acd[b2],
                          wait=accw, **kw)
                    nacc += 1
                    g2t += 1

    EXPDONE = s.after(acd[0], acd[1])

    fi = 0
    fdvem = {}
    for ec in range(8):
        for ts in range(4):
            tsl = slice(ts * 512, ts * 512 + 512)
            ps = C.pb[0 + (fi % 2)]
            for j in range(4):
                tc = ts * 4 + j
                aw = list(EXPDONE)
                if fi >= 1:
                    aw.append((trmm, fi))
                s.dma("sync", AT[j][:], W["acc"][:, tc, ec * 128:ec * 128 + 128],
                      fat_s[j], wait=aw if j == 0 else [])
            trw = [(fs, s.val(fs)) for fs in fat_s] + s.after(cld)
            if fi >= 2:
                trw.append(fdvem[fi - 2])

            def trans(eng, ps=ps):
                last = None
                for j in range(4):
                    last = eng.transpose(ps[:, j * 128:j * 128 + 128], AT[j][:],
                                         C.eye[:, :])
                return last

            s.op("tensor", trans, wait=trw, inc=[(trmm, 1)])
            b = fi % 2
            xw = list(EXPDONE) if fi < 2 else [fdvem[fi - 2]]
            s.dma("sync", XT[b][:], x_dram[:, ec, tsl], fxt_s[b], wait=xw)
            s.op("vector", lambda e, b=b, ec=ec, tsl=tsl: e.tensor_tensor(
                YT[b][:], XT[b][:], LNB[:, ec, tsl], ALU.add),
                 wait=[(fxt_s[b], s.val(fxt_s[b])), (trmm, fi + 1)] +
                      ([(yout, 16 * fi)] if fi >= 2 else []))
            fdvem[fi] = s.op("vector", lambda e, b=b, ps=ps: e.scalar_tensor_tensor(
                YT[b][:], ps[:, 0:512], 1.0, YT[b][:], ALU.bypass, ALU.add))
            s.dma("sync", y_dram[:, ec, tsl], YT[b][:], yout, wait=[fdvem[fi]])
            fi += 1
    s.op("sync", lambda e: None, wait=[(yout, 16 * 32)])



def build_l0ab(nc, C, mem, x_dram, y_dram, W, tag="l0b"):
    s = C.sched
    LNB = mem.alloc(tag + "_lnb", [128, 8, T], BF16)
    P = mem.alloc(tag + "_p", [128, T], BF16)
    PTS = mem.alloc(tag + "_pts", [128, 16, 128], BF16)
    WQKV = mem.alloc(tag + "_wqkv", [128, 8, 384], BF16)
    WPROJ = mem.alloc(tag + "_wproj", [128, 8, 128], BF16)
    BP = mem.alloc(tag + "_bp", [128, 8], F32)
    TRILM = mem.alloc(tag + "_trilm", [128, 128], F32)
    MX = mem.alloc(tag + "_mx", [128, 6], F32)
    XST = [mem.alloc(tag + f"_xst{i}", [128, 512], F32) for i in range(2)]
    YST = [mem.alloc(tag + f"_yst{i}", [128, 512], F32) for i in range(2)]
    mA = mem.mark()
    XS = mem.alloc(tag + "_xs", [128, 8, T], F32)
    endA = mem.mark()
    Q = mem.alloc(tag + "_q", [128, 8, T], BF16, at=mA)
    K_ = mem.alloc(tag + "_k", [128, 8, T], BF16, at=mA + 32768)
    mem.reset(endA)
    VTM = mem.alloc(tag + "_vtm", [128, 16, 1024], BF16, at=C.lnscr_off)
    AOUT = LNB

    xld = s.sem("g"  + "_xld")
    cldh = s.sem("g"  + "_cldh")
    cldg = s.sem("g"  + "_cldg")
    qkmm = s.sem("g"  + "_qkmm")
    smm = s.sem("g"  + "_smm")
    trm = s.sem("g"  + "_trm")
    avm = s.sem("g"  + "_avm")
    pjmm = s.sem("g"  + "_pjmm")
    xin2 = [s.sem("g"  + "_xin2a"), s.sem("g"  + "_xin2b")]
    yout = s.sem("g"  + "_yout")

    for i in range(4):
        s.dma("sync", XS[:, 2 * i:2 * i + 2, :], x_dram[:, 2 * i:2 * i + 2, :], xld)
    s.dma("sync", TRILM[:], W["trilm"], cldh)
    s.dma("sync", BP[:], W["bp"], cldh)
    s.dma("sync", C.eye[:], W["eye"], cldh)
    s.dma("gpsimd", WQKV[:], W["wqkv"], cldg)
    s.dma("gpsimd", WPROJ[:], W["wproj"], cldg)
    s.dma("gpsimd", C.eyeb[:], W["eye"], cldg)

    emit_init_consts(C)
    LNM = emit_ln_sbuf(C, XS, LNB, "g_ln", ready=s.after(xld) + C.cready)
    WRDY = s.after(cldg)

    qk_ev = {}
    nmm = 0
    banks = [C.pb5, C.pb6]
    for oc in range(16):
        g, lo = oc // 3 if oc < 8 else (oc - 8) // 3 + 0, 0
        flat = oc * 128 if oc < 8 else 1024 + (oc - 8) * 128
        g = flat // 384
        lo = flat % 384
        dst = Q if oc < 8 else K_
        ch = oc % 8
        for ts in range(4):
            tsl = slice(ts * 512, ts * 512 + 512)
            ps = banks[nmm % 2][:]
            w = [LNM] + WRDY
            if nmm >= 2:
                w.append(qk_ev[nmm - 2])
            s.op("tensor", lambda e, g=g, lo=lo, tsl=tsl, ps=ps: e.matmul(
                ps[:, 0:512], WQKV[:, g, lo:lo + 128], LNB[:, g, tsl],
                start=True, stop=True), wait=w, inc=[(qkmm, 1)])
            qk_ev[nmm] = s.op("scalar", lambda e, ps=ps, dst=dst, ch=ch, tsl=tsl:
                              e.copy(dst[:, ch, tsl], ps[:, 0:512]),
                              wait=[(qkmm, nmm + 1)])
            nmm += 1
    for tc in range(16):
        for (g, lv0, lv1, v0) in ((5, 128, 384, 0), (6, 0, 384, 256), (7, 0, 384, 640)):
            w_ = lv1 - lv0
            ps = banks[nmm % 2][:]
            wts = [LNM] + WRDY
            if nmm >= 2:
                wts.append(qk_ev[nmm - 2])
            s.op("tensor", lambda e, g=g, tc=tc, lv0=lv0, w_=w_, ps=ps: e.matmul(
                ps[:, 0:w_], LNB[:, g, tc * 128:tc * 128 + 128],
                WQKV[:, g, lv0:lv0 + w_], start=True, stop=True),
                 wait=wts, inc=[(qkmm, 1)])
            qk_ev[nmm] = s.op("scalar", lambda e, ps=ps, tc=tc, v0=v0, w_=w_:
                              e.copy(VTM[:, tc, v0:v0 + w_], ps[:, 0:w_]),
                              wait=[(qkmm, nmm + 1), LNM])
            nmm += 1
    QKVMM = s.after(qkmm)
    QKVM = qk_ev[nmm - 1]

    aexp_prev = None
    pts_prev = None
    aev_prev = None
    trn = 0
    avn = 0
    for h in range(16):
        qch, qb = h // 2, (h % 2) * 64
        for qi in range(16):
            kw_ = T
            kw1 = 1536
            ntail = kw_ - kw1
            nb = 0
            last_s = None
            for kb in range(0, kw_, 512):
                w_ = min(512, kw_ - kb)
                ps = C.pbS[:, kb:kb + w_] if kb < 1536 else C.pb4[:, 0:w_]
                wts = QKVMM + [QKVM]
                if aexp_prev is not None and nb == 0:
                    wts.append(aexp_prev)
                last_s = s.op("tensor", lambda e, qb=qb, qch=qch, qi=qi, kb=kb, w_=w_, ps=ps:
                              e.matmul(ps, Q[qb:qb + 64, qch, qi * 128:qi * 128 + 128],
                                       K_[qb:qb + 64, qch, kb:kb + w_],
                                       start=True, stop=True),
                              wait=wts, inc=[(smm, 1)])
                nb += 1
            smm_w = [(smm, s.val(smm))]
            s.op("vector", lambda e: e.tensor_reduce(
                MX[:, 0:1], C.pbS[:, 0:1536], AX.X, ALU.max), wait=smm_w)
            s.op("vector", lambda e, ntail=ntail: e.tensor_reduce(
                MX[:, 4:5], C.pb4[:, 0:ntail], AX.X, ALU.max))
            s.op("vector", lambda e: e.tensor_tensor(
                MX[:, 0:1], MX[:, 0:1], MX[:, 4:5], ALU.max))
            mneg = s.op("vector", lambda e: e.tensor_scalar(
                MX[:, 1:2], MX[:, 0:1], -0.125, None, ALU.mult))
            exw = [mneg]
            if trn > 0:
                exw.append((trm, trn))
            mex = s.op("scalar", lambda e, kw1=kw1: e.activation(
                P[:, 0:kw1], C.pbS[:, 0:kw1], AF.Exp, bias=MX[:, 1:2],
                scale=0.125, accum_out=MX[:, 2:3]), wait=exw)
            mex = s.op("scalar", lambda e, kw1=kw1, ntail=ntail: e.activation(
                P[:, kw1:kw1 + ntail], C.pb4[:, 0:ntail], AF.Exp,
                bias=MX[:, 1:2], scale=0.125, accum_out=MX[:, 5:6]))
            s.op("vector", lambda e: e.tensor_tensor(
                MX[:, 2:3], MX[:, 2:3], MX[:, 5:6], ALU.add), wait=[mex])
            aexp_prev = mex
            s.op("scalar", lambda e: e.activation(MX[:, 3:4], MX[:, 2:3], AF.Ln),
                 wait=s.mark("vector"))
            mri = s.op("scalar", lambda e: e.activation(
                MX[:, 3:4], MX[:, 3:4], AF.Exp, scale=-1.0))
            msc = s.op("vector", lambda e, kw_=kw_: e.tensor_scalar(
                P[:, 0:kw_], P[:, 0:kw_], MX[:, 3:4], None, ALU.mult), wait=[mri])
            nt = 16
            pts_marks = {}
            for b0 in range(0, nt, 8):
                bank = C.pbTa if (b0 // 8) % 2 == 0 else C.pbTb
                cnt = min(8, nt - b0)
                trw = [msc] + s.after(cldg)
                if pts_prev is not None:
                    trw.append(pts_prev)

                def trans(e, b0=b0, cnt=cnt, bank=bank):
                    last = None
                    for j in range(cnt):
                        last = e.transpose(bank[:, j * 128:j * 128 + 128],
                                           P[:, (b0 + j) * 128:(b0 + j) * 128 + 128],
                                           C.eyeb[:, :])
                    return last

                s.op("tensor", trans, wait=trw, inc=[(trm, 1)])
                trn += 1
                for j in range(cnt):
                    cpw = [(trm, trn)]
                    if aev_prev is not None:
                        cpw.append(aev_prev)
                    pts_marks[b0 + j] = s.op(
                        "vector", lambda e, j=j, b0=b0, bank=bank: e.tensor_copy(
                            PTS[:, b0 + j, :], bank[:, j * 128:j * 128 + 128]),
                        wait=cpw)
                pts_prev = pts_marks[b0 + cnt - 1]
            avw0 = [aev_prev] if aev_prev is not None else []

            def avmm(e, h=h, qi=qi, qb=qb):
                last = None
                for j in range(16):
                    last = e.matmul(C.pb6[qb:qb + 64, 0:128],
                                    VTM[:, j, h * 64:h * 64 + 64],
                                    PTS[:, j, :], start=(j == 0), stop=(j == 15))
                return last

            s.op("tensor", avmm, wait=avw0 + [pts_prev], inc=[(avm, 1)])
            avn += 1
            aev_prev = s.op("scalar", lambda e, qb=qb, qch=qch, qi=qi: e.copy(
                AOUT[qb:qb + 64, qch, qi * 128:qi * 128 + 128],
                C.pb6[qb:qb + 64, 0:128]),
                 wait=[(avm, avn)] + QKVMM)
    ATTND = aev_prev

    pi = 0
    pdvem = {}
    for oc in range(8):
        for ts in range(4):
            tsl = slice(ts * 512, ts * 512 + 512)
            ps = (C.pb5 if pi % 2 == 0 else C.pbS[:, 0:512])
            w = [ATTND] + WRDY
            if pi >= 2:
                w.append(pdvem[pi - 2])
            s.op("tensor", lambda e, oc=oc, tsl=tsl, ps=ps: e.matmul(
                ps[:, 0:512] if ps is C.pb5 else ps, WPROJ[:, oc, :],
                AOUT[:, oc, tsl], start=True, stop=True),
                 wait=w, inc=[(pjmm, 1)])
            b = pi % 2
            xw = [ATTND] if pi < 2 else [pdvem[pi - 2]]
            s.dma("sync", XST[b][:], x_dram[:, oc, tsl], xin2[b], wait=xw)
            s.op("vector", lambda e, b=b, oc=oc: e.tensor_scalar(
                XST[b][:], XST[b][:], 1.0, BP[:, oc:oc + 1], ALU.mult, ALU.add),
                 wait=[(xin2[b], s.val(xin2[b]))] + s.after(cldh))
            pdvem[pi] = s.op("vector", lambda e, b=b, ps=ps: e.scalar_tensor_tensor(
                YST[b][:], ps[:, 0:512] if ps is C.pb5 else ps, 1.0, XST[b][:],
                ALU.bypass, ALU.add),
                 wait=[(pjmm, pi + 1)] + ([(yout, 16 * pi)] if pi >= 2 else []))
            s.dma("sync", y_dram[:, oc, tsl], YST[b][:], yout, wait=[pdvem[pi]])
            pi += 1
    s.op("sync", lambda e: None, wait=[(yout, 16 * 32)])
    return NS(Q=Q, K=K_, VTM=VTM, AOUT=AOUT, LNB=LNB, P=P, yout=yout)



def build_l1aa(nc, C, mem, x_dram, y_dram, W, tag="l1a"):
    s = C.sched
    LNB = mem.alloc(tag + "_lnb", [128, 8, T], BF16)
    PS1 = mem.alloc(tag + "_ps1", [128, T], F32)
    P = mem.alloc(tag + "_p", [128, T], BF16)
    PTS = mem.alloc(tag + "_pts", [128, 16, 128], BF16)
    DSCR = mem.alloc(tag + "_dscr", [128, 128], F32)
    WQh = mem.alloc(tag + "_wq", [128, 8, 256], BF16)
    WKh = mem.alloc(tag + "_wk", [128, 8, 256], BF16)
    TRIL = mem.alloc(tag + "_tril", [128, 128], F32)
    TRIU = mem.alloc(tag + "_triu", [128, 128], F32)
    TRILB = mem.alloc(tag + "_trilb", [128, 128], BF16)
    IOTA = mem.alloc(tag + "_iota", [128, 1], F32)
    MX = mem.alloc(tag + "_mx", [128, 8], F32)
    XST = [mem.alloc(tag + f"_xst{i}", [128, 512], F32) for i in range(2)]
    YST = [mem.alloc(tag + f"_yst{i}", [128, 512], F32) for i in range(2)]
    mA = mem.mark()
    XS = mem.alloc(tag + "_xs", [128, 8, T], F32)
    endA = mem.mark()
    QA = mem.alloc(tag + "_qa", [128, 8, T], BF16, at=mA)
    KA = mem.alloc(tag + "_ka", [128, 8, T], BF16, at=mA + 32768)
    mem.reset(endA)
    VA = mem.alloc(tag + "_va", [128, 16, 1024], BF16, at=C.lnscr_off)
    WVh = mem.alloc(tag + "_wv", [128, 8, 1024], BF16)
    WOh = mem.alloc(tag + "_wo", [128, 8, 1024], BF16, at=mem.mark())
    AOUT = LNB

    xld = s.sem("g"  + "_xld")
    cldh = s.sem("g"  + "_cldh")
    cldg = s.sem("g"  + "_cldg")
    qkmm = s.sem("g"  + "_qkmm")
    smm = s.sem("g"  + "_smm")
    trm = s.sem("g"  + "_trm")
    avm = s.sem("g"  + "_avm")
    pjmm = s.sem("g"  + "_pjmm")
    xin2 = [s.sem("g"  + "_xin2a"), s.sem("g"  + "_xin2b")]
    yout = s.sem("g"  + "_yout")

    for i in range(4):
        s.dma("sync", XS[:, 2 * i:2 * i + 2, :], x_dram[:, 2 * i:2 * i + 2, :], xld)
    s.dma("sync", TRIL[:], W["tril01"], cldh)
    s.dma("sync", TRIU[:], W["triu01"], cldh)
    s.dma("sync", IOTA[:], W["iota1"], cldh)
    s.dma("sync", C.eye[:], W["eye"], cldh)
    s.dma("gpsimd", TRILB[:], W["tril01"], cldg)
    s.dma("gpsimd", WQh[:], W["wq"], cldg)
    s.dma("gpsimd", WKh[:], W["wk"], cldg)
    s.dma("gpsimd", C.eyeb[:], W["eye"], cldg)
    s.dma("gpsimd", WVh[:], W["wv"], cldg)

    emit_init_consts(C)
    LNM = emit_ln_sbuf(C, XS, LNB, "g_ln", ready=s.after(xld) + C.cready)
    WRDY = s.after(cldg)

    qk_ev = {}
    nmm = 0
    banks = [C.pb5, C.pb6]
    for which, (WX, DST) in enumerate(((WQh, QA), (WKh, KA))):
        for h in range(16):
            base = (h % 2) * 64
            for ts in range(4):
                tsl = slice(ts * 512, ts * 512 + 512)
                ps = banks[nmm % 2]
                w = [LNM] + WRDY
                if nmm >= 2:
                    w.append(qk_ev[nmm - 2])

                def mms(e, WX=WX, h=h, tsl=tsl, ps=ps, base=base):
                    last = None
                    for dc in range(8):
                        last = e.matmul(ps[base:base + 16, 0:512],
                                        WX[:, dc, h * 16:h * 16 + 16],
                                        LNB[:, dc, tsl],
                                        start=(dc == 0), stop=(dc == 7))
                    return last

                s.op("tensor", mms, wait=w, inc=[(qkmm, 1)])
                qk_ev[nmm] = s.op("scalar", lambda e, ps=ps, DST=DST, h=h, tsl=tsl,
                                  base=base: e.copy(
                                      DST[base:base + 16, h // 2, tsl],
                                      ps[base:base + 16, 0:512]),
                                  wait=[(qkmm, nmm + 1)])
                nmm += 1
    for tc in range(16):
        for es in range(2):
            esl = slice(es * 512, es * 512 + 512)
            ps = banks[nmm % 2]
            w = [LNM] + WRDY
            if nmm >= 2:
                w.append(qk_ev[nmm - 2])

            def mms(e, tc=tc, esl=esl, ps=ps):
                last = None
                for dc in range(8):
                    last = e.matmul(ps[:, 0:512], LNB[:, dc, tc * 128:tc * 128 + 128],
                                    WVh[:, dc, esl], start=(dc == 0), stop=(dc == 7))
                return last

            s.op("tensor", mms, wait=w, inc=[(qkmm, 1)])
            qk_ev[nmm] = s.op("scalar", lambda e, ps=ps, tc=tc, esl=esl: e.copy(
                VA[:, tc, esl], ps[:, 0:512]), wait=[(qkmm, nmm + 1), LNM])
            nmm += 1
    QKVMM = s.after(qkmm)
    QKVM = qk_ev[nmm - 1]
    s.dma("gpsimd", WOh[:], W["wo"], cldg)
    WOR = s.after(cldg)

    aexp_prev = None
    pts_prev = None
    aev_prev = None
    trn = 0
    avn = 0
    for h in range(16):
        qch, qb = h // 2, (h % 2) * 64
        for qi in range(16):
            kw_ = (qi + 1) * 128
            kw1 = min(kw_, 1536)
            ntail = kw_ - kw1
            nb = 0
            for kb in range(0, kw_, 512):
                w_ = min(512, kw_ - kb)
                ps = C.pbS[:, kb:kb + w_] if kb < 1536 else C.pb4[:, 0:w_]
                wts = QKVMM + [QKVM]
                if aexp_prev is not None and nb == 0:
                    wts.append(aexp_prev)
                s.op("tensor", lambda e, qb=qb, qch=qch, qi=qi, kb=kb, w_=w_, ps=ps:
                     e.matmul(ps, QA[qb:qb + 16, qch, qi * 128:qi * 128 + 128],
                              KA[qb:qb + 16, qch, kb:kb + w_],
                              start=True, stop=True),
                     wait=wts, inc=[(smm, 1)])
                nb += 1
            smm_w = [(smm, s.val(smm))]
            mex = s.op("scalar", lambda e, kw1=kw1: e.activation(
                PS1[:, 0:kw1], C.pbS[:, 0:kw1], AF.Square, bias=1.0,
                scale=0.25, accum_out=MX[:, 0:1]),
                 wait=smm_w + ([aexp_prev] if aexp_prev else []) + C.cready)
            if ntail:
                mex = s.op("scalar", lambda e, kw1=kw1, ntail=ntail: e.activation(
                    PS1[:, kw1:kw1 + ntail], C.pb4[:, 0:ntail], AF.Square,
                    bias=1.0, scale=0.25, accum_out=MX[:, 1:2]))
            else:
                mex = s.op("scalar", lambda e: e.activation(
                    MX[:, 1:2], C.cpool[:, 4:5], AF.Copy))
            aexp_prev = mex
            dstart = qi * 128
            mcorr = s.op("vector", lambda e, dstart=dstart: e.scalar_tensor_tensor(
                DSCR[:], PS1[:, dstart:dstart + 128], 1.0, TRIU[:],
                ALU.bypass, ALU.mult, accum_out=MX[:, 2:3]),
                 wait=[mex] + s.after(cldh))
            s.op("vector", lambda e: e.tensor_tensor(
                MX[:, 3:4], MX[:, 0:1], MX[:, 1:2], ALU.add))
            s.op("vector", lambda e: e.tensor_tensor(
                MX[:, 3:4], MX[:, 3:4], MX[:, 2:3], ALU.subtract))
            s.op("vector", lambda e, qi=qi: e.tensor_scalar(
                MX[:, 4:5], IOTA[:], 1.0, float(qi * 128), ALU.mult, ALU.add))
            s.op("vector", lambda e: e.tensor_tensor(
                MX[:, 3:4], MX[:, 3:4], MX[:, 4:5], ALU.add))
            mden = s.op("vector", lambda e: e.tensor_scalar(
                MX[:, 3:4], MX[:, 3:4], 0.5, 1e-12, ALU.mult, ALU.add))
            s.op("scalar", lambda e: e.activation(MX[:, 5:6], MX[:, 3:4], AF.Ln),
                 wait=[mden])
            s.op("scalar", lambda e: e.activation(MX[:, 5:6], MX[:, 5:6], AF.Exp,
                                                  scale=-1.0))
            mz2 = s.op("scalar", lambda e: e.activation(MX[:, 6:7], MX[:, 5:6],
                                                        AF.Copy, scale=0.5))
            trn_w = [(trm, trn)] if trn else []
            msc = s.op("vector", lambda e, kw_=kw_: e.tensor_scalar(
                P[:, 0:kw_], PS1[:, 0:kw_], MX[:, 6:7], MX[:, 6:7],
                ALU.mult, ALU.add), wait=[mz2] + trn_w)
            msc = s.op("vector", lambda e, dstart=dstart: e.tensor_tensor(
                P[:, dstart:dstart + 128], P[:, dstart:dstart + 128], TRILB[:],
                ALU.mult))
            nt = qi + 1
            pts_marks = {}
            for b0 in range(0, nt, 8):
                bank = C.pbTa if (b0 // 8) % 2 == 0 else C.pbTb
                cnt = min(8, nt - b0)
                trw = [msc] + WRDY
                if pts_prev is not None:
                    trw.append(pts_prev)

                def trans(e, b0=b0, cnt=cnt, bank=bank):
                    last = None
                    for j in range(cnt):
                        last = e.transpose(bank[:, j * 128:j * 128 + 128],
                                           P[:, (b0 + j) * 128:(b0 + j) * 128 + 128],
                                           C.eyeb[:, :])
                    return last

                s.op("tensor", trans, wait=trw, inc=[(trm, 1)])
                trn += 1
                for j in range(cnt):
                    cpw = [(trm, trn)]
                    if aev_prev is not None:
                        cpw.append(aev_prev)
                    pts_marks[b0 + j] = s.op(
                        "vector", lambda e, j=j, b0=b0, bank=bank: e.tensor_copy(
                            PTS[:, b0 + j, :], bank[:, j * 128:j * 128 + 128]),
                        wait=cpw)
                pts_prev = pts_marks[b0 + cnt - 1]
            avw0 = [aev_prev] if aev_prev is not None else []

            def avmm(e, h=h, qi=qi, qb=qb):
                last = None
                for j in range(qi + 1):
                    last = e.matmul(C.pb6[qb:qb + 64, 0:128],
                                    VA[:, j, h * 64:h * 64 + 64],
                                    PTS[:, j, :], start=(j == 0), stop=(j == qi))
                return last

            s.op("tensor", avmm, wait=avw0 + [pts_prev], inc=[(avm, 1)])
            avn += 1
            aev_prev = s.op("scalar", lambda e, qb=qb, h=h, qi=qi: e.copy(
                AOUT[qb:qb + 64, h // 2, qi * 128:qi * 128 + 128],
                C.pb6[qb:qb + 64, 0:128]),
                 wait=[(avm, avn)] + QKVMM)
    ATTND = aev_prev

    pi = 0
    pdvem = {}
    for ec in range(8):
        esl = slice(ec * 128, ec * 128 + 128)
        for ts in range(4):
            tsl = slice(ts * 512, ts * 512 + 512)
            ps = C.pb5 if pi % 2 == 0 else C.pbS[:, 0:512]
            psap = ps[:, 0:512] if pi % 2 == 0 else ps
            w = [ATTND] + WOR
            if pi >= 2:
                w.append(pdvem[pi - 2])

            def mms(e, esl=esl, tsl=tsl, psap=psap):
                last = None
                for dc in range(8):
                    last = e.matmul(psap, WOh[:, dc, esl], AOUT[:, dc, tsl],
                                    start=(dc == 0), stop=(dc == 7))
                return last

            s.op("tensor", mms, wait=w, inc=[(pjmm, 1)])
            b = pi % 2
            xw = [ATTND] if pi < 2 else [pdvem[pi - 2]]
            s.dma("sync", XST[b][:], x_dram[:, ec, tsl], xin2[b], wait=xw)
            pdvem[pi] = s.op("vector", lambda e, b=b, psap=psap: e.scalar_tensor_tensor(
                YST[b][:], psap, 1.0, XST[b][:], ALU.bypass, ALU.add),
                 wait=[(pjmm, pi + 1), (xin2[b], s.val(xin2[b]))] +
                      ([(yout, 16 * pi)] if pi >= 2 else []))
            s.dma("sync", y_dram[:, ec, tsl], YST[b][:], yout, wait=[pdvem[pi]])
            pi += 1
    s.op("sync", lambda e: None, wait=[(yout, 16 * 32)])
    return NS(QA=QA, KA=KA, VA=VA, AOUT=AOUT, P=P, PS1=PS1, yout=yout)



def build_l1ffn(nc, C, mem, x_dram, y_dram, W, tag="l1f"):
    s = C.sched
    TB = 1024
    NE = 4

    RMS1B = mem.alloc(tag + "_rms1", [128, 8, T], BF16)
    WEX4 = mem.alloc(tag + "_wex4", [128, 16, 4], F32)
    GW4 = mem.alloc(tag + "_gw4", [128, 8, 4], BF16)
    GB4 = mem.alloc(tag + "_gb4", [1, 4], BF16)
    RT4 = mem.alloc(tag + "_rt4", [128, 4], F32)
    EB1C = mem.alloc(tag + "_eb1c", [128, 4, 32], F32)
    EB2R = [mem.alloc(tag + f"_eb2r{i}", [1, 1024], BF16) for i in range(2)]
    EW1S = [mem.alloc(tag + f"_ew1s{i}", [128, 8, 128], BF16) for i in range(2)]
    AST = [mem.alloc(tag + f"_ast{i}", [128, 512], F32) for i in range(2)]
    NCOL = mem.alloc(tag + "_ncol", [128, 24], F32)
    m_latb = (mem.mark() + 31) // 32 * 32
    LATB = mem.alloc(tag + "_latb", [128, 4, T], BF16)
    WQT = mem.alloc(tag + "_wqt", [128, 8, 1024], BF16)
    arena0 = WQT.manual_sbuf_range[0]
    WDKV = mem.alloc(tag + "_wdkv", [128, 8, 512], BF16)
    COSB = mem.alloc(tag + "_cosb", [128, T], BF16)
    SINB = mem.alloc(tag + "_sinb", [128, T], BF16)
    P = mem.alloc(tag + "_p", [128, T], BF16)
    PTS = mem.alloc(tag + "_pts", [128, 16, 128], BF16)
    p_off = P.manual_sbuf_range[0]
    TRILM = mem.alloc(tag + "_trilm", [128, 128], F32)
    SWAPM = mem.alloc(tag + "_swapm", [128, 128], BF16)
    MX = mem.alloc(tag + "_mx", [128, 8], F32)
    TMPR = mem.alloc(tag + "_tmpr", [128, 512], F32)
    mem.reset(max(mem.mark(), m_latb + 65536))
    mA = mem.mark()
    XS = mem.alloc(tag + "_xs", [128, 8, T], F32)
    mem.reset(mA)
    QF = mem.alloc(tag + "_qf", [128, 8, T], BF16, at=mA)
    LAT = mem.alloc(tag + "_lat", [128, 4, T], F32, at=mA + 32768)
    KF = mem.alloc(tag + "_kf", [128, 8, T], BF16, at=mA + 32768)
    H = mem.alloc(tag + "_h", [128, 32, TB], BF16, at=mA)
    mem.reset(mA + 65536)

    AOUT = RMS1B
    RMS2B = RMS1B
    VTM = mem.alloc(tag + "_vtm", [128, 16, 1024], BF16, at=C.lnscr_off)
    EW2 = mem.alloc(tag + "_ew2", [128, 32, 1024], BF16, at=m_latb)
    XPS = [mem.alloc(tag + f"_xp{i}", [128, 512], F32, at=p_off + 2048 * i)
           for i in range(2)]
    YPS = [mem.alloc(tag + f"_yp{i}", [128, 512], F32, at=p_off + 4096 + 2048 * i)
           for i in range(2)]
    FXT = [mem.alloc(tag + f"_fxt{i}", [128, 512], F32, at=C.lnscr_off + 2048 * i)
           for i in range(2)]
    FUT = [mem.alloc(tag + f"_fut{i}", [128, 512], F32, at=C.lnscr_off + 4096 + 2048 * i)
           for i in range(2)]
    FYT = [mem.alloc(tag + f"_fyt{i}", [128, 512], F32, at=C.lnscr_off + 8192 + 2048 * i)
           for i in range(2)]
    FAT = [mem.alloc(tag + f"_fat{i}", [128, 128], F32, at=C.lnscr_off + 12288 + 512 * i)
           for i in range(4)]

    xld = s.sem("g"  + "_xld")
    cldh = s.sem("g"  + "_cldh")
    cldg = s.sem("g"  + "_cldg")
    wld = s.sem("g"  + "_wld")
    xinio = s.sem("g"  + "_xinio")
    xps_s = [s.sem("g"  + "_xpsa"), s.sem("g"  + "_xpsb")]
    yps_s = [s.sem("g"  + "_ypsa"), s.sem("g"  + "_ypsb")]
    w1l2 = [s.sem("g"  + "_w1l0"), s.sem("g"  + "_w1l1")]
    fat_s = [s.sem(tag + f"_fat{j}") for j in range(4)]
    fxt_s = [s.sem("g"  + "_fxa"), s.sem("g"  + "_fxb")]
    fut_s = [s.sem("g"  + "_fua"), s.sem("g"  + "_fub")]
    qkmm = s.sem("g"  + "_qkmm")
    rpmm = s.sem("g"  + "_rpmm")
    smm = s.sem("g"  + "_smm")
    trm = s.sem("g"  + "_trm")
    avm = s.sem("g"  + "_avm")
    pjmm = s.sem("g"  + "_pjmm")
    gmm4 = s.sem("g"  + "_gmm4")
    g1mm = s.sem("g"  + "_g1mm")
    g2mm = s.sem("g"  + "_g2mm")
    w1ld = s.sem("g"  + "_w1ld")
    w2ld = s.sem("g"  + "_w2ld")
    acd = [s.sem("g"  + "_acd0"), s.sem("g"  + "_acd1")]
    acrd = s.sem("g"  + "_acrd")
    trf = s.sem("g"  + "_trf")
    yout = s.sem("g"  + "_yout")
    rsq = s.sem("g"  + "_rsq")
    rbc = s.sem("g"  + "_rbc")

    for i in range(4):
        s.dma("sync", XS[:, 2 * i:2 * i + 2, :], x_dram[:, 2 * i:2 * i + 2, :], xld)
    s.dma("sync", TRILM[:], W["trilm"], cldh)
    s.dma("sync", C.eye[:], W["eye"], cldh)
    s.dma("sync", NCOL[:, 0:8], W["n1c"], cldh)
    s.dma("sync", NCOL[:, 8:16], W["n2c"], cldh)
    s.dma("sync", NCOL[:, 16:20], W["lngc"], cldh)
    s.dma("sync", NCOL[:, 20:24], W["lnbc"], cldh)
    s.dma("sync", EB1C[:], W["eb1c"], cldh)
    s.dma("gpsimd", C.eyeb[:], W["eye"], cldg)
    s.dma("gpsimd", COSB[:], W["cosb"], cldg)
    s.dma("gpsimd", SINB[:], W["sinb"], cldg)
    s.dma("gpsimd", SWAPM[:], W["swapm"], cldg)
    s.dma("gpsimd", GW4[:], W["gw4"], cldg)
    s.dma("gpsimd", GB4[:], W["gb4"], cldg)
    s.dma("gpsimd", WQT[:], W["wqt"], wld)
    s.dma("gpsimd", WDKV[:], W["wdkvt"], wld)
    ARW1 = s.after(wld)

    emit_init_consts(C)
    LNM = emit_ln_sbuf(C, XS, XS, "g_ln", ready=s.after(xld) + C.cready)
    for i in range(4):
        s.dma("sync", W["xin"][:, 2 * i:2 * i + 2, :], XS[:, 2 * i:2 * i + 2, :],
              xinio, wait=[LNM])
    XINW = s.after(xinio)

    def emit_rms(xs_sb, dst, gcol, name, prev_lnscr_user):
        sq_m = {}
        lastap = None
        prev_bc = None
        prev_rd = None
        for ts in range(4):
            tsl = slice(ts * 512, ts * 512 + 512)
            for c in range(8):
                w = [prev_lnscr_user] if ts == 0 and c == 0 else []
                if ts > 0:
                    w += [(rsq, s.val(rsq))]
                    w += [lastap] if lastap else []
                sq_m[c] = s.op("scalar", lambda e, c=c, tsl=tsl, xs_sb=xs_sb:
                               e.activation(C.sq8[:, c, :], xs_sb[:, c, tsl],
                                            AF.Square), wait=w)
            for c in range(8):
                s.op("tensor", lambda e, c=c: e.matmul(
                    C.pb[1][0:1, 0:512], C.ones_col[:, 0:1], C.sq8[:, c, :],
                    start=(c == 0), stop=(c == 7)),
                     wait=[sq_m[c]] + ([prev_rd] if c == 0 and prev_rd else []),
                     inc=([(rsq, 1)] if c == 7 else []))
            m_e2 = s.op("vector", lambda e: e.tensor_scalar(
                C.r_e2[:], C.pb[1][0:1, 0:512], 1.0 / 1024.0, None, ALU.mult),
                 wait=[(rsq, s.val(rsq))])
            prev_rd = m_e2
            s.op("scalar", lambda e: e.activation(
                C.r_tm[:], C.r_e2[:], AF.Ln, bias=C.cpool[0:1, 2:3]),
                 wait=[m_e2] + C.cready)
            m_rstd = s.op("scalar", lambda e: e.activation(
                C.r_e2[:], C.r_tm[:], AF.Exp, scale=-0.5))
            s.op("tensor", lambda e: e.matmul(
                C.pb[2][:, 0:512], C.ones_row[0:1, 0:128], C.r_e2[:],
                start=True, stop=True),
                 wait=[m_rstd] + ([prev_bc] if prev_bc else []), inc=[(rbc, 1)])
            cb = s.op("scalar", lambda e: e.copy(C.bc0[:], C.pb[2][:, 0:512]),
                      wait=[(rbc, s.val(rbc))] + ([lastap] if lastap else []))
            prev_bc = cb
            for c in range(8):
                lastap = s.op("vector", lambda e, c=c, tsl=tsl, xs_sb=xs_sb,
                              dst=dst, gcol=gcol: e.scalar_tensor_tensor(
                                  dst[:, c, tsl], xs_sb[:, c, tsl], gcol(c),
                                  C.bc0[:], ALU.mult, ALU.mult),
                              wait=[cb])
        return lastap

    RMS1M = emit_rms(XS, RMS1B, lambda c: NCOL[:, c:c + 1], "g_r1", LNM)

    qk_ev = {}
    nmm = 0
    banks = [C.pb5, C.pb6]

    def fm_gemm(DSTBUF, WT, nk, ec, tsl, extra_wait, rhsbuf, evac=None):
        nonlocal nmm
        ps = banks[nmm % 2]
        w = list(extra_wait)
        if nmm >= 2:
            w.append(qk_ev[nmm - 2])

        def mms(e, WT=WT, ec=ec, tsl=tsl, ps=ps, nk=nk, rhsbuf=rhsbuf):
            last = None
            for dc in range(nk):
                last = e.matmul(ps[:, 0:512], WT[:, dc, ec * 128:ec * 128 + 128],
                                rhsbuf[:, dc, tsl], start=(dc == 0), stop=(dc == nk - 1))
            return last

        s.op("tensor", mms, wait=w, inc=[(qkmm, 1)])
        if evac is None:
            qk_ev[nmm] = s.op("scalar", lambda e, ps=ps, DSTBUF=DSTBUF, ec=ec,
                              tsl=tsl: e.copy(DSTBUF[:, ec, tsl], ps[:, 0:512]),
                              wait=[(qkmm, nmm + 1)])
        else:
            qk_ev[nmm] = evac(ps, nmm)
        nmm += 1
        return qk_ev[nmm - 1]

    QRD = [RMS1M] + XINW + ARW1
    for ec in range(8):
        for ts in range(4):
            fm_gemm(QF, WQT, 8, ec, slice(ts * 512, ts * 512 + 512), QRD, RMS1B)
    for ec in range(4):
        for ts in range(4):
            tsl = slice(ts * 512, ts * 512 + 512)
            fm_gemm(LAT, WDKV, 8, ec, tsl, QRD, RMS1B)
    QDKV = qk_ev[nmm - 1]
    LATM = emit_ln_sbuf(C, LAT, LATB, "g_lln", nchunk=4,
                        ready=[QDKV] + C.cready + s.after(cldh),
                        affine=(lambda c: NCOL[:, 16 + c:17 + c],
                                lambda c: NCOL[:, 20 + c:21 + c]))
    WUKT = mem.alloc(tag + "_wukt", [128, 4, 1024], BF16, at=arena0)
    WUVT = mem.alloc(tag + "_wuvt", [128, 4, 1024], BF16, at=arena0 + 8192)
    s.dma("gpsimd", WUKT[:], W["wukt"], wld, wait=[(qkmm, s.val(qkmm))])
    s.dma("gpsimd", WUVT[:], W["wuvt"], wld)
    ARW2 = s.after(wld)
    for ec in range(8):
        for ts in range(4):
            fm_gemm(KF, WUKT, 4, ec, slice(ts * 512, ts * 512 + 512),
                    [LATM] + ARW2, LATB)
    for tc in range(16):
        for es in range(2):
            esl = slice(es * 512, es * 512 + 512)
            ps = banks[nmm % 2]
            w = [LATM] + ARW2
            if nmm >= 2:
                w.append(qk_ev[nmm - 2])

            def mmsv(e, tc=tc, esl=esl, ps=ps):
                last = None
                for lc in range(4):
                    last = e.matmul(ps[:, 0:512], LATB[:, lc, tc * 128:tc * 128 + 128],
                                    WUVT[:, lc, esl], start=(lc == 0), stop=(lc == 3))
                return last

            s.op("tensor", mmsv, wait=w, inc=[(qkmm, 1)])
            qk_ev[nmm] = s.op("scalar", lambda e, ps=ps, tc=tc, esl=esl: e.copy(
                VTM[:, tc, esl], ps[:, 0:512]), wait=[(qkmm, nmm + 1), LATM])
            nmm += 1
    QKVMM = s.after(qkmm)
    QKVM = qk_ev[nmm - 1]
    WOT = mem.alloc(tag + "_wot", [128, 8, 1024], BF16, at=arena0)
    s.dma("gpsimd", WOT[:], W["wot"], wld, wait=[(qkmm, s.val(qkmm))])
    WOR = s.after(wld)

    rp_prev = {}
    nrp = 0
    for BUF in (QF, KF):
        for c in range(8):
            for ts in range(4):
                tsl = slice(ts * 512, ts * 512 + 512)
                ps = banks[nrp % 2]
                w = [QKVM] + s.after(cldg)
                if nrp >= 2:
                    w.append(rp_prev[nrp - 2])
                s.op("tensor", lambda e, BUF=BUF, c=c, tsl=tsl, ps=ps: e.matmul(
                    ps[:, 0:512], SWAPM[:], BUF[:, c, tsl], start=True, stop=True),
                     wait=w, inc=[(rpmm, 1)])
                s.op("vector", lambda e, ps=ps, tsl=tsl: e.tensor_tensor(
                    TMPR[:], ps[:, 0:512], SINB[:, tsl], ALU.mult),
                     wait=[(rpmm, nrp + 1)])
                s.op("vector", lambda e, BUF=BUF, c=c, tsl=tsl: e.tensor_tensor(
                    BUF[:, c, tsl], BUF[:, c, tsl], COSB[:, tsl], ALU.mult))
                rp_prev[nrp] = s.op("vector", lambda e, BUF=BUF, c=c, tsl=tsl:
                                    e.tensor_tensor(BUF[:, c, tsl], BUF[:, c, tsl],
                                                    TMPR[:], ALU.add))
                nrp += 1
    ROPED = rp_prev[nrp - 1]

    UNITS = [[(12, 0)], [(13, 0)], [(14, 0)], [(15, 0)],
             [(8, 0), (0, 1536)], [(9, 0), (1, 1536)],
             [(10, 0), (2, 1536)], [(11, 0), (3, 1536)],
             [(7, 0), (4, 1024)], [(6, 0), (5, 1024)]]
    prev_exps = []
    aev_slot = [None, None]
    aev_last = None
    pts_prev = None
    trn = 0
    avn = 0
    for h in range(16):
        qch, qb = h // 2, (h % 2) * 64
        for unit in UNITS:
            first = True
            for qi, off in unit:
                kw_ = (qi + 1) * 128
                for kb in range(0, kw_, 512):
                    w_ = min(512, kw_ - kb)
                    wts = [ROPED]
                    if first:
                        wts += prev_exps
                        first = False
                    s.op("tensor", lambda e, qb=qb, qch=qch, qi=qi, kb=kb, w_=w_,
                         off=off: e.matmul(
                             C.pbS[:, off + kb:off + kb + w_],
                             QF[qb:qb + 64, qch, qi * 128:qi * 128 + 128],
                             KF[qb:qb + 64, qch, kb:kb + w_],
                             start=True, stop=True),
                         wait=wts, inc=[(smm, 1)])
            smv = s.val(smm)
            exps = []
            mscs = []
            for si, (qi, off) in enumerate(unit):
                kw_ = (qi + 1) * 128
                mb = 4 * si
                dstart = off + qi * 128
                s.op("vector", lambda e, dstart=dstart: e.tensor_tensor(
                    C.pbS[:, dstart:dstart + 128], C.pbS[:, dstart:dstart + 128],
                    TRILM[:], ALU.add),
                     wait=[(smm, smv)] + s.after(cldh))
                s.op("vector", lambda e, off=off, kw_=kw_, mb=mb: e.tensor_reduce(
                    MX[:, mb:mb + 1], C.pbS[:, off:off + kw_], AX.X, ALU.max))
                mneg = s.op("vector", lambda e, mb=mb: e.tensor_scalar(
                    MX[:, mb + 1:mb + 2], MX[:, mb:mb + 1], -0.125, None, ALU.mult))
                exw = [mneg]
                if trn > 0:
                    exw.append((trm, trn))
                mex = s.op("scalar", lambda e, off=off, kw_=kw_, mb=mb: e.activation(
                    P[:, off:off + kw_], C.pbS[:, off:off + kw_], AF.Exp,
                    bias=MX[:, mb + 1:mb + 2], scale=0.125,
                    accum_out=MX[:, mb + 2:mb + 3]), wait=exw)
                exps.append(mex)
                s.op("scalar", lambda e, mb=mb: e.activation(
                    MX[:, mb + 3:mb + 4], MX[:, mb + 2:mb + 3], AF.Ln))
                mri = s.op("scalar", lambda e, mb=mb: e.activation(
                    MX[:, mb + 3:mb + 4], MX[:, mb + 3:mb + 4], AF.Exp, scale=-1.0))
                mscs.append(s.op("vector", lambda e, off=off, kw_=kw_, mb=mb:
                                 e.tensor_scalar(P[:, off:off + kw_],
                                                 P[:, off:off + kw_],
                                                 MX[:, mb + 3:mb + 4], None,
                                                 ALU.mult), wait=[mri]))
            prev_exps = exps
            for si, (qi, off) in enumerate(unit):
                nt = qi + 1
                jbase = 0 if si == 0 else unit[0][0] + 1
                for b0 in range(0, nt, 8):
                    bank = C.pbTa if trn % 2 == 0 else C.pbTb
                    cnt = min(8, nt - b0)
                    trw = [mscs[si]] + s.after(cldg)
                    if pts_prev is not None:
                        trw.append(pts_prev)

                    def trans(e, b0=b0, cnt=cnt, bank=bank, off=off):
                        last = None
                        for j in range(cnt):
                            last = e.transpose(
                                bank[:, j * 128:j * 128 + 128],
                                P[:, off + (b0 + j) * 128:off + (b0 + j) * 128 + 128],
                                C.eyeb[:, :])
                        return last

                    s.op("tensor", trans, wait=trw, inc=[(trm, 1)])
                    trn += 1
                    for j in range(cnt):
                        cpw = [(trm, trn)]
                        if aev_slot[si] is not None:
                            cpw.append(aev_slot[si])
                        pts_prev = s.op(
                            "vector", lambda e, j=j, b0=b0, bank=bank, jbase=jbase:
                            e.tensor_copy(PTS[:, jbase + b0 + j, :],
                                          bank[:, j * 128:j * 128 + 128]),
                            wait=cpw)
                avw0 = [aev_slot[si]] if aev_slot[si] is not None else []

                def avmm(e, h=h, qi=qi, qb=qb, jbase=jbase, si=si):
                    last = None
                    for j in range(qi + 1):
                        last = e.matmul(C.pb6[qb:qb + 64, si * 128:si * 128 + 128],
                                        VTM[:, j, h * 64:h * 64 + 64],
                                        PTS[:, jbase + j, :],
                                        start=(j == 0), stop=(j == qi))
                    return last

                s.op("tensor", avmm, wait=avw0 + [pts_prev] + QKVMM, inc=[(avm, 1)])
                avn += 1
                aev_slot[si] = s.op("scalar", lambda e, qb=qb, h=h, qi=qi, si=si:
                                    e.copy(AOUT[qb:qb + 64, h // 2,
                                                qi * 128:qi * 128 + 128],
                                           C.pb6[qb:qb + 64, si * 128:si * 128 + 128]),
                                    wait=[(avm, avn)] + QKVMM)
                aev_last = aev_slot[si]
    ATTND = aev_last

    pi = 0
    pdvem = {}
    for ec in range(8):
        for ts in range(4):
            tsl = slice(ts * 512, ts * 512 + 512)
            ps = C.pb5 if pi % 2 == 0 else C.pbS[:, 0:512]
            psap = ps[:, 0:512] if pi % 2 == 0 else ps
            w = [ATTND] + WOR
            if pi >= 2:
                w.append(pdvem[pi - 2])

            def mmsp(e, ec=ec, tsl=tsl, psap=psap):
                last = None
                for dc in range(8):
                    last = e.matmul(psap, WOT[:, dc, ec * 128:ec * 128 + 128],
                                    AOUT[:, dc, tsl], start=(dc == 0), stop=(dc == 7))
                return last

            s.op("tensor", mmsp, wait=w, inc=[(pjmm, 1)])
            b = pi % 2
            xw = [ATTND] if pi < 2 else [pdvem[pi - 2]]
            s.dma("sync", XPS[b][:], W["xin"][:, ec, tsl], xps_s[b], wait=xw)
            pdvem[pi] = s.op("vector", lambda e, b=b, psap=psap: e.scalar_tensor_tensor(
                YPS[b][:], psap, 1.0, XPS[b][:], ALU.bypass, ALU.add),
                 wait=[(pjmm, pi + 1), (xps_s[b], s.val(xps_s[b]))])
            s.dma("sync", W["xin"][:, ec, tsl], YPS[b][:], xinio, wait=[pdvem[pi]])
            pi += 1
    X1W = s.after(xinio)
    X1M = pdvem[31]

    rs_m = {}
    lastap2 = None
    prev_bc2 = None
    prev_rd2 = None
    for ts in range(4):
        tsl = slice(ts * 512, ts * 512 + 512)
        for c in range(8):
            i = ts * 8 + c
            b = i % 2
            w = list(X1W) + [X1M] + ([ATTND] if i == 0 else [])
            if i >= 2:
                w.append(rs_m[i - 2])
            s.dma("sync", XPS[b][:], W["xin"][:, c, tsl], xps_s[b], wait=w)
            xt = s.val(xps_s[b])
            rs_m[i] = s.op("scalar", lambda e, b=b, c=c: e.activation(
                C.sq8[:, c, :], XPS[b][:], AF.Square),
                 wait=[(xps_s[b], xt), (pjmm, 32)] +
                      ([lastap2] if ts > 0 and c == 0 and lastap2 else []) +
                      ([(rsq, s.val(rsq))] if ts > 0 and c == 0 else []))
        for c in range(8):
            s.op("tensor", lambda e, c=c: e.matmul(
                C.pb[1][0:1, 0:512], C.ones_col[:, 0:1], C.sq8[:, c, :],
                start=(c == 0), stop=(c == 7)),
                 wait=[rs_m[ts * 8 + c]] + ([prev_rd2] if c == 0 and prev_rd2 else []),
                 inc=([(rsq, 1)] if c == 7 else []))
        m_e2 = s.op("vector", lambda e: e.tensor_scalar(
            C.r_e2[:], C.pb[1][0:1, 0:512], 1.0 / 1024.0, None, ALU.mult),
             wait=[(rsq, s.val(rsq))])
        prev_rd2 = m_e2
        s.op("scalar", lambda e: e.activation(
            C.r_tm[:], C.r_e2[:], AF.Ln, bias=C.cpool[0:1, 2:3]), wait=[m_e2])
        m_rstd = s.op("scalar", lambda e: e.activation(
            C.r_e2[:], C.r_tm[:], AF.Exp, scale=-0.5))
        s.op("tensor", lambda e: e.matmul(
            C.pb[2][:, 0:512], C.ones_row[0:1, 0:128], C.r_e2[:],
            start=True, stop=True),
             wait=[m_rstd] + ([prev_bc2] if prev_bc2 else []), inc=[(rbc, 1)])
        cb = s.op("scalar", lambda e: e.copy(C.bc0[:], C.pb[2][:, 0:512]),
                  wait=[(rbc, s.val(rbc))] + ([lastap2] if lastap2 else []))
        prev_bc2 = cb
        for c in range(8):
            i = ts * 8 + c
            b = i % 2
            s.dma("sync", YPS[b][:], W["xin"][:, c, tsl], yps_s[b],
                  wait=[lastap2] if lastap2 and i >= 2 else ([X1M] if i == 0 else []))
            xt = s.val(yps_s[b])
            lastap2 = s.op("vector", lambda e, b=b, c=c, tsl=tsl:
                           e.scalar_tensor_tensor(
                               RMS2B[:, c, tsl], YPS[b][:], NCOL[:, 8 + c:9 + c],
                               C.bc0[:], ALU.mult, ALU.mult),
                           wait=[(yps_s[b], xt), cb])
    RMS2M = lastap2

    g_prev = None
    for tc in range(16):
        tsl = slice(tc * 128, tc * 128 + 128)
        psg = C.pb[2]

        def gmms(e, tsl=tsl):
            last = None
            for dc in range(8):
                last = e.matmul(psg[0:128, 0:4], RMS2B[:, dc, tsl], GW4[:, dc, 0:4],
                                start=(dc == 0), stop=False)
            return e.matmul(psg[0:128, 0:4], C.ones_row_bf[0:1, 0:128], GB4[0:1, 0:4],
                            start=False, stop=True)

        w = [RMS2M] + s.after(cldg) + (s.mark("vector") if tc else []) + \
            ([(rbc, s.val(rbc))] if tc == 0 else [])
        s.op("tensor", gmms, wait=w, inc=[(gmm4, 1)])
        s.op("vector", lambda e: e.tensor_reduce(RT4[:, 0:1], psg[0:128, 0:4],
                                                 AX.X, ALU.max),
             wait=[(gmm4, tc + 1)])
        g_prev = s.op("vector", lambda e, tc=tc: e.tensor_scalar(
            WEX4[:, tc, 0:4], psg[0:128, 0:4], RT4[:, 0:1], None, ALU.is_ge))
    GATED4 = [g_prev]

    g1t = 0
    g2t = 0
    nacc = 0
    w1n = 0
    g1evm = {}
    g2evm = {}
    ew1_hist = {}
    for e in range(NE):
        ew2wait = [(g2mm, g2t)] if e > 0 else [X1M] + X1W + [RMS2M]
        s.dma("gpsimd", EW2[:], W["ew2"][e], w2ld, wait=ew2wait)
        s.dma("gpsimd", EB2R[e % 2][:], W["eb2r"][e:e + 1, :], w2ld)
        w2tgt = s.val(w2ld)
        for tb in range(2):
            for ic in range(32):
                b = w1n % 2
                dmawait = [(g1mm, ew1_hist[w1n - 2])] if w1n >= 2 else []
                s.dma("gpsimd", EW1S[b][:], W["ew1"][e, ic], w1l2[b], wait=dmawait)
                w1tgt = s.val(w1l2[b])
                for tsp in range(2):
                    ps = (C.pb5 if g1t % 2 == 0 else C.pb6)[:]

                    def mms(eng, b=b, tb=tb, tsp=tsp, ps=ps):
                        last = None
                        rs = slice(tb * TB + tsp * 512, tb * TB + tsp * 512 + 512)
                        for dc in range(8):
                            last = eng.matmul(ps[:, 0:512], EW1S[b][:, dc, :],
                                              RMS2B[:, dc, rs],
                                              start=(dc == 0), stop=(dc == 7))
                        return last

                    waits = [(w1l2[b], w1tgt), RMS2M]
                    if g1t >= 2:
                        waits.append(g1evm[g1t - 2])
                    s.op("tensor", mms, wait=waits, inc=[(g1mm, 1)])
                    psl = slice(tsp * 512, tsp * 512 + 512)
                    g1evm[g1t] = s.op("scalar", lambda eng, ps=ps, ic=ic, psl=psl,
                                      e=e: eng.activation(
                                          H[:, ic, psl], ps[:, 0:512], AF.Relu,
                                          bias=EB1C[:, e, ic:ic + 1]),
                                      wait=[(g1mm, g1t + 1), (g2mm, g2t)] +
                                           s.after(cldh))
                    g1t += 1
                ew1_hist[w1n] = g1t
                w1n += 1
            HREADY = g1evm[g1t - 1]
            for tsub in range(8):
                tc = tb * 8 + tsub
                hsl = slice(tsub * 128, tsub * 128 + 128)
                for dsp in range(2):
                    dsl = slice(dsp * 512, dsp * 512 + 512)
                    ps = C.pb[0 + (g2t % 2)]

                    def mms2(eng, hsl=hsl, dsl=dsl, ps=ps, e=e):
                        last = None
                        for ic in range(32):
                            last = eng.matmul(ps, H[:, ic, hsl], EW2[:, ic, dsl],
                                              start=(ic == 0), stop=False)
                        return eng.matmul(ps, C.ones_row_bf[0:1, 0:128],
                                          EB2R[e % 2][0:1, dsl], start=False, stop=True)

                    waits = [HREADY, (w2ld, w2tgt)]
                    if g2t >= 2:
                        waits.append(g2evm[g2t - 2])
                    s.op("tensor", mms2, wait=waits, inc=[(g2mm, 1)])
                    b2 = nacc % 2
                    evw = [(g2mm, g2t + 1)] + GATED4
                    if nacc >= 2:
                        evw.append((acd[b2], s.val(acd[b2])))
                    g2evm[g2t] = s.op("scalar", lambda eng, ps=ps, b2=b2, tc=tc,
                                      e=e: eng.activation(
                                          AST[b2][:], ps, AF.Copy,
                                          scale=WEX4[:, tc, e:e + 1]),
                                      wait=evw)
                    accw = [g2evm[g2t]]
                    if e > 0 and tb == 0 and tsub == 0 and dsp == 0:
                        accw += s.after(acd[0], acd[1])
                    kw = {} if e == 0 else {"accum_op": ALU.add}
                    s.dma("gpsimd", W["acc"][:, tc, dsl], AST[b2][:], acd[b2],
                          wait=accw, **kw)
                    nacc += 1
                    g2t += 1
    EXPDONE = s.after(acd[0], acd[1])

    fi = 0
    fdvem = {}
    for ec in range(8):
        for ts in range(4):
            tsl = slice(ts * 512, ts * 512 + 512)
            ps = C.pb[2] if fi % 2 == 0 else C.pb[3]
            for j in range(4):
                tc = ts * 4 + j
                aw = list(EXPDONE) + [RMS2M]
                if fi >= 1:
                    aw.append((trf, fi))
                s.dma("sync", FAT[j][:], W["acc"][:, tc, ec * 128:ec * 128 + 128],
                      fat_s[j], wait=aw if j == 0 else [])
            trw = [(fs, s.val(fs)) for fs in fat_s] + s.after(cldh)
            if fi >= 2:
                trw.append(fdvem[fi - 2])

            def trans(eng, ps=ps):
                last = None
                for j in range(4):
                    last = eng.transpose(ps[:, j * 128:j * 128 + 128], FAT[j][:],
                                         C.eye[:, :])
                return last

            s.op("tensor", trans, wait=trw, inc=[(trf, 1)])
            b = fi % 2
            xw = list(EXPDONE) if fi < 2 else [fdvem[fi - 2]]
            s.dma("sync", FXT[b][:], x_dram[:, ec, tsl], fxt_s[b], wait=xw)
            s.dma("sync", FUT[b][:], W["xin"][:, ec, tsl], fut_s[b])
            s.op("vector", lambda e, b=b: e.tensor_tensor(
                FYT[b][:], FXT[b][:], FUT[b][:], ALU.add),
                 wait=[(fxt_s[b], s.val(fxt_s[b])), (fut_s[b], s.val(fut_s[b])),
                       (trf, fi + 1)] +
                      ([(yout, 16 * fi)] if fi >= 2 else []))
            fdvem[fi] = s.op("vector", lambda e, b=b, ps=ps: e.scalar_tensor_tensor(
                FYT[b][:], ps[:, 0:512], 1.0, FYT[b][:], ALU.bypass, ALU.add))
            s.dma("sync", y_dram[:, ec, tsl], FYT[b][:], yout, wait=[fdvem[fi]])
            fi += 1
    s.op("sync", lambda e: None, wait=[(yout, 16 * 32)])
    return NS(QF=QF, KF=KF, VTM=VTM, AOUT=AOUT, LATB=LATB, RMS1B=RMS1B, yout=yout)






def build_l1ab(nc, C, mem, x_dram, y_dram, W, tag="l1b"):
    s = C.sched
    SEGS = ((512, 1), (1024, 2), (2048, 4))
    LNB = mem.alloc(tag + "_lnb", [128, 8, T], BF16)
    CBC = mem.alloc(tag + "_cbc", [128, 24], F32)
    TRILM = mem.alloc(tag + "_trilm", [128, 128], F32)
    BR = mem.alloc(tag + "_br", [1, 4096], BF16)
    MX = mem.alloc(tag + "_mx", [128, 8], F32)
    P = mem.alloc(tag + "_pp", [128, 512], BF16)
    QG = mem.alloc(tag + "_qg", [128, 4, 64], BF16)
    KG = mem.alloc(tag + "_kg", [128, 4, 64], BF16)
    VG = mem.alloc(tag + "_vg", [128, 4, 64], BF16)
    QGF = mem.alloc(tag + "_qgf", [64, 4, 128], BF16)
    KGF = mem.alloc(tag + "_kgf", [64, 4, 128], BF16)
    PTS = mem.alloc(tag + "_pts", [128, 4, 128], BF16)
    OBT4 = mem.alloc(tag + "_obt4", [128, 4, 64], BF16)
    MST4 = mem.alloc(tag + "_mst4", [128, 4, 2], F32)
    M3T = mem.alloc(tag + "_m3t", [128, 16, 3], F32)
    MHT = mem.alloc(tag + "_mht", [128, 16], F32)
    C3T = mem.alloc(tag + "_c3t", [128, 16, 3], F32)
    S3T = mem.alloc(tag + "_s3t", [128, 16, 3], F32)
    DEN = mem.alloc(tag + "_den", [128, 16], F32)
    YH = mem.alloc(tag + "_yh", [128, 64], F32)
    ZT = mem.alloc(tag + "_zt", [128, 512], F32)
    TMS = mem.alloc(tag + "_tms", [128, 512], BF16)
    XST = [mem.alloc(tag + f"_xst{i}", [128, 512], F32) for i in range(2)]
    YST = [mem.alloc(tag + f"_yst{i}", [128, 512], F32) for i in range(2)]
    QXT = mem.alloc(tag + "_qxt", [128, 8, 1024], BF16)
    _qxt0 = QXT.manual_sbuf_range[0]
    NTH = [mem.alloc(tag + f"_nth{i}", [128, 16, 64], BF16, at=_qxt0 + 2048 * i)
           for i in range(3)]
    CW = mem.alloc(tag + "_cw", [128, 8, 1024], BF16)
    QW2 = mem.alloc(tag + "_qw2", [128, 8, 1024], BF16)
    XS = mem.alloc(tag + "_xs", [128, 8, T], F32)
    AOUT = LNB

    xld = s.sem("g"  + "_xld")
    cldh = s.sem("g"  + "_cldh")
    cldg = s.sem("g"  + "_cldg")
    wstr = s.sem("g"  + "_wstr")
    pmm = s.sem("g"  + "_pmm")
    qkvo = s.sem("g"  + "_qkvo")
    zinit = s.sem("g"  + "_zinit")
    gld = s.sem("g"  + "_gld")
    glk = s.sem("g"  + "_glk")
    glv = s.sem("g"  + "_glv")
    tgmm = s.sem("g"  + "_tgmm")
    smm = s.sem("g"  + "_smm")
    trm = s.sem("g"  + "_trm")
    avm = s.sem("g"  + "_avm")
    mso = s.sem("g"  + "_mso")
    mrd = s.sem("g"  + "_mrd")
    mrdg = s.sem("g"  + "_mrdg")
    nbo = s.sem("g"  + "_nbo")
    aomm = s.sem("g"  + "_aomm")
    pjmm = s.sem("g"  + "_pjmm")
    xin2 = [s.sem("g"  + "_xin2a"), s.sem("g"  + "_xin2b")]
    yout = s.sem("g"  + "_yout")

    for i in range(4):
        s.dma("sync", XS[:, 2 * i:2 * i + 2, :], x_dram[:, 2 * i:2 * i + 2, :], xld)
    s.dma("sync", CBC[:], W["cbc"], cldh)
    s.dma("sync", TRILM[:], W["trilm"], cldh)
    s.dma("sync", C.eye[:], W["eye"], cldh)
    s.dma("gpsimd", BR[0:1, 0:1024], W["qbr"], cldg)
    s.dma("gpsimd", BR[0:1, 1024:2048], W["kbr"], cldg)
    s.dma("gpsimd", BR[0:1, 2048:3072], W["vbr"], cldg)
    s.dma("gpsimd", BR[0:1, 3072:4096], W["obr"], cldg)
    s.dma("gpsimd", C.eyeb[:], W["eye"], cldg)

    emit_init_consts(C)
    LNM = emit_ln_sbuf(C, XS, LNB, "g_ln", ready=s.after(xld) + C.cready)

    mz0 = s.op("vector", lambda e: e.memset(ZT[:], -1e30))
    s.dma("sync", W["m3"].rearrange("(a p) c -> p a c", p=128),
          bass.AP(ZT, 0, [[512, 128], [0, 16], [1, 48]]), zinit, wait=[mz0])
    mz1 = s.op("vector", lambda e: e.memset(ZT[:], 0.0), wait=[(zinit, s.val(zinit))])
    s.dma("sync", W["s3"].rearrange("(a p) c -> p a c", p=128),
          bass.AP(ZT, 0, [[512, 128], [0, 16], [1, 48]]), zinit, wait=[mz1])
    mzb = s.op("vector", lambda e: e.memset(TMS[:], 0.0))
    for nb in (W["nb2"], W["nb3"]):
        for half in range(2):
            s.dma("sync", nb.rearrange("(a p) c -> p a c", p=128)[:, :, 512 * half:512 * half + 512],
                  bass.AP(TMS, 0, [[512, 128], [0, 16], [1, 512]]), zinit, wait=[mzb])
    ZINIT = s.after(zinit)

    ev_hist = {}
    npm = 0
    banks = [C.pb5, C.pb6]
    for part, dst in enumerate((W["qd"], W["kd"], W["vd"])):
        s.dma("gpsimd", CW[:], W["caw"][:, :, 1024 * part:1024 * (part + 1)], wstr,
              wait=[(pmm, s.val(pmm))])
        s.dma("gpsimd", QW2[:], W[("qw", "kw", "vw")[part]], wstr)
        wtgt = s.val(wstr)
        brow = BR[0:1, 1024 * part:1024 * (part + 1)]
        for tb in range(2):
            for ec in range(8):
                for ts2 in range(2):
                    tsl = slice(tb * 1024 + ts2 * 512, tb * 1024 + ts2 * 512 + 512)
                    psl = slice(ts2 * 512, ts2 * 512 + 512)
                    ps = banks[npm % 2]
                    w = [LNM, (wstr, wtgt)] + s.after(cldh)
                    if npm >= 2:
                        w.append(ev_hist[npm - 2])

                    def mm1(e, ec=ec, tsl=tsl, ps=ps):
                        last = None
                        for dc in range(8):
                            last = e.matmul(ps[:, 0:512], CW[:, dc, ec * 128:ec * 128 + 128],
                                            LNB[:, dc, tsl], start=(dc == 0), stop=(dc == 7))
                        return last

                    s.op("tensor", mm1, wait=w, inc=[(pmm, 1)])
                    ev_hist[npm] = s.op("vector", lambda e, ps=ps, ec=ec, psl=psl,
                                        part=part: e.tensor_scalar(
                                            QXT[:, ec, psl], ps[:, 0:512],
                                            CBC[:, 8 * part + ec:8 * part + ec + 1],
                                            None, ALU.add),
                                        wait=[(pmm, npm + 1)])
                    npm += 1
            QXM = ev_hist[npm - 1]
            for tc8 in range(8):
                tcl = slice(tc8 * 128, tc8 * 128 + 128)
                for es in range(2):
                    esl = slice(es * 512, es * 512 + 512)
                    ps = banks[npm % 2]
                    w = [QXM] + s.after(cldg)
                    if npm >= 2:
                        w.append(ev_hist[npm - 2])

                    def mm2(e, tcl=tcl, esl=esl, ps=ps, brow=brow):
                        last = None
                        for dc in range(8):
                            last = e.matmul(ps[:, 0:512], QXT[:, dc, tcl],
                                            QW2[:, dc, esl], start=(dc == 0), stop=False)
                        return e.matmul(ps[:, 0:512], C.ones_row_bf[0:1, 0:128],
                                        brow[:, esl], start=False, stop=True)

                    s.op("tensor", mm2, wait=w, inc=[(pmm, 1)])
                    sc = 0.125 if part == 0 else 1.0
                    mev = s.op("scalar", lambda e, ps=ps, sc=sc: e.activation(
                        TMS[:], ps[:, 0:512], AF.Copy, scale=sc),
                        wait=[(pmm, npm + 1), (qkvo, s.val(qkvo)), (zinit, s.val(zinit))])
                    ev_hist[npm] = mev
                    tglob = tb * 1024 + tc8 * 128
                    s.dma("sync", dst.rearrange("(a p) c -> p a c", p=128)[:, tglob // 128, esl],
                          TMS[:], qkvo, wait=[mev])
                    npm += 1
    QKVW = s.after(qkvo)

    aev_prev = None
    exp_prev = None
    pts_prev = None
    obe_prev = None
    prev_merge = None
    for h in range(16):
        for b, (w_, r) in enumerate(SEGS):
            nseg = T // w_
            off = h % r
            for seg in range(nseg):
                base = seg * w_ + off
                s.dma("sync", QG[:],
                      bass.AP(W["qd"].tensor, base * 1024 + h * 64,
                              [[r * 1024, 128], [128 * r * 1024, 4], [1, 64]]),
                      gld, wait=[(tgmm, s.val(tgmm))] + QKVW)
                s.dma("scalar", KG[:],
                      bass.AP(W["kd"].tensor, base * 1024 + h * 64,
                              [[r * 1024, 128], [128 * r * 1024, 4], [1, 64]]),
                      glk, wait=[(tgmm, s.val(tgmm))] + QKVW)
                s.dma("gpsimd", VG[:],
                      bass.AP(W["vd"].tensor, base * 1024 + h * 64,
                              [[r * 1024, 128], [128 * r * 1024, 4], [1, 64]]),
                      glv, wait=[(avm, s.val(avm))] + QKVW)
                gtgt = s.val(gld)
                ktgt = s.val(glk)
                vtgt = s.val(glv)
                trw = [(gld, gtgt), (glk, ktgt)] + s.after(cldg)
                if exp_prev is not None:
                    trw.append(exp_prev)
                if pts_prev is not None:
                    trw.append(pts_prev)

                def tg(e):
                    last = None
                    for j in range(4):
                        last = e.transpose(C.pbTa[0:64, j * 128:j * 128 + 128],
                                           QG[:, j, :], C.eyeb[:, :])
                    for j in range(4):
                        last = e.transpose(C.pbTb[0:64, j * 128:j * 128 + 128],
                                           KG[:, j, :], C.eyeb[:, :])
                    return last

                s.op("tensor", tg, wait=trw, inc=[(tgmm, 1)])
                tgv = s.val(tgmm)
                mq = s.op("vector", lambda e: e.tensor_copy(
                    QGF[0:64, :, :],
                    C.pbTa[0:64, 0:512].rearrange("p (a b) -> p a b", a=4)),
                     wait=[(tgmm, tgv)] + ([exp_prev] if exp_prev else []))
                mk = s.op("vector", lambda e: e.tensor_copy(
                    KGF[0:64, :, :],
                    C.pbTb[0:64, 0:512].rearrange("p (a b) -> p a b", a=4)))
                for qt in range(4):
                    kw_ = (qt + 1) * 128
                    wts = [mk]
                    if exp_prev is not None:
                        wts.append(exp_prev)
                    s.op("tensor", lambda e, qt=qt, kw_=kw_: e.matmul(
                        C.pbS[:, 0:kw_], QGF[0:64, qt, :],
                        KGF[0:64, :, :].rearrange("p a b -> p (a b)")[:, 0:kw_],
                        start=True, stop=True),
                         wait=wts, inc=[(smm, 1)])
                    dstart = qt * 128
                    s.op("vector", lambda e, dstart=dstart: e.tensor_tensor(
                        C.pbS[:, dstart:dstart + 128], C.pbS[:, dstart:dstart + 128],
                        TRILM[:], ALU.add),
                         wait=[(smm, s.val(smm))] + s.after(cldh))
                    s.op("vector", lambda e, kw_=kw_: e.tensor_reduce(
                        MX[:, 0:1], C.pbS[:, 0:kw_], AX.X, ALU.max))
                    mneg = s.op("vector", lambda e: e.tensor_scalar(
                        MX[:, 1:2], MX[:, 0:1], -1.0, None, ALU.mult))
                    exw = [mneg]
                    if pts_prev is not None:
                        exw.append((trm, s.val(trm)))
                    exp_prev = s.op("scalar", lambda e, kw_=kw_: e.activation(
                        P[:, 0:kw_], C.pbS[:, 0:kw_], AF.Exp, bias=MX[:, 1:2],
                        accum_out=MX[:, 2:3]), wait=exw)
                    mcol = s.op("vector", lambda e, qt=qt: e.tensor_copy(
                        MST4[:, qt, 0:1], MX[:, 0:1]),
                                wait=[exp_prev] +
                                     ([(mso, s.val(mso))] if qt == 0 else []))
                    scol = s.op("vector", lambda e, qt=qt: e.tensor_copy(
                        MST4[:, qt, 1:2], MX[:, 2:3]))
                    if qt == 3:
                        s.dma("gpsimd", bass.AP(W["m3"].tensor, base * 48 + h * 3 + b,
                                                [[48 * r, 128], [128 * r * 48, 4],
                                                 [1, 1]]), MST4[:, :, 0:1],
                              mso, wait=[scol, (zinit, s.val(zinit))],
                              allow_slow_non_contiguous=True)
                        s.dma("gpsimd", bass.AP(W["s3"].tensor, base * 48 + h * 3 + b,
                                                [[48 * r, 128], [128 * r * 48, 4],
                                                 [1, 1]]), MST4[:, :, 1:2], mso,
                              allow_slow_non_contiguous=True)
                    trw2 = [exp_prev]
                    if pts_prev is not None:
                        trw2.append(pts_prev)

                    def trp(e, qt=qt):
                        last = None
                        for j in range(qt + 1):
                            last = e.transpose(
                                C.pbTa[:, 512 + j * 128:512 + j * 128 + 128],
                                P[:, j * 128:j * 128 + 128], C.eyeb[:, :])
                        return last

                    s.op("tensor", trp, wait=trw2, inc=[(trm, 1)])
                    ptm = None
                    for j in range(qt + 1):
                        cw_ = [(trm, s.val(trm))]
                        if obe_prev is not None:
                            cw_.append(obe_prev)
                        ptm = s.op("vector", lambda e, j=j: e.tensor_copy(
                            PTS[:, j, :], C.pbTa[:, 512 + j * 128:512 + j * 128 + 128]),
                            wait=cw_)
                    pts_prev = ptm

                    def avb(e, qt=qt):
                        last = None
                        for j in range(qt + 1):
                            last = e.matmul(C.pb6[:, 0:64], PTS[:, j, :],
                                            VG[:, j, :], start=(j == 0), stop=(j == qt))
                        return last

                    s.op("tensor", avb, wait=[pts_prev, (glv, vtgt)] +
                         ([obe_prev] if obe_prev else []), inc=[(avm, 1)])
                    obe_prev = s.op("scalar", lambda e, qt=qt: e.copy(
                        OBT4[:, qt, :], C.pb6[:, 0:64]),
                                    wait=[(avm, s.val(avm))] +
                                         ([(nbo, s.val(nbo))] if qt == 0 else []))
                    if qt == 3:
                        s.dma("scalar", bass.AP(W[("nb1", "nb2", "nb3")[b]].tensor,
                                                base * 1024 + h * 64,
                                                [[r * 1024, 128],
                                                 [128 * r * 1024, 4], [1, 64]]),
                              OBT4[:], nbo,
                              wait=[obe_prev, (zinit, s.val(zinit))])
        NBW = s.after(nbo, mso)
        mrw = list(NBW) + ([prev_merge] if prev_merge else [])
        s.dma("sync", M3T[:], bass.AP(W["m3"].tensor, h * 3,
                                      [[48, 128], [48 * 128, 16], [1, 3]]),
              mrd, wait=mrw)
        s.dma("sync", S3T[:], bass.AP(W["s3"].tensor, h * 3,
                                      [[48, 128], [48 * 128, 16], [1, 3]]), mrd)
        m3v = s.val(mrd)
        mh = s.op("vector", lambda e: e.tensor_reduce(MHT[:], M3T[:], AX.X, ALU.max),
                  wait=[(mrd, m3v)])
        mhb = bass.AP(MHT, 0, [[16, 128], [1, 16], [0, 3]])
        mdf = s.op("vector", lambda e: e.tensor_tensor(C3T[:], M3T[:], mhb, ALU.subtract))
        mc3 = s.op("scalar", lambda e: e.activation(C3T[:], C3T[:], AF.Exp), wait=[mdf])
        s.op("vector", lambda e: e.tensor_tensor(S3T[:], C3T[:], S3T[:], ALU.mult),
             wait=[mc3])
        mden = s.op("vector", lambda e: e.tensor_reduce(DEN[:], S3T[:], AX.X, ALU.add))
        s.op("scalar", lambda e: e.activation(DEN[:], DEN[:], AF.Ln), wait=[mden])
        mrden = s.op("scalar", lambda e: e.activation(DEN[:], DEN[:], AF.Exp, scale=-1.0))
        prev_tc_read = None
        ntw = list(NBW) + ([prev_merge] if prev_merge else [])
        for b, q, sm in ((0, "sync", mrd), (1, "gpsimd", mrdg), (2, "sync", mrd)):
            s.dma(q, NTH[b][:],
                  bass.AP(W[("nb1", "nb2", "nb3")[b]].tensor, h * 64,
                          [[1024, 128], [128 * 1024, 16], [1, 64]]),
                  sm, wait=ntw)
        NTV = s.after(mrd, mrdg)
        for tc in range(16):
            NT = [NTH[0][:, tc, :], NTH[1][:, tc, :], NTH[2][:, tc, :]]
            s.op("vector", lambda e, tc=tc, NT=NT: e.tensor_scalar(
                YH[:], NT[0], C3T[:, tc, 0:1], None, ALU.mult),
                 wait=NTV + [mrden, (aomm, s.val(aomm))])
            for b in (1, 2):
                s.op("vector", lambda e, b=b, tc=tc, NT=NT: e.tensor_scalar(
                    NT[b], NT[b], C3T[:, tc, b:b + 1], None, ALU.mult))
                s.op("vector", lambda e, b=b, NT=NT: e.tensor_tensor(
                    YH[:], YH[:], NT[b], ALU.add))
            myh = s.op("vector", lambda e, tc=tc: e.tensor_scalar(
                YH[:], YH[:], DEN[:, tc:tc + 1], None, ALU.mult))
            prev_tc_read = myh
            prev_merge = myh
            s.op("tensor", lambda e: e.transpose(C.pb4[0:64, 0:128], YH[:], C.eye[:, :]),
                 wait=[myh] + s.after(cldh) + ([aev_prev] if aev_prev else []),
                 inc=[(aomm, 1)])
            aev_prev = s.op("scalar", lambda e, h=h, tc=tc: e.copy(
                AOUT[(h % 2) * 64:(h % 2) * 64 + 64, h // 2, tc * 128:tc * 128 + 128],
                C.pb4[0:64, 0:128]), wait=[(aomm, s.val(aomm))])
    ATTND = aev_prev

    s.dma("gpsimd", CW[:], W["ow"], wstr, wait=[(pmm, s.val(pmm))])
    OWR = s.after(wstr)
    pi = 0
    pdvem = {}
    for ec in range(8):
        esl = slice(ec * 128, ec * 128 + 128)
        for ts in range(4):
            tsl = slice(ts * 512, ts * 512 + 512)
            psobj = C.pb5 if pi % 2 == 0 else None
            psap = C.pb5[:, 0:512] if pi % 2 == 0 else C.pbS[:, 0:512]
            w = [ATTND] + OWR + s.after(cldg)
            if pi >= 2:
                w.append(pdvem[pi - 2])

            def mmsp(e, esl=esl, tsl=tsl, psap=psap):
                last = None
                for dc in range(8):
                    last = e.matmul(psap, CW[:, dc, esl], AOUT[:, dc, tsl],
                                    start=(dc == 0), stop=False)
                return e.matmul(psap, BR[0:1, 3072 + esl.start:3072 + esl.stop],
                                C.ones_row_bf[0:1, 0:512],
                                start=False, stop=True)

            s.op("tensor", mmsp, wait=w, inc=[(pjmm, 1)])
            b = pi % 2
            xw = [ATTND] if pi < 2 else [pdvem[pi - 2]]
            s.dma("sync", XST[b][:], x_dram[:, ec, tsl], xin2[b], wait=xw)
            pdvem[pi] = s.op("vector", lambda e, b=b, psap=psap: e.scalar_tensor_tensor(
                YST[b][:], psap, 1.0, XST[b][:], ALU.bypass, ALU.add),
                 wait=[(pjmm, pi + 1), (xin2[b], s.val(xin2[b]))] +
                      ([(yout, 16 * pi)] if pi >= 2 else []))
            s.dma("sync", y_dram[:, ec, tsl], YST[b][:], yout, wait=[pdvem[pi]])
            pi += 1
    s.op("sync", lambda e: None, wait=[(yout, 16 * 32)])
    return NS(AOUT=AOUT, yout=yout)



_PROGS = {}


def _dram_ins(nc, specs):
    return {k: nc.dram_tensor(k, shp, F32, kind="ExternalInput").ap()
            for k, shp in specs}


def _mk_prog(name):
    if name in _PROGS:
        return _PROGS[name]
    nc = bass.Bass("TRN2", target_bir_lowering=False, detect_race_conditions=False)
    x = nc.dram_tensor("x", [128, 8, T], F32, kind="ExternalInput").ap()
    y = nc.dram_tensor("y", [128, 8, T], F32, kind="ExternalOutput").ap()
    if name == "l0aa":
        W = _dram_ins(nc, [("wr", [128, 8, 1024]), ("wk", [128, 8, 1024]),
                           ("wv", [128, 8, 1024]), ("wo", [128, 8, 1024]),
                           ("cols", [128, 56]), ("hmask", [128, 2]),
                           ("hmaskT", [2, 128])])
    elif name == "l0ffn":
        W = _dram_ins(nc, [("ew1", [9, 32, 128, 8, 128]), ("ew2", [9, 128, 32, 1024]),
                           ("eb1c", [128, 9, 32]), ("eb2r", [9, 1024]),
                           ("gw", [128, 8, 8]), ("gb", [1, 8]), ("eye", [128, 128])])
        W["acc"] = nc.dram_tensor("acc", [128, 16, 1024], F32).ap()
    elif name == "l0ab":
        W = _dram_ins(nc, [("wqkv", [128, 8, 384]), ("wproj", [128, 8, 128]),
                           ("bp", [128, 8]), ("trilm", [128, 128]),
                           ("eye", [128, 128])])
    elif name == "l1aa":
        W = _dram_ins(nc, [("wq", [128, 8, 256]), ("wk", [128, 8, 256]),
                           ("wv", [128, 8, 1024]), ("wo", [128, 8, 1024]),
                           ("tril01", [128, 128]), ("triu01", [128, 128]),
                           ("iota1", [128, 1]), ("eye", [128, 128])])
    elif name == "l1ffn":
        W = _dram_ins(nc, [("wqt", [128, 8, 1024]), ("wdkvt", [128, 8, 512]),
                           ("wukt", [128, 4, 1024]), ("wuvt", [128, 4, 1024]),
                           ("wot", [128, 8, 1024]),
                           ("n1c", [128, 8]), ("n2c", [128, 8]),
                           ("lngc", [128, 4]), ("lnbc", [128, 4]),
                           ("cosb", [128, T]), ("sinb", [128, T]),
                           ("swapm", [128, 128]), ("trilm", [128, 128]),
                           ("eye", [128, 128]), ("gw4", [128, 8, 4]), ("gb4", [1, 4]),
                           ("ew1", [4, 32, 128, 8, 128]), ("ew2", [4, 128, 32, 1024]),
                           ("eb1c", [128, 4, 32]), ("eb2r", [4, 1024])])
        W["xin"] = nc.dram_tensor("xin", [128, 8, T], F32).ap()
        W["acc"] = nc.dram_tensor("acc", [128, 16, 1024], F32).ap()
    elif name == "l1ab":
        W = _dram_ins(nc, [("caw", [128, 8, 3072]), ("cbc", [128, 24]),
                           ("qw", [128, 8, 1024]), ("kw", [128, 8, 1024]),
                           ("vw", [128, 8, 1024]), ("ow", [128, 8, 1024]),
                           ("qbr", [1, 1024]), ("kbr", [1, 1024]),
                           ("vbr", [1, 1024]), ("obr", [1, 1024]),
                           ("trilm", [128, 128]), ("eye", [128, 128])])
        for k in ("qd", "kd", "vd", "nb1", "nb2", "nb3"):
            W[k] = nc.dram_tensor(k, [T, 1024], BF16).ap()
        for k in ("m3", "s3"):
            W[k] = nc.dram_tensor(k, [T, 48], F32).ap()
    C, mem = setup_prog(nc)
    {"l0aa": build_l0aa, "l0ffn": build_l0ffn, "l0ab": build_l0ab,
     "l1aa": build_l1aa, "l1ffn": build_l1ffn, "l1ab": build_l1ab}[name](
        nc, C, mem, x, y, W)
    finish_prog(C)
    _PROGS[name] = nc
    return nc


def _prep_weights(inp):
    f32 = np.float32
    q = np.arange(128)[:, None]
    j = np.arange(128)[None, :]
    eye = np.eye(128, dtype=f32)
    trilm = np.where(j <= q, 0.0, -1e30).astype(f32)
    WS = {}
    cols = np.concatenate([
        host_col(inp["L0_attn_a_time_mix_r"]), host_col(inp["L0_attn_a_time_mix_k"]),
        host_col(inp["L0_attn_a_time_mix_v"]),
        host_col(inp["L0_attn_a_time_decay"].reshape(-1)),
        host_col(inp["L0_attn_a_time_first"].reshape(-1)),
        host_col(inp["L0_attn_a_gn_weight"]), host_col(inp["L0_attn_a_gn_bias"]),
    ], axis=1).astype(f32)
    hmask = np.zeros((128, 2), f32)
    hmask[:64, 0] = 1
    hmask[64:, 1] = 1
    WS["l0aa"] = {"wr": host_w(inp["L0_attn_a_W_r"].T), "wk": host_w(inp["L0_attn_a_W_k"].T),
                  "wv": host_w(inp["L0_attn_a_W_v"].T), "wo": host_w(inp["L0_attn_a_W_o"].T),
                  "cols": cols, "hmask": hmask,
                  "hmaskT": np.ascontiguousarray(hmask.T)}
    ew1_l, ew2_l, eb1_l, eb2_l = [], [], [], []
    for e in range(9):
        if e < 8:
            w1, b1 = inp["L0_ffn_ew1"][e], inp["L0_ffn_eb1"][e]
            w2, b2 = inp["L0_ffn_ew2"][e], inp["L0_ffn_eb2"][e]
        else:
            w1, b1 = inp["L0_ffn_sh_w1"], inp["L0_ffn_sh_b1"]
            w2, b2 = inp["L0_ffn_sh_w2"], inp["L0_ffn_sh_b2"]
        hw = host_w(w1).reshape(128, 8, 32, 128).transpose(2, 0, 1, 3)
        ew1_l.append(np.ascontiguousarray(hw))
        ew2_l.append(host_w(w2))
        eb1_l.append(host_col(b1))
        eb2_l.append(b2)
    WS["l0ffn"] = {"ew1": np.stack(ew1_l).astype(f32), "ew2": np.stack(ew2_l).astype(f32),
                   "eb1c": np.stack(eb1_l, axis=1).astype(f32),
                   "eb2r": np.stack(eb2_l).astype(f32),
                   "gw": host_w(inp["L0_ffn_gate_w"]),
                   "gb": inp["L0_ffn_gate_b"][None, :].astype(f32), "eye": eye}
    WS["l0ab"] = {"wqkv": np.ascontiguousarray(inp["L0_attn_b_w_qkv"].transpose(1, 0, 2)).astype(f32),
                  "wproj": np.ascontiguousarray(inp["L0_attn_b_w_proj"].transpose(1, 0, 2)).astype(f32),
                  "bp": host_col(inp["L0_attn_b_b_proj"]), "trilm": trilm, "eye": eye}
    WS["l1aa"] = {"wq": host_w(inp["L1_attn_a_Wq"]), "wk": host_w(inp["L1_attn_a_Wk"]),
                  "wv": host_w(inp["L1_attn_a_Wv"]), "wo": host_w(inp["L1_attn_a_Wo"]),
                  "tril01": (j <= q).astype(f32), "triu01": (j > q).astype(f32),
                  "iota1": (np.arange(128, dtype=f32) + 1.0)[:, None], "eye": eye}
    p = np.arange(128)
    j64 = p % 64
    jj = np.where(j64 < 32, j64, j64 - 32)
    inv = 1.0 / (10000.0 ** (jj / 32.0))
    tt = np.arange(T)
    ang = tt[None, :] * inv[:, None]
    cosb = np.cos(ang).astype(f32)
    sinb = (np.where(j64 < 32, -1.0, 1.0)[:, None] * np.sin(ang)).astype(f32)
    swapm = np.zeros((128, 128), f32)
    for m in range(128):
        sw = m + 32 if (m % 64) < 32 else m - 32
        swapm[sw, m] = 1.0
    ew1_l, ew2_l, eb1_l = [], [], []
    for e in range(4):
        hw = host_w(inp["L1_ffn_W1"][e].T).reshape(128, 8, 32, 128).transpose(2, 0, 1, 3)
        ew1_l.append(np.ascontiguousarray(hw))
        ew2_l.append(host_w(inp["L1_ffn_W2"][e].T))
        eb1_l.append(host_col(inp["L1_ffn_b1"][e]))
    WS["l1ffn"] = {"wqt": host_w(inp["L1_ffn_Wq"].T), "wdkvt": host_w(inp["L1_ffn_Wdkv"].T),
                   "wukt": host_w(inp["L1_ffn_Wuk"].T), "wuvt": host_w(inp["L1_ffn_Wuv"].T),
                   "wot": host_w(inp["L1_ffn_Wo"].T),
                   "n1c": host_col(inp["L1_ffn_norm1_w"]), "n2c": host_col(inp["L1_ffn_norm2_w"]),
                   "lngc": host_col(inp["L1_ffn_ln_g"]), "lnbc": host_col(inp["L1_ffn_ln_b"]),
                   "cosb": cosb, "sinb": sinb, "swapm": swapm, "trilm": trilm, "eye": eye,
                   "gw4": host_w(inp["L1_ffn_gate_w"].T),
                   "gb4": inp["L1_ffn_gate_b"][None, :].astype(f32),
                   "ew1": np.stack(ew1_l).astype(f32), "ew2": np.stack(ew2_l).astype(f32),
                   "eb1c": np.stack(eb1_l, axis=1).astype(f32),
                   "eb2r": inp["L1_ffn_b2"].astype(f32)}
    WS["l1ab"] = {"caw": host_w(inp["L1_attn_b_c_attn_w"]),
                  "cbc": host_col(inp["L1_attn_b_c_attn_b"]),
                  "qw": host_w(inp["L1_attn_b_q_w"]), "kw": host_w(inp["L1_attn_b_k_w"]),
                  "vw": host_w(inp["L1_attn_b_v_w"]), "ow": host_w(inp["L1_attn_b_o_w"]),
                  "qbr": inp["L1_attn_b_q_b"][None, :].astype(f32),
                  "kbr": inp["L1_attn_b_k_b"][None, :].astype(f32),
                  "vbr": inp["L1_attn_b_v_b"][None, :].astype(f32),
                  "obr": inp["L1_attn_b_o_b"][None, :].astype(f32),
                  "trilm": trilm, "eye": eye}
    return WS


def kernel(**inputs):
    inp = {k: np.asarray(v, np.float32) for k, v in inputs.items()}
    WS = _prep_weights(inp)
    x = inp["L0_attn_a_x"]
    cores = [host_fm(np.ascontiguousarray(x[i])) for i in range(8)]
    for name in ("l0aa", "l0ffn", "l0ab", "l0ffn", "l1aa", "l1ffn", "l1ab", "l1ffn"):
        nc = _mk_prog(name)
        in_maps = [dict(WS[name], x=cores[i]) for i in range(8)]
        res = run_bass_kernel_spmd(nc, in_maps, core_ids=list(range(8)))
        cores = [np.asarray(r["y"]) for r in res.results]
    return np.stack([host_fm_inv(c) for c in cores]).astype(np.float32)



_GROUP_SPECS = {
    "l0aa": [("wr", [128, 8, 1024]), ("wk", [128, 8, 1024]), ("wv", [128, 8, 1024]),
             ("wo", [128, 8, 1024]), ("cols", [128, 56]), ("hmask", [128, 2]),
             ("hmaskT", [2, 128])],
    "l0ffn": [("ew1", [9, 32, 128, 8, 128]), ("ew2", [9, 128, 32, 1024]),
              ("eb1c", [128, 9, 32]), ("eb2r", [9, 1024]), ("gw", [128, 8, 8]),
              ("gb", [1, 8])],
    "l0ab": [("wqkv", [128, 8, 384]), ("wproj", [128, 8, 128]), ("bp", [128, 8])],
    "l1aa": [("wq", [128, 8, 256]), ("wk", [128, 8, 256]), ("wv", [128, 8, 1024]),
             ("wo", [128, 8, 1024]), ("tril01", [128, 128]), ("triu01", [128, 128]),
             ("iota1", [128, 1])],
    "l1ffn": [("wqt", [128, 8, 1024]), ("wdkvt", [128, 8, 512]),
              ("wukt", [128, 4, 1024]), ("wuvt", [128, 4, 1024]),
              ("wot", [128, 8, 1024]), ("n1c", [128, 8]), ("n2c", [128, 8]),
              ("lngc", [128, 4]), ("lnbc", [128, 4]), ("cosb", [128, T]),
              ("sinb", [128, T]), ("swapm", [128, 128]), ("gw4", [128, 8, 4]),
              ("gb4", [1, 4]), ("ew1", [4, 32, 128, 8, 128]),
              ("ew2", [4, 128, 32, 1024]), ("eb1c", [128, 4, 32]),
              ("eb2r", [4, 1024])],
    "l1ab": [("caw", [128, 8, 3072]), ("cbc", [128, 24]), ("qw", [128, 8, 1024]),
             ("kw", [128, 8, 1024]), ("vw", [128, 8, 1024]), ("ow", [128, 8, 1024]),
             ("qbr", [1, 1024]), ("kbr", [1, 1024]), ("vbr", [1, 1024]),
             ("obr", [1, 1024])],
}
_NEED_EYE = {"l0ffn", "l0ab", "l1aa", "l1ffn", "l1ab"}
_NEED_TRILM = {"l0ab", "l1ffn", "l1ab"}
_SEQ = [("l0aa", None), ("l0ffn", None), ("l0ab", None), ("l0ffn", None),
        ("l1aa", None), ("l1ffn", None), ("l1ab", None), ("l1ffn", None)]


def _mk_fused():
    if "fused" in _PROGS:
        return _PROGS["fused"]
    nc = bass.Bass("TRN2", target_bir_lowering=False, detect_race_conditions=False)
    x0 = nc.dram_tensor("x", [128, 8, T], F32, kind="ExternalInput").ap()
    y = nc.dram_tensor("y", [128, 8, T], F32, kind="ExternalOutput").ap()
    xb = [nc.dram_tensor(f"xb{i}", [128, 8, T], F32).ap() for i in range(7)]
    bufs = [x0] + xb + [y]
    EYE = nc.dram_tensor("eye", [128, 128], F32, kind="ExternalInput").ap()
    TRI = nc.dram_tensor("trilm", [128, 128], F32, kind="ExternalInput").ap()
    Wg = {}
    for g, specs in _GROUP_SPECS.items():
        Wg[g] = {k: nc.dram_tensor(f"{g}_{k}", shp, F32, kind="ExternalInput").ap()
                 for k, shp in specs}
        if g in _NEED_EYE:
            Wg[g]["eye"] = EYE
        if g in _NEED_TRILM:
            Wg[g]["trilm"] = TRI
    Wg["l0ffn"]["acc"] = nc.dram_tensor("l0f_acc", [128, 16, 1024], F32).ap()
    Wg["l1ffn"]["acc"] = nc.dram_tensor("l1f_acc", [128, 16, 1024], F32).ap()
    Wg["l1ffn"]["xin"] = nc.dram_tensor("l1f_xin", [128, 8, T], F32).ap()
    for k in ("qd", "kd", "vd", "nb1", "nb2", "nb3"):
        Wg["l1ab"][k] = nc.dram_tensor("l1b_" + k, [T, 1024], BF16).ap()
    for k in ("m3", "s3"):
        Wg["l1ab"][k] = nc.dram_tensor("l1b_" + k, [T, 48], F32).ap()

    builders = {"l0aa": build_l0aa, "l0ffn": build_l0ffn, "l0ab": build_l0ab,
                "l1aa": build_l1aa, "l1ffn": build_l1ffn, "l1ab": build_l1ab}
    C, mem0 = setup_prog(nc)
    for k, (g, _) in enumerate(_SEQ):
        m = Mem(nc)
        m.off = C.common_end
        builders[g](nc, C, m, bufs[k], bufs[k + 1], Wg[g], tag=f"s{k}")
        if k < 7:
            C.sched.barrier()
    finish_prog(C)
    _PROGS["fused"] = nc
    return nc


def kernel(**inputs):
    inp = {k: np.asarray(v, np.float32) for k, v in inputs.items()}
    WS = _prep_weights(inp)
    x = inp["L0_attn_a_x"]
    nc = _mk_fused()
    im = {"eye": WS["l0ab"]["eye"], "trilm": WS["l0ab"]["trilm"]}
    for g in _GROUP_SPECS:
        for k, v in WS[g].items():
            if k in ("eye", "trilm"):
                continue
            im[f"{g}_{k}"] = v
    in_maps = [dict(im, x=host_fm(np.ascontiguousarray(x[i]))) for i in range(8)]
    res = run_bass_kernel_spmd(nc, in_maps, core_ids=list(range(8)))
    return np.stack([host_fm_inv(np.asarray(r["y"])) for r in res.results]).astype(np.float32)


# version 08
# speedup vs baseline: 1.0654x; 1.0085x over previous
import numpy as np
import ml_dtypes
import concourse.bass as bass
import concourse.mybir as mybir
from concourse.bass_utils import run_bass_kernel_spmd

F32 = mybir.dt.float32
BF16 = mybir.dt.bfloat16
AF = mybir.ActivationFunctionType
ALU = mybir.AluOpType
AX = mybir.AxisListType

ENGINES = ("sync", "tensor", "vector", "scalar", "gpsimd")
T = 2048


class Sched:

    CHAIN = ("vector", "scalar", "gpsimd")

    def __init__(self, nc):
        self.nc = nc
        self.q = {e: [] for e in ENGINES}
        self.counts = {}
        self.hmap = {}
        self.hbase = {}
        self.htotal = {}
        self.ns = 0
        self.ns_names = {0: []}
        self.free_handles = []
        self._ctxs = []
        self.marker = {e: None for e in ENGINES}
        self.chgen = 0
        self.semeng = {}
        self.dmasems = set()
        self.barn = 0
        self.sem("barA")
        self.sem("barB")

    def _pname(self, name):
        if name in ("barA", "barB"):
            return name
        return f"p{self.ns}_{name}"

    def sem(self, name):
        pname = self._pname(name)
        if pname not in self.hmap:
            if self.free_handles and pname not in ("barA", "barB"):
                h = self.free_handles.pop()
            else:
                cm = self.nc.semaphore(pname)
                h = cm.__enter__()
                self._ctxs.append(cm)
                self.htotal[h] = 0
            self.hmap[pname] = h
            self.hbase[pname] = self.htotal[h]
            self.counts[pname] = 0
            self.ns_names.setdefault(self.ns, []).append(pname)
        return pname

    def val(self, s):
        return self.counts[self._pn2(s)]

    def _pn2(self, s):
        return s if s in self.counts else self._pname(s)

    def after(self, *sems):
        return [(self._pn2(s), self.counts[self._pn2(s)]) for s in sems]

    def mark(self, eng):
        m = self.marker[eng]
        return [m] if m is not None else []

    def rotate_chain(self):
        self.chgen += 1
        for e in self.CHAIN:
            self.marker[e] = None

    def nsnew(self):
        self.ns += 1
        self.ns_names.setdefault(self.ns, [])
        for k in list(self.ns_names):
            if k <= self.ns - 2:
                for pname in self.ns_names.pop(k):
                    self.free_handles.append(self.hmap[pname])

    def barrier(self):
        self.barn += 1
        self.sem("barA")
        self.op("tensor", lambda e: e.nop(), inc=[("barA", 1)])
        self.op("sync", lambda e: e.nop(), inc=[("barA", 1)])
        w = (self.mark("vector") + self.mark("scalar") + self.mark("gpsimd")
             + [("barA", self.counts["barA"])])
        self.op("gpsimd", lambda e: e.nop(), wait=w, inc=[("barB", 1)], chain=False)
        tgt = [("barB", self.counts["barB"])]
        for e in ENGINES:
            self.op(e, lambda eng: None, wait=tgt, chain=False)
        for e in self.CHAIN:
            self.marker[e] = None
        self.nsnew()

    def op(self, eng, emit, wait=(), inc=(), chain=True, is_dma=False):
        wait = [(self._pn2(s), v) for s, v in wait]
        if eng in self.CHAIN and chain:
            assert not inc, f"user incs not allowed on chained {eng} op"
            if self.marker[eng] is not None:
                wait.append(self.marker[eng])
            cs = self.sem(f"ch{self.chgen}_{eng}")
            assert self.semeng.setdefault(cs, eng) == eng
            self.counts[cs] += 1
            self.htotal[self.hmap[cs]] += 1
            inc2 = [(cs, 1)]
            self.marker[eng] = (cs, self.counts[cs])
            self.q[eng].append((wait, emit, inc2))
            return self.marker[eng]
        inc2 = []
        for s, n in inc:
            s = self._pn2(s)
            if s not in ("barA", "barB"):
                if is_dma:
                    self.dmasems.add(s)
                else:
                    assert s not in self.dmasems, f"sem {s} used as dma+compute"
                assert self.semeng.setdefault(s, eng) == eng, \
                    f"sem {s} inc'd by {self.semeng[s]} and {eng}"
            self.counts[s] += n
            self.htotal[self.hmap[s]] += n
            inc2.append((s, n))
        self.q[eng].append((wait, emit, inc2))
        return inc2[-1] if inc2 else None

    def dma(self, eng, out, in_, sem, wait=(), **kw):
        self.op(eng, lambda e, out=out, in_=in_, kw=kw: e.dma_start(out=out, in_=in_, **kw),
                wait=wait, inc=[(sem, 16)], chain=False, is_dma=True)
        return (self._pn2(sem), self.counts[self._pn2(sem)])

    def emit_all(self, block):
        for ename in ENGINES:
            ops = self.q[ename]
            if not ops:
                continue

            def mk(ops, sched=self):
                def body(eng):
                    lastw = {}
                    for wait, emit, inc in ops:
                        for s, v in wait:
                            hv = sched.hbase[s] + v
                            h = sched.hmap[s]
                            if v > 0 and lastw.get(h, -1) < hv:
                                eng.wait_ge(h, hv)
                                lastw[h] = hv
                        instr = emit(eng)
                        for s, n in inc:
                            if instr is None:
                                instr = eng.nop()
                            instr.then_inc(sched.hmap[s], n)
                return body

            getattr(block, ename)(mk(list(ops)))

    def close(self):
        for cm in reversed(self._ctxs):
            cm.__exit__(None, None, None)


class NS:
    def __init__(self, **kw):
        self.__dict__.update(kw)


class Mem:

    def __init__(self, nc):
        self.nc = nc
        self.base = (nc.sbuf_base + 31) // 32 * 32
        self.top = nc.sbuf_top
        self.off = self.base

    def alloc(self, name, shape, dtype, at=None):
        per_part = int(np.prod(shape[1:])) * mybir.dt.size(dtype)
        off = self.off if at is None else at
        off = (off + 31) // 32 * 32
        assert off + per_part <= self.top, (
            f"SBUF overflow: {name} at {off} + {per_part} > {self.top}")
        h = self.nc.alloc_sbuf_tensor_at(name, shape, dtype, offset=off)
        if at is None:
            self.off = off + per_part
        return h

    def mark(self):
        return self.off

    def reset(self, mark):
        self.off = mark



def host_fm(x):
    Tn, Dn = x.shape
    return np.ascontiguousarray(x.T.reshape(Dn // 128, 128, Tn).transpose(1, 0, 2))


def host_fm_inv(y):
    p, nch, Tn = y.shape
    return np.ascontiguousarray(y.transpose(1, 0, 2).reshape(nch * 128, Tn).T)


def host_w(w):
    Din, Dout = w.shape
    return np.ascontiguousarray(w.reshape(Din // 128, 128, Dout).transpose(1, 0, 2))


def host_col(v):
    Dn = v.shape[0]
    return np.ascontiguousarray(v.reshape(Dn // 128, 128).T)



def emit_ln_sbuf(C, xs, lnb, name, nchunk=8, Tn=T, ready=(), affine=None):
    s = C.sched
    nts = Tn // 512
    Dn = nchunk * 128
    mmsum = s.sem(name + "_mmsum")
    mmsq = s.sem(name + "_mmsq")
    ps_sum, ps_sq, ps_b0, ps_b1 = C.pb[0], C.pb[1], C.pb[2], C.pb[3]
    sq8, bc0, bc1 = C.sq8, C.bc0, C.bc1
    r_mu, r_e2, r_tm = C.r_mu, C.r_e2, C.r_tm
    sq_marks = {}
    last_apply = None
    prev_bc_copies = None
    prev_row_reads = None
    for ts in range(nts):
        tsl = slice(ts * 512, ts * 512 + 512)
        for c in range(nchunk):
            w = list(ready)
            if ts > 0:
                w += [(mmsq, ts)]
                w += [last_apply]
            sq_marks[c] = s.op("scalar", lambda e, c=c, tsl=tsl: e.activation(
                sq8[:, c, :], xs[:, c, tsl], AF.Square), wait=w)
        for c in range(nchunk):
            w = list(ready) if c == 0 else []
            if c == 0 and prev_row_reads:
                w += prev_row_reads
            s.op("tensor", lambda e, c=c, tsl=tsl: e.matmul(
                ps_sum[0:1, 0:512], C.ones_col[:, 0:1], xs[:, c, tsl],
                start=(c == 0), stop=(c == nchunk - 1)),
                 wait=w, inc=([(mmsum, 1)] if c == nchunk - 1 else []))
        for c in range(nchunk):
            s.op("tensor", lambda e, c=c: e.matmul(
                ps_sq[0:1, 0:512], C.ones_col[:, 0:1], sq8[:, c, :],
                start=(c == 0), stop=(c == nchunk - 1)),
                 wait=[sq_marks[c]],
                 inc=([(mmsq, 1)] if c == nchunk - 1 else []))
        m_mu = s.op("vector", lambda e: e.tensor_scalar(
            r_mu[:], ps_sum[0:1, 0:512], 1.0 / Dn, None, ALU.mult),
             wait=[(mmsum, ts + 1)] + (prev_bc_copies or []))
        m_e2 = s.op("vector", lambda e: e.tensor_scalar(
            r_e2[:], ps_sq[0:1, 0:512], 1.0 / Dn, None, ALU.mult),
             wait=[(mmsq, ts + 1)])
        prev_row_reads = [m_e2]
        s.op("vector", lambda e: e.tensor_tensor(
            r_tm[:], r_mu[:], r_mu[:], ALU.mult))
        m_var = s.op("vector", lambda e: e.tensor_tensor(
            r_e2[:], r_e2[:], r_tm[:], ALU.subtract))
        s.op("scalar", lambda e: e.activation(
            r_tm[:], r_e2[:], AF.Ln, bias=C.cpool[0:1, 0:1]),
             wait=[m_var] + C.cready)
        m_rstd = s.op("scalar", lambda e: e.activation(
            r_e2[:], r_tm[:], AF.Exp, scale=-0.5))
        bcm = s.sem(name + "_bcm")
        s.op("tensor", lambda e: e.matmul(
            ps_b0[:, 0:512], C.ones_row[0:1, 0:128], r_e2[:],
            start=True, stop=True),
             wait=[m_rstd] + (prev_bc_copies or []), inc=[(bcm, 1)])
        s.op("tensor", lambda e: e.matmul(
            ps_b1[:, 0:512], C.ones_row[0:1, 0:128], r_mu[:],
            start=True, stop=True),
             wait=[m_mu] + (prev_bc_copies or []), inc=[(bcm, 1)])
        c0 = s.op("scalar", lambda e: e.copy(bc0[:], ps_b0[:, 0:512]),
                  wait=[(bcm, 2 * (ts + 1))] + ([last_apply] if last_apply else []))
        c1 = s.op("scalar", lambda e: e.copy(bc1[:], ps_b1[:, 0:512]))
        prev_bc_copies = [c0, c1]
        for c in range(nchunk):
            s.op("vector", lambda e, c=c, tsl=tsl: e.tensor_tensor(
                sq8[:, c, :], xs[:, c, tsl], bc1[:], ALU.subtract),
                 wait=[c1, (mmsq, ts + 1)])
            if affine is None:
                last_apply = s.op("vector", lambda e, c=c, tsl=tsl: e.tensor_tensor(
                    lnb[:, c, tsl], sq8[:, c, :], bc0[:], ALU.mult))
            else:
                gfn, bfn = affine
                s.op("vector", lambda e, c=c: e.tensor_tensor(
                    sq8[:, c, :], sq8[:, c, :], bc0[:], ALU.mult))
                last_apply = s.op("vector", lambda e, c=c, tsl=tsl: e.tensor_scalar(
                    lnb[:, c, tsl], sq8[:, c, :], gfn(c), bfn(c),
                    ALU.mult, ALU.add))
    return last_apply


def setup_prog(nc):
    mem = Mem(nc)
    C = NS(sched=Sched(nc), nc=nc, mem=mem)
    C._psum_ctxs = []

    def ps(name, shape, dt):
        cm = nc.psum_tensor(name, shape, dt)
        h = cm.__enter__()
        C._psum_ctxs.append(cm)
        return h

    C.pbS = ps("pbS", [128, 2048], F32)
    C.pb4 = C.pbS[:, 1536:2048]
    C.pb5 = ps("pb5", [128, 512], F32)
    C.pb6 = ps("pb6", [128, 512], F32)
    C.pbTa = ps("pbTa", [128, 1024], BF16)
    C.pbTb = ps("pbTb", [128, 1024], BF16)
    C.pb = [C.pbS[:, 0:512], C.pbS[:, 512:1024], C.pbS[:, 1024:1536], C.pb4]
    alloc_common(mem, C)
    C.common_end = mem.mark()
    return C, mem


def finish_prog(C):
    with C.nc.Block() as block:
        C.sched.emit_all(block)
    C.sched.close()
    for cm in reversed(C._psum_ctxs):
        cm.__exit__(None, None, None)


def alloc_common(mem, C, Tn=T):
    C.ones_col = mem.alloc("ones_col", [128, 8], F32)
    C.ones_row = mem.alloc("ones_row", [128, 128], F32)
    C.ones_row_bf = mem.alloc("ones_row_bf", [128, 512], BF16)
    C.eye = mem.alloc("eye", [128, 128], F32)
    C.eyeb = mem.alloc("eyeb", [128, 128], BF16)
    C.cpool = mem.alloc("cpool", [128, 8], F32)
    C.cvals = [1e-5, 1.0, 1e-6, 1e-9, 0.0, 0.5, 2.0, -1.0]
    C.EPS5 = C.cpool[:, 0:1]
    C.EPS6 = C.cpool[:, 2:3]
    C.lnscr_off = (mem.mark() + 31) // 32 * 32
    C.sq8 = mem.alloc("ln_sq8", [128, 8, 512], F32)
    C.r_mu = mem.alloc("ln_rmu", [1, 512], F32)
    C.r_e2 = mem.alloc("ln_re2", [1, 512], F32)
    C.r_tm = mem.alloc("ln_rtm", [1, 512], F32)
    C.bc0 = mem.alloc("ln_bc0", [128, 512], F32)
    C.bc1 = mem.alloc("ln_bc1", [128, 512], F32)
    mem.reset(max(mem.mark(), C.lnscr_off + 32768))


def emit_init_consts(C):
    s = C.sched
    s.op("vector", lambda e: e.memset(C.ones_col[:], 1.0))
    for i, v in enumerate(C.cvals):
        s.op("vector", lambda e, i=i, v=v: e.memset(C.cpool[:, i:i + 1], v))
    s.op("vector", lambda e: e.memset(C.ones_row[:], 1.0))
    m = s.op("vector", lambda e: e.memset(C.ones_row_bf[:], 1.0))
    C.cready = [m]
    return m



def build_l0aa(nc, C, mem, x_dram, y_dram, W, tag="l0aa"):
    s = C.sched
    GN_N = float(T * 64)

    C.hmask = mem.alloc(tag + "_hmask", [128, 2], F32)
    C.hmaskT = mem.alloc(tag + "_hmaskT", [2, 128], F32)
    LNB = mem.alloc(tag + "_lnb", [128, 8, T], BF16)
    COLS = mem.alloc(tag + "_cols", [128, 56], F32)
    DCOLS = mem.alloc(tag + "_dcols", [128, 40], F32)
    SCOL = mem.alloc(tag + "_scol", [128, 2], F32)
    STAT2 = mem.alloc(tag + "_stat2", [2, 8], F32)
    CBUF = mem.alloc(tag + "_cbuf", [128, 4], F32)
    mA = mem.mark()
    XS = mem.alloc(tag + "_xs", [128, 8, T], F32)
    endA = mem.mark()
    XR = mem.alloc(tag + "_xr", [128, 8, T], BF16, at=mA)
    XK = mem.alloc(tag + "_xk", [128, 8, T], BF16, at=mA + 32768)
    mem.reset(endA)
    XV = mem.alloc(tag + "_xv", [128, 8, T], BF16)
    WK = mem.alloc(tag + "_wk", [128, 8, 1024], BF16, at=C.lnscr_off)
    WV = mem.alloc(tag + "_wv", [128, 8, 1024], BF16, at=C.lnscr_off + 16384)
    mB = mem.mark()
    WR = mem.alloc(tag + "_wr", [128, 8, 1024], BF16, at=mB)
    EK = mem.alloc(tag + "_ek", [128, T], F32)
    VB = mem.alloc(tag + "_vb", [128, T], F32)
    PB = mem.alloc(tag + "_pb", [128, T], F32)
    NUM = mem.alloc(tag + "_num", [128, T], F32)
    DEN = mem.alloc(tag + "_den", [128, T], F32)
    WO = mem.alloc(tag + "_wo", [128, 8, 1024], BF16, at=mB)
    XST = [mem.alloc(tag + f"_xst{i}", [128, 512], F32, at=mB + 16384 + 2048 * i)
           for i in range(2)]
    YST = [mem.alloc(tag + f"_yst{i}", [128, 512], F32, at=mB + 16384 + 4096 + 2048 * i)
           for i in range(2)]
    NORMED = XR
    RSIG = LNB

    xld = s.sem("g"  + "_xld")
    cld = s.sem("g"  + "_cld")
    wld1 = s.sem("g"  + "_wld1")
    wld2 = s.sem("g"  + "_wld2")
    wld3 = s.sem("g"  + "_wld3")
    rmm = s.sem("g"  + "_rmm")
    kvmm = s.sem("g"  + "_kvmm")
    auxmm = s.sem("g"  + "_auxmm")
    bcmm2 = s.sem("g"  + "_bcmm2")
    womm = s.sem("g"  + "_womm")
    xin2 = [s.sem("g"  + "_xin2a"), s.sem("g"  + "_xin2b")]
    yout = s.sem("g"  + "_yout")

    for i in range(4):
        s.dma("sync", XS[:, 2 * i:2 * i + 2, :], x_dram[:, 2 * i:2 * i + 2, :], xld)
    s.dma("sync", COLS[:], W["cols"], cld)
    s.dma("sync", C.hmask[:], W["hmask"], cld)
    s.dma("sync", C.hmaskT[:], W["hmaskT"], cld)
    s.dma("gpsimd", WR[:], W["wr"], wld1)
    hm_ready = s.after(cld)

    lam_m = None
    for j, so in enumerate((0, 8, 16)):
        lam_m = s.op("vector", lambda e, j=j, so=so: e.tensor_scalar(
            DCOLS[:, 8 * j:8 * j + 8], COLS[:, so:so + 8], -1.0, 1.0,
            ALU.mult, ALU.add), wait=s.after(cld))
    s.op("scalar", lambda e: e.activation(DCOLS[:, 24:32], COLS[:, 24:32], AF.Exp),
         wait=s.after(cld))
    s.op("scalar", lambda e: e.activation(DCOLS[:, 24:32], DCOLS[:, 24:32],
                                          AF.Exp, scale=-1.0))
    eu_m = s.op("scalar", lambda e: e.activation(DCOLS[:, 32:40], COLS[:, 32:40], AF.Exp))

    emit_init_consts(C)
    LNM = emit_ln_sbuf(C, XS, LNB, "g_ln", ready=s.after(xld) + C.cready)

    s.dma("gpsimd", WK[:], W["wk"], wld2, wait=[LNM])
    s.dma("gpsimd", WV[:], W["wv"], wld2, wait=[LNM])
    W2R = s.after(wld2)

    mix_last = {}
    for (XM, co, do, nm) in ((XR, 0, 0, "r"), (XK, 8, 8, "k"), (XV, 16, 16, "v")):
        for c in range(8):
            s.op("vector", lambda e, XM=XM, c=c, co=co: e.tensor_scalar(
                XM[:, c, :], LNB[:, c, :], COLS[:, co + c:co + c + 1], None, ALU.mult),
                 wait=[LNM, lam_m])
            mix_last[nm] = s.op("vector", lambda e, XM=XM, c=c, do=do: e.scalar_tensor_tensor(
                XM[:, c, 1:T], LNB[:, c, 0:T - 1], DCOLS[:, do + c:do + c + 1],
                XM[:, c, 1:T], ALU.mult, ALU.add))
    MIXED = mix_last["v"]

    rbanks = [C.pb5[:], C.pb6[:], C.pb[0], C.pb[1]]
    revm = {}
    ri = 0
    for ec in range(8):
        esl = slice(ec * 128, ec * 128 + 128)
        for ts in range(4):
            tsl = slice(ts * 512, ts * 512 + 512)
            ps = rbanks[ri % 4]
            waits = [MIXED] + s.after(wld1)
            if ri >= 4:
                waits.append(revm[ri - 4])

            def mms(e, esl=esl, tsl=tsl, ps=ps):
                last = None
                for dc in range(8):
                    last = e.matmul(ps[:, 0:512], WR[:, dc, esl], XR[:, dc, tsl],
                                    start=(dc == 0), stop=(dc == 7))
                return last

            s.op("tensor", mms, wait=waits, inc=[(rmm, 1)])
            revm[ri] = s.op("scalar", lambda e, ps=ps, tsl=tsl, ec=ec:
                            e.activation(RSIG[:, ec, tsl], ps[:, 0:512], AF.Sigmoid),
                            wait=[(rmm, ri + 1), MIXED])
            ri += 1
    RDONE = revm[31]

    kvi = 0
    kvbanks = [C.pb5[:], C.pb6[:]]
    kvevm = {}
    ckdm = {}
    prev_stat_rd = None
    prev_cbuf_rd = None
    for ec in range(8):
        esl = slice(ec * 128, ec * 128 + 128)
        for which, (WX, XMIX, dst) in enumerate(((WK, XK, EK), (WV, XV, VB))):
            for ts in range(4):
                tsl = slice(ts * 512, ts * 512 + 512)
                ps = kvbanks[kvi % 2]
                waits = [MIXED] + W2R
                if kvi < 2:
                    waits.append(RDONE)
                else:
                    waits.append(kvevm[kvi - 2])

                def mms(e, WX=WX, XMIX=XMIX, esl=esl, tsl=tsl, ps=ps):
                    last = None
                    for dc in range(8):
                        last = e.matmul(ps[:, 0:512], WX[:, dc, esl], XMIX[:, dc, tsl],
                                        start=(dc == 0), stop=(dc == 7))
                    return last

                s.op("tensor", mms, wait=waits, inc=[(kvmm, 1)])
                ewaits = [(kvmm, kvi + 1)]
                if ec > 0:
                    ewaits.append(ckdm[ec - 1])
                if which == 0:
                    kvevm[kvi] = s.op("scalar", lambda e, ps=ps, tsl=tsl: e.activation(
                        EK[:, tsl], ps[:, 0:512], AF.Exp), wait=ewaits)
                else:
                    kvevm[kvi] = s.op("scalar", lambda e, ps=ps, tsl=tsl: e.copy(
                        VB[:, tsl], ps[:, 0:512]), wait=ewaits)
                kvi += 1
        wbc = bass.AP(DCOLS, 24 + ec, [[40, 128], [0, T]])
        euc = DCOLS[:, 32 + ec:32 + ec + 1]
        s.op("vector", lambda e: e.tensor_tensor(PB[:], EK[:], VB[:], ALU.mult),
             wait=[kvevm[kvi - 1], eu_m])
        s.op("vector", lambda e, wbc=wbc: e.tensor_tensor_scan(
            NUM[:], wbc, PB[:], 0.0, ALU.mult, ALU.add))
        s.op("vector", lambda e, wbc=wbc: e.tensor_tensor_scan(
            DEN[:], wbc, EK[:], 0.0, ALU.mult, ALU.add))
        s.op("vector", lambda e, euc=euc: e.scalar_tensor_tensor(
            VB[:, 1:T], PB[:, 1:T], euc, NUM[:, 0:T - 1], ALU.mult, ALU.add))
        s.op("vector", lambda e, euc=euc: e.tensor_scalar(
            VB[:, 0:1], PB[:, 0:1], euc, None, ALU.mult))
        s.op("vector", lambda e, euc=euc: e.tensor_scalar(
            EK[:], EK[:], euc, 1e-9, ALU.mult, ALU.add))
        s.op("vector", lambda e: e.tensor_tensor(
            PB[:, 1:T], EK[:, 1:T], DEN[:, 0:T - 1], ALU.add))
        mBv = s.op("vector", lambda e: e.tensor_copy(PB[:, 0:1], EK[:, 0:1]))
        s.op("scalar", lambda e: e.activation(NUM[:], PB[:], AF.Ln), wait=[mBv])
        mrb = s.op("scalar", lambda e: e.activation(EK[:], NUM[:], AF.Exp, scale=-1.0))
        s.op("vector", lambda e: e.tensor_tensor(DEN[:], VB[:], EK[:], ALU.mult),
             wait=[mrb, RDONE])
        s.op("vector", lambda e, ec=ec: e.scalar_tensor_tensor(
            VB[:], DEN[:], 1.0, RSIG[:, ec, :], ALU.bypass, ALU.mult,
            accum_out=SCOL[:, 0:1]))
        mscr = s.op("vector", lambda e: e.scalar_tensor_tensor(
            PB[:], VB[:], 1.0, VB[:], ALU.bypass, ALU.mult,
            accum_out=SCOL[:, 1:2]))
        s.op("tensor", lambda e: e.matmul(C.pb[2][0:2, 0:2], C.hmask[:, 0:2],
                                          SCOL[:, 0:2], start=True, stop=True),
             wait=[mscr] + hm_ready + ([prev_stat_rd] if prev_stat_rd else []),
             inc=[(auxmm, 1)])
        prev_stat_rd = s.op("vector", lambda e: e.tensor_scalar(
            STAT2[:, 0:2], C.pb[2][0:2, 0:2], 1.0 / GN_N, None, ALU.mult),
             wait=[(auxmm, ec + 1)])
        s.op("vector", lambda e: e.tensor_tensor(
            STAT2[:, 2:3], STAT2[:, 0:1], STAT2[:, 0:1], ALU.mult))
        mvar = s.op("vector", lambda e: e.tensor_tensor(
            STAT2[:, 3:4], STAT2[:, 1:2], STAT2[:, 2:3], ALU.subtract))
        s.op("scalar", lambda e: e.activation(
            STAT2[:, 4:5], STAT2[:, 3:4], AF.Ln, bias=C.cpool[0:2, 0:1]),
             wait=[mvar] + C.cready)
        mrstd = s.op("scalar", lambda e: e.activation(
            STAT2[:, 1:2], STAT2[:, 4:5], AF.Exp, scale=-0.5))
        s.op("tensor", lambda e: e.matmul(C.pb[3][:, 0:2], C.hmaskT[0:2, :],
                                          STAT2[:, 0:2], start=True, stop=True),
             wait=[mrstd] + ([prev_cbuf_rd] if prev_cbuf_rd else []),
             inc=[(bcmm2, 1)])
        s.op("vector", lambda e, ec=ec: e.tensor_tensor(
            CBUF[:, 0:1], C.pb[3][:, 1:2], COLS[:, 40 + ec:40 + ec + 1], ALU.mult),
             wait=[(bcmm2, ec + 1)])
        prev_cbuf_rd = s.op("vector", lambda e: e.tensor_tensor(
            CBUF[:, 1:2], C.pb[3][:, 0:1], CBUF[:, 0:1], ALU.mult))
        s.op("vector", lambda e, ec=ec: e.tensor_tensor(
            CBUF[:, 2:3], COLS[:, 48 + ec:48 + ec + 1], CBUF[:, 1:2], ALU.subtract))
        ckdm[ec] = s.op("vector", lambda e, ec=ec: e.tensor_scalar(
            NORMED[:, ec, :], VB[:], CBUF[:, 0:1], CBUF[:, 2:3], ALU.mult, ALU.add))

    CKD = ckdm[7]
    s.dma("gpsimd", WO[:], W["wo"], wld3, wait=[CKD])

    wi = 0
    wbanks = [C.pb5[:], C.pb6[:], C.pb[0], C.pb[1]]
    wodvem = {}
    for ec in range(8):
        esl = slice(ec * 128, ec * 128 + 128)
        for ts in range(4):
            tsl = slice(ts * 512, ts * 512 + 512)
            ps = wbanks[wi % 4]
            waits = [CKD] + s.after(wld3)
            if wi >= 4:
                waits.append(wodvem[wi - 4])

            def mms(e, esl=esl, tsl=tsl, ps=ps):
                last = None
                for dc in range(8):
                    last = e.matmul(ps[:, 0:512], WO[:, dc, esl], NORMED[:, dc, tsl],
                                    start=(dc == 0), stop=(dc == 7))
                return last

            s.op("tensor", mms, wait=waits, inc=[(womm, 1)])
            b = wi % 2
            xw = [CKD] if wi < 2 else [wodvem[wi - 2]]
            s.dma("sync", XST[b][:], x_dram[:, ec, tsl], xin2[b], wait=xw)
            wodvem[wi] = s.op("vector", lambda e, ps=ps, b=b: e.scalar_tensor_tensor(
                YST[b][:], ps[:, 0:512], 1.0, XST[b][:], ALU.bypass, ALU.add),
                 wait=[(womm, wi + 1), (xin2[b], s.val(xin2[b]))] +
                      ([(yout, 16 * wi)] if wi >= 2 else []))
            s.dma("sync", y_dram[:, ec, tsl], YST[b][:], yout, wait=[wodvem[wi]])
            wi += 1
    s.op("sync", lambda e: None, wait=[(yout, 16 * 32)])



def build_l0ffn(nc, C, mem, x_dram, y_dram, W, tag="l0f"):
    s = C.sched
    TB = 1024
    NTB = T // TB
    NE = 9

    LNB = mem.alloc(tag + "_lnb", [128, 8, T], BF16)
    WEX = mem.alloc(tag + "_wex", [128, 16, 9], F32)
    RT = mem.alloc(tag + "_rt", [128, 8], F32)
    P8 = mem.alloc(tag + "_p8", [128, 8], F32)
    M8 = mem.alloc(tag + "_m8", [128, 8], F32)
    PM8 = mem.alloc(tag + "_pm8", [128, 8], F32)
    GW = mem.alloc(tag + "_gw", [128, 8, 8], BF16)
    GB = mem.alloc(tag + "_gb", [1, 8], BF16)
    EB1C = mem.alloc(tag + "_eb1c", [128, 9, 32], F32)
    _eb2r = mem.alloc(tag + "_eb2r", [1, 1024], BF16)
    EB2R = [_eb2r, _eb2r]
    EW1S = [mem.alloc(tag + f"_ew1s{i}", [128, 8, 128], BF16) for i in range(2)]
    AST = [mem.alloc(tag + f"_ast{i}", [128, 512], F32) for i in range(2)]
    mA = mem.mark()
    XS = mem.alloc(tag + "_xs", [128, 8, T], F32)
    endA = mem.mark()
    H = mem.alloc(tag + "_h", [128, 32, TB], BF16, at=mA)
    XT = [mem.alloc(tag + f"_xt{i}", [128, 512], F32, at=mA + 4096 * i) for i in range(2)]
    YT = [mem.alloc(tag + f"_yt{i}", [128, 512], F32, at=mA + 8192 + 4096 * i) for i in range(2)]
    AT = [mem.alloc(tag + f"_at{i}", [128, 128], F32, at=mA + 16384 + 1024 * i) for i in range(4)]
    mem.reset(endA)
    EW2 = mem.alloc(tag + "_ew2", [128, 32, 1024], BF16)

    xld = s.sem("g"  + "_xld")
    cld = s.sem("g"  + "_cld")
    cldg = s.sem("g"  + "_cldg")
    w1ld = s.sem("g"  + "_w1ld")
    w2ld = s.sem("g"  + "_w2ld")
    gmm = s.sem("g"  + "_gmm")
    g1mm = s.sem("g"  + "_g1mm")
    g2mm = s.sem("g"  + "_g2mm")
    acd = [s.sem("g"  + "_acd0"), s.sem("g"  + "_acd1")]
    w1l2 = [s.sem("g"  + "_w1l0"), s.sem("g"  + "_w1l1")]
    fat_s = [s.sem(tag + f"_fat{j}") for j in range(4)]
    fxt_s = [s.sem("g"  + "_fxa"), s.sem("g"  + "_fxb")]
    trmm = s.sem("g"  + "_trmm")
    yout = s.sem("g"  + "_yout")

    for i in range(4):
        s.dma("sync", XS[:, 2 * i:2 * i + 2, :], x_dram[:, 2 * i:2 * i + 2, :], xld)
    s.dma("gpsimd", GW[:], W["gw"], cldg)
    s.dma("gpsimd", GB[:], W["gb"], cldg)
    s.dma("sync", EB1C[:], W["eb1c"], cld)
    s.dma("sync", C.eye[:], W["eye"], cld)
    s.dma("gpsimd", C.eyeb[:], W["eye"], cldg)

    emit_init_consts(C)
    LNM = emit_ln_sbuf(C, XS, LNB, "g_ln", ready=s.after(xld) + C.cready)

    mwex = s.op("vector", lambda e: e.memset(WEX[:, :, 8], 1.0))

    grd_prev = None
    gr_last = None
    for tc in range(16):
        tsl = slice(tc * 128, tc * 128 + 128)
        psg = C.pb[2]

        def gmms(e, tsl=tsl):
            last = None
            for dc in range(8):
                last = e.matmul(psg[0:128, 0:8], LNB[:, dc, tsl], GW[:, dc, 0:8],
                                start=(dc == 0), stop=False)
            return e.matmul(psg[0:128, 0:8], C.ones_row_bf[0:1, 0:128], GB[0:1, 0:8],
                            start=False, stop=True)

        w = [LNM] + s.after(cld, cldg) + C.cready
        if grd_prev:
            w += [grd_prev, gr_last]
        s.op("tensor", gmms, wait=w, inc=[(gmm, 1)])
        s.op("vector", lambda e: e.tensor_reduce(RT[:, 0:1], psg[0:128, 0:8], AX.X, ALU.max),
             wait=[(gmm, tc + 1)])
        mnm1 = s.op("vector", lambda e: e.tensor_scalar(RT[:, 1:2], RT[:, 0:1], -1.0,
                                                        None, ALU.mult))
        grd_prev = s.op("scalar", lambda e: e.activation(
            P8[:], psg[0:128, 0:8], AF.Exp, bias=RT[:, 1:2], accum_out=RT[:, 2:3]),
             wait=[(gmm, tc + 1), mnm1] + ([gr_last] if gr_last else []))
        s.op("scalar", lambda e: e.activation(RT[:, 3:4], RT[:, 2:3], AF.Ln))
        mrS = s.op("scalar", lambda e: e.activation(RT[:, 4:5], RT[:, 3:4], AF.Exp,
                                                    scale=-1.0))
        s.op("vector", lambda e: e.tensor_scalar(P8[:], P8[:], RT[:, 4:5], None, ALU.mult),
             wait=[mrS])
        s.op("vector", lambda e: e.tensor_reduce(RT[:, 5:6], P8[:], AX.X, ALU.max))
        s.op("vector", lambda e: e.tensor_scalar(M8[:], P8[:], RT[:, 5:6], None, ALU.is_ge))
        s.op("vector", lambda e: e.scalar_tensor_tensor(PM8[:], P8[:], 1.0, M8[:],
                                                        ALU.bypass, ALU.mult))
        s.op("vector", lambda e: e.tensor_tensor(PM8[:], P8[:], PM8[:], ALU.subtract))
        s.op("vector", lambda e: e.tensor_reduce(RT[:, 6:7], PM8[:], AX.X, ALU.max))
        s.op("vector", lambda e: e.tensor_tensor(RT[:, 7:8], RT[:, 5:6], RT[:, 6:7], ALU.add))
        mws = s.op("vector", lambda e: e.tensor_scalar(RT[:, 7:8], RT[:, 7:8], 1.0, 1e-9,
                                                       ALU.mult, ALU.add))
        s.op("scalar", lambda e: e.activation(RT[:, 3:4], RT[:, 7:8], AF.Ln), wait=[mws])
        mrw = s.op("scalar", lambda e: e.activation(RT[:, 4:5], RT[:, 3:4], AF.Exp,
                                                    scale=-1.0))
        s.op("vector", lambda e: e.tensor_scalar(M8[:], P8[:], RT[:, 6:7], None, ALU.is_ge),
             wait=[mrw])
        s.op("vector", lambda e: e.tensor_tensor(M8[:], P8[:], M8[:], ALU.mult))
        gr_last = s.op("vector", lambda e, tc=tc: e.tensor_scalar(
            WEX[:, tc, 0:8], M8[:], RT[:, 4:5], None, ALU.mult))
    GATED = [gr_last, mwex]

    g1t = 0
    g2t = 0
    nacc = 0
    w1n = 0
    g1evm = {}
    g2evm = {}
    ew1_hist = {}
    for e in range(NE):
        ew2wait = [(g2mm, g2t)] if e > 0 else []
        s.dma("gpsimd", EW2[:], W["ew2"][e], w2ld, wait=ew2wait)
        s.dma("gpsimd", EB2R[e % 2][:], W["eb2r"][e:e + 1, :], w2ld)
        w2tgt = s.val(w2ld)
        for tb in range(NTB):
            for ic in range(32):
                b = w1n % 2
                dmawait = [(g1mm, ew1_hist[w1n - 2])] if w1n >= 2 else []
                s.dma("gpsimd", EW1S[b][:], W["ew1"][e, ic], w1l2[b], wait=dmawait)
                w1tgt = s.val(w1l2[b])
                for tsp in range(TB // 512):
                    ps = (C.pb5 if g1t % 2 == 0 else C.pb6)[:]

                    def mms(eng, b=b, tb=tb, tsp=tsp, ps=ps):
                        last = None
                        rs = slice(tb * TB + tsp * 512, tb * TB + tsp * 512 + 512)
                        for dc in range(8):
                            last = eng.matmul(ps[:, 0:512], EW1S[b][:, dc, :],
                                              LNB[:, dc, rs],
                                              start=(dc == 0), stop=(dc == 7))
                        return last

                    waits = [(w1l2[b], w1tgt), LNM]
                    if g1t >= 2:
                        waits.append(g1evm[g1t - 2])
                    s.op("tensor", mms, wait=waits, inc=[(g1mm, 1)])
                    psl = slice(tsp * 512, tsp * 512 + 512)
                    g1evm[g1t] = s.op("scalar", lambda eng, ps=ps, ic=ic, psl=psl, e=e:
                                      eng.activation(H[:, ic, psl], ps[:, 0:512],
                                                     AF.Gelu_apprx_tanh,
                                                     bias=EB1C[:, e, ic:ic + 1]),
                                      wait=[(g1mm, g1t + 1), (g2mm, g2t)] + s.after(cld))
                    g1t += 1
                ew1_hist[w1n] = g1t
                w1n += 1
            HREADY = g1evm[g1t - 1]
            for tsub in range(TB // 128):
                tc = tb * (TB // 128) + tsub
                hsl = slice(tsub * 128, tsub * 128 + 128)
                for dsp in range(2):
                    dsl = slice(dsp * 512, dsp * 512 + 512)
                    ps = C.pb[0 + (g2t % 2)]

                    def mms2(eng, hsl=hsl, dsl=dsl, ps=ps, e=e):
                        last = None
                        for ic in range(32):
                            last = eng.matmul(ps[:, 0:512], H[:, ic, hsl],
                                              EW2[:, ic, dsl],
                                              start=(ic == 0), stop=False)
                        return eng.matmul(ps[:, 0:512], C.ones_row_bf[0:1, 0:128],
                                          EB2R[e % 2][0:1, dsl], start=False, stop=True)

                    waits = [HREADY, (w2ld, w2tgt)]
                    if g2t >= 2:
                        waits.append(g2evm[g2t - 2])
                    s.op("tensor", mms2, wait=waits, inc=[(g2mm, 1)])
                    b2 = nacc % 2
                    evw = [(g2mm, g2t + 1)] + GATED
                    if nacc >= 2:
                        evw.append((acd[b2], s.val(acd[b2])))
                    g2evm[g2t] = s.op("scalar", lambda eng, ps=ps, b2=b2, tc=tc, e=e:
                                      eng.activation(AST[b2][:], ps[:, 0:512], AF.Copy,
                                                     scale=WEX[:, tc, e:e + 1]),
                                      wait=evw)
                    accw = [g2evm[g2t]]
                    if e > 0 and tb == 0 and tsub == 0 and dsp == 0:
                        accw += s.after(acd[0], acd[1])
                    kw = {} if e == 0 else {"accum_op": ALU.add}
                    s.dma("gpsimd", W["acc"][:, tc, dsl], AST[b2][:], acd[b2],
                          wait=accw, **kw)
                    nacc += 1
                    g2t += 1

    EXPDONE = s.after(acd[0], acd[1])

    fi = 0
    fdvem = {}
    for ec in range(8):
        for ts in range(4):
            tsl = slice(ts * 512, ts * 512 + 512)
            ps = C.pb[0 + (fi % 2)]
            for j in range(4):
                tc = ts * 4 + j
                aw = list(EXPDONE)
                if fi >= 1:
                    aw.append((trmm, fi))
                s.dma("sync", AT[j][:], W["acc"][:, tc, ec * 128:ec * 128 + 128],
                      fat_s[j], wait=aw if j == 0 else [])
            trw = [(fs, s.val(fs)) for fs in fat_s] + s.after(cld)
            if fi >= 2:
                trw.append(fdvem[fi - 2])

            def trans(eng, ps=ps):
                last = None
                for j in range(4):
                    last = eng.transpose(ps[:, j * 128:j * 128 + 128], AT[j][:],
                                         C.eye[:, :])
                return last

            s.op("tensor", trans, wait=trw, inc=[(trmm, 1)])
            b = fi % 2
            xw = list(EXPDONE) if fi < 2 else [fdvem[fi - 2]]
            s.dma("sync", XT[b][:], x_dram[:, ec, tsl], fxt_s[b], wait=xw)
            s.op("vector", lambda e, b=b, ec=ec, tsl=tsl: e.tensor_tensor(
                YT[b][:], XT[b][:], LNB[:, ec, tsl], ALU.add),
                 wait=[(fxt_s[b], s.val(fxt_s[b])), (trmm, fi + 1)] +
                      ([(yout, 16 * fi)] if fi >= 2 else []))
            fdvem[fi] = s.op("vector", lambda e, b=b, ps=ps: e.scalar_tensor_tensor(
                YT[b][:], ps[:, 0:512], 1.0, YT[b][:], ALU.bypass, ALU.add))
            s.dma("sync", y_dram[:, ec, tsl], YT[b][:], yout, wait=[fdvem[fi]])
            fi += 1
    s.op("sync", lambda e: None, wait=[(yout, 16 * 32)])



def build_l0ab(nc, C, mem, x_dram, y_dram, W, tag="l0b"):
    s = C.sched
    LNB = mem.alloc(tag + "_lnb", [128, 8, T], BF16)
    P = mem.alloc(tag + "_p", [128, T], BF16)
    PTS = mem.alloc(tag + "_pts", [128, 16, 128], BF16)
    WQKV = mem.alloc(tag + "_wqkv", [128, 8, 384], BF16)
    WPROJ = mem.alloc(tag + "_wproj", [128, 8, 128], BF16)
    BP = mem.alloc(tag + "_bp", [128, 8], F32)
    TRILM = mem.alloc(tag + "_trilm", [128, 128], F32)
    MX = mem.alloc(tag + "_mx", [128, 6], F32)
    XST = [mem.alloc(tag + f"_xst{i}", [128, 512], F32) for i in range(2)]
    YST = [mem.alloc(tag + f"_yst{i}", [128, 512], F32) for i in range(2)]
    mA = mem.mark()
    XS = mem.alloc(tag + "_xs", [128, 8, T], F32)
    endA = mem.mark()
    Q = mem.alloc(tag + "_q", [128, 8, T], BF16, at=mA)
    K_ = mem.alloc(tag + "_k", [128, 8, T], BF16, at=mA + 32768)
    mem.reset(endA)
    VTM = mem.alloc(tag + "_vtm", [128, 16, 1024], BF16, at=C.lnscr_off)
    AOUT = LNB

    xld = s.sem("g"  + "_xld")
    cldh = s.sem("g"  + "_cldh")
    cldg = s.sem("g"  + "_cldg")
    qkmm = s.sem("g"  + "_qkmm")
    smm = s.sem("g"  + "_smm")
    trm = s.sem("g"  + "_trm")
    avm = s.sem("g"  + "_avm")
    pjmm = s.sem("g"  + "_pjmm")
    xin2 = [s.sem("g"  + "_xin2a"), s.sem("g"  + "_xin2b")]
    yout = s.sem("g"  + "_yout")

    for i in range(4):
        s.dma("sync", XS[:, 2 * i:2 * i + 2, :], x_dram[:, 2 * i:2 * i + 2, :], xld)
    s.dma("sync", TRILM[:], W["trilm"], cldh)
    s.dma("sync", BP[:], W["bp"], cldh)
    s.dma("sync", C.eye[:], W["eye"], cldh)
    s.dma("gpsimd", WQKV[:], W["wqkv"], cldg)
    s.dma("gpsimd", WPROJ[:], W["wproj"], cldg)
    s.dma("gpsimd", C.eyeb[:], W["eye"], cldg)

    emit_init_consts(C)
    LNM = emit_ln_sbuf(C, XS, LNB, "g_ln", ready=s.after(xld) + C.cready)
    WRDY = s.after(cldg)

    qk_ev = {}
    nmm = 0
    banks = [C.pb5, C.pb6]
    for oc in range(16):
        g, lo = oc // 3 if oc < 8 else (oc - 8) // 3 + 0, 0
        flat = oc * 128 if oc < 8 else 1024 + (oc - 8) * 128
        g = flat // 384
        lo = flat % 384
        dst = Q if oc < 8 else K_
        ch = oc % 8
        for ts in range(4):
            tsl = slice(ts * 512, ts * 512 + 512)
            ps = banks[nmm % 2][:]
            w = [LNM] + WRDY
            if nmm >= 2:
                w.append(qk_ev[nmm - 2])
            s.op("tensor", lambda e, g=g, lo=lo, tsl=tsl, ps=ps: e.matmul(
                ps[:, 0:512], WQKV[:, g, lo:lo + 128], LNB[:, g, tsl],
                start=True, stop=True), wait=w, inc=[(qkmm, 1)])
            qk_ev[nmm] = s.op("scalar", lambda e, ps=ps, dst=dst, ch=ch, tsl=tsl:
                              e.copy(dst[:, ch, tsl], ps[:, 0:512]),
                              wait=[(qkmm, nmm + 1)])
            nmm += 1
    for tc in range(16):
        for (g, lv0, lv1, v0) in ((5, 128, 384, 0), (6, 0, 384, 256), (7, 0, 384, 640)):
            w_ = lv1 - lv0
            ps = banks[nmm % 2][:]
            wts = [LNM] + WRDY
            if nmm >= 2:
                wts.append(qk_ev[nmm - 2])
            s.op("tensor", lambda e, g=g, tc=tc, lv0=lv0, w_=w_, ps=ps: e.matmul(
                ps[:, 0:w_], LNB[:, g, tc * 128:tc * 128 + 128],
                WQKV[:, g, lv0:lv0 + w_], start=True, stop=True),
                 wait=wts, inc=[(qkmm, 1)])
            qk_ev[nmm] = s.op("scalar", lambda e, ps=ps, tc=tc, v0=v0, w_=w_:
                              e.copy(VTM[:, tc, v0:v0 + w_], ps[:, 0:w_]),
                              wait=[(qkmm, nmm + 1), LNM])
            nmm += 1
    QKVMM = s.after(qkmm)
    QKVM = qk_ev[nmm - 1]

    aexp_prev = None
    pts_prev = None
    aev_prev = None
    trn = 0
    avn = 0
    for h in range(16):
        qch, qb = h // 2, (h % 2) * 64
        for qi in range(16):
            kw_ = T
            kw1 = 1536
            ntail = kw_ - kw1
            nb = 0
            last_s = None
            for kb in range(0, kw_, 512):
                w_ = min(512, kw_ - kb)
                ps = C.pbS[:, kb:kb + w_] if kb < 1536 else C.pb4[:, 0:w_]
                wts = QKVMM + [QKVM]
                if aexp_prev is not None and nb == 0:
                    wts.append(aexp_prev)
                last_s = s.op("tensor", lambda e, qb=qb, qch=qch, qi=qi, kb=kb, w_=w_, ps=ps:
                              e.matmul(ps, Q[qb:qb + 64, qch, qi * 128:qi * 128 + 128],
                                       K_[qb:qb + 64, qch, kb:kb + w_],
                                       start=True, stop=True),
                              wait=wts, inc=[(smm, 1)])
                nb += 1
            smm_w = [(smm, s.val(smm))]
            s.op("vector", lambda e: e.tensor_reduce(
                MX[:, 0:1], C.pbS[:, 0:1536], AX.X, ALU.max), wait=smm_w)
            s.op("vector", lambda e, ntail=ntail: e.tensor_reduce(
                MX[:, 4:5], C.pb4[:, 0:ntail], AX.X, ALU.max))
            s.op("vector", lambda e: e.tensor_tensor(
                MX[:, 0:1], MX[:, 0:1], MX[:, 4:5], ALU.max))
            mneg = s.op("vector", lambda e: e.tensor_scalar(
                MX[:, 1:2], MX[:, 0:1], -0.125, None, ALU.mult))
            exw = [mneg]
            if trn > 0:
                exw.append((trm, trn))
            mex = s.op("scalar", lambda e, kw1=kw1: e.activation(
                P[:, 0:kw1], C.pbS[:, 0:kw1], AF.Exp, bias=MX[:, 1:2],
                scale=0.125, accum_out=MX[:, 2:3]), wait=exw)
            mex = s.op("scalar", lambda e, kw1=kw1, ntail=ntail: e.activation(
                P[:, kw1:kw1 + ntail], C.pb4[:, 0:ntail], AF.Exp,
                bias=MX[:, 1:2], scale=0.125, accum_out=MX[:, 5:6]))
            s.op("vector", lambda e: e.tensor_tensor(
                MX[:, 2:3], MX[:, 2:3], MX[:, 5:6], ALU.add), wait=[mex])
            aexp_prev = mex
            s.op("scalar", lambda e: e.activation(MX[:, 3:4], MX[:, 2:3], AF.Ln),
                 wait=s.mark("vector"))
            mri = s.op("scalar", lambda e: e.activation(
                MX[:, 3:4], MX[:, 3:4], AF.Exp, scale=-1.0))
            msc = s.op("vector", lambda e, kw_=kw_: e.tensor_scalar(
                P[:, 0:kw_], P[:, 0:kw_], MX[:, 3:4], None, ALU.mult), wait=[mri])
            nt = 16
            pts_marks = {}
            for b0 in range(0, nt, 8):
                bank = C.pbTa if (b0 // 8) % 2 == 0 else C.pbTb
                cnt = min(8, nt - b0)
                trw = [msc] + s.after(cldg)
                if pts_prev is not None:
                    trw.append(pts_prev)

                def trans(e, b0=b0, cnt=cnt, bank=bank):
                    last = None
                    for j in range(cnt):
                        last = e.transpose(bank[:, j * 128:j * 128 + 128],
                                           P[:, (b0 + j) * 128:(b0 + j) * 128 + 128],
                                           C.eyeb[:, :])
                    return last

                s.op("tensor", trans, wait=trw, inc=[(trm, 1)])
                trn += 1
                for j in range(cnt):
                    cpw = [(trm, trn)]
                    if aev_prev is not None:
                        cpw.append(aev_prev)
                    pts_marks[b0 + j] = s.op(
                        "vector", lambda e, j=j, b0=b0, bank=bank: e.tensor_copy(
                            PTS[:, b0 + j, :], bank[:, j * 128:j * 128 + 128]),
                        wait=cpw)
                pts_prev = pts_marks[b0 + cnt - 1]
            avw0 = [aev_prev] if aev_prev is not None else []

            def avmm(e, h=h, qi=qi, qb=qb):
                last = None
                for j in range(16):
                    last = e.matmul(C.pb6[qb:qb + 64, 0:128],
                                    VTM[:, j, h * 64:h * 64 + 64],
                                    PTS[:, j, :], start=(j == 0), stop=(j == 15))
                return last

            s.op("tensor", avmm, wait=avw0 + [pts_prev], inc=[(avm, 1)])
            avn += 1
            aev_prev = s.op("scalar", lambda e, qb=qb, qch=qch, qi=qi: e.copy(
                AOUT[qb:qb + 64, qch, qi * 128:qi * 128 + 128],
                C.pb6[qb:qb + 64, 0:128]),
                 wait=[(avm, avn)] + QKVMM)
    ATTND = aev_prev

    pi = 0
    pdvem = {}
    for oc in range(8):
        for ts in range(4):
            tsl = slice(ts * 512, ts * 512 + 512)
            ps = (C.pb5 if pi % 2 == 0 else C.pbS[:, 0:512])
            w = [ATTND] + WRDY
            if pi >= 2:
                w.append(pdvem[pi - 2])
            s.op("tensor", lambda e, oc=oc, tsl=tsl, ps=ps: e.matmul(
                ps[:, 0:512] if ps is C.pb5 else ps, WPROJ[:, oc, :],
                AOUT[:, oc, tsl], start=True, stop=True),
                 wait=w, inc=[(pjmm, 1)])
            b = pi % 2
            xw = [ATTND] if pi < 2 else [pdvem[pi - 2]]
            s.dma("sync", XST[b][:], x_dram[:, oc, tsl], xin2[b], wait=xw)
            s.op("vector", lambda e, b=b, oc=oc: e.tensor_scalar(
                XST[b][:], XST[b][:], 1.0, BP[:, oc:oc + 1], ALU.mult, ALU.add),
                 wait=[(xin2[b], s.val(xin2[b]))] + s.after(cldh))
            pdvem[pi] = s.op("vector", lambda e, b=b, ps=ps: e.scalar_tensor_tensor(
                YST[b][:], ps[:, 0:512] if ps is C.pb5 else ps, 1.0, XST[b][:],
                ALU.bypass, ALU.add),
                 wait=[(pjmm, pi + 1)] + ([(yout, 16 * pi)] if pi >= 2 else []))
            s.dma("sync", y_dram[:, oc, tsl], YST[b][:], yout, wait=[pdvem[pi]])
            pi += 1
    s.op("sync", lambda e: None, wait=[(yout, 16 * 32)])
    return NS(Q=Q, K=K_, VTM=VTM, AOUT=AOUT, LNB=LNB, P=P, yout=yout)



def build_l1aa(nc, C, mem, x_dram, y_dram, W, tag="l1a"):
    s = C.sched
    LNB = mem.alloc(tag + "_lnb", [128, 8, T], BF16)
    PS1 = mem.alloc(tag + "_ps1", [128, T], F32)
    P = mem.alloc(tag + "_p", [128, T], BF16)
    PTS = mem.alloc(tag + "_pts", [128, 16, 128], BF16)
    DSCR = mem.alloc(tag + "_dscr", [128, 128], F32)
    WQh = mem.alloc(tag + "_wq", [128, 8, 256], BF16)
    WKh = mem.alloc(tag + "_wk", [128, 8, 256], BF16)
    TRIL = mem.alloc(tag + "_tril", [128, 128], F32)
    TRIU = mem.alloc(tag + "_triu", [128, 128], F32)
    TRILB = mem.alloc(tag + "_trilb", [128, 128], BF16)
    IOTA = mem.alloc(tag + "_iota", [128, 1], F32)
    MX = mem.alloc(tag + "_mx", [128, 8], F32)
    XST = [mem.alloc(tag + f"_xst{i}", [128, 512], F32) for i in range(2)]
    YST = [mem.alloc(tag + f"_yst{i}", [128, 512], F32) for i in range(2)]
    mA = mem.mark()
    XS = mem.alloc(tag + "_xs", [128, 8, T], F32)
    endA = mem.mark()
    QA = mem.alloc(tag + "_qa", [128, 8, T], BF16, at=mA)
    KA = mem.alloc(tag + "_ka", [128, 8, T], BF16, at=mA + 32768)
    mem.reset(endA)
    VA = mem.alloc(tag + "_va", [128, 16, 1024], BF16, at=C.lnscr_off)
    WVh = mem.alloc(tag + "_wv", [128, 8, 1024], BF16)
    WOh = mem.alloc(tag + "_wo", [128, 8, 1024], BF16, at=mem.mark())
    AOUT = LNB

    xld = s.sem("g"  + "_xld")
    cldh = s.sem("g"  + "_cldh")
    cldg = s.sem("g"  + "_cldg")
    qkmm = s.sem("g"  + "_qkmm")
    smm = s.sem("g"  + "_smm")
    trm = s.sem("g"  + "_trm")
    avm = s.sem("g"  + "_avm")
    pjmm = s.sem("g"  + "_pjmm")
    xin2 = [s.sem("g"  + "_xin2a"), s.sem("g"  + "_xin2b")]
    yout = s.sem("g"  + "_yout")

    for i in range(4):
        s.dma("sync", XS[:, 2 * i:2 * i + 2, :], x_dram[:, 2 * i:2 * i + 2, :], xld)
    s.dma("sync", TRIL[:], W["tril01"], cldh)
    s.dma("sync", TRIU[:], W["triu01"], cldh)
    s.dma("sync", IOTA[:], W["iota1"], cldh)
    s.dma("sync", C.eye[:], W["eye"], cldh)
    s.dma("gpsimd", TRILB[:], W["tril01"], cldg)
    s.dma("gpsimd", WQh[:], W["wq"], cldg)
    s.dma("gpsimd", WKh[:], W["wk"], cldg)
    s.dma("gpsimd", C.eyeb[:], W["eye"], cldg)
    s.dma("gpsimd", WVh[:], W["wv"], cldg)

    emit_init_consts(C)
    LNM = emit_ln_sbuf(C, XS, LNB, "g_ln", ready=s.after(xld) + C.cready)
    WRDY = s.after(cldg)

    qk_ev = {}
    nmm = 0
    banks = [C.pb5, C.pb6]
    for which, (WX, DST) in enumerate(((WQh, QA), (WKh, KA))):
        for h in range(16):
            base = (h % 2) * 64
            for ts in range(4):
                tsl = slice(ts * 512, ts * 512 + 512)
                ps = banks[nmm % 2]
                w = [LNM] + WRDY
                if nmm >= 2:
                    w.append(qk_ev[nmm - 2])

                def mms(e, WX=WX, h=h, tsl=tsl, ps=ps, base=base):
                    last = None
                    for dc in range(8):
                        last = e.matmul(ps[base:base + 16, 0:512],
                                        WX[:, dc, h * 16:h * 16 + 16],
                                        LNB[:, dc, tsl],
                                        start=(dc == 0), stop=(dc == 7))
                    return last

                s.op("tensor", mms, wait=w, inc=[(qkmm, 1)])
                qk_ev[nmm] = s.op("scalar", lambda e, ps=ps, DST=DST, h=h, tsl=tsl,
                                  base=base: e.copy(
                                      DST[base:base + 16, h // 2, tsl],
                                      ps[base:base + 16, 0:512]),
                                  wait=[(qkmm, nmm + 1)])
                nmm += 1
    for tc in range(16):
        for es in range(2):
            esl = slice(es * 512, es * 512 + 512)
            ps = banks[nmm % 2]
            w = [LNM] + WRDY
            if nmm >= 2:
                w.append(qk_ev[nmm - 2])

            def mms(e, tc=tc, esl=esl, ps=ps):
                last = None
                for dc in range(8):
                    last = e.matmul(ps[:, 0:512], LNB[:, dc, tc * 128:tc * 128 + 128],
                                    WVh[:, dc, esl], start=(dc == 0), stop=(dc == 7))
                return last

            s.op("tensor", mms, wait=w, inc=[(qkmm, 1)])
            qk_ev[nmm] = s.op("scalar", lambda e, ps=ps, tc=tc, esl=esl: e.copy(
                VA[:, tc, esl], ps[:, 0:512]), wait=[(qkmm, nmm + 1), LNM])
            nmm += 1
    QKVMM = s.after(qkmm)
    QKVM = qk_ev[nmm - 1]
    s.dma("gpsimd", WOh[:], W["wo"], cldg)
    WOR = s.after(cldg)

    aexp_prev = None
    pts_prev = None
    aev_prev = None
    trn = 0
    avn = 0
    for h in range(16):
        qch, qb = h // 2, (h % 2) * 64
        for qi in range(16):
            kw_ = (qi + 1) * 128
            kw1 = min(kw_, 1536)
            ntail = kw_ - kw1
            nb = 0
            for kb in range(0, kw_, 512):
                w_ = min(512, kw_ - kb)
                ps = C.pbS[:, kb:kb + w_] if kb < 1536 else C.pb4[:, 0:w_]
                wts = QKVMM + [QKVM]
                if aexp_prev is not None and nb == 0:
                    wts.append(aexp_prev)
                s.op("tensor", lambda e, qb=qb, qch=qch, qi=qi, kb=kb, w_=w_, ps=ps:
                     e.matmul(ps, QA[qb:qb + 16, qch, qi * 128:qi * 128 + 128],
                              KA[qb:qb + 16, qch, kb:kb + w_],
                              start=True, stop=True),
                     wait=wts, inc=[(smm, 1)])
                nb += 1
            smm_w = [(smm, s.val(smm))]
            mex = s.op("scalar", lambda e, kw1=kw1: e.activation(
                PS1[:, 0:kw1], C.pbS[:, 0:kw1], AF.Square, bias=1.0,
                scale=0.25, accum_out=MX[:, 0:1]),
                 wait=smm_w + ([aexp_prev] if aexp_prev else []) + C.cready)
            if ntail:
                mex = s.op("scalar", lambda e, kw1=kw1, ntail=ntail: e.activation(
                    PS1[:, kw1:kw1 + ntail], C.pb4[:, 0:ntail], AF.Square,
                    bias=1.0, scale=0.25, accum_out=MX[:, 1:2]))
            else:
                mex = s.op("scalar", lambda e: e.activation(
                    MX[:, 1:2], C.cpool[:, 4:5], AF.Copy))
            aexp_prev = mex
            dstart = qi * 128
            mcorr = s.op("vector", lambda e, dstart=dstart: e.scalar_tensor_tensor(
                DSCR[:], PS1[:, dstart:dstart + 128], 1.0, TRIU[:],
                ALU.bypass, ALU.mult, accum_out=MX[:, 2:3]),
                 wait=[mex] + s.after(cldh))
            s.op("vector", lambda e: e.tensor_tensor(
                MX[:, 3:4], MX[:, 0:1], MX[:, 1:2], ALU.add))
            s.op("vector", lambda e: e.tensor_tensor(
                MX[:, 3:4], MX[:, 3:4], MX[:, 2:3], ALU.subtract))
            s.op("vector", lambda e, qi=qi: e.tensor_scalar(
                MX[:, 4:5], IOTA[:], 1.0, float(qi * 128), ALU.mult, ALU.add))
            s.op("vector", lambda e: e.tensor_tensor(
                MX[:, 3:4], MX[:, 3:4], MX[:, 4:5], ALU.add))
            mden = s.op("vector", lambda e: e.tensor_scalar(
                MX[:, 3:4], MX[:, 3:4], 0.5, 1e-12, ALU.mult, ALU.add))
            s.op("scalar", lambda e: e.activation(MX[:, 5:6], MX[:, 3:4], AF.Ln),
                 wait=[mden])
            s.op("scalar", lambda e: e.activation(MX[:, 5:6], MX[:, 5:6], AF.Exp,
                                                  scale=-1.0))
            mz2 = s.op("scalar", lambda e: e.activation(MX[:, 6:7], MX[:, 5:6],
                                                        AF.Copy, scale=0.5))
            trn_w = [(trm, trn)] if trn else []
            msc = s.op("vector", lambda e, kw_=kw_: e.tensor_scalar(
                P[:, 0:kw_], PS1[:, 0:kw_], MX[:, 6:7], MX[:, 6:7],
                ALU.mult, ALU.add), wait=[mz2] + trn_w)
            msc = s.op("vector", lambda e, dstart=dstart: e.tensor_tensor(
                P[:, dstart:dstart + 128], P[:, dstart:dstart + 128], TRILB[:],
                ALU.mult))
            nt = qi + 1
            pts_marks = {}
            for b0 in range(0, nt, 8):
                bank = C.pbTa if (b0 // 8) % 2 == 0 else C.pbTb
                cnt = min(8, nt - b0)
                trw = [msc] + WRDY
                if pts_prev is not None:
                    trw.append(pts_prev)

                def trans(e, b0=b0, cnt=cnt, bank=bank):
                    last = None
                    for j in range(cnt):
                        last = e.transpose(bank[:, j * 128:j * 128 + 128],
                                           P[:, (b0 + j) * 128:(b0 + j) * 128 + 128],
                                           C.eyeb[:, :])
                    return last

                s.op("tensor", trans, wait=trw, inc=[(trm, 1)])
                trn += 1
                for j in range(cnt):
                    cpw = [(trm, trn)]
                    if aev_prev is not None:
                        cpw.append(aev_prev)
                    pts_marks[b0 + j] = s.op(
                        "vector", lambda e, j=j, b0=b0, bank=bank: e.tensor_copy(
                            PTS[:, b0 + j, :], bank[:, j * 128:j * 128 + 128]),
                        wait=cpw)
                pts_prev = pts_marks[b0 + cnt - 1]
            avw0 = [aev_prev] if aev_prev is not None else []

            def avmm(e, h=h, qi=qi, qb=qb):
                last = None
                for j in range(qi + 1):
                    last = e.matmul(C.pb6[qb:qb + 64, 0:128],
                                    VA[:, j, h * 64:h * 64 + 64],
                                    PTS[:, j, :], start=(j == 0), stop=(j == qi))
                return last

            s.op("tensor", avmm, wait=avw0 + [pts_prev], inc=[(avm, 1)])
            avn += 1
            aev_prev = s.op("scalar", lambda e, qb=qb, h=h, qi=qi: e.copy(
                AOUT[qb:qb + 64, h // 2, qi * 128:qi * 128 + 128],
                C.pb6[qb:qb + 64, 0:128]),
                 wait=[(avm, avn)] + QKVMM)
    ATTND = aev_prev

    pi = 0
    pdvem = {}
    for ec in range(8):
        esl = slice(ec * 128, ec * 128 + 128)
        for ts in range(4):
            tsl = slice(ts * 512, ts * 512 + 512)
            ps = C.pb5 if pi % 2 == 0 else C.pbS[:, 0:512]
            psap = ps[:, 0:512] if pi % 2 == 0 else ps
            w = [ATTND] + WOR
            if pi >= 2:
                w.append(pdvem[pi - 2])

            def mms(e, esl=esl, tsl=tsl, psap=psap):
                last = None
                for dc in range(8):
                    last = e.matmul(psap, WOh[:, dc, esl], AOUT[:, dc, tsl],
                                    start=(dc == 0), stop=(dc == 7))
                return last

            s.op("tensor", mms, wait=w, inc=[(pjmm, 1)])
            b = pi % 2
            xw = [ATTND] if pi < 2 else [pdvem[pi - 2]]
            s.dma("sync", XST[b][:], x_dram[:, ec, tsl], xin2[b], wait=xw)
            pdvem[pi] = s.op("vector", lambda e, b=b, psap=psap: e.scalar_tensor_tensor(
                YST[b][:], psap, 1.0, XST[b][:], ALU.bypass, ALU.add),
                 wait=[(pjmm, pi + 1), (xin2[b], s.val(xin2[b]))] +
                      ([(yout, 16 * pi)] if pi >= 2 else []))
            s.dma("sync", y_dram[:, ec, tsl], YST[b][:], yout, wait=[pdvem[pi]])
            pi += 1
    s.op("sync", lambda e: None, wait=[(yout, 16 * 32)])
    return NS(QA=QA, KA=KA, VA=VA, AOUT=AOUT, P=P, PS1=PS1, yout=yout)



def build_l1ffn(nc, C, mem, x_dram, y_dram, W, tag="l1f"):
    s = C.sched
    TB = 1024
    NE = 4

    RMS1B = mem.alloc(tag + "_rms1", [128, 8, T], BF16)
    WEX4 = mem.alloc(tag + "_wex4", [128, 16, 4], F32)
    GW4 = mem.alloc(tag + "_gw4", [128, 8, 4], BF16)
    GB4 = mem.alloc(tag + "_gb4", [1, 4], BF16)
    RT4 = mem.alloc(tag + "_rt4", [128, 4], F32)
    EB1C = mem.alloc(tag + "_eb1c", [128, 4, 32], F32)
    EB2R = [mem.alloc(tag + f"_eb2r{i}", [1, 1024], BF16) for i in range(2)]
    EW1S = [mem.alloc(tag + f"_ew1s{i}", [128, 8, 128], BF16) for i in range(2)]
    AST = [mem.alloc(tag + f"_ast{i}", [128, 512], F32) for i in range(2)]
    NCOL = mem.alloc(tag + "_ncol", [128, 24], F32)
    m_latb = (mem.mark() + 31) // 32 * 32
    LATB = mem.alloc(tag + "_latb", [128, 4, T], BF16)
    WQT = mem.alloc(tag + "_wqt", [128, 8, 1024], BF16)
    arena0 = WQT.manual_sbuf_range[0]
    WDKV = mem.alloc(tag + "_wdkv", [128, 8, 512], BF16)
    COSB = mem.alloc(tag + "_cosb", [128, T], BF16)
    SINB = mem.alloc(tag + "_sinb", [128, T], BF16)
    P = mem.alloc(tag + "_p", [128, T], BF16)
    PTS = mem.alloc(tag + "_pts", [128, 16, 128], BF16)
    p_off = P.manual_sbuf_range[0]
    TRILM = mem.alloc(tag + "_trilm", [128, 128], F32)
    SWAPM = mem.alloc(tag + "_swapm", [128, 128], BF16)
    MX = mem.alloc(tag + "_mx", [128, 16], F32)
    TMPR = mem.alloc(tag + "_tmpr", [128, 512], F32)
    mem.reset(max(mem.mark(), m_latb + 65536))
    mA = mem.mark()
    XS = mem.alloc(tag + "_xs", [128, 8, T], F32)
    mem.reset(mA)
    P2 = mem.alloc(tag + "_p2", [128, T], BF16, at=COSB.manual_sbuf_range[0])
    QF = mem.alloc(tag + "_qf", [128, 8, T], BF16, at=mA)
    LAT = mem.alloc(tag + "_lat", [128, 4, T], F32, at=mA + 32768)
    KF = mem.alloc(tag + "_kf", [128, 8, T], BF16, at=mA + 32768)
    H = mem.alloc(tag + "_h", [128, 32, TB], BF16, at=mA)
    mem.reset(mA + 65536)

    AOUT = RMS1B
    RMS2B = RMS1B
    VTM = mem.alloc(tag + "_vtm", [128, 16, 1024], BF16, at=C.lnscr_off)
    EW2 = mem.alloc(tag + "_ew2", [128, 32, 1024], BF16, at=m_latb)
    XPS = [mem.alloc(tag + f"_xp{i}", [128, 512], F32, at=p_off + 2048 * i)
           for i in range(2)]
    YPS = [mem.alloc(tag + f"_yp{i}", [128, 512], F32, at=p_off + 4096 + 2048 * i)
           for i in range(2)]
    FXT = [mem.alloc(tag + f"_fxt{i}", [128, 512], F32, at=C.lnscr_off + 2048 * i)
           for i in range(2)]
    FUT = [mem.alloc(tag + f"_fut{i}", [128, 512], F32, at=C.lnscr_off + 4096 + 2048 * i)
           for i in range(2)]
    FYT = [mem.alloc(tag + f"_fyt{i}", [128, 512], F32, at=C.lnscr_off + 8192 + 2048 * i)
           for i in range(2)]
    FAT = [mem.alloc(tag + f"_fat{i}", [128, 128], F32, at=C.lnscr_off + 12288 + 512 * i)
           for i in range(4)]

    xld = s.sem("g"  + "_xld")
    cldh = s.sem("g"  + "_cldh")
    cldg = s.sem("g"  + "_cldg")
    wld = s.sem("g"  + "_wld")
    xinio = s.sem("g"  + "_xinio")
    xps_s = [s.sem("g"  + "_xpsa"), s.sem("g"  + "_xpsb")]
    yps_s = [s.sem("g"  + "_ypsa"), s.sem("g"  + "_ypsb")]
    w1l2 = [s.sem("g"  + "_w1l0"), s.sem("g"  + "_w1l1")]
    fat_s = [s.sem(tag + f"_fat{j}") for j in range(4)]
    fxt_s = [s.sem("g"  + "_fxa"), s.sem("g"  + "_fxb")]
    fut_s = [s.sem("g"  + "_fua"), s.sem("g"  + "_fub")]
    qkmm = s.sem("g"  + "_qkmm")
    rpmm = s.sem("g"  + "_rpmm")
    smm = s.sem("g"  + "_smm")
    trm = s.sem("g"  + "_trm")
    avm = s.sem("g"  + "_avm")
    pjmm = s.sem("g"  + "_pjmm")
    gmm4 = s.sem("g"  + "_gmm4")
    g1mm = s.sem("g"  + "_g1mm")
    g2mm = s.sem("g"  + "_g2mm")
    w1ld = s.sem("g"  + "_w1ld")
    w2ld = s.sem("g"  + "_w2ld")
    acd = [s.sem("g"  + "_acd0"), s.sem("g"  + "_acd1")]
    acrd = s.sem("g"  + "_acrd")
    trf = s.sem("g"  + "_trf")
    yout = s.sem("g"  + "_yout")
    rsq = s.sem("g"  + "_rsq")
    rbc = s.sem("g"  + "_rbc")

    for i in range(4):
        s.dma("sync", XS[:, 2 * i:2 * i + 2, :], x_dram[:, 2 * i:2 * i + 2, :], xld)
    s.dma("sync", TRILM[:], W["trilm"], cldh)
    s.dma("sync", C.eye[:], W["eye"], cldh)
    s.dma("sync", NCOL[:, 0:8], W["n1c"], cldh)
    s.dma("sync", NCOL[:, 8:16], W["n2c"], cldh)
    s.dma("sync", NCOL[:, 16:20], W["lngc"], cldh)
    s.dma("sync", NCOL[:, 20:24], W["lnbc"], cldh)
    s.dma("sync", EB1C[:], W["eb1c"], cldh)
    s.dma("gpsimd", C.eyeb[:], W["eye"], cldg)
    s.dma("gpsimd", COSB[:], W["cosb"], cldg)
    s.dma("gpsimd", SINB[:], W["sinb"], cldg)
    s.dma("gpsimd", SWAPM[:], W["swapm"], cldg)
    s.dma("gpsimd", GW4[:], W["gw4"], cldg)
    s.dma("gpsimd", GB4[:], W["gb4"], cldg)
    s.dma("gpsimd", WQT[:], W["wqt"], wld)
    s.dma("gpsimd", WDKV[:], W["wdkvt"], wld)
    ARW1 = s.after(wld)

    emit_init_consts(C)
    LNM = emit_ln_sbuf(C, XS, XS, "g_ln", ready=s.after(xld) + C.cready)
    for i in range(4):
        s.dma("sync", W["xin"][:, 2 * i:2 * i + 2, :], XS[:, 2 * i:2 * i + 2, :],
              xinio, wait=[LNM])
    XINW = s.after(xinio)

    def emit_rms(xs_sb, dst, gcol, name, prev_lnscr_user):
        sq_m = {}
        lastap = None
        prev_bc = None
        prev_rd = None
        for ts in range(4):
            tsl = slice(ts * 512, ts * 512 + 512)
            for c in range(8):
                w = [prev_lnscr_user] if ts == 0 and c == 0 else []
                if ts > 0:
                    w += [(rsq, s.val(rsq))]
                    w += [lastap] if lastap else []
                sq_m[c] = s.op("scalar", lambda e, c=c, tsl=tsl, xs_sb=xs_sb:
                               e.activation(C.sq8[:, c, :], xs_sb[:, c, tsl],
                                            AF.Square), wait=w)
            for c in range(8):
                s.op("tensor", lambda e, c=c: e.matmul(
                    C.pb[1][0:1, 0:512], C.ones_col[:, 0:1], C.sq8[:, c, :],
                    start=(c == 0), stop=(c == 7)),
                     wait=[sq_m[c]] + ([prev_rd] if c == 0 and prev_rd else []),
                     inc=([(rsq, 1)] if c == 7 else []))
            m_e2 = s.op("vector", lambda e: e.tensor_scalar(
                C.r_e2[:], C.pb[1][0:1, 0:512], 1.0 / 1024.0, None, ALU.mult),
                 wait=[(rsq, s.val(rsq))])
            prev_rd = m_e2
            s.op("scalar", lambda e: e.activation(
                C.r_tm[:], C.r_e2[:], AF.Ln, bias=C.cpool[0:1, 2:3]),
                 wait=[m_e2] + C.cready)
            m_rstd = s.op("scalar", lambda e: e.activation(
                C.r_e2[:], C.r_tm[:], AF.Exp, scale=-0.5))
            s.op("tensor", lambda e: e.matmul(
                C.pb[2][:, 0:512], C.ones_row[0:1, 0:128], C.r_e2[:],
                start=True, stop=True),
                 wait=[m_rstd] + ([prev_bc] if prev_bc else []), inc=[(rbc, 1)])
            cb = s.op("scalar", lambda e: e.copy(C.bc0[:], C.pb[2][:, 0:512]),
                      wait=[(rbc, s.val(rbc))] + ([lastap] if lastap else []))
            prev_bc = cb
            for c in range(8):
                lastap = s.op("vector", lambda e, c=c, tsl=tsl, xs_sb=xs_sb,
                              dst=dst, gcol=gcol: e.scalar_tensor_tensor(
                                  dst[:, c, tsl], xs_sb[:, c, tsl], gcol(c),
                                  C.bc0[:], ALU.mult, ALU.mult),
                              wait=[cb])
        return lastap

    RMS1M = emit_rms(XS, RMS1B, lambda c: NCOL[:, c:c + 1], "g_r1", LNM)

    qk_ev = {}
    nmm = 0
    banks = [C.pb5, C.pb6]

    def fm_gemm(DSTBUF, WT, nk, ec, tsl, extra_wait, rhsbuf, evac=None):
        nonlocal nmm
        ps = banks[nmm % 2]
        w = list(extra_wait)
        if nmm >= 2:
            w.append(qk_ev[nmm - 2])

        def mms(e, WT=WT, ec=ec, tsl=tsl, ps=ps, nk=nk, rhsbuf=rhsbuf):
            last = None
            for dc in range(nk):
                last = e.matmul(ps[:, 0:512], WT[:, dc, ec * 128:ec * 128 + 128],
                                rhsbuf[:, dc, tsl], start=(dc == 0), stop=(dc == nk - 1))
            return last

        s.op("tensor", mms, wait=w, inc=[(qkmm, 1)])
        if evac is None:
            qk_ev[nmm] = s.op("scalar", lambda e, ps=ps, DSTBUF=DSTBUF, ec=ec,
                              tsl=tsl: e.copy(DSTBUF[:, ec, tsl], ps[:, 0:512]),
                              wait=[(qkmm, nmm + 1)])
        else:
            qk_ev[nmm] = evac(ps, nmm)
        nmm += 1
        return qk_ev[nmm - 1]

    QRD = [RMS1M] + XINW + ARW1
    for ec in range(8):
        for ts in range(4):
            fm_gemm(QF, WQT, 8, ec, slice(ts * 512, ts * 512 + 512), QRD, RMS1B)
    for ec in range(4):
        for ts in range(4):
            tsl = slice(ts * 512, ts * 512 + 512)
            fm_gemm(LAT, WDKV, 8, ec, tsl, QRD, RMS1B)
    QDKV = qk_ev[nmm - 1]
    LATM = emit_ln_sbuf(C, LAT, LATB, "g_lln", nchunk=4,
                        ready=[QDKV] + C.cready + s.after(cldh),
                        affine=(lambda c: NCOL[:, 16 + c:17 + c],
                                lambda c: NCOL[:, 20 + c:21 + c]))
    WUKT = mem.alloc(tag + "_wukt", [128, 4, 1024], BF16, at=arena0)
    WUVT = mem.alloc(tag + "_wuvt", [128, 4, 1024], BF16, at=arena0 + 8192)
    s.dma("gpsimd", WUKT[:], W["wukt"], wld, wait=[(qkmm, s.val(qkmm))])
    s.dma("gpsimd", WUVT[:], W["wuvt"], wld)
    ARW2 = s.after(wld)
    for ec in range(8):
        for ts in range(4):
            fm_gemm(KF, WUKT, 4, ec, slice(ts * 512, ts * 512 + 512),
                    [LATM] + ARW2, LATB)
    for tc in range(16):
        for es in range(2):
            esl = slice(es * 512, es * 512 + 512)
            ps = banks[nmm % 2]
            w = [LATM] + ARW2
            if nmm >= 2:
                w.append(qk_ev[nmm - 2])

            def mmsv(e, tc=tc, esl=esl, ps=ps):
                last = None
                for lc in range(4):
                    last = e.matmul(ps[:, 0:512], LATB[:, lc, tc * 128:tc * 128 + 128],
                                    WUVT[:, lc, esl], start=(lc == 0), stop=(lc == 3))
                return last

            s.op("tensor", mmsv, wait=w, inc=[(qkmm, 1)])
            qk_ev[nmm] = s.op("scalar", lambda e, ps=ps, tc=tc, esl=esl: e.copy(
                VTM[:, tc, esl], ps[:, 0:512]), wait=[(qkmm, nmm + 1), LATM])
            nmm += 1
    QKVMM = s.after(qkmm)
    QKVM = qk_ev[nmm - 1]
    WOT = mem.alloc(tag + "_wot", [128, 8, 1024], BF16, at=arena0)
    s.dma("gpsimd", WOT[:], W["wot"], wld, wait=[(qkmm, s.val(qkmm))])
    WOR = s.after(wld)

    rp_prev = {}
    nrp = 0
    for BUF in (QF, KF):
        for c in range(8):
            for ts in range(4):
                tsl = slice(ts * 512, ts * 512 + 512)
                ps = banks[nrp % 2]
                w = [QKVM] + s.after(cldg)
                if nrp >= 2:
                    w.append(rp_prev[nrp - 2])
                s.op("tensor", lambda e, BUF=BUF, c=c, tsl=tsl, ps=ps: e.matmul(
                    ps[:, 0:512], SWAPM[:], BUF[:, c, tsl], start=True, stop=True),
                     wait=w, inc=[(rpmm, 1)])
                s.op("vector", lambda e, ps=ps, tsl=tsl: e.tensor_tensor(
                    TMPR[:], ps[:, 0:512], SINB[:, tsl], ALU.mult),
                     wait=[(rpmm, nrp + 1)])
                s.op("vector", lambda e, BUF=BUF, c=c, tsl=tsl: e.tensor_tensor(
                    BUF[:, c, tsl], BUF[:, c, tsl], COSB[:, tsl], ALU.mult))
                rp_prev[nrp] = s.op("vector", lambda e, BUF=BUF, c=c, tsl=tsl:
                                    e.tensor_tensor(BUF[:, c, tsl], BUF[:, c, tsl],
                                                    TMPR[:], ALU.add))
                nrp += 1
    ROPED = rp_prev[nrp - 1]

    UNITS = [[(12, 0)], [(13, 0)], [(14, 0)], [(15, 0)],
             [(8, 0), (0, 1536)], [(9, 0), (1, 1536)],
             [(10, 0), (2, 1536)], [(11, 0), (3, 1536)],
             [(7, 0), (4, 1024)], [(6, 0), (5, 1024)]]
    prev_exps = []
    aev_slot = [None, None]
    aev_last = None
    pts_prev = None
    trn = 0
    avn = 0
    ucount = 0
    pend = None

    def stage3(h, qb, qch, unit, mscs, pbuf):
        nonlocal pts_prev, trn, avn, aev_last
        for si, (qi, off) in enumerate(unit):
            nt = qi + 1
            jbase = 0 if si == 0 else unit[0][0] + 1
            for b0 in range(0, nt, 8):
                bank = C.pbTa if trn % 2 == 0 else C.pbTb
                cnt = min(8, nt - b0)
                trw = [mscs[si]] + s.after(cldg)
                if pts_prev is not None:
                    trw.append(pts_prev)

                def trans(e, b0=b0, cnt=cnt, bank=bank, off=off, pbuf=pbuf):
                    last = None
                    for j in range(cnt):
                        last = e.transpose(
                            bank[:, j * 128:j * 128 + 128],
                            pbuf[:, off + (b0 + j) * 128:off + (b0 + j) * 128 + 128],
                            C.eyeb[:, :])
                    return last

                s.op("tensor", trans, wait=trw, inc=[(trm, 1)])
                trn += 1
                for j in range(cnt):
                    cpw = [(trm, trn)]
                    if aev_slot[si] is not None:
                        cpw.append(aev_slot[si])
                    pts_prev = s.op(
                        "vector", lambda e, j=j, b0=b0, bank=bank, jbase=jbase:
                        e.tensor_copy(PTS[:, jbase + b0 + j, :],
                                      bank[:, j * 128:j * 128 + 128]),
                        wait=cpw)
            avw0 = [aev_slot[si]] if aev_slot[si] is not None else []

            def avmm(e, h=h, qi=qi, qb=qb, jbase=jbase, si=si):
                last = None
                for j in range(qi + 1):
                    last = e.matmul(C.pb6[qb:qb + 64, si * 128:si * 128 + 128],
                                    VTM[:, j, h * 64:h * 64 + 64],
                                    PTS[:, jbase + j, :],
                                    start=(j == 0), stop=(j == qi))
                return last

            s.op("tensor", avmm, wait=avw0 + [pts_prev] + QKVMM, inc=[(avm, 1)])
            avn += 1
            aev_slot[si] = s.op("scalar", lambda e, qb=qb, h=h, qi=qi, si=si:
                                e.copy(AOUT[qb:qb + 64, h // 2,
                                            qi * 128:qi * 128 + 128],
                                       C.pb6[qb:qb + 64, si * 128:si * 128 + 128]),
                                wait=[(avm, avn)] + QKVMM)
            aev_last = aev_slot[si]

    for h in range(16):
        qch, qb = h // 2, (h % 2) * 64
        for unit in UNITS:
            pbuf = P if ucount % 2 == 0 else P2
            first = True
            for qi, off in unit:
                kw_ = (qi + 1) * 128
                for kb in range(0, kw_, 512):
                    w_ = min(512, kw_ - kb)
                    wts = [ROPED]
                    if first:
                        wts += prev_exps
                        first = False
                    s.op("tensor", lambda e, qb=qb, qch=qch, qi=qi, kb=kb, w_=w_,
                         off=off: e.matmul(
                             C.pbS[:, off + kb:off + kb + w_],
                             QF[qb:qb + 64, qch, qi * 128:qi * 128 + 128],
                             KF[qb:qb + 64, qch, kb:kb + w_],
                             start=True, stop=True),
                         wait=wts, inc=[(smm, 1)])
            smv = s.val(smm)
            exps = []
            mscs = []
            for si, (qi, off) in enumerate(unit):
                kw_ = (qi + 1) * 128
                mb = (ucount % 2) * 8 + 4 * si
                dstart = off + qi * 128
                s.op("vector", lambda e, dstart=dstart: e.tensor_tensor(
                    C.pbS[:, dstart:dstart + 128], C.pbS[:, dstart:dstart + 128],
                    TRILM[:], ALU.add),
                     wait=[(smm, smv)] + s.after(cldh))
                s.op("vector", lambda e, off=off, kw_=kw_, mb=mb: e.tensor_reduce(
                    MX[:, mb:mb + 1], C.pbS[:, off:off + kw_], AX.X, ALU.max))
                mneg = s.op("vector", lambda e, mb=mb: e.tensor_scalar(
                    MX[:, mb + 1:mb + 2], MX[:, mb:mb + 1], -0.125, None, ALU.mult))
                exw = [mneg]
                if trn > 0:
                    exw.append((trm, trn))
                mex = s.op("scalar", lambda e, off=off, kw_=kw_, mb=mb, pbuf=pbuf:
                           e.activation(
                               pbuf[:, off:off + kw_], C.pbS[:, off:off + kw_],
                               AF.Exp, bias=MX[:, mb + 1:mb + 2], scale=0.125,
                               accum_out=MX[:, mb + 2:mb + 3]), wait=exw)
                exps.append(mex)
                s.op("scalar", lambda e, mb=mb: e.activation(
                    MX[:, mb + 3:mb + 4], MX[:, mb + 2:mb + 3], AF.Ln))
                mri = s.op("scalar", lambda e, mb=mb: e.activation(
                    MX[:, mb + 3:mb + 4], MX[:, mb + 3:mb + 4], AF.Exp, scale=-1.0))
                mscs.append(s.op("vector", lambda e, off=off, kw_=kw_, mb=mb,
                                 pbuf=pbuf: e.tensor_scalar(
                                     pbuf[:, off:off + kw_], pbuf[:, off:off + kw_],
                                     MX[:, mb + 3:mb + 4], None, ALU.mult),
                                 wait=[mri]))
            prev_exps = exps
            if pend is not None:
                stage3(*pend)
            pend = (h, qb, qch, unit, mscs, pbuf)
            ucount += 1
    stage3(*pend)
    ATTND = aev_last

    pi = 0
    pdvem = {}
    for ec in range(8):
        for ts in range(4):
            tsl = slice(ts * 512, ts * 512 + 512)
            ps = C.pb5 if pi % 2 == 0 else C.pbS[:, 0:512]
            psap = ps[:, 0:512] if pi % 2 == 0 else ps
            w = [ATTND] + WOR
            if pi >= 2:
                w.append(pdvem[pi - 2])

            def mmsp(e, ec=ec, tsl=tsl, psap=psap):
                last = None
                for dc in range(8):
                    last = e.matmul(psap, WOT[:, dc, ec * 128:ec * 128 + 128],
                                    AOUT[:, dc, tsl], start=(dc == 0), stop=(dc == 7))
                return last

            s.op("tensor", mmsp, wait=w, inc=[(pjmm, 1)])
            b = pi % 2
            xw = [ATTND] if pi < 2 else [pdvem[pi - 2]]
            s.dma("sync", XPS[b][:], W["xin"][:, ec, tsl], xps_s[b], wait=xw)
            pdvem[pi] = s.op("vector", lambda e, b=b, psap=psap: e.scalar_tensor_tensor(
                YPS[b][:], psap, 1.0, XPS[b][:], ALU.bypass, ALU.add),
                 wait=[(pjmm, pi + 1), (xps_s[b], s.val(xps_s[b]))])
            s.dma("sync", W["xin"][:, ec, tsl], YPS[b][:], xinio, wait=[pdvem[pi]])
            pi += 1
    X1W = s.after(xinio)
    X1M = pdvem[31]

    rs_m = {}
    lastap2 = None
    prev_bc2 = None
    prev_rd2 = None
    for ts in range(4):
        tsl = slice(ts * 512, ts * 512 + 512)
        for c in range(8):
            i = ts * 8 + c
            b = i % 2
            w = list(X1W) + [X1M] + ([ATTND] if i == 0 else [])
            if i >= 2:
                w.append(rs_m[i - 2])
            s.dma("sync", XPS[b][:], W["xin"][:, c, tsl], xps_s[b], wait=w)
            xt = s.val(xps_s[b])
            rs_m[i] = s.op("scalar", lambda e, b=b, c=c: e.activation(
                C.sq8[:, c, :], XPS[b][:], AF.Square),
                 wait=[(xps_s[b], xt), (pjmm, 32)] +
                      ([lastap2] if ts > 0 and c == 0 and lastap2 else []) +
                      ([(rsq, s.val(rsq))] if ts > 0 and c == 0 else []))
        for c in range(8):
            s.op("tensor", lambda e, c=c: e.matmul(
                C.pb[1][0:1, 0:512], C.ones_col[:, 0:1], C.sq8[:, c, :],
                start=(c == 0), stop=(c == 7)),
                 wait=[rs_m[ts * 8 + c]] + ([prev_rd2] if c == 0 and prev_rd2 else []),
                 inc=([(rsq, 1)] if c == 7 else []))
        m_e2 = s.op("vector", lambda e: e.tensor_scalar(
            C.r_e2[:], C.pb[1][0:1, 0:512], 1.0 / 1024.0, None, ALU.mult),
             wait=[(rsq, s.val(rsq))])
        prev_rd2 = m_e2
        s.op("scalar", lambda e: e.activation(
            C.r_tm[:], C.r_e2[:], AF.Ln, bias=C.cpool[0:1, 2:3]), wait=[m_e2])
        m_rstd = s.op("scalar", lambda e: e.activation(
            C.r_e2[:], C.r_tm[:], AF.Exp, scale=-0.5))
        s.op("tensor", lambda e: e.matmul(
            C.pb[2][:, 0:512], C.ones_row[0:1, 0:128], C.r_e2[:],
            start=True, stop=True),
             wait=[m_rstd] + ([prev_bc2] if prev_bc2 else []), inc=[(rbc, 1)])
        cb = s.op("scalar", lambda e: e.copy(C.bc0[:], C.pb[2][:, 0:512]),
                  wait=[(rbc, s.val(rbc))] + ([lastap2] if lastap2 else []))
        prev_bc2 = cb
        for c in range(8):
            i = ts * 8 + c
            b = i % 2
            s.dma("sync", YPS[b][:], W["xin"][:, c, tsl], yps_s[b],
                  wait=[lastap2] if lastap2 and i >= 2 else ([X1M] if i == 0 else []))
            xt = s.val(yps_s[b])
            lastap2 = s.op("vector", lambda e, b=b, c=c, tsl=tsl:
                           e.scalar_tensor_tensor(
                               RMS2B[:, c, tsl], YPS[b][:], NCOL[:, 8 + c:9 + c],
                               C.bc0[:], ALU.mult, ALU.mult),
                           wait=[(yps_s[b], xt), cb])
    RMS2M = lastap2

    g_prev = None
    for tc in range(16):
        tsl = slice(tc * 128, tc * 128 + 128)
        psg = C.pb[2]

        def gmms(e, tsl=tsl):
            last = None
            for dc in range(8):
                last = e.matmul(psg[0:128, 0:4], RMS2B[:, dc, tsl], GW4[:, dc, 0:4],
                                start=(dc == 0), stop=False)
            return e.matmul(psg[0:128, 0:4], C.ones_row_bf[0:1, 0:128], GB4[0:1, 0:4],
                            start=False, stop=True)

        w = [RMS2M] + s.after(cldg) + (s.mark("vector") if tc else []) + \
            ([(rbc, s.val(rbc))] if tc == 0 else [])
        s.op("tensor", gmms, wait=w, inc=[(gmm4, 1)])
        s.op("vector", lambda e: e.tensor_reduce(RT4[:, 0:1], psg[0:128, 0:4],
                                                 AX.X, ALU.max),
             wait=[(gmm4, tc + 1)])
        g_prev = s.op("vector", lambda e, tc=tc: e.tensor_scalar(
            WEX4[:, tc, 0:4], psg[0:128, 0:4], RT4[:, 0:1], None, ALU.is_ge))
    GATED4 = [g_prev]

    g1t = 0
    g2t = 0
    nacc = 0
    w1n = 0
    g1evm = {}
    g2evm = {}
    ew1_hist = {}
    for e in range(NE):
        ew2wait = [(g2mm, g2t)] if e > 0 else [X1M] + X1W + [RMS2M]
        s.dma("gpsimd", EW2[:], W["ew2"][e], w2ld, wait=ew2wait)
        s.dma("gpsimd", EB2R[e % 2][:], W["eb2r"][e:e + 1, :], w2ld)
        w2tgt = s.val(w2ld)
        for tb in range(2):
            for ic in range(32):
                b = w1n % 2
                dmawait = [(g1mm, ew1_hist[w1n - 2])] if w1n >= 2 else []
                s.dma("gpsimd", EW1S[b][:], W["ew1"][e, ic], w1l2[b], wait=dmawait)
                w1tgt = s.val(w1l2[b])
                for tsp in range(2):
                    ps = (C.pb5 if g1t % 2 == 0 else C.pb6)[:]

                    def mms(eng, b=b, tb=tb, tsp=tsp, ps=ps):
                        last = None
                        rs = slice(tb * TB + tsp * 512, tb * TB + tsp * 512 + 512)
                        for dc in range(8):
                            last = eng.matmul(ps[:, 0:512], EW1S[b][:, dc, :],
                                              RMS2B[:, dc, rs],
                                              start=(dc == 0), stop=(dc == 7))
                        return last

                    waits = [(w1l2[b], w1tgt), RMS2M]
                    if g1t >= 2:
                        waits.append(g1evm[g1t - 2])
                    s.op("tensor", mms, wait=waits, inc=[(g1mm, 1)])
                    psl = slice(tsp * 512, tsp * 512 + 512)
                    g1evm[g1t] = s.op("scalar", lambda eng, ps=ps, ic=ic, psl=psl,
                                      e=e: eng.activation(
                                          H[:, ic, psl], ps[:, 0:512], AF.Relu,
                                          bias=EB1C[:, e, ic:ic + 1]),
                                      wait=[(g1mm, g1t + 1), (g2mm, g2t)] +
                                           s.after(cldh))
                    g1t += 1
                ew1_hist[w1n] = g1t
                w1n += 1
            HREADY = g1evm[g1t - 1]
            for tsub in range(8):
                tc = tb * 8 + tsub
                hsl = slice(tsub * 128, tsub * 128 + 128)
                for dsp in range(2):
                    dsl = slice(dsp * 512, dsp * 512 + 512)
                    ps = C.pb[0 + (g2t % 2)]

                    def mms2(eng, hsl=hsl, dsl=dsl, ps=ps, e=e):
                        last = None
                        for ic in range(32):
                            last = eng.matmul(ps, H[:, ic, hsl], EW2[:, ic, dsl],
                                              start=(ic == 0), stop=False)
                        return eng.matmul(ps, C.ones_row_bf[0:1, 0:128],
                                          EB2R[e % 2][0:1, dsl], start=False, stop=True)

                    waits = [HREADY, (w2ld, w2tgt)]
                    if g2t >= 2:
                        waits.append(g2evm[g2t - 2])
                    s.op("tensor", mms2, wait=waits, inc=[(g2mm, 1)])
                    b2 = nacc % 2
                    evw = [(g2mm, g2t + 1)] + GATED4
                    if nacc >= 2:
                        evw.append((acd[b2], s.val(acd[b2])))
                    g2evm[g2t] = s.op("scalar", lambda eng, ps=ps, b2=b2, tc=tc,
                                      e=e: eng.activation(
                                          AST[b2][:], ps, AF.Copy,
                                          scale=WEX4[:, tc, e:e + 1]),
                                      wait=evw)
                    accw = [g2evm[g2t]]
                    if e > 0 and tb == 0 and tsub == 0 and dsp == 0:
                        accw += s.after(acd[0], acd[1])
                    kw = {} if e == 0 else {"accum_op": ALU.add}
                    s.dma("gpsimd", W["acc"][:, tc, dsl], AST[b2][:], acd[b2],
                          wait=accw, **kw)
                    nacc += 1
                    g2t += 1
    EXPDONE = s.after(acd[0], acd[1])

    fi = 0
    fdvem = {}
    for ec in range(8):
        for ts in range(4):
            tsl = slice(ts * 512, ts * 512 + 512)
            ps = C.pb[2] if fi % 2 == 0 else C.pb[3]
            for j in range(4):
                tc = ts * 4 + j
                aw = list(EXPDONE) + [RMS2M]
                if fi >= 1:
                    aw.append((trf, fi))
                s.dma("sync", FAT[j][:], W["acc"][:, tc, ec * 128:ec * 128 + 128],
                      fat_s[j], wait=aw if j == 0 else [])
            trw = [(fs, s.val(fs)) for fs in fat_s] + s.after(cldh)
            if fi >= 2:
                trw.append(fdvem[fi - 2])

            def trans(eng, ps=ps):
                last = None
                for j in range(4):
                    last = eng.transpose(ps[:, j * 128:j * 128 + 128], FAT[j][:],
                                         C.eye[:, :])
                return last

            s.op("tensor", trans, wait=trw, inc=[(trf, 1)])
            b = fi % 2
            xw = list(EXPDONE) if fi < 2 else [fdvem[fi - 2]]
            s.dma("sync", FXT[b][:], x_dram[:, ec, tsl], fxt_s[b], wait=xw)
            s.dma("sync", FUT[b][:], W["xin"][:, ec, tsl], fut_s[b])
            s.op("vector", lambda e, b=b: e.tensor_tensor(
                FYT[b][:], FXT[b][:], FUT[b][:], ALU.add),
                 wait=[(fxt_s[b], s.val(fxt_s[b])), (fut_s[b], s.val(fut_s[b])),
                       (trf, fi + 1)] +
                      ([(yout, 16 * fi)] if fi >= 2 else []))
            fdvem[fi] = s.op("vector", lambda e, b=b, ps=ps: e.scalar_tensor_tensor(
                FYT[b][:], ps[:, 0:512], 1.0, FYT[b][:], ALU.bypass, ALU.add))
            s.dma("sync", y_dram[:, ec, tsl], FYT[b][:], yout, wait=[fdvem[fi]])
            fi += 1
    s.op("sync", lambda e: None, wait=[(yout, 16 * 32)])
    return NS(QF=QF, KF=KF, VTM=VTM, AOUT=AOUT, LATB=LATB, RMS1B=RMS1B, yout=yout)






def build_l1ab(nc, C, mem, x_dram, y_dram, W, tag="l1b"):
    s = C.sched
    SEGS = ((512, 1), (1024, 2), (2048, 4))
    LNB = mem.alloc(tag + "_lnb", [128, 8, T], BF16)
    CBC = mem.alloc(tag + "_cbc", [128, 24], F32)
    TRILM = mem.alloc(tag + "_trilm", [128, 128], F32)
    BR = mem.alloc(tag + "_br", [1, 4096], BF16)
    MX = mem.alloc(tag + "_mx", [128, 8], F32)
    P = mem.alloc(tag + "_pp", [128, 512], BF16)
    QG = mem.alloc(tag + "_qg", [128, 4, 64], BF16)
    KG = mem.alloc(tag + "_kg", [128, 4, 64], BF16)
    VG = mem.alloc(tag + "_vg", [128, 4, 64], BF16)
    QGF = mem.alloc(tag + "_qgf", [64, 4, 128], BF16)
    KGF = mem.alloc(tag + "_kgf", [64, 4, 128], BF16)
    PTS = mem.alloc(tag + "_pts", [128, 4, 128], BF16)
    OBT4 = mem.alloc(tag + "_obt4", [128, 4, 64], BF16)
    MST4 = mem.alloc(tag + "_mst4", [128, 4, 2], F32)
    M3T = mem.alloc(tag + "_m3t", [128, 16, 3], F32)
    MHT = mem.alloc(tag + "_mht", [128, 16], F32)
    C3T = mem.alloc(tag + "_c3t", [128, 16, 3], F32)
    S3T = mem.alloc(tag + "_s3t", [128, 16, 3], F32)
    DEN = mem.alloc(tag + "_den", [128, 16], F32)
    YH = mem.alloc(tag + "_yh", [128, 64], F32)
    ZT = mem.alloc(tag + "_zt", [128, 512], F32)
    TMS = mem.alloc(tag + "_tms", [128, 512], BF16)
    XST = [mem.alloc(tag + f"_xst{i}", [128, 512], F32) for i in range(2)]
    YST = [mem.alloc(tag + f"_yst{i}", [128, 512], F32) for i in range(2)]
    QXT = mem.alloc(tag + "_qxt", [128, 8, 1024], BF16)
    _qxt0 = QXT.manual_sbuf_range[0]
    NTH = [mem.alloc(tag + f"_nth{i}", [128, 16, 64], BF16, at=_qxt0 + 2048 * i)
           for i in range(3)]
    CW = mem.alloc(tag + "_cw", [128, 8, 1024], BF16)
    QW2 = mem.alloc(tag + "_qw2", [128, 8, 1024], BF16)
    XS = mem.alloc(tag + "_xs", [128, 8, T], F32)
    AOUT = LNB

    xld = s.sem("g"  + "_xld")
    cldh = s.sem("g"  + "_cldh")
    cldg = s.sem("g"  + "_cldg")
    wstr = s.sem("g"  + "_wstr")
    pmm = s.sem("g"  + "_pmm")
    qkvo = s.sem("g"  + "_qkvo")
    zinit = s.sem("g"  + "_zinit")
    gld = s.sem("g"  + "_gld")
    glk = s.sem("g"  + "_glk")
    glv = s.sem("g"  + "_glv")
    tgmm = s.sem("g"  + "_tgmm")
    smm = s.sem("g"  + "_smm")
    trm = s.sem("g"  + "_trm")
    avm = s.sem("g"  + "_avm")
    mso = s.sem("g"  + "_mso")
    mrd = s.sem("g"  + "_mrd")
    mrdg = s.sem("g"  + "_mrdg")
    nbo = s.sem("g"  + "_nbo")
    aomm = s.sem("g"  + "_aomm")
    pjmm = s.sem("g"  + "_pjmm")
    xin2 = [s.sem("g"  + "_xin2a"), s.sem("g"  + "_xin2b")]
    yout = s.sem("g"  + "_yout")

    for i in range(4):
        s.dma("sync", XS[:, 2 * i:2 * i + 2, :], x_dram[:, 2 * i:2 * i + 2, :], xld)
    s.dma("sync", CBC[:], W["cbc"], cldh)
    s.dma("sync", TRILM[:], W["trilm"], cldh)
    s.dma("sync", C.eye[:], W["eye"], cldh)
    s.dma("gpsimd", BR[0:1, 0:1024], W["qbr"], cldg)
    s.dma("gpsimd", BR[0:1, 1024:2048], W["kbr"], cldg)
    s.dma("gpsimd", BR[0:1, 2048:3072], W["vbr"], cldg)
    s.dma("gpsimd", BR[0:1, 3072:4096], W["obr"], cldg)
    s.dma("gpsimd", C.eyeb[:], W["eye"], cldg)

    emit_init_consts(C)
    LNM = emit_ln_sbuf(C, XS, LNB, "g_ln", ready=s.after(xld) + C.cready)

    mz0 = s.op("vector", lambda e: e.memset(ZT[:], -1e30))
    s.dma("sync", W["m3"].rearrange("(a p) c -> p a c", p=128),
          bass.AP(ZT, 0, [[512, 128], [0, 16], [1, 48]]), zinit, wait=[mz0])
    mz1 = s.op("vector", lambda e: e.memset(ZT[:], 0.0), wait=[(zinit, s.val(zinit))])
    s.dma("sync", W["s3"].rearrange("(a p) c -> p a c", p=128),
          bass.AP(ZT, 0, [[512, 128], [0, 16], [1, 48]]), zinit, wait=[mz1])
    mzb = s.op("vector", lambda e: e.memset(TMS[:], 0.0))
    for nb in (W["nb2"], W["nb3"]):
        for half in range(2):
            s.dma("sync", nb.rearrange("(a p) c -> p a c", p=128)[:, :, 512 * half:512 * half + 512],
                  bass.AP(TMS, 0, [[512, 128], [0, 16], [1, 512]]), zinit, wait=[mzb])
    ZINIT = s.after(zinit)

    ev_hist = {}
    npm = 0
    banks = [C.pb5, C.pb6]
    for part, dst in enumerate((W["qd"], W["kd"], W["vd"])):
        s.dma("gpsimd", CW[:], W["caw"][:, :, 1024 * part:1024 * (part + 1)], wstr,
              wait=[(pmm, s.val(pmm))])
        s.dma("gpsimd", QW2[:], W[("qw", "kw", "vw")[part]], wstr)
        wtgt = s.val(wstr)
        brow = BR[0:1, 1024 * part:1024 * (part + 1)]
        for tb in range(2):
            for ec in range(8):
                for ts2 in range(2):
                    tsl = slice(tb * 1024 + ts2 * 512, tb * 1024 + ts2 * 512 + 512)
                    psl = slice(ts2 * 512, ts2 * 512 + 512)
                    ps = banks[npm % 2]
                    w = [LNM, (wstr, wtgt)] + s.after(cldh)
                    if npm >= 2:
                        w.append(ev_hist[npm - 2])

                    def mm1(e, ec=ec, tsl=tsl, ps=ps):
                        last = None
                        for dc in range(8):
                            last = e.matmul(ps[:, 0:512], CW[:, dc, ec * 128:ec * 128 + 128],
                                            LNB[:, dc, tsl], start=(dc == 0), stop=(dc == 7))
                        return last

                    s.op("tensor", mm1, wait=w, inc=[(pmm, 1)])
                    ev_hist[npm] = s.op("vector", lambda e, ps=ps, ec=ec, psl=psl,
                                        part=part: e.tensor_scalar(
                                            QXT[:, ec, psl], ps[:, 0:512],
                                            CBC[:, 8 * part + ec:8 * part + ec + 1],
                                            None, ALU.add),
                                        wait=[(pmm, npm + 1)])
                    npm += 1
            QXM = ev_hist[npm - 1]
            for tc8 in range(8):
                tcl = slice(tc8 * 128, tc8 * 128 + 128)
                for es in range(2):
                    esl = slice(es * 512, es * 512 + 512)
                    ps = banks[npm % 2]
                    w = [QXM] + s.after(cldg)
                    if npm >= 2:
                        w.append(ev_hist[npm - 2])

                    def mm2(e, tcl=tcl, esl=esl, ps=ps, brow=brow):
                        last = None
                        for dc in range(8):
                            last = e.matmul(ps[:, 0:512], QXT[:, dc, tcl],
                                            QW2[:, dc, esl], start=(dc == 0), stop=False)
                        return e.matmul(ps[:, 0:512], C.ones_row_bf[0:1, 0:128],
                                        brow[:, esl], start=False, stop=True)

                    s.op("tensor", mm2, wait=w, inc=[(pmm, 1)])
                    sc = 0.125 if part == 0 else 1.0
                    mev = s.op("scalar", lambda e, ps=ps, sc=sc: e.activation(
                        TMS[:], ps[:, 0:512], AF.Copy, scale=sc),
                        wait=[(pmm, npm + 1), (qkvo, s.val(qkvo)), (zinit, s.val(zinit))])
                    ev_hist[npm] = mev
                    tglob = tb * 1024 + tc8 * 128
                    s.dma("sync", dst.rearrange("(a p) c -> p a c", p=128)[:, tglob // 128, esl],
                          TMS[:], qkvo, wait=[mev])
                    npm += 1
    QKVW = s.after(qkvo)

    aev_prev = None
    exp_prev = None
    pts_prev = None
    obe_prev = None
    prev_merge = None
    for h in range(16):
        for b, (w_, r) in enumerate(SEGS):
            nseg = T // w_
            off = h % r
            for seg in range(nseg):
                base = seg * w_ + off
                s.dma("sync", QG[:],
                      bass.AP(W["qd"].tensor, base * 1024 + h * 64,
                              [[r * 1024, 128], [128 * r * 1024, 4], [1, 64]]),
                      gld, wait=[(tgmm, s.val(tgmm))] + QKVW)
                s.dma("scalar", KG[:],
                      bass.AP(W["kd"].tensor, base * 1024 + h * 64,
                              [[r * 1024, 128], [128 * r * 1024, 4], [1, 64]]),
                      glk, wait=[(tgmm, s.val(tgmm))] + QKVW)
                s.dma("gpsimd", VG[:],
                      bass.AP(W["vd"].tensor, base * 1024 + h * 64,
                              [[r * 1024, 128], [128 * r * 1024, 4], [1, 64]]),
                      glv, wait=[(avm, s.val(avm))] + QKVW)
                gtgt = s.val(gld)
                ktgt = s.val(glk)
                vtgt = s.val(glv)
                trw = [(gld, gtgt), (glk, ktgt)] + s.after(cldg)
                if exp_prev is not None:
                    trw.append(exp_prev)
                if pts_prev is not None:
                    trw.append(pts_prev)

                def tg(e):
                    last = None
                    for j in range(4):
                        last = e.transpose(C.pbTa[0:64, j * 128:j * 128 + 128],
                                           QG[:, j, :], C.eyeb[:, :])
                    for j in range(4):
                        last = e.transpose(C.pbTb[0:64, j * 128:j * 128 + 128],
                                           KG[:, j, :], C.eyeb[:, :])
                    return last

                s.op("tensor", tg, wait=trw, inc=[(tgmm, 1)])
                tgv = s.val(tgmm)
                mq = s.op("vector", lambda e: e.tensor_copy(
                    QGF[0:64, :, :],
                    C.pbTa[0:64, 0:512].rearrange("p (a b) -> p a b", a=4)),
                     wait=[(tgmm, tgv)] + ([exp_prev] if exp_prev else []))
                mk = s.op("vector", lambda e: e.tensor_copy(
                    KGF[0:64, :, :],
                    C.pbTb[0:64, 0:512].rearrange("p (a b) -> p a b", a=4)))
                for qt in range(4):
                    kw_ = (qt + 1) * 128
                    wts = [mk]
                    if exp_prev is not None:
                        wts.append(exp_prev)
                    s.op("tensor", lambda e, qt=qt, kw_=kw_: e.matmul(
                        C.pbS[:, 0:kw_], QGF[0:64, qt, :],
                        KGF[0:64, :, :].rearrange("p a b -> p (a b)")[:, 0:kw_],
                        start=True, stop=True),
                         wait=wts, inc=[(smm, 1)])
                    dstart = qt * 128
                    s.op("vector", lambda e, dstart=dstart: e.tensor_tensor(
                        C.pbS[:, dstart:dstart + 128], C.pbS[:, dstart:dstart + 128],
                        TRILM[:], ALU.add),
                         wait=[(smm, s.val(smm))] + s.after(cldh))
                    s.op("vector", lambda e, kw_=kw_: e.tensor_reduce(
                        MX[:, 0:1], C.pbS[:, 0:kw_], AX.X, ALU.max))
                    mneg = s.op("vector", lambda e: e.tensor_scalar(
                        MX[:, 1:2], MX[:, 0:1], -1.0, None, ALU.mult))
                    exw = [mneg]
                    if pts_prev is not None:
                        exw.append((trm, s.val(trm)))
                    exp_prev = s.op("scalar", lambda e, kw_=kw_: e.activation(
                        P[:, 0:kw_], C.pbS[:, 0:kw_], AF.Exp, bias=MX[:, 1:2],
                        accum_out=MX[:, 2:3]), wait=exw)
                    mcol = s.op("vector", lambda e, qt=qt: e.tensor_copy(
                        MST4[:, qt, 0:1], MX[:, 0:1]),
                                wait=[exp_prev] +
                                     ([(mso, s.val(mso))] if qt == 0 else []))
                    scol = s.op("vector", lambda e, qt=qt: e.tensor_copy(
                        MST4[:, qt, 1:2], MX[:, 2:3]))
                    if qt == 3:
                        s.dma("gpsimd", bass.AP(W["m3"].tensor, base * 48 + h * 3 + b,
                                                [[48 * r, 128], [128 * r * 48, 4],
                                                 [1, 1]]), MST4[:, :, 0:1],
                              mso, wait=[scol, (zinit, s.val(zinit))],
                              allow_slow_non_contiguous=True)
                        s.dma("gpsimd", bass.AP(W["s3"].tensor, base * 48 + h * 3 + b,
                                                [[48 * r, 128], [128 * r * 48, 4],
                                                 [1, 1]]), MST4[:, :, 1:2], mso,
                              allow_slow_non_contiguous=True)
                    trw2 = [exp_prev]
                    if pts_prev is not None:
                        trw2.append(pts_prev)

                    def trp(e, qt=qt):
                        last = None
                        for j in range(qt + 1):
                            last = e.transpose(
                                C.pbTa[:, 512 + j * 128:512 + j * 128 + 128],
                                P[:, j * 128:j * 128 + 128], C.eyeb[:, :])
                        return last

                    s.op("tensor", trp, wait=trw2, inc=[(trm, 1)])
                    ptm = None
                    for j in range(qt + 1):
                        cw_ = [(trm, s.val(trm))]
                        if obe_prev is not None:
                            cw_.append(obe_prev)
                        ptm = s.op("vector", lambda e, j=j: e.tensor_copy(
                            PTS[:, j, :], C.pbTa[:, 512 + j * 128:512 + j * 128 + 128]),
                            wait=cw_)
                    pts_prev = ptm

                    def avb(e, qt=qt):
                        last = None
                        for j in range(qt + 1):
                            last = e.matmul(C.pb6[:, 0:64], PTS[:, j, :],
                                            VG[:, j, :], start=(j == 0), stop=(j == qt))
                        return last

                    s.op("tensor", avb, wait=[pts_prev, (glv, vtgt)] +
                         ([obe_prev] if obe_prev else []), inc=[(avm, 1)])
                    obe_prev = s.op("scalar", lambda e, qt=qt: e.copy(
                        OBT4[:, qt, :], C.pb6[:, 0:64]),
                                    wait=[(avm, s.val(avm))] +
                                         ([(nbo, s.val(nbo))] if qt == 0 else []))
                    if qt == 3:
                        s.dma("scalar", bass.AP(W[("nb1", "nb2", "nb3")[b]].tensor,
                                                base * 1024 + h * 64,
                                                [[r * 1024, 128],
                                                 [128 * r * 1024, 4], [1, 64]]),
                              OBT4[:], nbo,
                              wait=[obe_prev, (zinit, s.val(zinit))])
        NBW = s.after(nbo, mso)
        mrw = list(NBW) + ([prev_merge] if prev_merge else [])
        s.dma("sync", M3T[:], bass.AP(W["m3"].tensor, h * 3,
                                      [[48, 128], [48 * 128, 16], [1, 3]]),
              mrd, wait=mrw)
        s.dma("sync", S3T[:], bass.AP(W["s3"].tensor, h * 3,
                                      [[48, 128], [48 * 128, 16], [1, 3]]), mrd)
        m3v = s.val(mrd)
        mh = s.op("vector", lambda e: e.tensor_reduce(MHT[:], M3T[:], AX.X, ALU.max),
                  wait=[(mrd, m3v)])
        mhb = bass.AP(MHT, 0, [[16, 128], [1, 16], [0, 3]])
        mdf = s.op("vector", lambda e: e.tensor_tensor(C3T[:], M3T[:], mhb, ALU.subtract))
        mc3 = s.op("scalar", lambda e: e.activation(C3T[:], C3T[:], AF.Exp), wait=[mdf])
        s.op("vector", lambda e: e.tensor_tensor(S3T[:], C3T[:], S3T[:], ALU.mult),
             wait=[mc3])
        mden = s.op("vector", lambda e: e.tensor_reduce(DEN[:], S3T[:], AX.X, ALU.add))
        s.op("scalar", lambda e: e.activation(DEN[:], DEN[:], AF.Ln), wait=[mden])
        mrden = s.op("scalar", lambda e: e.activation(DEN[:], DEN[:], AF.Exp, scale=-1.0))
        prev_tc_read = None
        ntw = list(NBW) + ([prev_merge] if prev_merge else [])
        for b, q, sm in ((0, "sync", mrd), (1, "gpsimd", mrdg), (2, "sync", mrd)):
            s.dma(q, NTH[b][:],
                  bass.AP(W[("nb1", "nb2", "nb3")[b]].tensor, h * 64,
                          [[1024, 128], [128 * 1024, 16], [1, 64]]),
                  sm, wait=ntw)
        NTV = s.after(mrd, mrdg)
        for tc in range(16):
            NT = [NTH[0][:, tc, :], NTH[1][:, tc, :], NTH[2][:, tc, :]]
            s.op("vector", lambda e, tc=tc, NT=NT: e.tensor_scalar(
                YH[:], NT[0], C3T[:, tc, 0:1], None, ALU.mult),
                 wait=NTV + [mrden, (aomm, s.val(aomm))])
            for b in (1, 2):
                s.op("vector", lambda e, b=b, tc=tc, NT=NT: e.tensor_scalar(
                    NT[b], NT[b], C3T[:, tc, b:b + 1], None, ALU.mult))
                s.op("vector", lambda e, b=b, NT=NT: e.tensor_tensor(
                    YH[:], YH[:], NT[b], ALU.add))
            myh = s.op("vector", lambda e, tc=tc: e.tensor_scalar(
                YH[:], YH[:], DEN[:, tc:tc + 1], None, ALU.mult))
            prev_tc_read = myh
            prev_merge = myh
            s.op("tensor", lambda e: e.transpose(C.pb4[0:64, 0:128], YH[:], C.eye[:, :]),
                 wait=[myh] + s.after(cldh) + ([aev_prev] if aev_prev else []),
                 inc=[(aomm, 1)])
            aev_prev = s.op("scalar", lambda e, h=h, tc=tc: e.copy(
                AOUT[(h % 2) * 64:(h % 2) * 64 + 64, h // 2, tc * 128:tc * 128 + 128],
                C.pb4[0:64, 0:128]), wait=[(aomm, s.val(aomm))])
    ATTND = aev_prev

    s.dma("gpsimd", CW[:], W["ow"], wstr, wait=[(pmm, s.val(pmm))])
    OWR = s.after(wstr)
    pi = 0
    pdvem = {}
    for ec in range(8):
        esl = slice(ec * 128, ec * 128 + 128)
        for ts in range(4):
            tsl = slice(ts * 512, ts * 512 + 512)
            psobj = C.pb5 if pi % 2 == 0 else None
            psap = C.pb5[:, 0:512] if pi % 2 == 0 else C.pbS[:, 0:512]
            w = [ATTND] + OWR + s.after(cldg)
            if pi >= 2:
                w.append(pdvem[pi - 2])

            def mmsp(e, esl=esl, tsl=tsl, psap=psap):
                last = None
                for dc in range(8):
                    last = e.matmul(psap, CW[:, dc, esl], AOUT[:, dc, tsl],
                                    start=(dc == 0), stop=False)
                return e.matmul(psap, BR[0:1, 3072 + esl.start:3072 + esl.stop],
                                C.ones_row_bf[0:1, 0:512],
                                start=False, stop=True)

            s.op("tensor", mmsp, wait=w, inc=[(pjmm, 1)])
            b = pi % 2
            xw = [ATTND] if pi < 2 else [pdvem[pi - 2]]
            s.dma("sync", XST[b][:], x_dram[:, ec, tsl], xin2[b], wait=xw)
            pdvem[pi] = s.op("vector", lambda e, b=b, psap=psap: e.scalar_tensor_tensor(
                YST[b][:], psap, 1.0, XST[b][:], ALU.bypass, ALU.add),
                 wait=[(pjmm, pi + 1), (xin2[b], s.val(xin2[b]))] +
                      ([(yout, 16 * pi)] if pi >= 2 else []))
            s.dma("sync", y_dram[:, ec, tsl], YST[b][:], yout, wait=[pdvem[pi]])
            pi += 1
    s.op("sync", lambda e: None, wait=[(yout, 16 * 32)])
    return NS(AOUT=AOUT, yout=yout)



_PROGS = {}


def _dram_ins(nc, specs):
    return {k: nc.dram_tensor(k, shp, F32, kind="ExternalInput").ap()
            for k, shp in specs}


def _mk_prog(name):
    if name in _PROGS:
        return _PROGS[name]
    nc = bass.Bass("TRN2", target_bir_lowering=False, detect_race_conditions=False)
    x = nc.dram_tensor("x", [128, 8, T], F32, kind="ExternalInput").ap()
    y = nc.dram_tensor("y", [128, 8, T], F32, kind="ExternalOutput").ap()
    if name == "l0aa":
        W = _dram_ins(nc, [("wr", [128, 8, 1024]), ("wk", [128, 8, 1024]),
                           ("wv", [128, 8, 1024]), ("wo", [128, 8, 1024]),
                           ("cols", [128, 56]), ("hmask", [128, 2]),
                           ("hmaskT", [2, 128])])
    elif name == "l0ffn":
        W = _dram_ins(nc, [("ew1", [9, 32, 128, 8, 128]), ("ew2", [9, 128, 32, 1024]),
                           ("eb1c", [128, 9, 32]), ("eb2r", [9, 1024]),
                           ("gw", [128, 8, 8]), ("gb", [1, 8]), ("eye", [128, 128])])
        W["acc"] = nc.dram_tensor("acc", [128, 16, 1024], F32).ap()
    elif name == "l0ab":
        W = _dram_ins(nc, [("wqkv", [128, 8, 384]), ("wproj", [128, 8, 128]),
                           ("bp", [128, 8]), ("trilm", [128, 128]),
                           ("eye", [128, 128])])
    elif name == "l1aa":
        W = _dram_ins(nc, [("wq", [128, 8, 256]), ("wk", [128, 8, 256]),
                           ("wv", [128, 8, 1024]), ("wo", [128, 8, 1024]),
                           ("tril01", [128, 128]), ("triu01", [128, 128]),
                           ("iota1", [128, 1]), ("eye", [128, 128])])
    elif name == "l1ffn":
        W = _dram_ins(nc, [("wqt", [128, 8, 1024]), ("wdkvt", [128, 8, 512]),
                           ("wukt", [128, 4, 1024]), ("wuvt", [128, 4, 1024]),
                           ("wot", [128, 8, 1024]),
                           ("n1c", [128, 8]), ("n2c", [128, 8]),
                           ("lngc", [128, 4]), ("lnbc", [128, 4]),
                           ("cosb", [128, T]), ("sinb", [128, T]),
                           ("swapm", [128, 128]), ("trilm", [128, 128]),
                           ("eye", [128, 128]), ("gw4", [128, 8, 4]), ("gb4", [1, 4]),
                           ("ew1", [4, 32, 128, 8, 128]), ("ew2", [4, 128, 32, 1024]),
                           ("eb1c", [128, 4, 32]), ("eb2r", [4, 1024])])
        W["xin"] = nc.dram_tensor("xin", [128, 8, T], F32).ap()
        W["acc"] = nc.dram_tensor("acc", [128, 16, 1024], F32).ap()
    elif name == "l1ab":
        W = _dram_ins(nc, [("caw", [128, 8, 3072]), ("cbc", [128, 24]),
                           ("qw", [128, 8, 1024]), ("kw", [128, 8, 1024]),
                           ("vw", [128, 8, 1024]), ("ow", [128, 8, 1024]),
                           ("qbr", [1, 1024]), ("kbr", [1, 1024]),
                           ("vbr", [1, 1024]), ("obr", [1, 1024]),
                           ("trilm", [128, 128]), ("eye", [128, 128])])
        for k in ("qd", "kd", "vd", "nb1", "nb2", "nb3"):
            W[k] = nc.dram_tensor(k, [T, 1024], BF16).ap()
        for k in ("m3", "s3"):
            W[k] = nc.dram_tensor(k, [T, 48], F32).ap()
    C, mem = setup_prog(nc)
    {"l0aa": build_l0aa, "l0ffn": build_l0ffn, "l0ab": build_l0ab,
     "l1aa": build_l1aa, "l1ffn": build_l1ffn, "l1ab": build_l1ab}[name](
        nc, C, mem, x, y, W)
    finish_prog(C)
    _PROGS[name] = nc
    return nc


def _prep_weights(inp):
    f32 = np.float32
    q = np.arange(128)[:, None]
    j = np.arange(128)[None, :]
    eye = np.eye(128, dtype=f32)
    trilm = np.where(j <= q, 0.0, -1e30).astype(f32)
    WS = {}
    cols = np.concatenate([
        host_col(inp["L0_attn_a_time_mix_r"]), host_col(inp["L0_attn_a_time_mix_k"]),
        host_col(inp["L0_attn_a_time_mix_v"]),
        host_col(inp["L0_attn_a_time_decay"].reshape(-1)),
        host_col(inp["L0_attn_a_time_first"].reshape(-1)),
        host_col(inp["L0_attn_a_gn_weight"]), host_col(inp["L0_attn_a_gn_bias"]),
    ], axis=1).astype(f32)
    hmask = np.zeros((128, 2), f32)
    hmask[:64, 0] = 1
    hmask[64:, 1] = 1
    WS["l0aa"] = {"wr": host_w(inp["L0_attn_a_W_r"].T), "wk": host_w(inp["L0_attn_a_W_k"].T),
                  "wv": host_w(inp["L0_attn_a_W_v"].T), "wo": host_w(inp["L0_attn_a_W_o"].T),
                  "cols": cols, "hmask": hmask,
                  "hmaskT": np.ascontiguousarray(hmask.T)}
    ew1_l, ew2_l, eb1_l, eb2_l = [], [], [], []
    for e in range(9):
        if e < 8:
            w1, b1 = inp["L0_ffn_ew1"][e], inp["L0_ffn_eb1"][e]
            w2, b2 = inp["L0_ffn_ew2"][e], inp["L0_ffn_eb2"][e]
        else:
            w1, b1 = inp["L0_ffn_sh_w1"], inp["L0_ffn_sh_b1"]
            w2, b2 = inp["L0_ffn_sh_w2"], inp["L0_ffn_sh_b2"]
        hw = host_w(w1).reshape(128, 8, 32, 128).transpose(2, 0, 1, 3)
        ew1_l.append(np.ascontiguousarray(hw))
        ew2_l.append(host_w(w2))
        eb1_l.append(host_col(b1))
        eb2_l.append(b2)
    WS["l0ffn"] = {"ew1": np.stack(ew1_l).astype(f32), "ew2": np.stack(ew2_l).astype(f32),
                   "eb1c": np.stack(eb1_l, axis=1).astype(f32),
                   "eb2r": np.stack(eb2_l).astype(f32),
                   "gw": host_w(inp["L0_ffn_gate_w"]),
                   "gb": inp["L0_ffn_gate_b"][None, :].astype(f32), "eye": eye}
    WS["l0ab"] = {"wqkv": np.ascontiguousarray(inp["L0_attn_b_w_qkv"].transpose(1, 0, 2)).astype(f32),
                  "wproj": np.ascontiguousarray(inp["L0_attn_b_w_proj"].transpose(1, 0, 2)).astype(f32),
                  "bp": host_col(inp["L0_attn_b_b_proj"]), "trilm": trilm, "eye": eye}
    WS["l1aa"] = {"wq": host_w(inp["L1_attn_a_Wq"]), "wk": host_w(inp["L1_attn_a_Wk"]),
                  "wv": host_w(inp["L1_attn_a_Wv"]), "wo": host_w(inp["L1_attn_a_Wo"]),
                  "tril01": (j <= q).astype(f32), "triu01": (j > q).astype(f32),
                  "iota1": (np.arange(128, dtype=f32) + 1.0)[:, None], "eye": eye}
    p = np.arange(128)
    j64 = p % 64
    jj = np.where(j64 < 32, j64, j64 - 32)
    inv = 1.0 / (10000.0 ** (jj / 32.0))
    tt = np.arange(T)
    ang = tt[None, :] * inv[:, None]
    cosb = np.cos(ang).astype(f32)
    sinb = (np.where(j64 < 32, -1.0, 1.0)[:, None] * np.sin(ang)).astype(f32)
    swapm = np.zeros((128, 128), f32)
    for m in range(128):
        sw = m + 32 if (m % 64) < 32 else m - 32
        swapm[sw, m] = 1.0
    ew1_l, ew2_l, eb1_l = [], [], []
    for e in range(4):
        hw = host_w(inp["L1_ffn_W1"][e].T).reshape(128, 8, 32, 128).transpose(2, 0, 1, 3)
        ew1_l.append(np.ascontiguousarray(hw))
        ew2_l.append(host_w(inp["L1_ffn_W2"][e].T))
        eb1_l.append(host_col(inp["L1_ffn_b1"][e]))
    WS["l1ffn"] = {"wqt": host_w(inp["L1_ffn_Wq"].T), "wdkvt": host_w(inp["L1_ffn_Wdkv"].T),
                   "wukt": host_w(inp["L1_ffn_Wuk"].T), "wuvt": host_w(inp["L1_ffn_Wuv"].T),
                   "wot": host_w(inp["L1_ffn_Wo"].T),
                   "n1c": host_col(inp["L1_ffn_norm1_w"]), "n2c": host_col(inp["L1_ffn_norm2_w"]),
                   "lngc": host_col(inp["L1_ffn_ln_g"]), "lnbc": host_col(inp["L1_ffn_ln_b"]),
                   "cosb": cosb, "sinb": sinb, "swapm": swapm, "trilm": trilm, "eye": eye,
                   "gw4": host_w(inp["L1_ffn_gate_w"].T),
                   "gb4": inp["L1_ffn_gate_b"][None, :].astype(f32),
                   "ew1": np.stack(ew1_l).astype(f32), "ew2": np.stack(ew2_l).astype(f32),
                   "eb1c": np.stack(eb1_l, axis=1).astype(f32),
                   "eb2r": inp["L1_ffn_b2"].astype(f32)}
    WS["l1ab"] = {"caw": host_w(inp["L1_attn_b_c_attn_w"]),
                  "cbc": host_col(inp["L1_attn_b_c_attn_b"]),
                  "qw": host_w(inp["L1_attn_b_q_w"]), "kw": host_w(inp["L1_attn_b_k_w"]),
                  "vw": host_w(inp["L1_attn_b_v_w"]), "ow": host_w(inp["L1_attn_b_o_w"]),
                  "qbr": inp["L1_attn_b_q_b"][None, :].astype(f32),
                  "kbr": inp["L1_attn_b_k_b"][None, :].astype(f32),
                  "vbr": inp["L1_attn_b_v_b"][None, :].astype(f32),
                  "obr": inp["L1_attn_b_o_b"][None, :].astype(f32),
                  "trilm": trilm, "eye": eye}
    return WS


def kernel(**inputs):
    inp = {k: np.asarray(v, np.float32) for k, v in inputs.items()}
    WS = _prep_weights(inp)
    x = inp["L0_attn_a_x"]
    cores = [host_fm(np.ascontiguousarray(x[i])) for i in range(8)]
    for name in ("l0aa", "l0ffn", "l0ab", "l0ffn", "l1aa", "l1ffn", "l1ab", "l1ffn"):
        nc = _mk_prog(name)
        in_maps = [dict(WS[name], x=cores[i]) for i in range(8)]
        res = run_bass_kernel_spmd(nc, in_maps, core_ids=list(range(8)))
        cores = [np.asarray(r["y"]) for r in res.results]
    return np.stack([host_fm_inv(c) for c in cores]).astype(np.float32)



_GROUP_SPECS = {
    "l0aa": [("wr", [128, 8, 1024]), ("wk", [128, 8, 1024]), ("wv", [128, 8, 1024]),
             ("wo", [128, 8, 1024]), ("cols", [128, 56]), ("hmask", [128, 2]),
             ("hmaskT", [2, 128])],
    "l0ffn": [("ew1", [9, 32, 128, 8, 128]), ("ew2", [9, 128, 32, 1024]),
              ("eb1c", [128, 9, 32]), ("eb2r", [9, 1024]), ("gw", [128, 8, 8]),
              ("gb", [1, 8])],
    "l0ab": [("wqkv", [128, 8, 384]), ("wproj", [128, 8, 128]), ("bp", [128, 8])],
    "l1aa": [("wq", [128, 8, 256]), ("wk", [128, 8, 256]), ("wv", [128, 8, 1024]),
             ("wo", [128, 8, 1024]), ("tril01", [128, 128]), ("triu01", [128, 128]),
             ("iota1", [128, 1])],
    "l1ffn": [("wqt", [128, 8, 1024]), ("wdkvt", [128, 8, 512]),
              ("wukt", [128, 4, 1024]), ("wuvt", [128, 4, 1024]),
              ("wot", [128, 8, 1024]), ("n1c", [128, 8]), ("n2c", [128, 8]),
              ("lngc", [128, 4]), ("lnbc", [128, 4]), ("cosb", [128, T]),
              ("sinb", [128, T]), ("swapm", [128, 128]), ("gw4", [128, 8, 4]),
              ("gb4", [1, 4]), ("ew1", [4, 32, 128, 8, 128]),
              ("ew2", [4, 128, 32, 1024]), ("eb1c", [128, 4, 32]),
              ("eb2r", [4, 1024])],
    "l1ab": [("caw", [128, 8, 3072]), ("cbc", [128, 24]), ("qw", [128, 8, 1024]),
             ("kw", [128, 8, 1024]), ("vw", [128, 8, 1024]), ("ow", [128, 8, 1024]),
             ("qbr", [1, 1024]), ("kbr", [1, 1024]), ("vbr", [1, 1024]),
             ("obr", [1, 1024])],
}
_NEED_EYE = {"l0ffn", "l0ab", "l1aa", "l1ffn", "l1ab"}
_NEED_TRILM = {"l0ab", "l1ffn", "l1ab"}
_SEQ = [("l0aa", None), ("l0ffn", None), ("l0ab", None), ("l0ffn", None),
        ("l1aa", None), ("l1ffn", None), ("l1ab", None), ("l1ffn", None)]


def _mk_fused():
    if "fused" in _PROGS:
        return _PROGS["fused"]
    nc = bass.Bass("TRN2", target_bir_lowering=False, detect_race_conditions=False)
    x0 = nc.dram_tensor("x", [128, 8, T], F32, kind="ExternalInput").ap()
    y = nc.dram_tensor("y", [128, 8, T], F32, kind="ExternalOutput").ap()
    xb = [nc.dram_tensor(f"xb{i}", [128, 8, T], F32).ap() for i in range(7)]
    bufs = [x0] + xb + [y]
    EYE = nc.dram_tensor("eye", [128, 128], F32, kind="ExternalInput").ap()
    TRI = nc.dram_tensor("trilm", [128, 128], F32, kind="ExternalInput").ap()
    Wg = {}
    for g, specs in _GROUP_SPECS.items():
        Wg[g] = {k: nc.dram_tensor(f"{g}_{k}", shp, F32, kind="ExternalInput").ap()
                 for k, shp in specs}
        if g in _NEED_EYE:
            Wg[g]["eye"] = EYE
        if g in _NEED_TRILM:
            Wg[g]["trilm"] = TRI
    Wg["l0ffn"]["acc"] = nc.dram_tensor("l0f_acc", [128, 16, 1024], F32).ap()
    Wg["l1ffn"]["acc"] = nc.dram_tensor("l1f_acc", [128, 16, 1024], F32).ap()
    Wg["l1ffn"]["xin"] = nc.dram_tensor("l1f_xin", [128, 8, T], F32).ap()
    for k in ("qd", "kd", "vd", "nb1", "nb2", "nb3"):
        Wg["l1ab"][k] = nc.dram_tensor("l1b_" + k, [T, 1024], BF16).ap()
    for k in ("m3", "s3"):
        Wg["l1ab"][k] = nc.dram_tensor("l1b_" + k, [T, 48], F32).ap()

    builders = {"l0aa": build_l0aa, "l0ffn": build_l0ffn, "l0ab": build_l0ab,
                "l1aa": build_l1aa, "l1ffn": build_l1ffn, "l1ab": build_l1ab}
    C, mem0 = setup_prog(nc)
    for k, (g, _) in enumerate(_SEQ):
        m = Mem(nc)
        m.off = C.common_end
        builders[g](nc, C, m, bufs[k], bufs[k + 1], Wg[g], tag=f"s{k}")
        if k < 7:
            C.sched.barrier()
    finish_prog(C)
    _PROGS["fused"] = nc
    return nc


def kernel(**inputs):
    inp = {k: np.asarray(v, np.float32) for k, v in inputs.items()}
    WS = _prep_weights(inp)
    x = inp["L0_attn_a_x"]
    nc = _mk_fused()
    im = {"eye": WS["l0ab"]["eye"], "trilm": WS["l0ab"]["trilm"]}
    for g in _GROUP_SPECS:
        for k, v in WS[g].items():
            if k in ("eye", "trilm"):
                continue
            im[f"{g}_{k}"] = v
    in_maps = [dict(im, x=host_fm(np.ascontiguousarray(x[i]))) for i in range(8)]
    res = run_bass_kernel_spmd(nc, in_maps, core_ids=list(range(8)))
    return np.stack([host_fm_inv(np.asarray(r["y"])) for r in res.results]).astype(np.float32)
